# version 01
# baseline (speedup 1.0000x reference)
import numpy as np
import ml_dtypes
import concourse.bass as bass
import concourse.mybir as mybir
from concourse.bass_utils import run_bass_kernel_spmd

F32 = mybir.dt.float32
BF16 = mybir.dt.bfloat16
AF = mybir.ActivationFunctionType
ALU = mybir.AluOpType
AX = mybir.AxisListType
BF16NP = ml_dtypes.bfloat16

B, S, D = 2, 4096, 2048
NCORES = 8
T = 1024
NEG = -1.0e30
MAGIC = 12582912.0
LN_EPS = 1e-5


class Sem:
    __slots__ = ("name", "n", "h")

    def __init__(self, name):
        self.name, self.n, self.h = name, 0, None


class Sched:
    def __init__(self, nc):
        self.nc = nc
        self.steps = []
        self.sems = {}
        self.cmp = self.sem("cmp")
        self.pe = self.sem("pe")

    def sem(self, name):
        s = self.sems.get(name)
        if s is None:
            s = Sem(name)
            self.sems[name] = s
        return s

    def st(self, eng, fn, waits=(), incs=(), auto=True, engine_inc=True):
        ws = []
        if auto:
            if self.cmp.n:
                ws.append((self.cmp, self.cmp.n))
            if self.pe.n:
                ws.append((self.pe, self.pe.n))
        for w in waits:
            ws.append((w, w.n) if isinstance(w, Sem) else w)
        cs = [(c, 1) if isinstance(c, Sem) else c for c in incs]
        if engine_inc:
            cs.append((self.pe, 1) if eng == "tensor" else (self.cmp, 1))
        for (sm, n) in cs:
            sm.n += n
        self.steps.append((eng, fn, ws, cs))

    def ops(self, eng, fns, waits=()):
        for i, fn in enumerate(fns):
            self.st(eng, fn, waits=waits if i == 0 else ())

    def dma(self, eng, out_ap, in_ap, sem, waits=(), accum=False):
        def fn(e, out_ap=out_ap, in_ap=in_ap, accum=accum):
            if accum:
                return e.dma_start(out=out_ap, in_=in_ap, accum_op=mybir.AluOpType.add)
            return e.dma_start(out=out_ap, in_=in_ap)
        self.st(eng, fn, waits=waits, incs=[(sem, 16)], engine_inc=False)

    def run(self):
        nc = self.nc
        for s in self.sems.values():
            s.h = nc.alloc_semaphore(s.name)
        with nc.Block() as block:
            for engname in ("sync", "gpsimd", "tensor", "vector", "scalar"):
                mine = [s for s in self.steps if s[0] == engname]
                if not mine:
                    continue

                def go(eng, mine=mine):
                    for _, fn, ws, cs in mine:
                        for (sm, v) in ws:
                            if v > 0:
                                eng.wait_ge(sm.h, v)
                        last = fn(eng)
                        for (sm, n) in cs:
                            tgt = last if last is not None else eng.nop()
                            tgt.then_inc(sm.h, n)
                            last = None
                getattr(block, engname)(go)


def accmm(sc, psum_ap, pairs, waits=()):
    def fn(t):
        last = None
        n = len(pairs)
        for i, (l, r) in enumerate(pairs):
            last = t.matmul(psum_ap, l, r, start=(i == 0), stop=(i == n - 1))
        return last
    sc.st("tensor", fn, waits=waits)



def pope_tables():
    half = 32
    theta = 1.0 / (10000.0 ** (2.0 * np.arange(half, dtype=np.float64) / 64.0))
    ang = np.arange(S, dtype=np.float64)[:, None] * theta[None, :]
    return (np.cos(ang).astype(np.float32).T.copy(),
            np.sin(ang).astype(np.float32).T.copy())


def rope1_tables():
    inv = 1.0 / (10000.0 ** (np.arange(0, 128, 2, dtype=np.float64) / 128.0))
    fr = np.arange(S, dtype=np.float64)[:, None] * inv[None, :]
    emb = np.concatenate([fr, fr], axis=1)
    return (np.cos(emb).astype(np.float32).T.copy(),
            np.sin(emb).astype(np.float32).T.copy())



def build_launch1():
    nc = bass.Bass("TRN2", target_bir_lowering=False)
    x = nc.dram_tensor("x_fm", [D, T], F32, kind="ExternalInput").ap()
    w_dkv = nc.dram_tensor("w_dkv", [D, 512], F32, kind="ExternalInput").ap()
    w_dq = nc.dram_tensor("w_dq", [D, 1024], F32, kind="ExternalInput").ap()
    w_kr = nc.dram_tensor("w_kr", [D, 64], F32, kind="ExternalInput").ap()
    gw_t = nc.dram_tensor("gw_t", [D, 8], F32, kind="ExternalInput").ap()
    delta = nc.dram_tensor("delta", [1, 32], F32, kind="ExternalInput").ap()
    cos0s = nc.dram_tensor("cos0s", [32, T], F32, kind="ExternalInput").ap()
    sin0s = nc.dram_tensor("sin0s", [32, T], F32, kind="ExternalInput").ap()

    o_xn = nc.dram_tensor("xn_fm", [D, T], BF16, kind="ExternalOutput").ap()
    o_ckv = nc.dram_tensor("ckv_fm", [512, T], BF16, kind="ExternalOutput").ap()
    o_cq = nc.dram_tensor("cq_fm", [1024, T], BF16, kind="ExternalOutput").ap()
    o_kr = nc.dram_tensor("kr_fm", [64, T], BF16, kind="ExternalOutput").ap()
    o_log = nc.dram_tensor("logits", [T, 8], F32, kind="ExternalOutput").ap()

    sc = Sched(nc)
    ld = sc.sem("ld")
    ldw = sc.sem("ldw")
    stv = sc.sem("stv")

    sx = nc.alloc_sbuf_tensor("sx", [128, 16, T], F32)
    sxn = nc.alloc_sbuf_tensor("sxn", [128, 16, T], BF16)
    swdkv = nc.alloc_sbuf_tensor("swdkv", [128, 16, 512], BF16)
    swdq = nc.alloc_sbuf_tensor("swdq", [128, 16, 512], BF16)
    swkr = nc.alloc_sbuf_tensor("swkr", [128, 16, 64], BF16)
    sgw = nc.alloc_sbuf_tensor("sgw", [128, 16, 8], F32)
    ssq = nc.alloc_sbuf_tensor("ssq", [128, 2, 512], F32)
    sones = nc.alloc_sbuf_tensor("sones", [128, 1], F32)
    srow1 = nc.alloc_sbuf_tensor("srow1", [1, 128], F32)
    srow_stats = nc.alloc_sbuf_tensor("srow_stats", [1, 2, 512], F32)
    srow_ab = nc.alloc_sbuf_tensor("srow_ab", [1, 2, 512], F32)
    sA = nc.alloc_sbuf_tensor("sA", [128, 512], F32)
    sB = nc.alloc_sbuf_tensor("sB", [128, 512], F32)
    s_mr_col = nc.alloc_sbuf_tensor("s_mr_col", [128, 2], F32)
    sSrow = nc.alloc_sbuf_tensor("sSrow", [1, 8], F32)
    sSb = nc.alloc_sbuf_tensor("sSb", [128, 8], F32)
    slog = nc.alloc_sbuf_tensor("slog", [128, 8, 8], F32)
    sckv = nc.alloc_sbuf_tensor("sckv", [128, 4, T], BF16)
    scq = nc.alloc_sbuf_tensor("scq", [128, 8, T], BF16)
    skr_raw = nc.alloc_sbuf_tensor("skr_raw", [64, T], F32)
    skr = nc.alloc_sbuf_tensor("skr", [64, T], BF16)
    ssp = nc.alloc_sbuf_tensor("ssp", [64, T], F32)
    ssp2 = nc.alloc_sbuf_tensor("ssp2", [64, T], F32)
    sdelta = nc.alloc_sbuf_tensor("sdelta", [1, 32], F32)
    sdrow = nc.alloc_sbuf_tensor("sdrow", [1, 4, 32], F32)
    sdcol = nc.alloc_sbuf_tensor("sdcol", [32, 2], F32)
    scos0 = nc.alloc_sbuf_tensor("scos0", [32, T], F32)
    ssin0 = nc.alloc_sbuf_tensor("ssin0", [32, T], F32)
    seps = nc.alloc_sbuf_tensor("seps", [1, 1], F32)
    smu2 = nc.alloc_sbuf_tensor("smu2", [32, T], F32)
    smu2b = nc.alloc_sbuf_tensor("smu2b", [32, T], F32)

    ones_col = nc.inline_tensor(np.ones((128, 1), np.float32), name="c_ones_col")
    ones_row = nc.inline_tensor(np.ones((1, 128), np.float32), name="c_ones_row")
    eps_c = nc.inline_tensor(np.full((1, 1), LN_EPS, np.float32), name="c_eps")

    p_big = nc.alloc_psum_tensor("p_big", [128, 512], F32)
    p_big2 = nc.alloc_psum_tensor("p_big2", [128, 512], F32)
    p_s1 = nc.alloc_psum_tensor("p_s1", [1, 512], F32)
    p_s2 = nc.alloc_psum_tensor("p_s2", [1, 512], F32)
    p_bc = nc.alloc_psum_tensor("p_bc", [128, 512], F32)
    p_bc2 = nc.alloc_psum_tensor("p_bc2", [128, 512], F32)
    p_col = nc.alloc_psum_tensor("p_col", [128, 2], F32)
    p_row8 = nc.alloc_psum_tensor("p_row8", [1, 8], F32)

    sc.dma("sync", sx[:], x.rearrange("(c p) t -> p c t", p=128), ld)
    sc.dma("sync", sones[:], ones_col.ap(), ld)
    sc.dma("sync", srow1[:], ones_row.ap(), ld)
    sc.dma("sync", sdelta[:], delta, ld)
    sc.dma("sync", scos0[:], cos0s, ld)
    sc.dma("sync", ssin0[:], sin0s, ld)
    sc.dma("sync", sgw[:], gw_t.rearrange("(c p) o -> p c o", p=128), ld)
    sc.dma("sync", seps[:], eps_c.ap(), ld)
    ld_all = ld.n
    sc.dma("gpsimd", swdkv[:], w_dkv.rearrange("(c p) o -> p c o", p=128), ldw)
    sc.dma("gpsimd", swdq[:], w_dq.rearrange("(c p) o -> p c o", p=128)[:, :, 0:512], ldw)
    sc.dma("gpsimd", swkr[:], w_kr.rearrange("(c p) o -> p c o", p=128), ldw)
    ldw_all = ldw.n

    accmm(sc, p_row8[:], [(sones[:], sgw[:, c, :]) for c in range(16)],
          waits=[(ld, ld_all)])
    sc.st("scalar", lambda s: s.copy(sSrow[:], p_row8[:]))
    sc.st("tensor", lambda t: t.matmul(p_bc[:, 0:8], srow1[:], sSrow[:], start=True, stop=True))
    sc.st("scalar", lambda s: s.copy(sSb[:], p_bc[:, 0:8]))

    for tt in range(2):
        t0 = tt * 512
        accmm(sc, p_s1[:], [(sones[:], sx[:, c, t0:t0 + 512]) for c in range(16)],
              waits=[(ld, ld_all)])
        for c in range(16):
            sl = c % 2
            sc.st("vector",
                  lambda v, c=c, sl=sl, t0=t0:
                  v.tensor_mul(ssq[:, sl, :], sx[:, c, t0:t0 + 512], sx[:, c, t0:t0 + 512]),
                  waits=[(ld, ld_all)])
            sc.st("tensor",
                  lambda t, c=c, sl=sl:
                  t.matmul(p_s2[:], sones[:], ssq[:, sl, :], start=(c == 0), stop=(c == 15)))

        sc.ops("vector", [
            lambda v: v.tensor_scalar_mul(srow_stats[:, 0, :], p_s1[:], 1.0 / 2048.0),
            lambda v: v.tensor_scalar_mul(srow_stats[:, 1, :], p_s2[:], 1.0 / 2048.0),
            lambda v: v.tensor_mul(srow_ab[:, 0, :], srow_stats[:, 0, :], srow_stats[:, 0, :]),
            lambda v: v.tensor_sub(srow_ab[:, 1, :], srow_stats[:, 1, :], srow_ab[:, 0, :]),
        ])

        sc.ops("scalar", [
            lambda s: s.activation(srow_ab[:, 0, :], srow_ab[:, 1, :], AF.Ln, bias=seps[:], scale=1.0),
            lambda s: s.activation(srow_ab[:, 0, :], srow_ab[:, 0, :], AF.Exp, bias=0.0, scale=-0.5),
        ], waits=[(ld, ld_all)])

        sc.ops("vector", [
            lambda v: v.tensor_mul(srow_ab[:, 1, :], srow_stats[:, 0, :], srow_ab[:, 0, :]),
            lambda v: v.tensor_scalar_mul(srow_ab[:, 1, :], srow_ab[:, 1, :], -1.0),
        ])

        def bc_ab(t, tt=tt):
            t.matmul(p_bc[:], srow1[:], srow_ab[:, 0, :], start=True, stop=True)
            return t.matmul(p_bc2[:], srow1[:], srow_ab[:, 1, :], start=True, stop=True)
        sc.st("tensor", bc_ab)

        def evac_ab(s):
            s.copy(sA[:], p_bc[:])
            return s.copy(sB[:], p_bc2[:])
        sc.st("scalar", evac_ab)

        xn_ops = []
        for c in range(16):
            xn_ops.append(lambda v, c=c, t0=t0: v.tensor_mul(ssq[:, c % 2, :], sx[:, c, t0:t0 + 512], sA[:]))
            xn_ops.append(lambda v, c=c, t0=t0: v.tensor_add(sxn[:, c, t0:t0 + 512], ssq[:, c % 2, :], sB[:]))
        sc.ops("vector", xn_ops)

        for q in range(4):
            ti = tt * 4 + q
            tq = ti * 128
            accmm(sc, p_big[:, 0:8],
                  [(sx[:, c, tq:tq + 128], sgw[:, c, :]) for c in range(16)],
                  waits=[(ld, ld_all)])

            def tr_mr(t, off=q * 128):
                t.matmul(p_col[:, 0:1], srow_stats[0:1, 0, off:off + 128], sones[0:1, 0:1],
                         start=True, stop=True)
                return t.matmul(p_col[:, 1:2], srow_ab[0:1, 0, off:off + 128], sones[0:1, 0:1],
                                start=True, stop=True)
            sc.st("tensor", tr_mr)
            sc.st("scalar", lambda s: s.copy(s_mr_col[:], p_col[:, 0:2]))

            sc.ops("vector", [
                lambda v, ti=ti: v.tensor_scalar(slog[:, ti, :], sSb[:], s_mr_col[:, 0:1], None, ALU.mult),
                lambda v, ti=ti: v.tensor_sub(slog[:, ti, :], p_big[:, 0:8], slog[:, ti, :]),
                lambda v, ti=ti: v.tensor_scalar(slog[:, ti, :], slog[:, ti, :], s_mr_col[:, 1:2], None, ALU.mult),
            ])

    sc.dma("sync", o_xn.rearrange("(c p) t -> p c t", p=128), sxn[:], stv)
    sc.dma("sync", o_log.rearrange("(q p) o -> p q o", p=128), slog[:], stv)

    ldw2 = sc.sem("ldw2")
    for (wt, outt, nout, wsem, wval) in ((swdkv, sckv, 4, ldw, ldw_all),
                                         (swdq, scq, 4, ldw, ldw_all)):
        for oc in range(nout):
            for tt in range(2):
                t0 = tt * 512
                pb = p_big if (oc + tt) % 2 == 0 else p_big2
                accmm(sc, pb[:],
                      [(wt[:, c, oc * 128:(oc + 1) * 128], sxn[:, c, t0:t0 + 512]) for c in range(16)],
                      waits=[(wsem, wval)])
                sc.st("scalar",
                      lambda s, outt=outt, oc=oc, t0=t0, pb=pb:
                      s.copy(outt[:, oc, t0:t0 + 512], pb[:]))
    sc.dma("gpsimd", swdq[:], w_dq.rearrange("(c p) o -> p c o", p=128)[:, :, 512:1024], ldw2)
    for oc in range(4):
        for tt in range(2):
            t0 = tt * 512
            pb = p_big if (oc + tt) % 2 == 0 else p_big2
            accmm(sc, pb[:],
                  [(swdq[:, c, oc * 128:(oc + 1) * 128], sxn[:, c, t0:t0 + 512]) for c in range(16)],
                  waits=[(ldw2, 16)])
            sc.st("scalar",
                  lambda s, oc=oc, t0=t0, pb=pb:
                  s.copy(scq[:, 4 + oc, t0:t0 + 512], pb[:]))
    for tt in range(2):
        t0 = tt * 512
        pb = p_big if tt % 2 == 0 else p_big2
        accmm(sc, pb[0:64, :],
              [(swkr[:, c, :], sxn[:, c, t0:t0 + 512]) for c in range(16)],
              waits=[(ldw, ldw_all)])
        sc.st("scalar", lambda s, t0=t0, pb=pb: s.copy(skr_raw[:, t0:t0 + 512], pb[0:64, :]))

    sc.dma("sync", o_ckv.rearrange("(c p) t -> p c t", p=128), sckv[:], stv)
    sc.dma("sync", o_cq.rearrange("(c p) t -> p c t", p=128), scq[:], stv)

    sc.ops("vector", [
        lambda v: v.tensor_scalar(sdrow[:, 0, :], sdelta[:], 1.0 / (2.0 * np.pi), MAGIC, ALU.mult, ALU.add),
        lambda v: v.tensor_scalar(sdrow[:, 0, :], sdrow[:, 0, :], MAGIC, None, ALU.subtract),
        lambda v: v.tensor_scalar(sdrow[:, 0, :], sdrow[:, 0, :], 2.0 * np.pi, None, ALU.mult),
        lambda v: v.tensor_sub(sdrow[:, 1, :], sdelta[:], sdrow[:, 0, :]),
        lambda v: v.tensor_scalar_mul(sdrow[:, 2, :], sdrow[:, 1, :], 0.5),
    ], waits=[(ld, ld_all)])

    sc.ops("scalar", [
        lambda s: s.activation(sdrow[:, 1, :], sdrow[:, 1, :], AF.Sin, bias=0.0, scale=1.0),
        lambda s: s.activation(sdrow[:, 2, :], sdrow[:, 2, :], AF.Sin, bias=0.0, scale=1.0),
    ])

    sc.ops("vector", [
        lambda v: v.tensor_mul(sdrow[:, 3, :], sdrow[:, 2, :], sdrow[:, 2, :]),
        lambda v: v.tensor_scalar(sdrow[:, 3, :], sdrow[:, 3, :], -2.0, 1.0, ALU.mult, ALU.add),
    ])

    def tr_ds(t):
        t.matmul(p_col[0:32, 0:1], sdrow[0:1, 3, :], sones[0:1, 0:1], start=True, stop=True)
        return t.matmul(p_col[0:32, 1:2], sdrow[0:1, 1, :], sones[0:1, 0:1], start=True, stop=True)
    sc.st("tensor", tr_ds)
    sc.st("scalar", lambda s: s.copy(sdcol[:], p_col[0:32, 0:2]))

    sc.ops("vector", [
        lambda v: v.tensor_scalar(ssp[0:32, :], ssin0[:], sdcol[:, 1:2], None, ALU.mult),
        lambda v: v.tensor_scalar(ssp2[0:32, :], scos0[:], sdcol[:, 1:2], None, ALU.mult),
        lambda v: v.tensor_scalar(scos0[:], scos0[:], sdcol[:, 0:1], None, ALU.mult),
        lambda v: v.tensor_scalar(ssin0[:], ssin0[:], sdcol[:, 0:1], None, ALU.mult),
        lambda v: v.tensor_sub(scos0[:], scos0[:], ssp[0:32, :]),
        lambda v: v.tensor_add(ssin0[:], ssin0[:], ssp2[0:32, :]),
    ])

    sc.ops("scalar", [
        lambda s: s.activation(ssp[:], skr_raw[:], AF.Abs, bias=0.0, scale=1.0),
        lambda s: s.activation(ssp[:], ssp[:], AF.Exp, bias=0.0, scale=-1.0),
        lambda s: s.activation(ssp[:], ssp[:], AF.Ln, bias=sones[0:64, :], scale=1.0),
    ])

    sc.ops("vector", [
        lambda v: v.tensor_scalar_max(ssp2[:], skr_raw[:], 0.0),
        lambda v: v.tensor_add(ssp2[:], ssp2[:], ssp[:]),
    ])

    sc.ops("vector", [
        lambda v: v.tensor_copy(smu2[:], ssp2[32:64, :]),
        lambda v: v.tensor_mul(ssp[0:32, :], ssp2[0:32, :], scos0[:]),
        lambda v: v.tensor_mul(smu2b[:], smu2[:], ssin0[:]),
        lambda v: v.tensor_sub(skr[0:32, :], ssp[0:32, :], smu2b[:]),
        lambda v: v.tensor_mul(ssp[0:32, :], ssp2[0:32, :], ssin0[:]),
        lambda v: v.tensor_mul(smu2b[:], smu2[:], scos0[:]),
        lambda v: v.tensor_add(smu2b[:], ssp[0:32, :], smu2b[:]),
        lambda v: v.tensor_copy(skr[32:64, :], smu2b[:]),
    ])

    sc.dma("sync", o_kr, skr[:], stv)
    sc.st("sync", lambda e: None, waits=[(stv, stv.n)])
    sc.run()
    return nc


_L1_CACHE = {}


def run_launch1(inputs, xs_fm):
    if "nc" not in _L1_CACHE:
        _L1_CACHE["nc"] = build_launch1()
    nc = _L1_CACHE["nc"]
    cos0, sin0 = pope_tables()
    in_maps = []
    for c in range(NCORES):
        pos0 = (c % 4) * T
        in_maps.append({
            "x_fm": xs_fm[c],
            "w_dkv": np.ascontiguousarray(inputs["L0_attn_w_dkv"], np.float32),
            "w_dq": np.ascontiguousarray(inputs["L0_attn_w_dq"], np.float32),
            "w_kr": np.ascontiguousarray(inputs["L0_attn_w_kr"], np.float32),
            "gw_t": np.ascontiguousarray(np.asarray(inputs["L0_ffn_gate_w"], np.float32).T),
            "delta": np.asarray(inputs["L0_attn_delta"], np.float32).reshape(1, 32),
            "cos0s": np.ascontiguousarray(cos0[:, pos0:pos0 + T]),
            "sin0s": np.ascontiguousarray(sin0[:, pos0:pos0 + T]),
        })
    res = run_bass_kernel_spmd(nc, in_maps, core_ids=list(range(NCORES)))
    return res.results





SC192 = 1.0 / np.sqrt(192.0)


def build_launch2():
    nc = bass.Bass("TRN2", target_bir_lowering=False)
    ckv = nc.dram_tensor("ckv", [512, S], BF16, kind="ExternalInput").ap()
    krf = nc.dram_tensor("krf", [64, S], BF16, kind="ExternalInput").ap()
    cq = nc.dram_tensor("cq", [1024, S], BF16, kind="ExternalInput").ap()
    wuq = nc.dram_tensor("wuq", [1024, 512], F32, kind="ExternalInput").ap()
    wqr = nc.dram_tensor("wqr", [1024, 256], F32, kind="ExternalInput").ap()
    wuk = nc.dram_tensor("wuk", [512, 512], F32, kind="ExternalInput").ap()
    wuv = nc.dram_tensor("wuv", [512, 512], F32, kind="ExternalInput").ap()
    delta = nc.dram_tensor("delta", [1, 32], F32, kind="ExternalInput").ap()
    o_attn = nc.dram_tensor("attn", [4, 128, S], BF16, kind="ExternalOutput").ap()

    cos0, sin0 = pope_tables()
    c_cos0 = nc.inline_tensor(cos0.astype(BF16NP), name="c_cos0")
    c_sin0 = nc.inline_tensor(sin0.astype(BF16NP), name="c_sin0")
    masks = np.where((np.arange(4)[:, None, None] * 128 + np.arange(128)[None, :, None])
                     > np.arange(512)[None, None, :], NEG, 0.0).astype(np.float32)
    c_masks = nc.inline_tensor(masks.reshape(4 * 128, 512), name="c_masks")
    ones_col = nc.inline_tensor(np.ones((128, 1), np.float32), name="c_ones_col")
    ones_colb = nc.inline_tensor(np.ones((128, 1), np.float32).astype(BF16NP), name="c_ones_colb")
    ones_row = nc.inline_tensor(np.ones((1, 128), np.float32), name="c_ones_row")

    sc = Sched(nc)
    ld = sc.sem("ld")
    ldw = sc.sem("ldw")
    ldq = sc.sem("ldq")
    stv = sc.sem("stv")

    sckv = nc.alloc_sbuf_tensor("sckv", [128, 4, S], BF16)
    skr = nc.alloc_sbuf_tensor("skr", [64, S], BF16)
    scqb = nc.alloc_sbuf_tensor("scqb", [128, 2, 8, 512], BF16)
    suq = nc.alloc_sbuf_tensor("suq", [128, 8, 512], BF16)
    sqrw = nc.alloc_sbuf_tensor("sqrw", [128, 8, 256], BF16)
    suk = nc.alloc_sbuf_tensor("suk", [128, 4, 512], BF16)
    suv = nc.alloc_sbuf_tensor("suv", [128, 4, 512], BF16)
    sK = nc.alloc_sbuf_tensor("sK", [128, 8, 512], BF16)
    sV = nc.alloc_sbuf_tensor("sV", [128, 32, 128], BF16)
    sQ = nc.alloc_sbuf_tensor("sQ", [128, 8, 512], BF16)
    sQr = nc.alloc_sbuf_tensor("sQr", [64, S], BF16)
    sqr_raw = nc.alloc_sbuf_tensor("sqr_raw", [64, 512], F32)
    ssp = nc.alloc_sbuf_tensor("ssp", [64, 512], F32)
    ssp2 = nc.alloc_sbuf_tensor("ssp2", [64, 512], F32)
    smu2 = nc.alloc_sbuf_tensor("smu2", [32, 512], F32)
    smu2b = nc.alloc_sbuf_tensor("smu2b", [32, 512], F32)
    sPt = nc.alloc_sbuf_tensor("sPt", [128, 32, 512], BF16)
    sOut = nc.alloc_sbuf_tensor("sOut", [128, S], BF16)
    smask = nc.alloc_sbuf_tensor("smask", [128, 4, 512], F32)
    scos = nc.alloc_sbuf_tensor("scos", [32, S], BF16)
    ssin = nc.alloc_sbuf_tensor("ssin", [32, S], BF16)
    ssq = nc.alloc_sbuf_tensor("ssq", [128, 512], F32)
    stiny = nc.alloc_sbuf_tensor("stiny", [1, 2, 8], F32)
    sscal = nc.alloc_sbuf_tensor("sscal", [1, 8], F32)
    sbias = nc.alloc_sbuf_tensor("sbias", [128, 1], F32)
    sRec = nc.alloc_sbuf_tensor("sRec", [128, 512], F32)
    srd = nc.alloc_sbuf_tensor("srd", [1, 512], F32)
    sones = nc.alloc_sbuf_tensor("sones", [128, 1], F32)
    sonesb = nc.alloc_sbuf_tensor("sonesb", [128, 1], BF16)
    srow1 = nc.alloc_sbuf_tensor("srow1", [1, 128], F32)
    sdelta = nc.alloc_sbuf_tensor("sdelta", [1, 32], F32)
    sdrow = nc.alloc_sbuf_tensor("sdrow", [1, 4, 32], F32)
    sdcol = nc.alloc_sbuf_tensor("sdcol", [32, 2], F32)

    p_s = nc.alloc_psum_tensor("p_s", [128, 512], F32)
    p_s2 = nc.alloc_psum_tensor("p_s2", [128, 512], F32)
    p_o = nc.alloc_psum_tensor("p_o", [128, 512], F32)
    p_d = nc.alloc_psum_tensor("p_d", [1, 512], F32)
    p_bc = nc.alloc_psum_tensor("p_bc", [128, 512], F32)
    p_row = nc.alloc_psum_tensor("p_row", [1, 512], F32)
    p_col = nc.alloc_psum_tensor("p_col", [128, 2], F32)

    sc.dma("sync", sckv[:], ckv.rearrange("(c p) t -> p c t", p=128), ld)
    sc.dma("sync", skr[:], krf, ld)
    sc.dma("sync", smask[:], c_masks.ap().rearrange("(m p) t -> p m t", p=128), ld)
    sc.dma("sync", scos[:], c_cos0.ap(), ld)
    sc.dma("sync", ssin[:], c_sin0.ap(), ld)
    sc.dma("sync", sones[:], ones_col.ap(), ld)
    sc.dma("sync", sonesb[:], ones_colb.ap(), ld)
    sc.dma("sync", srow1[:], ones_row.ap(), ld)
    sc.dma("sync", sdelta[:], delta, ld)
    ld_all = ld.n
    sc.dma("gpsimd", suq[:], wuq.rearrange("(c p) o -> p c o", p=128), ldw)
    sc.dma("gpsimd", sqrw[:], wqr.rearrange("(c p) o -> p c o", p=128), ldw)
    sc.dma("gpsimd", suk[:], wuk.rearrange("(c p) o -> p c o", p=128), ldw)
    sc.dma("gpsimd", suv[:], wuv.rearrange("(c p) o -> p c o", p=128), ldw)
    ldw_all = ldw.n

    sc.ops("vector", [
        lambda v: v.tensor_scalar(sdrow[:, 0, :], sdelta[:], 1.0 / (2.0 * np.pi), MAGIC, ALU.mult, ALU.add),
        lambda v: v.tensor_scalar(sdrow[:, 0, :], sdrow[:, 0, :], MAGIC, None, ALU.subtract),
        lambda v: v.tensor_scalar(sdrow[:, 0, :], sdrow[:, 0, :], 2.0 * np.pi, None, ALU.mult),
        lambda v: v.tensor_sub(sdrow[:, 1, :], sdelta[:], sdrow[:, 0, :]),
        lambda v: v.tensor_scalar_mul(sdrow[:, 2, :], sdrow[:, 1, :], 0.5),
    ], waits=[(ld, ld_all)])
    sc.ops("scalar", [
        lambda s: s.activation(sdrow[:, 1, :], sdrow[:, 1, :], AF.Sin, bias=0.0, scale=1.0),
        lambda s: s.activation(sdrow[:, 2, :], sdrow[:, 2, :], AF.Sin, bias=0.0, scale=1.0),
    ])
    sc.ops("vector", [
        lambda v: v.tensor_mul(sdrow[:, 3, :], sdrow[:, 2, :], sdrow[:, 2, :]),
        lambda v: v.tensor_scalar(sdrow[:, 3, :], sdrow[:, 3, :], -2.0, 1.0, ALU.mult, ALU.add),
    ])
    def tr_ds(t):
        t.matmul(p_col[0:32, 0:1], sdrow[0:1, 3, :], sones[0:1, 0:1], start=True, stop=True)
        return t.matmul(p_col[0:32, 1:2], sdrow[0:1, 1, :], sones[0:1, 0:1], start=True, stop=True)
    sc.st("tensor", tr_ds)
    sc.st("scalar", lambda s: s.copy(sdcol[:], p_col[0:32, 0:2]))
    sstrip = nc.alloc_sbuf_tensor("sstrip", [32, 2, 1024], BF16)
    for st0 in range(0, S, 1024):
        sl = (slice(0, 32), slice(st0, st0 + 1024))
        sc.ops("vector", [
            lambda v, sl=sl: v.tensor_scalar(sstrip[:, 0, :], ssin[sl[0], sl[1]], sdcol[:, 1:2], None, ALU.mult),
            lambda v, sl=sl: v.tensor_scalar(sstrip[:, 1, :], scos[sl[0], sl[1]], sdcol[:, 1:2], None, ALU.mult),
            lambda v, sl=sl: v.tensor_scalar(scos[sl[0], sl[1]], scos[sl[0], sl[1]], sdcol[:, 0:1], None, ALU.mult),
            lambda v, sl=sl: v.tensor_scalar(ssin[sl[0], sl[1]], ssin[sl[0], sl[1]], sdcol[:, 0:1], None, ALU.mult),
            lambda v, sl=sl: v.tensor_sub(scos[sl[0], sl[1]], scos[sl[0], sl[1]], sstrip[:, 0, :]),
            lambda v, sl=sl: v.tensor_add(ssin[sl[0], sl[1]], ssin[sl[0], sl[1]], sstrip[:, 1, :]),
        ])

    for u in range(4):
        for kb8 in range(8):
            accmm(sc, p_s[:], [(suk[:, c, u * 128:(u + 1) * 128], sckv[:, c, kb8 * 512:(kb8 + 1) * 512]) for c in range(4)],
                  waits=[(ldw, ldw_all), (ld, ld_all), (stv, stv.n)])
            sc.st("scalar", lambda s, kb8=kb8: s.copy(sK[:, kb8, :], p_s[:]))
        for kb2 in range(32):
            accmm(sc, p_s[:, 0:128], [(sckv[:, c, kb2 * 128:(kb2 + 1) * 128], suv[:, c, u * 128:(u + 1) * 128]) for c in range(4)])
            sc.st("scalar", lambda s, kb2=kb2: s.copy(sV[:, kb2, :], p_s[:, 0:128]))
        for kb8 in range(8):
            sc.st("vector", lambda v, kb8=kb8: v.tensor_mul(ssq[:], sK[:, kb8, :], sK[:, kb8, :]))
            sc.st("vector", lambda v, kb8=kb8: v.tensor_mul(ssp[:], skr[:, kb8 * 512:(kb8 + 1) * 512], skr[:, kb8 * 512:(kb8 + 1) * 512]))
            accmm(sc, p_row[:], [(sones[:], ssq[:]), (sones[0:64, :], ssp[:])])
            sc.st("vector", lambda v, kb8=kb8: v.tensor_reduce(stiny[0:1, 0, kb8:kb8 + 1], p_row[:], AX.X, ALU.max))

        for qb in range(8):
            par = qb % 2
            sc.dma("gpsimd", scqb[:, par, :, :],
                   cq.rearrange("(c p) t -> p c t", p=128)[:, :, qb * 512:(qb + 1) * 512], ldq)
            ldq_now = ldq.n
            accmm(sc, p_s[:], [(suq[:, c, u * 128:(u + 1) * 128], scqb[:, par, c, :]) for c in range(8)],
                  waits=[(ldq, ldq_now)])
            sc.st("scalar", lambda s, qb=qb: s.copy(sQ[:, qb, :], p_s[:]))
            accmm(sc, p_s2[0:64, :],
                  [(sqrw[:, c, u * 64:(u + 1) * 64], scqb[:, par, c, :]) for c in range(8)],
                  waits=[(ldq, ldq_now)])
            sc.st("scalar", lambda s: s.copy(sqr_raw[:], p_s2[0:64, :]))
            sc.ops("scalar", [
                lambda s: s.activation(ssp[:], sqr_raw[:], AF.Abs, bias=0.0, scale=1.0),
                lambda s: s.activation(ssp[:], ssp[:], AF.Exp, bias=0.0, scale=-1.0),
                lambda s: s.activation(ssp[:], ssp[:], AF.Ln, bias=sones[0:64, :], scale=1.0),
            ])
            sc.ops("vector", [
                lambda v: v.tensor_scalar_max(ssp2[:], sqr_raw[:], 0.0),
                lambda v: v.tensor_add(ssp2[:], ssp2[:], ssp[:]),
            ])
            q0 = qb * 512
            sc.ops("vector", [
                lambda v: v.tensor_copy(smu2[:], ssp2[32:64, :]),
                lambda v, q0=q0: v.tensor_mul(ssp[0:32, :], ssp2[0:32, :], scos[:, q0:q0 + 512]),
                lambda v, q0=q0: v.tensor_mul(smu2b[:], smu2[:], ssin[:, q0:q0 + 512]),
                lambda v, q0=q0: v.tensor_sub(sQr[0:32, q0:q0 + 512], ssp[0:32, :], smu2b[:]),
                lambda v, q0=q0: v.tensor_mul(ssp[0:32, :], ssp2[0:32, :], ssin[:, q0:q0 + 512]),
                lambda v, q0=q0: v.tensor_mul(smu2b[:], smu2[:], scos[:, q0:q0 + 512]),
                lambda v: v.tensor_add(smu2b[:], ssp[0:32, :], smu2b[:]),
                lambda v, q0=q0: v.tensor_copy(sQr[32:64, q0:q0 + 512], smu2b[:]),
            ])
            sc.st("vector", lambda v, qb=qb: v.tensor_mul(ssq[:], sQ[:, qb, :], sQ[:, qb, :]))
            sc.st("vector", lambda v, q0=q0: v.tensor_mul(ssp[:], sQr[:, q0:q0 + 512], sQr[:, q0:q0 + 512]))
            accmm(sc, p_row[:], [(sones[:], ssq[:]), (sones[0:64, :], ssp[:])])
            sc.st("vector", lambda v, qb=qb: v.tensor_reduce(stiny[0:1, 1, qb:qb + 1], p_row[:], AX.X, ALU.max))

        sc.ops("vector", [
            lambda v: v.tensor_reduce(sscal[0:1, 0:1], stiny[0:1, 0, :], AX.X, ALU.max),
            lambda v: v.tensor_reduce(sscal[0:1, 1:2], stiny[0:1, 1, :], AX.X, ALU.max),
            lambda v: v.tensor_mul(sscal[0:1, 2:3], sscal[0:1, 0:1], sscal[0:1, 1:2]),
        ])
        sc.ops("scalar", [
            lambda s: s.activation(sscal[0:1, 3:4], sscal[0:1, 2:3], AF.Ln, bias=0.0, scale=1.0),
            lambda s: s.activation(sscal[0:1, 3:4], sscal[0:1, 3:4], AF.Exp, bias=0.0, scale=0.5),
        ])
        sc.st("vector", lambda v: v.tensor_scalar_mul(sscal[0:1, 4:5], sscal[0:1, 3:4], -SC192))
        sc.st("tensor", lambda t: t.matmul(p_col[:, 0:1], srow1[:], sscal[0:1, 4:5], start=True, stop=True))
        sc.st("scalar", lambda s: s.copy(sbias[:], p_col[:, 0:1]))

        for qc in range(8):
            nkb = 4 * (qc + 1)
            for kb in range(nkb):
                ps = p_s if kb % 2 == 0 else p_s2
                pairs = [(sK[:, kb // 4, (kb % 4) * 128:(kb % 4 + 1) * 128], sQ[:, qc, :]),
                         (skr[:, kb * 128:(kb + 1) * 128], sQr[:, qc * 512:(qc + 1) * 512])]
                accmm(sc, ps[:], pairs)
                if kb // 4 == qc:
                    m = kb % 4
                    sc.st("vector", lambda v, ps=ps, m=m: v.tensor_add(ps[:], ps[:], smask[:, m, :]))
                sc.st("scalar", lambda s, ps=ps, kb=kb: s.activation(
                    sPt[:, kb, :], ps[:], AF.Exp, bias=sbias[:], scale=SC192))
            accmm(sc, p_o[:], [(sV[:, kb, :], sPt[:, kb, :]) for kb in range(nkb)])
            accmm(sc, p_d[:], [(sonesb[:], sPt[:, kb, :]) for kb in range(nkb)])
            sc.st("vector", lambda v: v.reciprocal(srd[:], p_d[:]))
            sc.st("tensor", lambda t: t.matmul(p_bc[:], srow1[:], srd[:], start=True, stop=True))
            sc.st("scalar", lambda s: s.copy(sRec[:], p_bc[:]))
            sc.st("vector", lambda v, qc=qc: v.tensor_mul(sOut[:, qc * 512:(qc + 1) * 512], p_o[:], sRec[:]))
        sc.dma("sync", o_attn[u], sOut[:], stv)

    sc.st("sync", lambda e: None, waits=[(stv, stv.n)])
    sc.run()
    return nc


_L2_CACHE = {}


def run_launch2(inputs, ckv_full, kr_full, cq_full):
    if "nc" not in _L2_CACHE:
        _L2_CACHE["nc"] = build_launch2()
    nc = _L2_CACHE["nc"]
    wuq = np.asarray(inputs["L0_attn_w_uq"], np.float32)
    wqr = np.asarray(inputs["L0_attn_w_qr"], np.float32)
    wuk = np.asarray(inputs["L0_attn_w_uk"], np.float32)
    wuv = np.asarray(inputs["L0_attn_w_uv"], np.float32)
    in_maps = []
    for c in range(NCORES):
        s = c // 4
        hb = (c % 4) * 4
        in_maps.append({
            "ckv": ckv_full[s],
            "krf": kr_full[s],
            "cq": cq_full[s],
            "wuq": np.ascontiguousarray(wuq[:, hb * 128:(hb + 4) * 128]),
            "wqr": np.ascontiguousarray(wqr[:, hb * 64:(hb + 4) * 64]),
            "wuk": np.ascontiguousarray(wuk[:, hb * 128:(hb + 4) * 128]),
            "wuv": np.ascontiguousarray(wuv[:, hb * 128:(hb + 4) * 128]),
            "delta": np.asarray(inputs["L0_attn_delta"], np.float32).reshape(1, 32),
        })
    res = run_bass_kernel_spmd(nc, in_maps, core_ids=list(range(NCORES)))
    return res.results





def build_launch3():
    nc = bass.Bass("TRN2", target_bir_lowering=False)
    x = nc.dram_tensor("x_fm", [D, T], F32, kind="ExternalInput").ap()
    xn = nc.dram_tensor("xn_fm", [D, T], BF16, kind="ExternalInput").ap()
    att = nc.dram_tensor("att_fm", [D, T], BF16, kind="ExternalInput").ap()
    logits = nc.dram_tensor("logits", [T, 8], F32, kind="ExternalInput").ap()
    gbias = nc.dram_tensor("gbias", [1, 8], F32, kind="ExternalInput").ap()
    wout = nc.dram_tensor("wout", [D, D], F32, kind="ExternalInput").ap()
    sgw = nc.dram_tensor("sgw", [D, 5632], F32, kind="ExternalInput").ap()
    sdw = nc.dram_tensor("sdw", [2816, D], F32, kind="ExternalInput").ap()
    upw = nc.dram_tensor("upw", [8, D, 2816], F32, kind="ExternalInput").ap()
    dnw = nc.dram_tensor("dnw", [8, 1408, D], F32, kind="ExternalInput").ap()
    o_x1 = nc.dram_tensor("x1_fm", [D, T], F32, kind="ExternalOutput").ap()

    ones_col = nc.inline_tensor(np.ones((128, 1), np.float32), name="c_ones_col")
    ones_row = nc.inline_tensor(np.ones((1, 128), np.float32), name="c_ones_row")
    id8 = nc.inline_tensor(np.eye(8, dtype=np.float32), name="c_id8")
    id128 = nc.inline_tensor(np.eye(128, dtype=np.float32), name="c_id128")

    sc = Sched(nc)
    ld = sc.sem("ld")
    ldw = sc.sem("ldw")
    stv = sc.sem("stv")

    sxn = nc.alloc_sbuf_tensor("sxn", [128, 16, T], BF16)
    satt = nc.alloc_sbuf_tensor("satt", [128, 16, T], BF16)
    sacc = nc.alloc_sbuf_tensor("sacc", [128, 16, T], F32)
    sh = nc.alloc_sbuf_tensor("sh", [128, 11, T], BF16)
    swu = nc.alloc_sbuf_tensor("swu", [128, 2, 16, 128], BF16)
    swd = nc.alloc_sbuf_tensor("swd", [128, 2, 11, 128], BF16)
    swo = nc.alloc_sbuf_tensor("swo", [128, 2, 16, 128], BF16)
    sxs = nc.alloc_sbuf_tensor("sxs", [128, 2, T], F32)
    ssil = nc.alloc_sbuf_tensor("ssil", [128, 512], F32)
    st2 = nc.alloc_sbuf_tensor("st2", [128, 512], F32)
    slog = nc.alloc_sbuf_tensor("slog", [128, 8, 8], F32)
    ssig = nc.alloc_sbuf_tensor("ssig", [128, 8, 8], F32)
    swe = nc.alloc_sbuf_tensor("swe", [128, 8, 8], F32)
    sm1 = nc.alloc_sbuf_tensor("sm1", [128, 4], F32)
    sBb = nc.alloc_sbuf_tensor("sBb", [128, 8], F32)
    sgb = nc.alloc_sbuf_tensor("sgb", [1, 8], F32)
    swet = nc.alloc_sbuf_tensor("swet", [8, 8, 128], F32)
    sweRow = nc.alloc_sbuf_tensor("sweRow", [1, T], F32)
    sweb = nc.alloc_sbuf_tensor("sweb", [128, T], BF16)
    sones = nc.alloc_sbuf_tensor("sones", [128, 1], F32)
    srow1 = nc.alloc_sbuf_tensor("srow1", [1, 128], F32)
    sid8 = nc.alloc_sbuf_tensor("sid8", [8, 8], F32)
    sid128 = nc.alloc_sbuf_tensor("sid128", [128, 128], F32)

    p_a = nc.alloc_psum_tensor("p_a", [128, 512], F32)
    p_b = nc.alloc_psum_tensor("p_b", [128, 512], F32)
    p_c = nc.alloc_psum_tensor("p_c", [128, 512], F32)
    p_d = nc.alloc_psum_tensor("p_d", [128, 512], F32)
    p_t = nc.alloc_psum_tensor("p_t", [128, 128], F32)
    p_r = nc.alloc_psum_tensor("p_r", [1, 512], F32)

    sc.dma("sync", sxn[:], xn.rearrange("(c p) t -> p c t", p=128), ld)
    sc.dma("sync", satt[:], att.rearrange("(c p) t -> p c t", p=128), ld)
    sc.dma("sync", slog[:], logits.rearrange("(q p) o -> p q o", p=128), ld)
    sc.dma("sync", sgb[:], gbias, ld)
    sc.dma("sync", sones[:], ones_col.ap(), ld)
    sc.dma("sync", srow1[:], ones_row.ap(), ld)
    sc.dma("sync", sid8[:], id8.ap(), ld)
    sc.dma("sync", sid128[:], id128.ap(), ld)
    ld_all = ld.n

    for ti in range(8):
        sc.st("scalar", lambda s, ti=ti: s.activation(ssig[:, ti, :], slog[:, ti, :], AF.Sigmoid, bias=0.0, scale=1.0),
              waits=[(ld, ld_all)])
    sc.st("tensor", lambda t: t.matmul(p_t[:, 0:8], srow1[:], sgb[:], start=True, stop=True))
    sc.st("scalar", lambda s: s.copy(sBb[:], p_t[:, 0:8]))
    for ti in range(8):
        sc.ops("vector", [
            lambda v, ti=ti: v.tensor_add(swe[:, ti, :], ssig[:, ti, :], sBb[:]),
            lambda v, ti=ti: v.tensor_reduce(sm1[:, 0:1], swe[:, ti, :], AX.X, ALU.max),
            lambda v, ti=ti: v.tensor_scalar(sm1[:, 1:2], sm1[:, 0:1], 1.0, None, ALU.mult),
            lambda v, ti=ti: v.tensor_scalar(st2[:, 0:8], swe[:, ti, :], sm1[:, 0:1], None, ALU.is_ge),
            lambda v, ti=ti: v.tensor_scalar(ssil[:, 0:8], st2[:, 0:8], NEG, None, ALU.mult),
            lambda v, ti=ti: v.tensor_add(swe[:, ti, :], swe[:, ti, :], ssil[:, 0:8]),
            lambda v, ti=ti: v.tensor_reduce(sm1[:, 2:3], swe[:, ti, :], AX.X, ALU.max),
            lambda v, ti=ti: v.tensor_scalar(ssil[:, 0:8], swe[:, ti, :], sm1[:, 2:3], None, ALU.is_ge),
            lambda v, ti=ti: v.tensor_add(st2[:, 0:8], st2[:, 0:8], ssil[:, 0:8]),
            lambda v, ti=ti: v.tensor_mul(swe[:, ti, :], st2[:, 0:8], ssig[:, ti, :]),
            lambda v, ti=ti: v.tensor_reduce(sm1[:, 3:4], swe[:, ti, :], AX.X, ALU.add),
            lambda v, ti=ti: v.reciprocal(sm1[:, 3:4], sm1[:, 3:4]),
            lambda v, ti=ti: v.tensor_scalar(swe[:, ti, :], swe[:, ti, :], sm1[:, 3:4], None, ALU.mult),
        ])
    for ti in range(8):
        sc.st("tensor", lambda t, ti=ti: t.transpose(p_t[0:8, 0:128], swe[:, ti, :], sid128[:]))
        sc.st("scalar", lambda s, ti=ti: s.copy(swet[:, ti, :], p_t[0:8, 0:128]))

    for oc in range(16):
        par = oc % 2
        sc.dma("sync", sxs[:, par, :], x.rearrange("(c p) t -> p c t", p=128)[:, oc, :], ld)
        sc.dma("gpsimd", swo[:, par, :, :],
               wout.rearrange("(c p) o -> p c o", p=128)[:, :, oc * 128:(oc + 1) * 128], ldw)
        ldn, lwn = ld.n, ldw.n
        for tt in range(2):
            pp = p_a if tt == 0 else p_b
            accmm(sc, pp[:], [(swo[:, par, hc, :], satt[:, hc, tt * 512:(tt + 1) * 512]) for hc in range(16)],
                  waits=[(ld, ldn), (ldw, lwn)])
        sc.st("vector", lambda v, oc=oc, par=par: v.tensor_add(sacc[:, oc, 0:512], p_a[:], sxs[:, par, 0:512]))
        sc.st("vector", lambda v, oc=oc, par=par: v.tensor_add(sacc[:, oc, 512:1024], p_b[:], sxs[:, par, 512:1024]))


    for half in range(2):
        for j in range(11):
            jj = half * 11 + j
            par = j % 2
            sc.dma("gpsimd", swu[:, par, :, :],
                   sgw.rearrange("(c p) o -> p c o", p=128)[:, :, jj * 128:jj * 128 + 128], ldw)
            sc.dma("gpsimd", swo[:, par, :, :],
                   sgw.rearrange("(c p) o -> p c o", p=128)[:, :, 2816 + jj * 128:2816 + jj * 128 + 128], ldw)
            lwn = ldw.n
            for tt in range(2):
                pg = p_a if tt == 0 else p_b
                pu = p_c if tt == 0 else p_d
                accmm(sc, pu[:], [(swu[:, par, c, :], sxn[:, c, tt * 512:(tt + 1) * 512]) for c in range(16)],
                      waits=[(ldw, lwn)])
                accmm(sc, pg[:], [(swo[:, par, c, :], sxn[:, c, tt * 512:(tt + 1) * 512]) for c in range(16)],
                      waits=[(ldw, lwn)])
                sc.st("scalar", lambda s, pg=pg: s.activation(ssil[:], pg[:], AF.Silu, bias=0.0, scale=1.0))
                sc.st("vector", lambda v, j=j, tt=tt, pu=pu: v.tensor_mul(sh[:, j, tt * 512:(tt + 1) * 512], ssil[:], pu[:]))
        for oc in range(16):
            par = oc % 2
            sc.dma("gpsimd", swd[:, par, :, :],
                   sdw.rearrange("(c p) o -> p c o", p=128)[:, half * 11:(half + 1) * 11, oc * 128:(oc + 1) * 128], ldw)
            lwn = ldw.n
            for tt in range(2):
                pp = p_a if tt == 0 else p_b
                accmm(sc, pp[:], [(swd[:, par, j, :], sh[:, j, tt * 512:(tt + 1) * 512]) for j in range(11)],
                      waits=[(ldw, lwn)])
            sc.st("vector", lambda v, oc=oc: v.tensor_add(sacc[:, oc, 0:512], sacc[:, oc, 0:512], p_a[:]))
            sc.st("vector", lambda v, oc=oc: v.tensor_add(sacc[:, oc, 512:1024], sacc[:, oc, 512:1024], p_b[:]))

    for e in range(8):
        for ti in range(8):
            def row_e(t, e=e, ti=ti):
                return t.matmul(p_r[0:1, 0:128], sid8[:, e:e + 1], swet[:, ti, :], start=True, stop=True)
            sc.st("tensor", row_e)
            sc.st("scalar", lambda s, ti=ti: s.copy(sweRow[:, ti * 128:(ti + 1) * 128], p_r[0:1, 0:128]))
        for tt in range(2):
            sc.st("tensor", lambda t, tt=tt: t.matmul(p_a[:] if tt == 0 else p_b[:], srow1[:], sweRow[:, tt * 512:(tt + 1) * 512], start=True, stop=True))
        sc.st("scalar", lambda s: s.copy(sweb[:, 0:512], p_a[:]))
        sc.st("scalar", lambda s: s.copy(sweb[:, 512:1024], p_b[:]))
        for j in range(11):
            par = j % 2
            sc.dma("gpsimd", swu[:, par, :, :],
                   upw[e].rearrange("(c p) o -> p c o", p=128)[:, :, j * 128:j * 128 + 128], ldw)
            sc.dma("gpsimd", swo[:, par, :, :],
                   upw[e].rearrange("(c p) o -> p c o", p=128)[:, :, 1408 + j * 128:1408 + j * 128 + 128], ldw)
            lwn = ldw.n
            for tt in range(2):
                pg = p_a if tt == 0 else p_b
                pu = p_c if tt == 0 else p_d
                accmm(sc, pg[:], [(swu[:, par, c, :], sxn[:, c, tt * 512:(tt + 1) * 512]) for c in range(16)],
                      waits=[(ldw, lwn)])
                accmm(sc, pu[:], [(swo[:, par, c, :], sxn[:, c, tt * 512:(tt + 1) * 512]) for c in range(16)],
                      waits=[(ldw, lwn)])
                sc.st("scalar", lambda s, pg=pg: s.activation(ssil[:], pg[:], AF.Silu, bias=0.0, scale=1.0))
                sc.st("vector", lambda v, pu=pu: v.tensor_mul(st2[:], ssil[:], pu[:]))
                sc.st("vector", lambda v, j=j, tt=tt: v.tensor_mul(sh[:, j, tt * 512:(tt + 1) * 512], st2[:], sweb[:, tt * 512:(tt + 1) * 512]))
        for oc in range(16):
            par = oc % 2
            sc.dma("gpsimd", swd[:, par, :, :],
                   dnw[e].rearrange("(c p) o -> p c o", p=128)[:, :, oc * 128:(oc + 1) * 128], ldw)
            lwn = ldw.n
            for tt in range(2):
                pp = p_a if tt == 0 else p_b
                accmm(sc, pp[:], [(swd[:, par, j, :], sh[:, j, tt * 512:(tt + 1) * 512]) for j in range(11)],
                      waits=[(ldw, lwn)])
            sc.st("vector", lambda v, oc=oc: v.tensor_add(sacc[:, oc, 0:512], sacc[:, oc, 0:512], p_a[:]))
            sc.st("vector", lambda v, oc=oc: v.tensor_add(sacc[:, oc, 512:1024], sacc[:, oc, 512:1024], p_b[:]))

    sc.dma("sync", o_x1.rearrange("(c p) t -> p c t", p=128), sacc[:], stv)
    sc.st("sync", lambda e: None, waits=[(stv, stv.n)])
    sc.run()
    return nc


_L3_CACHE = {}


def run_launch3(inputs, xs_fm, xn_l, att_l, log_l):
    if "nc" not in _L3_CACHE:
        _L3_CACHE["nc"] = build_launch3()
    nc = _L3_CACHE["nc"]
    sgw_t = np.ascontiguousarray(np.asarray(inputs["L0_ffn_shared_gw"], np.float32).T)
    sdw_t = np.ascontiguousarray(np.asarray(inputs["L0_ffn_shared_dw"], np.float32).T)
    upw_t = np.ascontiguousarray(np.asarray(inputs["L0_ffn_up_w"], np.float32).transpose(0, 2, 1))
    dnw_t = np.ascontiguousarray(np.asarray(inputs["L0_ffn_down_w"], np.float32).transpose(0, 2, 1))
    wout = np.ascontiguousarray(np.asarray(inputs["L0_attn_w_out"], np.float32))
    gb = np.asarray(inputs["L0_ffn_gate_bias"], np.float32).reshape(1, 8)
    in_maps = []
    for c in range(NCORES):
        in_maps.append({
            "x_fm": xs_fm[c], "xn_fm": xn_l[c], "att_fm": att_l[c],
            "logits": log_l[c], "gbias": gb, "wout": wout,
            "sgw": sgw_t, "sdw": sdw_t, "upw": upw_t, "dnw": dnw_t,
        })
    res = run_bass_kernel_spmd(nc, in_maps, core_ids=list(range(NCORES)))
    return res.results





def build_launch4():
    nc = bass.Bass("TRN2", target_bir_lowering=False)
    x1 = nc.dram_tensor("x1_fm", [D, T], F32, kind="ExternalInput").ap()
    wq = nc.dram_tensor("wq", [D, D], F32, kind="ExternalInput").ap()
    wk = nc.dram_tensor("wk", [D, 512], F32, kind="ExternalInput").ap()
    wv = nc.dram_tensor("wv", [D, 512], F32, kind="ExternalInput").ap()
    cos1s = nc.dram_tensor("cos1s", [128, T], F32, kind="ExternalInput").ap()
    sin1s = nc.dram_tensor("sin1s", [128, T], F32, kind="ExternalInput").ap()
    o_xn = nc.dram_tensor("xn1_fm", [D, T], BF16, kind="ExternalOutput").ap()
    o_q = nc.dram_tensor("q_fm", [D, T], BF16, kind="ExternalOutput").ap()
    o_k = nc.dram_tensor("k_fm", [512, T], BF16, kind="ExternalOutput").ap()
    o_v = nc.dram_tensor("v_tm", [T, 512], BF16, kind="ExternalOutput").ap()

    ones_col = nc.inline_tensor(np.ones((128, 1), np.float32), name="c_ones_col")
    ones_row = nc.inline_tensor(np.ones((1, 128), np.float32), name="c_ones_row")
    eps_c = nc.inline_tensor(np.full((1, 1), LN_EPS, np.float32), name="c_eps")

    sc = Sched(nc)
    ld = sc.sem("ld")
    ldw = sc.sem("ldw")
    ldq = sc.sem("ldq")
    stv = sc.sem("stv")

    sx1 = nc.alloc_sbuf_tensor("sx1", [128, 16, 512], F32)
    sxn = nc.alloc_sbuf_tensor("sxn", [128, 16, 512], BF16)
    swk = nc.alloc_sbuf_tensor("swk", [128, 16, 512], BF16)
    swv = nc.alloc_sbuf_tensor("swv", [128, 16, 512], BF16)
    swq = nc.alloc_sbuf_tensor("swq", [128, 2, 16, 128], BF16)
    ssq = nc.alloc_sbuf_tensor("ssq", [128, 2, 512], F32)
    srow_stats = nc.alloc_sbuf_tensor("srow_stats", [1, 2, 512], F32)
    srow_ab = nc.alloc_sbuf_tensor("srow_ab", [1, 2, 512], F32)
    sA = nc.alloc_sbuf_tensor("sA", [128, 512], F32)
    sB = nc.alloc_sbuf_tensor("sB", [128, 512], F32)
    sones = nc.alloc_sbuf_tensor("sones", [128, 1], F32)
    srow1 = nc.alloc_sbuf_tensor("srow1", [1, 128], F32)
    seps = nc.alloc_sbuf_tensor("seps", [1, 1], F32)
    scos = nc.alloc_sbuf_tensor("scos", [128, T], F32)
    ssin = nc.alloc_sbuf_tensor("ssin", [128, T], F32)
    srp = nc.alloc_sbuf_tensor("srp", [128, 512], BF16)
    sdn = nc.alloc_sbuf_tensor("sdn", [64, 512], BF16)
    sup = nc.alloc_sbuf_tensor("sup", [128, 512], BF16)
    st1 = nc.alloc_sbuf_tensor("st1", [128, 512], F32)
    st2b = nc.alloc_sbuf_tensor("st2b", [128, 512], F32)
    sprj = nc.alloc_sbuf_tensor("sprj", [128, 512], BF16)

    p_a = nc.alloc_psum_tensor("p_a", [128, 512], F32)
    p_b = nc.alloc_psum_tensor("p_b", [128, 512], F32)
    p_s1 = nc.alloc_psum_tensor("p_s1", [1, 512], F32)
    p_s2 = nc.alloc_psum_tensor("p_s2", [1, 512], F32)
    p_bc = nc.alloc_psum_tensor("p_bc", [128, 512], F32)
    p_bc2 = nc.alloc_psum_tensor("p_bc2", [128, 512], F32)

    sc.dma("sync", sones[:], ones_col.ap(), ld)
    sc.dma("sync", srow1[:], ones_row.ap(), ld)
    sc.dma("sync", seps[:], eps_c.ap(), ld)
    sc.dma("sync", scos[:], cos1s, ld)
    sc.dma("sync", ssin[:], sin1s, ld)
    ld_all = ld.n
    sc.dma("gpsimd", swk[:], wk.rearrange("(c p) o -> p c o", p=128), ldw)
    sc.dma("gpsimd", swv[:], wv.rearrange("(c p) o -> p c o", p=128), ldw)
    ldw_all = ldw.n

    def rope_block(src128, t0, dst_store):
        c0, c1 = scos[0:64, t0:t0 + 512], scos[64:128, t0:t0 + 512]
        s0, s1 = ssin[0:64, t0:t0 + 512], ssin[64:128, t0:t0 + 512]
        sc.ops("vector", [
            lambda v: v.tensor_copy(sdn[:], src128[64:128, :]),
            lambda v: v.tensor_copy(sup[64:128, :], src128[0:64, :]),
            lambda v: v.tensor_mul(st1[0:64, :], src128[0:64, :], c0),
            lambda v: v.tensor_mul(st2b[0:64, :], sdn[:], s0),
            lambda v: v.tensor_sub(srp[0:64, :], st1[0:64, :], st2b[0:64, :]),
            lambda v: v.tensor_mul(st1[64:128, :], src128[64:128, :], c1),
            lambda v: v.tensor_mul(st2b[64:128, :], sup[64:128, :], s1),
            lambda v: v.tensor_add(srp[64:128, :], st1[64:128, :], st2b[64:128, :]),
        ])
        dst_store()

    for th in range(2):
        t0 = th * 512
        sc.dma("sync", sx1[:], x1.rearrange("(c p) t -> p c t", p=128)[:, :, t0:t0 + 512], ld,
               waits=[(stv, stv.n)])
        ldx = ld.n
        accmm(sc, p_s1[:], [(sones[:], sx1[:, c, :]) for c in range(16)], waits=[(ld, ldx)])
        for c in range(16):
            sc.st("vector", lambda v, c=c: v.tensor_mul(ssq[:, c % 2, :], sx1[:, c, :], sx1[:, c, :]),
                  waits=[(ld, ldx)])
            sc.st("tensor", lambda t, c=c: t.matmul(p_s2[:], sones[:], ssq[:, c % 2, :],
                                                    start=(c == 0), stop=(c == 15)))
        sc.ops("vector", [
            lambda v: v.tensor_scalar_mul(srow_stats[:, 0, :], p_s1[:], 1.0 / 2048.0),
            lambda v: v.tensor_scalar_mul(srow_stats[:, 1, :], p_s2[:], 1.0 / 2048.0),
            lambda v: v.tensor_mul(srow_ab[:, 0, :], srow_stats[:, 0, :], srow_stats[:, 0, :]),
            lambda v: v.tensor_sub(srow_ab[:, 1, :], srow_stats[:, 1, :], srow_ab[:, 0, :]),
        ])
        sc.ops("scalar", [
            lambda s: s.activation(srow_ab[:, 0, :], srow_ab[:, 1, :], AF.Ln, bias=seps[:], scale=1.0),
            lambda s: s.activation(srow_ab[:, 0, :], srow_ab[:, 0, :], AF.Exp, bias=0.0, scale=-0.5),
        ], waits=[(ld, ld_all)])
        sc.ops("vector", [
            lambda v: v.tensor_mul(srow_ab[:, 1, :], srow_stats[:, 0, :], srow_ab[:, 0, :]),
            lambda v: v.tensor_scalar_mul(srow_ab[:, 1, :], srow_ab[:, 1, :], -1.0),
        ])
        def bc_ab(t):
            t.matmul(p_bc[:], srow1[:], srow_ab[:, 0, :], start=True, stop=True)
            return t.matmul(p_bc2[:], srow1[:], srow_ab[:, 1, :], start=True, stop=True)
        sc.st("tensor", bc_ab)
        def evac_ab(s):
            s.copy(sA[:], p_bc[:])
            return s.copy(sB[:], p_bc2[:])
        sc.st("scalar", evac_ab)
        xn_ops = []
        for c in range(16):
            xn_ops.append(lambda v, c=c: v.tensor_mul(ssq[:, c % 2, :], sx1[:, c, :], sA[:]))
            xn_ops.append(lambda v, c=c: v.tensor_add(sxn[:, c, :], ssq[:, c % 2, :], sB[:]))
        sc.ops("vector", xn_ops, waits=[(stv, stv.n)])
        sc.dma("sync", o_xn.rearrange("(c p) t -> p c t", p=128)[:, :, t0:t0 + 512], sxn[:], stv)

        for ti in range(4):
            accmm(sc, p_a[:], [(sxn[:, c, ti * 128:(ti + 1) * 128], swv[:, c, :]) for c in range(16)],
                  waits=[(ldw, ldw_all)])
            sc.st("scalar", lambda s: s.copy(sprj[:], p_a[:]), waits=[(stv, stv.n)])
            sc.dma("sync", o_v[t0 + ti * 128:t0 + (ti + 1) * 128, :], sprj[:], stv)

        for g in range(4):
            accmm(sc, p_a[:], [(swk[:, c, g * 128:(g + 1) * 128], sxn[:, c, :]) for c in range(16)],
                  waits=[(ldw, ldw_all)])
            sc.st("scalar", lambda s: s.copy(sprj[:], p_a[:]), waits=[(stv, stv.n)])
            def store_k(g=g, t0=t0):
                sc.dma("sync", o_k.rearrange("(g p) t -> p g t", p=128)[:, g, t0:t0 + 512], srp[:], stv)
            rope_block(sprj[:], t0, store_k)

        for h in range(16):
            par = h % 2
            sc.dma("gpsimd", swq[:, par, :, :],
                   wq.rearrange("(c p) o -> p c o", p=128)[:, :, h * 128:(h + 1) * 128], ldq)
            lqn = ldq.n
            accmm(sc, p_a[:], [(swq[:, par, c, :], sxn[:, c, :]) for c in range(16)],
                  waits=[(ldq, lqn)])
            sc.st("scalar", lambda s: s.copy(sprj[:], p_a[:]), waits=[(stv, stv.n)])
            def store_q(h=h, t0=t0):
                sc.dma("sync", o_q.rearrange("(h p) t -> p h t", p=128)[:, h, t0:t0 + 512], srp[:], stv)
            rope_block(sprj[:], t0, store_q)

    sc.st("sync", lambda e: None, waits=[(stv, stv.n)])
    sc.run()
    return nc


_L4_CACHE = {}


def run_launch4(inputs, x1_l):
    if "nc" not in _L4_CACHE:
        _L4_CACHE["nc"] = build_launch4()
    nc = _L4_CACHE["nc"]
    cos1, sin1 = rope1_tables()
    wq = np.ascontiguousarray(np.asarray(inputs["L1_attn_Wq"], np.float32))
    wk = np.ascontiguousarray(np.asarray(inputs["L1_attn_Wk"], np.float32))
    wv = np.ascontiguousarray(np.asarray(inputs["L1_attn_Wv"], np.float32))
    in_maps = []
    for c in range(NCORES):
        pos0 = (c % 4) * T
        in_maps.append({
            "x1_fm": x1_l[c], "wq": wq, "wk": wk, "wv": wv,
            "cos1s": np.ascontiguousarray(cos1[:, pos0:pos0 + T]),
            "sin1s": np.ascontiguousarray(sin1[:, pos0:pos0 + T]),
        })
    res = run_bass_kernel_spmd(nc, in_maps, core_ids=list(range(NCORES)))
    return res.results



SC128 = 1.0 / np.sqrt(128.0)


def build_launch5():
    nc = bass.Bass("TRN2", target_bir_lowering=False)
    qp = nc.dram_tensor("qp", [512, S], BF16, kind="ExternalInput").ap()
    kg = nc.dram_tensor("kg", [128, S], BF16, kind="ExternalInput").ap()
    vg = nc.dram_tensor("vg", [S, 128], BF16, kind="ExternalInput").ap()
    hsp = nc.dram_tensor("hsp", [1, 4], F32, kind="ExternalInput").ap()
    o_attn = nc.dram_tensor("attn", [4, 128, S], BF16, kind="ExternalOutput").ap()

    masks = np.where((np.arange(4)[:, None, None] * 128 + np.arange(128)[None, :, None])
                     > np.arange(512)[None, None, :], NEG, 0.0).astype(np.float32)
    c_masks = nc.inline_tensor(masks.reshape(4 * 128, 512), name="c_masks")
    ones_col = nc.inline_tensor(np.ones((128, 1), np.float32), name="c_ones_col")
    ones_colb = nc.inline_tensor(np.ones((128, 1), np.float32).astype(BF16NP), name="c_ones_colb")
    ones_row = nc.inline_tensor(np.ones((1, 128), np.float32), name="c_ones_row")

    sc = Sched(nc)
    ld = sc.sem("ld")
    ldq = sc.sem("ldq")
    stv = sc.sem("stv")

    sKg = nc.alloc_sbuf_tensor("sKg", [128, S], BF16)
    sV = nc.alloc_sbuf_tensor("sV", [128, 32, 128], BF16)
    sQ = nc.alloc_sbuf_tensor("sQ", [128, S], BF16)
    sPt = nc.alloc_sbuf_tensor("sPt", [128, 32, 512], BF16)
    sOut = nc.alloc_sbuf_tensor("sOut", [128, S], BF16)
    smask = nc.alloc_sbuf_tensor("smask", [128, 4, 512], F32)
    ssq = nc.alloc_sbuf_tensor("ssq", [128, 512], F32)
    stiny = nc.alloc_sbuf_tensor("stiny", [1, 2, 8], F32)
    sscal = nc.alloc_sbuf_tensor("sscal", [1, 8], F32)
    sbias = nc.alloc_sbuf_tensor("sbias", [128, 1], F32)
    sRec = nc.alloc_sbuf_tensor("sRec", [128, 512], F32)
    srd = nc.alloc_sbuf_tensor("srd", [1, 512], F32)
    sones = nc.alloc_sbuf_tensor("sones", [128, 1], F32)
    sonesb = nc.alloc_sbuf_tensor("sonesb", [128, 1], BF16)
    srow1 = nc.alloc_sbuf_tensor("srow1", [1, 128], F32)
    shs = nc.alloc_sbuf_tensor("shs", [1, 4], F32)

    p_s = nc.alloc_psum_tensor("p_s", [128, 512], F32)
    p_s2 = nc.alloc_psum_tensor("p_s2", [128, 512], F32)
    p_o = nc.alloc_psum_tensor("p_o", [128, 512], F32)
    p_d = nc.alloc_psum_tensor("p_d", [1, 512], F32)
    p_bc = nc.alloc_psum_tensor("p_bc", [128, 512], F32)
    p_row = nc.alloc_psum_tensor("p_row", [1, 512], F32)
    p_col = nc.alloc_psum_tensor("p_col", [128, 2], F32)

    sc.dma("sync", sKg[:], kg, ld)
    sc.dma("sync", sV[:], vg.rearrange("(b p) d -> p b d", p=128), ld)
    sc.dma("sync", smask[:], c_masks.ap().rearrange("(m p) t -> p m t", p=128), ld)
    sc.dma("sync", sones[:], ones_col.ap(), ld)
    sc.dma("sync", sonesb[:], ones_colb.ap(), ld)
    sc.dma("sync", srow1[:], ones_row.ap(), ld)
    sc.dma("sync", shs[:], hsp, ld)
    ld_all = ld.n

    for kb8 in range(8):
        sc.st("vector", lambda v, kb8=kb8: v.tensor_mul(ssq[:], sKg[:, kb8 * 512:(kb8 + 1) * 512], sKg[:, kb8 * 512:(kb8 + 1) * 512]),
              waits=[(ld, ld_all)])
        accmm(sc, p_row[:], [(sones[:], ssq[:])])
        sc.st("vector", lambda v, kb8=kb8: v.tensor_reduce(stiny[0:1, 0, kb8:kb8 + 1], p_row[:], AX.X, ALU.max))

    for u in range(4):
        sc.dma("sync", sQ[:], qp[u * 128:(u + 1) * 128, :], ldq, waits=[(stv, stv.n)])
        lqn = ldq.n
        for qb in range(8):
            sc.st("vector", lambda v, qb=qb: v.tensor_mul(ssq[:], sQ[:, qb * 512:(qb + 1) * 512], sQ[:, qb * 512:(qb + 1) * 512]),
                  waits=[(ldq, lqn)])
            accmm(sc, p_row[:], [(sones[:], ssq[:])])
            sc.st("vector", lambda v, qb=qb: v.tensor_reduce(stiny[0:1, 1, qb:qb + 1], p_row[:], AX.X, ALU.max))
        sc.ops("vector", [
            lambda v: v.tensor_reduce(sscal[0:1, 0:1], stiny[0:1, 0, :], AX.X, ALU.max),
            lambda v: v.tensor_reduce(sscal[0:1, 1:2], stiny[0:1, 1, :], AX.X, ALU.max),
            lambda v: v.tensor_mul(sscal[0:1, 2:3], sscal[0:1, 0:1], sscal[0:1, 1:2]),
        ])
        sc.ops("scalar", [
            lambda s: s.activation(sscal[0:1, 3:4], sscal[0:1, 2:3], AF.Ln, bias=0.0, scale=1.0),
            lambda s: s.activation(sscal[0:1, 3:4], sscal[0:1, 3:4], AF.Exp, bias=0.0, scale=0.5),
        ])
        sc.st("vector", lambda v: v.tensor_scalar_mul(sscal[0:1, 4:5], sscal[0:1, 3:4], -SC128))
        sc.st("tensor", lambda t: t.matmul(p_col[:, 0:1], srow1[:], sscal[0:1, 4:5], start=True, stop=True))
        sc.st("scalar", lambda s: s.copy(sbias[:], p_col[:, 0:1]))

        for qc in range(8):
            nkb = 4 * (qc + 1)
            for kb in range(nkb):
                ps = p_s if kb % 2 == 0 else p_s2
                accmm(sc, ps[:], [(sKg[:, kb * 128:(kb + 1) * 128], sQ[:, qc * 512:(qc + 1) * 512])])
                if kb // 4 == qc:
                    m = kb % 4
                    sc.st("vector", lambda v, ps=ps, m=m: v.tensor_add(ps[:], ps[:], smask[:, m, :]))
                sc.st("scalar", lambda s, ps=ps, kb=kb: s.activation(
                    sPt[:, kb, :], ps[:], AF.Exp, bias=sbias[:], scale=SC128))
            accmm(sc, p_o[:], [(sV[:, kb, :], sPt[:, kb, :]) for kb in range(nkb)])
            accmm(sc, p_d[:], [(sonesb[:], sPt[:, kb, :]) for kb in range(nkb)])
            sc.st("vector", lambda v: v.reciprocal(srd[:], p_d[:]))
            sc.st("vector", lambda v, u=u: v.tensor_scalar(srd[:], srd[:], shs[0:1, u:u + 1], None, ALU.mult))
            sc.st("tensor", lambda t: t.matmul(p_bc[:], srow1[:], srd[:], start=True, stop=True))
            sc.st("scalar", lambda s: s.copy(sRec[:], p_bc[:]))
            sc.st("vector", lambda v, qc=qc: v.tensor_mul(sOut[:, qc * 512:(qc + 1) * 512], p_o[:], sRec[:]))
        sc.dma("sync", o_attn[u], sOut[:], stv)

    sc.st("sync", lambda e: None, waits=[(stv, stv.n)])
    sc.run()
    return nc


_L5_CACHE = {}


def run_launch5(inputs, q_full, k_full, v_full):
    if "nc" not in _L5_CACHE:
        _L5_CACHE["nc"] = build_launch5()
    nc = _L5_CACHE["nc"]
    hs = np.asarray(inputs["L1_attn_head_scale"], np.float32).reshape(16)
    in_maps = []
    for c in range(NCORES):
        s = c // 4
        g = c % 4
        in_maps.append({
            "qp": np.ascontiguousarray(q_full[s][g * 512:(g + 1) * 512, :]),
            "kg": np.ascontiguousarray(k_full[s][g * 128:(g + 1) * 128, :]),
            "vg": np.ascontiguousarray(v_full[s][:, g * 128:(g + 1) * 128]),
            "hsp": np.ascontiguousarray(hs[g * 4:(g + 1) * 4].reshape(1, 4)),
        })
    res = run_bass_kernel_spmd(nc, in_maps, core_ids=list(range(NCORES)))
    return res.results





RT2048 = float(np.sqrt(2048.0))
RT8192 = float(np.sqrt(8192.0))


def _hadamard128():
    h = np.array([[1.0]], np.float32)
    for _ in range(7):
        h = np.block([[h, h], [h, -h]])
    return h


def build_launch6():
    nc = bass.Bass("TRN2", target_bir_lowering=False)
    x1 = nc.dram_tensor("x1_fm", [D, T], F32, kind="ExternalInput").ap()
    xn1 = nc.dram_tensor("xn1_fm", [D, T], BF16, kind="ExternalInput").ap()
    att = nc.dram_tensor("att_fm", [D, T], BF16, kind="ExternalInput").ap()
    wo = nc.dram_tensor("wo", [D, D], F32, kind="ExternalInput").ap()
    wg = nc.dram_tensor("wg", [D, 8192], F32, kind="ExternalInput").ap()
    wu = nc.dram_tensor("wu", [D, 8192], F32, kind="ExternalInput").ap()
    wd = nc.dram_tensor("wd", [8192, D], F32, kind="ExternalInput").ap()
    o_x2 = nc.dram_tensor("x2_fm", [D, T], F32, kind="ExternalOutput").ap()
    h2scr = nc.dram_tensor("h2scr", [64, 128, T], BF16, kind="Internal").ap()

    ones_col = nc.inline_tensor(np.ones((128, 1), np.float32), name="c_ones_col")
    ones_row = nc.inline_tensor(np.ones((1, 128), np.float32), name="c_ones_row")
    magic_col = nc.inline_tensor(np.full((128, 1), MAGIC, np.float32), name="c_magic")
    h128 = nc.inline_tensor(_hadamard128().astype(BF16NP), name="c_h128")
    id128 = nc.inline_tensor(np.eye(128, dtype=np.float32), name="c_id128")

    sc = Sched(nc)
    ld = sc.sem("ld")
    ldw = sc.sem("ldw")
    sth = sc.sem("sth")
    ldh = sc.sem("ldh")
    stv = sc.sem("stv")

    sxn = nc.alloc_sbuf_tensor("sxn", [128, 16, T], BF16)
    sU = nc.alloc_sbuf_tensor("sU", [128, 64, 512], BF16)
    sF1 = nc.alloc_sbuf_tensor("sF1", [128, 64, 128], BF16)
    sF2 = nc.alloc_sbuf_tensor("sF2", [128, 64, 128], BF16)
    swb = nc.alloc_sbuf_tensor("swb", [128, 16, 256], BF16)
    swdb = nc.alloc_sbuf_tensor("swdb", [128, 64, 128], BF16)
    sTg = nc.alloc_sbuf_tensor("sTg", [128, 16, 128], BF16)
    sTu = nc.alloc_sbuf_tensor("sTu", [128, 16, 128], BF16)
    sqz = nc.alloc_sbuf_tensor("sqz", [128, 2048], F32)
    sgt = nc.alloc_sbuf_tensor("sgt", [128, 512], F32)
    sut = nc.alloc_sbuf_tensor("sut", [128, 512], F32)
    ssl = nc.alloc_sbuf_tensor("ssl", [128, 512], F32)
    sh2 = nc.alloc_sbuf_tensor("sh2", [128, 512], BF16)
    sMxF = nc.alloc_sbuf_tensor("sMxF", [128, 512], F32)
    sAbs = nc.alloc_sbuf_tensor("sAbs", [128, 512], F32)
    sMxC = nc.alloc_sbuf_tensor("sMxC", [128, 4], F32)
    srowm = nc.alloc_sbuf_tensor("srowm", [1, 8], F32)
    srowq = nc.alloc_sbuf_tensor("srowq", [1, 2, 512], F32)
    sQs = nc.alloc_sbuf_tensor("sQs", [128, 1, 512], BF16)
    sRg = nc.alloc_sbuf_tensor("sRg", [128, 512], F32)
    sRu = nc.alloc_sbuf_tensor("sRu", [128, 512], F32)
    sRd = nc.alloc_sbuf_tensor("sRd", [128, 512], F32)
    sAcc = nc.alloc_sbuf_tensor("sAcc", [128, 3], F32)
    sInv = nc.alloc_sbuf_tensor("sInv", [128, 3], F32)
    sScl = nc.alloc_sbuf_tensor("sScl", [128, 3], F32)
    sredc = nc.alloc_sbuf_tensor("sredc", [128, 1], F32)
    sones = nc.alloc_sbuf_tensor("sones", [128, 1], F32)
    srow1 = nc.alloc_sbuf_tensor("srow1", [1, 128], F32)
    smgc = nc.alloc_sbuf_tensor("smgc", [128, 1], F32)
    sH = nc.alloc_sbuf_tensor("sH", [128, 128], BF16)
    sid = nc.alloc_sbuf_tensor("sid", [128, 128], F32)
    sx1s = nc.alloc_sbuf_tensor("sx1s", [128, T], F32)
    sot = nc.alloc_sbuf_tensor("sot", [128, T], F32)

    p_a = nc.alloc_psum_tensor("p_a", [128, 512], F32)
    p_b = nc.alloc_psum_tensor("p_b", [128, 512], F32)
    p_t = nc.alloc_psum_tensor("p_t", [128, 128], F32)
    p_r = nc.alloc_psum_tensor("p_r", [1, 512], F32)
    p_bc = nc.alloc_psum_tensor("p_bc", [128, 512], F32)

    sxq = sU[:, 32:48, :]
    sc.dma("sync", sxn[:], xn1.rearrange("(c p) t -> p c t", p=128), ld)
    sc.dma("sync", sones[:], ones_col.ap(), ld)
    sc.dma("sync", srow1[:], ones_row.ap(), ld)
    sc.dma("sync", smgc[:], magic_col.ap(), ld)
    sc.dma("sync", sH[:], h128.ap(), ld)
    sc.dma("sync", sid[:], id128.ap(), ld)
    ld_all = ld.n

    scan_view = sU[:].rearrange("p a b -> p (a b)")
    first = [True, True, True]
    def scan_tiles(m, dram_tiles, width):
        for i, ap in enumerate(dram_tiles):
            par = i % 2
            dst = scan_view[:, par * 16384:par * 16384 + width]
            sc.dma("gpsimd", dst, ap, ldw)
            lwn = ldw.n
            if first[m]:
                sc.st("vector", lambda v, m=m, dst=dst: v.tensor_reduce(
                    sAcc[:, m:m + 1], dst, AX.X, ALU.add, apply_absolute_value=True),
                    waits=[(ldw, lwn)])
                first[m] = False
            else:
                sc.st("vector", lambda v, dst=dst: v.tensor_reduce(
                    sredc[:], dst, AX.X, ALU.add, apply_absolute_value=True),
                    waits=[(ldw, lwn)])
                sc.st("vector", lambda v, m=m: v.tensor_add(sAcc[:, m:m + 1], sAcc[:, m:m + 1], sredc[:]))
    wg_r = wg.rearrange("(c p) o -> p c o", p=128)
    wu_r = wu.rearrange("(c p) o -> p c o", p=128)
    wd_r = wd.rearrange("(c p) o -> p c o", p=128)
    scan_tiles(0, [wg_r[:, c, :] for c in range(16)], 8192)
    scan_tiles(1, [wu_r[:, c, :] for c in range(16)], 8192)
    scan_tiles(2, [wd_r[:, c, :] for c in range(64)], 2048)
    accmm(sc, p_t[0:1, 0:3], [(sones[:], sAcc[:])])
    sc.st("scalar", lambda s: s.copy(srowm[0:1, 0:3], p_t[0:1, 0:3]))
    sc.ops("vector", [
        lambda v: v.tensor_scalar(srowm[0:1, 0:3], srowm[0:1, 0:3], 1.0 / 16777216.0, 1e-5, ALU.mult, ALU.max),
        lambda v: v.reciprocal(srowm[0:1, 3:6], srowm[0:1, 0:3]),
    ])
    for m in range(3):
        sc.st("tensor", lambda t, m=m: t.matmul(p_t[:, 0:1], srow1[:], srowm[0:1, 3 + m:4 + m], start=True, stop=True))
        sc.st("scalar", lambda s, m=m: s.copy(sInv[:, m:m + 1], p_t[:, 0:1]))
        sc.st("tensor", lambda t, m=m: t.matmul(p_t[:, 0:1], srow1[:], srowm[0:1, m:m + 1], start=True, stop=True))
        sc.st("scalar", lambda s, m=m: s.copy(sScl[:, m:m + 1], p_t[:, 0:1]))

    satt = sU[:].rearrange("p a b -> p (a b)")[:, 0:16384].rearrange("p (c t) -> p c t", t=T)
    sc.dma("sync", satt, att.rearrange("(c p) t -> p c t", p=128), ld)
    ld_att = ld.n
    for oc in range(16):
        sc.dma("sync", sx1s[:], x1.rearrange("(c p) t -> p c t", p=128)[:, oc, :], ld,
               waits=[(stv, stv.n)])
        sc.dma("gpsimd", swb[:, :, 0:128],
               wo.rearrange("(c p) o -> p c o", p=128)[:, :, oc * 128:(oc + 1) * 128], ldw)
        ldn, lwn = ld.n, ldw.n
        for tt in range(2):
            pp = p_a if tt == 0 else p_b
            accmm(sc, pp[:], [(swb[:, hc, 0:128], satt[:, hc, tt * 512:(tt + 1) * 512]) for hc in range(16)],
                  waits=[(ld, ldn), (ldw, lwn)])
        sc.st("vector", lambda v: v.tensor_add(sot[:, 0:512], p_a[:], sx1s[:, 0:512]))
        sc.st("vector", lambda v: v.tensor_add(sot[:, 512:1024], p_b[:], sx1s[:, 512:1024]))
        sc.dma("sync", o_x2.rearrange("(c p) t -> p c t", p=128)[:, oc, :], sot[:], stv)
    stv_phaseA = stv.n

    def butterfly_stages(src3, bufA3, bufB3, nch, fd):
        nst = int(np.log2(nch))
        cur = src3
        bufs = [bufA3, bufB3]
        for st_i in range(nst):
            h = 1 << st_i
            g = nch // (2 * h)
            out = bufs[st_i % 2]
            ina = cur.rearrange("p (g two h) f -> p g two h f", two=2, h=h)
            outa = out.rearrange("p (g two h) f -> p g two h f", two=2, h=h)
            sc.st("vector", lambda v, ina=ina, outa=outa: v.tensor_add(
                outa[:, :, 0, :, :], ina[:, :, 0, :, :], ina[:, :, 1, :, :]))
            sc.st("vector", lambda v, ina=ina, outa=outa: v.tensor_sub(
                outa[:, :, 1, :, :], ina[:, :, 0, :, :], ina[:, :, 1, :, :]))
            cur = out
        return cur

    for th in range(2):
        t0 = th * 512
        fA = sF1[:].rearrange("p a b -> p (a b)").rearrange("p (c t) -> p c t", t=512)
        fB = sF2[:].rearrange("p a b -> p (a b)").rearrange("p (c t) -> p c t", t=512)
        fin = butterfly_stages(sxn[:, :, t0:t0 + 512], fA, fB, 16, 512)
        sfv = fA
        for c in range(16):
            sc.st("tensor", lambda t, c=c, fin=fin: t.matmul(p_a[:], sH[:], fin[:, c, :], start=True, stop=True))
            sc.st("scalar", lambda s, c=c: s.activation(sAbs[:], p_a[:], AF.Abs, bias=0.0, scale=1.0))
            sc.st("scalar", lambda s, c=c, sfv=sfv: s.copy(sfv[:, c, :], p_a[:]))
            if c == 0:
                sc.st("vector", lambda v: v.tensor_copy(sMxF[:], sAbs[:]))
            else:
                sc.st("vector", lambda v: v.tensor_max(sMxF[:], sMxF[:], sAbs[:]))
        for ti in range(4):
            sc.st("tensor", lambda t, ti=ti: t.transpose(p_t[:], sMxF[:, ti * 128:(ti + 1) * 128], sid[:]))
            sc.st("vector", lambda v, ti=ti: v.tensor_reduce(sMxC[:, ti:ti + 1], p_t[:], AX.X, ALU.max))
        for ti in range(4):
            sc.st("tensor", lambda t, ti=ti: t.transpose(p_t[0:1, 0:128], sMxC[:, ti:ti + 1], sid[:]))
            sc.st("scalar", lambda s, ti=ti: s.copy(srowq[0:1, 0, ti * 128:(ti + 1) * 128], p_t[0:1, 0:128]))
        sc.ops("vector", [
            lambda v: v.tensor_scalar_max(srowq[:, 0, :], srowq[:, 0, :], 1e-5 * RT2048),
            lambda v: v.tensor_scalar_mul(srowq[:, 1, :], srowq[:, 0, :], 1.0 / (127.0 * RT2048)),
            lambda v: v.reciprocal(srowq[:, 0, :], srowq[:, 0, :]),
            lambda v: v.tensor_scalar_mul(srowq[:, 0, :], srowq[:, 0, :], 127.0),
        ])
        sc.st("tensor", lambda t: t.matmul(p_bc[:], srow1[:], srowq[0:1, 0, :], start=True, stop=True))
        sc.st("scalar", lambda s: s.copy(sQs[:, 0, :], p_bc[:]))
        sc.st("tensor", lambda t: t.matmul(p_bc[:], srow1[:], srowq[0:1, 1, :], start=True, stop=True))
        sc.st("vector", lambda v: v.tensor_scalar(sRg[:], p_bc[:], sScl[:, 0:1], None, ALU.mult))
        sc.st("vector", lambda v: v.tensor_scalar(sRu[:], p_bc[:], sScl[:, 1:2], None, ALU.mult))
        for g4 in range(4):
            cs = slice(g4 * 4, g4 * 4 + 4)
            qv = sqz[:].rearrange("p (c t) -> p c t", t=512)[:, 0:4, :]
            def qmul(v, cs=cs, qv=qv, sfv=sfv):
                a, b = bass.broadcast_tensor_aps(sfv[:, cs, :], sQs[:])
                return v.tensor_tensor(qv, a, b, ALU.mult)
            sc.st("vector", qmul)
            sc.st("scalar", lambda s, qv=qv: s.activation(qv, qv, AF.Identity, bias=smgc[:], scale=1.0))
            sc.st("vector", lambda v, qv=qv: v.tensor_scalar(qv, qv, MAGIC, 127.0, ALU.subtract, ALU.min))
            sc.st("vector", lambda v, cs=cs, qv=qv: v.tensor_scalar(sxq[:, cs, :], qv, -128.0, None, ALU.max))

        sqzv = sqz[:].rearrange("p (c o) -> p c o", o=128)
        for hc in range(64):
            o0 = hc * 128
            for (wsrc, dstT, m) in ((wg_r, sTg, 0), (wu_r, sTu, 1)):
                sc.dma("sync", sqzv, wsrc[:, :, o0:o0 + 128], ldw)
                lwn = ldw.n
                dv = dstT[:].rearrange("p c o -> p (c o)")
                sc.st("scalar", lambda s, m=m: s.activation(
                    sqz[:], sqz[:], AF.Identity, bias=smgc[:], scale=sInv[:, m:m + 1]),
                    waits=[(ldw, lwn)])
                sc.st("vector", lambda v: v.tensor_scalar(sqz[:], sqz[:], MAGIC, 1.0, ALU.subtract, ALU.min))
                sc.st("vector", lambda v, dv=dv: v.tensor_scalar(dv, sqz[:], -1.0, None, ALU.max))
            accmm(sc, p_a[:], [(sTg[:, c, 0:128], sxq[:, c, :]) for c in range(16)])
            accmm(sc, p_b[:], [(sTu[:, c, 0:128], sxq[:, c, :]) for c in range(16)])
            sc.st("vector", lambda v: v.tensor_mul(sgt[:], p_a[:], sRg[:]))
            sc.st("scalar", lambda s: s.activation(ssl[:], sgt[:], AF.Silu, bias=0.0, scale=1.0))
            sc.st("vector", lambda v: v.tensor_mul(sut[:], p_b[:], sRu[:]))
            sc.st("vector", lambda v: v.tensor_mul(sh2[:], ssl[:], sut[:]), waits=[(sth, sth.n)])
            sc.dma("sync", h2scr.rearrange("h p t -> p h t")[:, hc, t0:t0 + 512], sh2[:], sth)

        for tq in range(4):
            tq0 = t0 + tq * 128
            sc.dma("sync", sF1[:], h2scr.rearrange("h p t -> p h t")[:, :, tq0:tq0 + 128], ldh,
                   waits=[(sth, sth.n)])
            ldh_n = ldh.n
            sc.st("vector", lambda v: v.engine_nop(), waits=[(ldh, ldh_n)])
            fin8 = butterfly_stages(sF1[:], sF2[:], sF1[:], 64, 128)
            other = sF1[:] if fin8.tensor.name == "sF2" else sF2[:]
            for g4 in range(16):
                sc.st("tensor", lambda t, g4=g4, fin8=fin8: t.matmul(
                    p_a[:], sH[:], fin8[:, g4 * 4:(g4 + 1) * 4, :].rearrange("p c t -> p (c t)"),
                    start=True, stop=True))
                sc.st("scalar", lambda s: s.activation(sAbs[:], p_a[:], AF.Abs, bias=0.0, scale=1.0))
                sc.st("scalar", lambda s, g4=g4, other=other: s.copy(
                    other[:, g4 * 4:(g4 + 1) * 4, :].rearrange("p c t -> p (c t)"), p_a[:]))
                if g4 == 0:
                    sc.st("vector", lambda v: v.tensor_copy(sMxF[:], sAbs[:]))
                else:
                    sc.st("vector", lambda v: v.tensor_max(sMxF[:], sMxF[:], sAbs[:]))
            sc.ops("vector", [
                lambda v: v.tensor_max(sMxF[:, 0:128], sMxF[:, 0:128], sMxF[:, 128:256]),
                lambda v: v.tensor_max(sMxF[:, 256:384], sMxF[:, 256:384], sMxF[:, 384:512]),
                lambda v: v.tensor_max(sMxF[:, 0:128], sMxF[:, 0:128], sMxF[:, 256:384]),
            ])
            sc.st("tensor", lambda t: t.transpose(p_t[:], sMxF[:, 0:128], sid[:]))
            sc.st("vector", lambda v: v.tensor_reduce(sMxC[:, 0:1], p_t[:], AX.X, ALU.max))
            sc.st("tensor", lambda t: t.transpose(p_t[0:1, 0:128], sMxC[:, 0:1], sid[:]))
            sc.st("scalar", lambda s: s.copy(srowq[0:1, 0, 0:128], p_t[0:1, 0:128]))
            sc.ops("vector", [
                lambda v: v.tensor_scalar_max(srowq[:, 0, 0:128], srowq[:, 0, 0:128], 1e-5 * RT8192),
                lambda v: v.tensor_scalar_mul(srowq[:, 1, 0:128], srowq[:, 0, 0:128], 1.0 / (127.0 * RT8192)),
                lambda v: v.reciprocal(srowq[:, 0, 0:128], srowq[:, 0, 0:128]),
                lambda v: v.tensor_scalar_mul(srowq[:, 0, 0:128], srowq[:, 0, 0:128], 127.0),
            ])
            sc.st("tensor", lambda t: t.matmul(p_bc[:, 0:128], srow1[:], srowq[0:1, 0, 0:128], start=True, stop=True))
            sc.st("scalar", lambda s: s.copy(sQs[:, 0, 0:128], p_bc[:, 0:128]))
            sc.st("tensor", lambda t: t.matmul(p_bc[:, 0:128], srow1[:], srowq[0:1, 1, 0:128], start=True, stop=True))
            sc.st("vector", lambda v, tq=tq: v.tensor_scalar(sRd[:, tq * 128:(tq + 1) * 128], p_bc[:, 0:128], sScl[:, 2:3], None, ALU.mult))
            for g16 in range(4):
                cs = slice(g16 * 16, g16 * 16 + 16)
                qv = sqz[:].rearrange("p (c t) -> p c t", t=128)[:, 0:16, :]
                def qmul8(v, cs=cs, qv=qv, other=other):
                    a, b = bass.broadcast_tensor_aps(other[:, cs, :], sQs[:, :, 0:128])
                    return v.tensor_tensor(qv, a, b, ALU.mult)
                sc.st("vector", qmul8)
                sc.st("scalar", lambda s, qv=qv: s.activation(qv, qv, AF.Identity, bias=smgc[:], scale=1.0))
                sc.st("vector", lambda v, qv=qv: v.tensor_scalar(qv, qv, MAGIC, 127.0, ALU.subtract, ALU.min))
                sc.st("vector", lambda v, cs=cs, qv=qv, tq=tq: v.tensor_scalar(
                    sU[:, cs, tq * 128:(tq + 1) * 128], qv, -128.0, None, ALU.max))

        for oc in range(16):
            for qt in range(4):
                view = swdb[:, qt * 16:(qt + 1) * 16, :].rearrange("p c o -> p (c o)")
                sc.dma("sync", sqzv, wd_r[:, qt * 16:(qt + 1) * 16, oc * 128:(oc + 1) * 128], ldw)
                lwn = ldw.n
                sc.st("scalar", lambda s: s.activation(
                    sqz[:], sqz[:], AF.Identity, bias=smgc[:], scale=sInv[:, 2:3]),
                    waits=[(ldw, lwn)])
                sc.st("vector", lambda v: v.tensor_scalar(sqz[:], sqz[:], MAGIC, 1.0, ALU.subtract, ALU.min))
                sc.st("vector", lambda v, view=view: v.tensor_scalar(view, sqz[:], -1.0, None, ALU.max))
            accmm(sc, p_a[:], [(swdb[:, c, :], sU[:, c, :]) for c in range(64)])
            sc.st("vector", lambda v: v.tensor_mul(sot[:, 0:512], p_a[:], sRd[:]), waits=[(stv, stv.n)])
            sc.dma("gpsimd", o_x2.rearrange("(c p) t -> p c t", p=128)[:, oc, t0:t0 + 512],
                   sot[:, 0:512], stv, waits=[(stv, stv_phaseA)], accum=True)

    sc.st("sync", lambda e: None, waits=[(stv, stv.n), (sth, sth.n)])
    sc.run()
    return nc


_L6_CACHE = {}


def run_launch6(inputs, x1_l, xn1_l, att1_l):
    if "nc" not in _L6_CACHE:
        _L6_CACHE["nc"] = build_launch6()
    nc = _L6_CACHE["nc"]
    wo = np.ascontiguousarray(np.asarray(inputs["L1_attn_Wo"], np.float32))
    wg_t = np.ascontiguousarray(np.asarray(inputs["L1_ffn_w_gate"], np.float32).T)
    wu_t = np.ascontiguousarray(np.asarray(inputs["L1_ffn_w_up"], np.float32).T)
    wd_t = np.ascontiguousarray(np.asarray(inputs["L1_ffn_w_down"], np.float32).T)
    in_maps = []
    for c in range(NCORES):
        in_maps.append({
            "x1_fm": x1_l[c], "xn1_fm": xn1_l[c], "att_fm": att1_l[c],
            "wo": wo, "wg": wg_t, "wu": wu_t, "wd": wd_t,
        })
    res = run_bass_kernel_spmd(nc, in_maps, core_ids=list(range(NCORES)))
    return res.results


def kernel(**inputs):
    x = np.ascontiguousarray(np.asarray(inputs["L0_attn_x"], np.float32).reshape(B * S, D))
    xs_fm = [np.ascontiguousarray(x[c * T:(c + 1) * T].T) for c in range(NCORES)]

    r1 = run_launch1(inputs, xs_fm)
    ckv_full = [np.ascontiguousarray(np.concatenate([r1[4 * s + j]["ckv_fm"] for j in range(4)], axis=1)) for s in range(2)]
    kr_full = [np.ascontiguousarray(np.concatenate([r1[4 * s + j]["kr_fm"] for j in range(4)], axis=1)) for s in range(2)]
    cq_full = [np.ascontiguousarray(np.concatenate([r1[4 * s + j]["cq_fm"] for j in range(4)], axis=1)) for s in range(2)]

    r2 = run_launch2(inputs, ckv_full, kr_full, cq_full)
    att_seq = [np.concatenate([np.asarray(r2[4 * s + j]["attn"]).reshape(512, S) for j in range(4)], axis=0)
               for s in range(2)]
    att_l = [np.ascontiguousarray(att_seq[c // 4][:, (c % 4) * T:(c % 4 + 1) * T]) for c in range(NCORES)]
    xn_l = [r1[c]["xn_fm"] for c in range(NCORES)]
    log_l = [r1[c]["logits"] for c in range(NCORES)]

    r3 = run_launch3(inputs, xs_fm, xn_l, att_l, log_l)
    x1_l = [np.ascontiguousarray(r3[c]["x1_fm"]) for c in range(NCORES)]

    r4 = run_launch4(inputs, x1_l)
    q_full = [np.ascontiguousarray(np.concatenate([np.asarray(r4[4 * s + j]["q_fm"]) for j in range(4)], axis=1)) for s in range(2)]
    k_full = [np.ascontiguousarray(np.concatenate([np.asarray(r4[4 * s + j]["k_fm"]) for j in range(4)], axis=1)) for s in range(2)]
    v_full = [np.ascontiguousarray(np.concatenate([np.asarray(r4[4 * s + j]["v_tm"]) for j in range(4)], axis=0)) for s in range(2)]
    xn1_l = [r4[c]["xn1_fm"] for c in range(NCORES)]

    r5 = run_launch5(inputs, q_full, k_full, v_full)
    att1_seq = [np.concatenate([np.asarray(r5[4 * s + j]["attn"]).reshape(512, S) for j in range(4)], axis=0)
                for s in range(2)]
    att1_l = [np.ascontiguousarray(att1_seq[c // 4][:, (c % 4) * T:(c % 4 + 1) * T]) for c in range(NCORES)]

    r6 = run_launch6(inputs, x1_l, xn1_l, att1_l)
    x2 = np.concatenate([np.asarray(r6[c]["x2_fm"], np.float32).T for c in range(NCORES)], axis=0)
    return np.ascontiguousarray(x2.reshape(B, S, D))


# version 02
# speedup vs baseline: 1.0038x; 1.0038x over previous
import numpy as np
import ml_dtypes
import concourse.bass as bass
import concourse.mybir as mybir
from concourse.bass_utils import run_bass_kernel_spmd

F32 = mybir.dt.float32
BF16 = mybir.dt.bfloat16
AF = mybir.ActivationFunctionType
ALU = mybir.AluOpType
AX = mybir.AxisListType
BF16NP = ml_dtypes.bfloat16

B, S, D = 2, 4096, 2048
NCORES = 8
T = 1024
NEG = -1.0e30
MAGIC = 12582912.0
LN_EPS = 1e-5


class Sem:
    __slots__ = ("name", "n", "h")

    def __init__(self, name):
        self.name, self.n, self.h = name, 0, None


class Sched:
    def __init__(self, nc):
        self.nc = nc
        self.steps = []
        self.sems = {}
        self.cmp = self.sem("cmp")
        self.pe = self.sem("pe")

    def sem(self, name):
        s = self.sems.get(name)
        if s is None:
            s = Sem(name)
            self.sems[name] = s
        return s

    def st(self, eng, fn, waits=(), incs=(), auto=True, engine_inc=True):
        ws = []
        if auto:
            if self.cmp.n:
                ws.append((self.cmp, self.cmp.n))
            if self.pe.n:
                ws.append((self.pe, self.pe.n))
        for w in waits:
            ws.append((w, w.n) if isinstance(w, Sem) else w)
        cs = [(c, 1) if isinstance(c, Sem) else c for c in incs]
        if engine_inc:
            cs.append((self.pe, 1) if eng == "tensor" else (self.cmp, 1))
        for (sm, n) in cs:
            sm.n += n
        self.steps.append((eng, fn, ws, cs))

    def ops(self, eng, fns, waits=()):
        for i, fn in enumerate(fns):
            self.st(eng, fn, waits=waits if i == 0 else ())

    def dma(self, eng, out_ap, in_ap, sem, waits=(), accum=False):
        def fn(e, out_ap=out_ap, in_ap=in_ap, accum=accum):
            if accum:
                return e.dma_start(out=out_ap, in_=in_ap, accum_op=mybir.AluOpType.add)
            return e.dma_start(out=out_ap, in_=in_ap)
        self.st(eng, fn, waits=waits, incs=[(sem, 16)], engine_inc=False)

    def run(self):
        nc = self.nc
        for s in self.sems.values():
            s.h = nc.alloc_semaphore(s.name)
        with nc.Block() as block:
            for engname in ("sync", "gpsimd", "tensor", "vector", "scalar"):
                mine = [s for s in self.steps if s[0] == engname]
                if not mine:
                    continue

                def go(eng, mine=mine):
                    for _, fn, ws, cs in mine:
                        for (sm, v) in ws:
                            if v > 0:
                                eng.wait_ge(sm.h, v)
                        last = fn(eng)
                        for (sm, n) in cs:
                            tgt = last if last is not None else eng.nop()
                            tgt.then_inc(sm.h, n)
                            last = None
                getattr(block, engname)(go)


def accmm(sc, psum_ap, pairs, waits=()):
    def fn(t):
        last = None
        n = len(pairs)
        for i, (l, r) in enumerate(pairs):
            last = t.matmul(psum_ap, l, r, start=(i == 0), stop=(i == n - 1))
        return last
    sc.st("tensor", fn, waits=waits)



def pope_tables():
    half = 32
    theta = 1.0 / (10000.0 ** (2.0 * np.arange(half, dtype=np.float64) / 64.0))
    ang = np.arange(S, dtype=np.float64)[:, None] * theta[None, :]
    return (np.cos(ang).astype(np.float32).T.copy(),
            np.sin(ang).astype(np.float32).T.copy())


def rope1_tables():
    inv = 1.0 / (10000.0 ** (np.arange(0, 128, 2, dtype=np.float64) / 128.0))
    fr = np.arange(S, dtype=np.float64)[:, None] * inv[None, :]
    emb = np.concatenate([fr, fr], axis=1)
    return (np.cos(emb).astype(np.float32).T.copy(),
            np.sin(emb).astype(np.float32).T.copy())



def build_launch1():
    nc = bass.Bass("TRN2", target_bir_lowering=False)
    x = nc.dram_tensor("x_fm", [D, T], F32, kind="ExternalInput").ap()
    w_dkv = nc.dram_tensor("w_dkv", [D, 512], F32, kind="ExternalInput").ap()
    w_dq = nc.dram_tensor("w_dq", [D, 1024], F32, kind="ExternalInput").ap()
    w_kr = nc.dram_tensor("w_kr", [D, 64], F32, kind="ExternalInput").ap()
    gw_t = nc.dram_tensor("gw_t", [D, 8], F32, kind="ExternalInput").ap()
    delta = nc.dram_tensor("delta", [1, 32], F32, kind="ExternalInput").ap()
    cos0s = nc.dram_tensor("cos0s", [32, T], F32, kind="ExternalInput").ap()
    sin0s = nc.dram_tensor("sin0s", [32, T], F32, kind="ExternalInput").ap()

    o_xn = nc.dram_tensor("xn_fm", [D, T], BF16, kind="ExternalOutput").ap()
    o_ckv = nc.dram_tensor("ckv_fm", [512, T], BF16, kind="ExternalOutput").ap()
    o_cq = nc.dram_tensor("cq_fm", [1024, T], BF16, kind="ExternalOutput").ap()
    o_kr = nc.dram_tensor("kr_fm", [64, T], BF16, kind="ExternalOutput").ap()
    o_log = nc.dram_tensor("logits", [T, 8], F32, kind="ExternalOutput").ap()

    sc = Sched(nc)
    ld = sc.sem("ld")
    ldw = sc.sem("ldw")
    stv = sc.sem("stv")

    sx = nc.alloc_sbuf_tensor("sx", [128, 16, T], F32)
    sxn = nc.alloc_sbuf_tensor("sxn", [128, 16, T], BF16)
    swdkv = nc.alloc_sbuf_tensor("swdkv", [128, 16, 512], BF16)
    swdq = nc.alloc_sbuf_tensor("swdq", [128, 16, 512], BF16)
    swkr = nc.alloc_sbuf_tensor("swkr", [128, 16, 64], BF16)
    sgw = nc.alloc_sbuf_tensor("sgw", [128, 16, 8], F32)
    ssq = nc.alloc_sbuf_tensor("ssq", [128, 2, 512], F32)
    sones = nc.alloc_sbuf_tensor("sones", [128, 1], F32)
    srow1 = nc.alloc_sbuf_tensor("srow1", [1, 128], F32)
    srow_stats = nc.alloc_sbuf_tensor("srow_stats", [1, 2, 512], F32)
    srow_ab = nc.alloc_sbuf_tensor("srow_ab", [1, 2, 512], F32)
    sA = nc.alloc_sbuf_tensor("sA", [128, 512], F32)
    sB = nc.alloc_sbuf_tensor("sB", [128, 512], F32)
    s_mr_col = nc.alloc_sbuf_tensor("s_mr_col", [128, 2], F32)
    sSrow = nc.alloc_sbuf_tensor("sSrow", [1, 8], F32)
    sSb = nc.alloc_sbuf_tensor("sSb", [128, 8], F32)
    slog = nc.alloc_sbuf_tensor("slog", [128, 8, 8], F32)
    sckv = nc.alloc_sbuf_tensor("sckv", [128, 4, T], BF16)
    scq = nc.alloc_sbuf_tensor("scq", [128, 8, T], BF16)
    skr_raw = nc.alloc_sbuf_tensor("skr_raw", [64, T], F32)
    skr = nc.alloc_sbuf_tensor("skr", [64, T], BF16)
    ssp = nc.alloc_sbuf_tensor("ssp", [64, T], F32)
    ssp2 = nc.alloc_sbuf_tensor("ssp2", [64, T], F32)
    sdelta = nc.alloc_sbuf_tensor("sdelta", [1, 32], F32)
    sdrow = nc.alloc_sbuf_tensor("sdrow", [1, 4, 32], F32)
    sdcol = nc.alloc_sbuf_tensor("sdcol", [32, 2], F32)
    scos0 = nc.alloc_sbuf_tensor("scos0", [32, T], F32)
    ssin0 = nc.alloc_sbuf_tensor("ssin0", [32, T], F32)
    seps = nc.alloc_sbuf_tensor("seps", [1, 1], F32)
    smu2 = nc.alloc_sbuf_tensor("smu2", [32, T], F32)
    smu2b = nc.alloc_sbuf_tensor("smu2b", [32, T], F32)

    ones_col = nc.inline_tensor(np.ones((128, 1), np.float32), name="c_ones_col")
    ones_row = nc.inline_tensor(np.ones((1, 128), np.float32), name="c_ones_row")
    eps_c = nc.inline_tensor(np.full((1, 1), LN_EPS, np.float32), name="c_eps")

    p_big = nc.alloc_psum_tensor("p_big", [128, 512], F32)
    p_big2 = nc.alloc_psum_tensor("p_big2", [128, 512], F32)
    p_s1 = nc.alloc_psum_tensor("p_s1", [1, 512], F32)
    p_s2 = nc.alloc_psum_tensor("p_s2", [1, 512], F32)
    p_bc = nc.alloc_psum_tensor("p_bc", [128, 512], F32)
    p_bc2 = nc.alloc_psum_tensor("p_bc2", [128, 512], F32)
    p_col = nc.alloc_psum_tensor("p_col", [128, 2], F32)
    p_row8 = nc.alloc_psum_tensor("p_row8", [1, 8], F32)

    sc.dma("sync", sx[:], x.rearrange("(c p) t -> p c t", p=128), ld)
    sc.dma("sync", sones[:], ones_col.ap(), ld)
    sc.dma("sync", srow1[:], ones_row.ap(), ld)
    sc.dma("sync", sdelta[:], delta, ld)
    sc.dma("sync", scos0[:], cos0s, ld)
    sc.dma("sync", ssin0[:], sin0s, ld)
    sc.dma("sync", sgw[:], gw_t.rearrange("(c p) o -> p c o", p=128), ld)
    sc.dma("sync", seps[:], eps_c.ap(), ld)
    ld_all = ld.n
    sc.dma("gpsimd", swdkv[:], w_dkv.rearrange("(c p) o -> p c o", p=128), ldw)
    sc.dma("gpsimd", swdq[:], w_dq.rearrange("(c p) o -> p c o", p=128)[:, :, 0:512], ldw)
    sc.dma("gpsimd", swkr[:], w_kr.rearrange("(c p) o -> p c o", p=128), ldw)
    ldw_all = ldw.n

    accmm(sc, p_row8[:], [(sones[:], sgw[:, c, :]) for c in range(16)],
          waits=[(ld, ld_all)])
    sc.st("scalar", lambda s: s.copy(sSrow[:], p_row8[:]))
    sc.st("tensor", lambda t: t.matmul(p_bc[:, 0:8], srow1[:], sSrow[:], start=True, stop=True))
    sc.st("scalar", lambda s: s.copy(sSb[:], p_bc[:, 0:8]))

    for tt in range(2):
        t0 = tt * 512
        accmm(sc, p_s1[:], [(sones[:], sx[:, c, t0:t0 + 512]) for c in range(16)],
              waits=[(ld, ld_all)])
        for c in range(16):
            sl = c % 2
            sc.st("vector",
                  lambda v, c=c, sl=sl, t0=t0:
                  v.tensor_mul(ssq[:, sl, :], sx[:, c, t0:t0 + 512], sx[:, c, t0:t0 + 512]),
                  waits=[(ld, ld_all)])
            sc.st("tensor",
                  lambda t, c=c, sl=sl:
                  t.matmul(p_s2[:], sones[:], ssq[:, sl, :], start=(c == 0), stop=(c == 15)))

        sc.ops("vector", [
            lambda v: v.tensor_scalar_mul(srow_stats[:, 0, :], p_s1[:], 1.0 / 2048.0),
            lambda v: v.tensor_scalar_mul(srow_stats[:, 1, :], p_s2[:], 1.0 / 2048.0),
            lambda v: v.tensor_mul(srow_ab[:, 0, :], srow_stats[:, 0, :], srow_stats[:, 0, :]),
            lambda v: v.tensor_sub(srow_ab[:, 1, :], srow_stats[:, 1, :], srow_ab[:, 0, :]),
        ])

        sc.ops("scalar", [
            lambda s: s.activation(srow_ab[:, 0, :], srow_ab[:, 1, :], AF.Ln, bias=seps[:], scale=1.0),
            lambda s: s.activation(srow_ab[:, 0, :], srow_ab[:, 0, :], AF.Exp, bias=0.0, scale=-0.5),
        ], waits=[(ld, ld_all)])

        sc.ops("vector", [
            lambda v: v.tensor_mul(srow_ab[:, 1, :], srow_stats[:, 0, :], srow_ab[:, 0, :]),
            lambda v: v.tensor_scalar_mul(srow_ab[:, 1, :], srow_ab[:, 1, :], -1.0),
        ])

        def bc_ab(t, tt=tt):
            t.matmul(p_bc[:], srow1[:], srow_ab[:, 0, :], start=True, stop=True)
            return t.matmul(p_bc2[:], srow1[:], srow_ab[:, 1, :], start=True, stop=True)
        sc.st("tensor", bc_ab)

        def evac_ab(s):
            s.copy(sA[:], p_bc[:])
            return s.copy(sB[:], p_bc2[:])
        sc.st("scalar", evac_ab)

        xn_ops = []
        for c in range(16):
            xn_ops.append(lambda v, c=c, t0=t0: v.tensor_mul(ssq[:, c % 2, :], sx[:, c, t0:t0 + 512], sA[:]))
            xn_ops.append(lambda v, c=c, t0=t0: v.tensor_add(sxn[:, c, t0:t0 + 512], ssq[:, c % 2, :], sB[:]))
        sc.ops("vector", xn_ops)

        for q in range(4):
            ti = tt * 4 + q
            tq = ti * 128
            accmm(sc, p_big[:, 0:8],
                  [(sx[:, c, tq:tq + 128], sgw[:, c, :]) for c in range(16)],
                  waits=[(ld, ld_all)])

            def tr_mr(t, off=q * 128):
                t.matmul(p_col[:, 0:1], srow_stats[0:1, 0, off:off + 128], sones[0:1, 0:1],
                         start=True, stop=True)
                return t.matmul(p_col[:, 1:2], srow_ab[0:1, 0, off:off + 128], sones[0:1, 0:1],
                                start=True, stop=True)
            sc.st("tensor", tr_mr)
            sc.st("scalar", lambda s: s.copy(s_mr_col[:], p_col[:, 0:2]))

            sc.ops("vector", [
                lambda v, ti=ti: v.tensor_scalar(slog[:, ti, :], sSb[:], s_mr_col[:, 0:1], None, ALU.mult),
                lambda v, ti=ti: v.tensor_sub(slog[:, ti, :], p_big[:, 0:8], slog[:, ti, :]),
                lambda v, ti=ti: v.tensor_scalar(slog[:, ti, :], slog[:, ti, :], s_mr_col[:, 1:2], None, ALU.mult),
            ])

    sc.dma("sync", o_xn.rearrange("(c p) t -> p c t", p=128), sxn[:], stv)
    sc.dma("sync", o_log.rearrange("(q p) o -> p q o", p=128), slog[:], stv)

    ldw2 = sc.sem("ldw2")
    for (wt, outt, nout, wsem, wval) in ((swdkv, sckv, 4, ldw, ldw_all),
                                         (swdq, scq, 4, ldw, ldw_all)):
        for oc in range(nout):
            for tt in range(2):
                t0 = tt * 512
                pb = p_big if (oc + tt) % 2 == 0 else p_big2
                accmm(sc, pb[:],
                      [(wt[:, c, oc * 128:(oc + 1) * 128], sxn[:, c, t0:t0 + 512]) for c in range(16)],
                      waits=[(wsem, wval)])
                sc.st("scalar",
                      lambda s, outt=outt, oc=oc, t0=t0, pb=pb:
                      s.copy(outt[:, oc, t0:t0 + 512], pb[:]))
    sc.dma("gpsimd", swdq[:], w_dq.rearrange("(c p) o -> p c o", p=128)[:, :, 512:1024], ldw2)
    for oc in range(4):
        for tt in range(2):
            t0 = tt * 512
            pb = p_big if (oc + tt) % 2 == 0 else p_big2
            accmm(sc, pb[:],
                  [(swdq[:, c, oc * 128:(oc + 1) * 128], sxn[:, c, t0:t0 + 512]) for c in range(16)],
                  waits=[(ldw2, 16)])
            sc.st("scalar",
                  lambda s, oc=oc, t0=t0, pb=pb:
                  s.copy(scq[:, 4 + oc, t0:t0 + 512], pb[:]))
    for tt in range(2):
        t0 = tt * 512
        pb = p_big if tt % 2 == 0 else p_big2
        accmm(sc, pb[0:64, :],
              [(swkr[:, c, :], sxn[:, c, t0:t0 + 512]) for c in range(16)],
              waits=[(ldw, ldw_all)])
        sc.st("scalar", lambda s, t0=t0, pb=pb: s.copy(skr_raw[:, t0:t0 + 512], pb[0:64, :]))

    sc.dma("sync", o_ckv.rearrange("(c p) t -> p c t", p=128), sckv[:], stv)
    sc.dma("sync", o_cq.rearrange("(c p) t -> p c t", p=128), scq[:], stv)

    sc.ops("vector", [
        lambda v: v.tensor_scalar(sdrow[:, 0, :], sdelta[:], 1.0 / (2.0 * np.pi), MAGIC, ALU.mult, ALU.add),
        lambda v: v.tensor_scalar(sdrow[:, 0, :], sdrow[:, 0, :], MAGIC, None, ALU.subtract),
        lambda v: v.tensor_scalar(sdrow[:, 0, :], sdrow[:, 0, :], 2.0 * np.pi, None, ALU.mult),
        lambda v: v.tensor_sub(sdrow[:, 1, :], sdelta[:], sdrow[:, 0, :]),
        lambda v: v.tensor_scalar_mul(sdrow[:, 2, :], sdrow[:, 1, :], 0.5),
    ], waits=[(ld, ld_all)])

    sc.ops("scalar", [
        lambda s: s.activation(sdrow[:, 1, :], sdrow[:, 1, :], AF.Sin, bias=0.0, scale=1.0),
        lambda s: s.activation(sdrow[:, 2, :], sdrow[:, 2, :], AF.Sin, bias=0.0, scale=1.0),
    ])

    sc.ops("vector", [
        lambda v: v.tensor_mul(sdrow[:, 3, :], sdrow[:, 2, :], sdrow[:, 2, :]),
        lambda v: v.tensor_scalar(sdrow[:, 3, :], sdrow[:, 3, :], -2.0, 1.0, ALU.mult, ALU.add),
    ])

    def tr_ds(t):
        t.matmul(p_col[0:32, 0:1], sdrow[0:1, 3, :], sones[0:1, 0:1], start=True, stop=True)
        return t.matmul(p_col[0:32, 1:2], sdrow[0:1, 1, :], sones[0:1, 0:1], start=True, stop=True)
    sc.st("tensor", tr_ds)
    sc.st("scalar", lambda s: s.copy(sdcol[:], p_col[0:32, 0:2]))

    sc.ops("vector", [
        lambda v: v.tensor_scalar(ssp[0:32, :], ssin0[:], sdcol[:, 1:2], None, ALU.mult),
        lambda v: v.tensor_scalar(ssp2[0:32, :], scos0[:], sdcol[:, 1:2], None, ALU.mult),
        lambda v: v.tensor_scalar(scos0[:], scos0[:], sdcol[:, 0:1], None, ALU.mult),
        lambda v: v.tensor_scalar(ssin0[:], ssin0[:], sdcol[:, 0:1], None, ALU.mult),
        lambda v: v.tensor_sub(scos0[:], scos0[:], ssp[0:32, :]),
        lambda v: v.tensor_add(ssin0[:], ssin0[:], ssp2[0:32, :]),
    ])

    sc.ops("scalar", [
        lambda s: s.activation(ssp[:], skr_raw[:], AF.Abs, bias=0.0, scale=1.0),
        lambda s: s.activation(ssp[:], ssp[:], AF.Exp, bias=0.0, scale=-1.0),
        lambda s: s.activation(ssp[:], ssp[:], AF.Ln, bias=sones[0:64, :], scale=1.0),
    ])

    sc.ops("vector", [
        lambda v: v.tensor_scalar_max(ssp2[:], skr_raw[:], 0.0),
        lambda v: v.tensor_add(ssp2[:], ssp2[:], ssp[:]),
    ])

    sc.ops("vector", [
        lambda v: v.tensor_copy(smu2[:], ssp2[32:64, :]),
        lambda v: v.tensor_mul(ssp[0:32, :], ssp2[0:32, :], scos0[:]),
        lambda v: v.tensor_mul(smu2b[:], smu2[:], ssin0[:]),
        lambda v: v.tensor_sub(skr[0:32, :], ssp[0:32, :], smu2b[:]),
        lambda v: v.tensor_mul(ssp[0:32, :], ssp2[0:32, :], ssin0[:]),
        lambda v: v.tensor_mul(smu2b[:], smu2[:], scos0[:]),
        lambda v: v.tensor_add(smu2b[:], ssp[0:32, :], smu2b[:]),
        lambda v: v.tensor_copy(skr[32:64, :], smu2b[:]),
    ])

    sc.dma("sync", o_kr, skr[:], stv)
    sc.st("sync", lambda e: None, waits=[(stv, stv.n)])
    sc.run()
    return nc


_L1_CACHE = {}


def run_launch1(inputs, xs_fm):
    if "nc" not in _L1_CACHE:
        _L1_CACHE["nc"] = build_launch1()
    nc = _L1_CACHE["nc"]
    cos0, sin0 = pope_tables()
    in_maps = []
    for c in range(NCORES):
        pos0 = (c % 4) * T
        in_maps.append({
            "x_fm": xs_fm[c],
            "w_dkv": np.ascontiguousarray(inputs["L0_attn_w_dkv"], np.float32),
            "w_dq": np.ascontiguousarray(inputs["L0_attn_w_dq"], np.float32),
            "w_kr": np.ascontiguousarray(inputs["L0_attn_w_kr"], np.float32),
            "gw_t": np.ascontiguousarray(np.asarray(inputs["L0_ffn_gate_w"], np.float32).T),
            "delta": np.asarray(inputs["L0_attn_delta"], np.float32).reshape(1, 32),
            "cos0s": np.ascontiguousarray(cos0[:, pos0:pos0 + T]),
            "sin0s": np.ascontiguousarray(sin0[:, pos0:pos0 + T]),
        })
    res = run_bass_kernel_spmd(nc, in_maps, core_ids=list(range(NCORES)))
    return res.results





SC192 = 1.0 / np.sqrt(192.0)


def build_launch2():
    nc = bass.Bass("TRN2", target_bir_lowering=False)
    ckv = nc.dram_tensor("ckv", [512, S], BF16, kind="ExternalInput").ap()
    krf = nc.dram_tensor("krf", [64, S], BF16, kind="ExternalInput").ap()
    cq = nc.dram_tensor("cq", [1024, S], BF16, kind="ExternalInput").ap()
    wuq = nc.dram_tensor("wuq", [1024, 512], F32, kind="ExternalInput").ap()
    wqr = nc.dram_tensor("wqr", [1024, 256], F32, kind="ExternalInput").ap()
    wuk = nc.dram_tensor("wuk", [512, 512], F32, kind="ExternalInput").ap()
    wuv = nc.dram_tensor("wuv", [512, 512], F32, kind="ExternalInput").ap()
    delta = nc.dram_tensor("delta", [1, 32], F32, kind="ExternalInput").ap()
    o_attn = nc.dram_tensor("attn", [4, 128, S], BF16, kind="ExternalOutput").ap()

    cos0, sin0 = pope_tables()
    c_cos0 = nc.inline_tensor(cos0.astype(BF16NP), name="c_cos0")
    c_sin0 = nc.inline_tensor(sin0.astype(BF16NP), name="c_sin0")
    masks = np.where((np.arange(4)[:, None, None] * 128 + np.arange(128)[None, :, None])
                     > np.arange(512)[None, None, :], NEG, 0.0).astype(np.float32)
    c_masks = nc.inline_tensor(masks.reshape(4 * 128, 512), name="c_masks")
    ones_col = nc.inline_tensor(np.ones((128, 1), np.float32), name="c_ones_col")
    ones_colb = nc.inline_tensor(np.ones((128, 1), np.float32).astype(BF16NP), name="c_ones_colb")
    ones_row = nc.inline_tensor(np.ones((1, 128), np.float32), name="c_ones_row")

    sc = Sched(nc)
    ld = sc.sem("ld")
    ldw = sc.sem("ldw")
    ldq = sc.sem("ldq")
    stv = sc.sem("stv")

    sckv = nc.alloc_sbuf_tensor("sckv", [128, 4, S], BF16)
    skr = nc.alloc_sbuf_tensor("skr", [64, S], BF16)
    scqb = nc.alloc_sbuf_tensor("scqb", [128, 2, 8, 512], BF16)
    suq = nc.alloc_sbuf_tensor("suq", [128, 8, 512], BF16)
    sqrw = nc.alloc_sbuf_tensor("sqrw", [128, 8, 256], BF16)
    suk = nc.alloc_sbuf_tensor("suk", [128, 4, 512], BF16)
    suv = nc.alloc_sbuf_tensor("suv", [128, 4, 512], BF16)
    sK = nc.alloc_sbuf_tensor("sK", [128, 8, 512], BF16)
    sV = nc.alloc_sbuf_tensor("sV", [128, 32, 128], BF16)
    sQ = nc.alloc_sbuf_tensor("sQ", [128, 8, 512], BF16)
    sQr = nc.alloc_sbuf_tensor("sQr", [64, S], BF16)
    sqr_raw = nc.alloc_sbuf_tensor("sqr_raw", [64, 512], F32)
    ssp = nc.alloc_sbuf_tensor("ssp", [64, 512], F32)
    ssp2 = nc.alloc_sbuf_tensor("ssp2", [64, 512], F32)
    smu2 = nc.alloc_sbuf_tensor("smu2", [32, 512], F32)
    smu2b = nc.alloc_sbuf_tensor("smu2b", [32, 512], F32)
    sPt = nc.alloc_sbuf_tensor("sPt", [128, 32, 512], BF16)
    sOut = nc.alloc_sbuf_tensor("sOut", [128, S], BF16)
    smask = nc.alloc_sbuf_tensor("smask", [128, 4, 512], F32)
    scos = nc.alloc_sbuf_tensor("scos", [32, S], BF16)
    ssin = nc.alloc_sbuf_tensor("ssin", [32, S], BF16)
    ssq = nc.alloc_sbuf_tensor("ssq", [128, 512], F32)
    stiny = nc.alloc_sbuf_tensor("stiny", [1, 2, 8], F32)
    sscal = nc.alloc_sbuf_tensor("sscal", [1, 8], F32)
    sbias = nc.alloc_sbuf_tensor("sbias", [128, 1], F32)
    sRec = nc.alloc_sbuf_tensor("sRec", [128, 512], F32)
    srd = nc.alloc_sbuf_tensor("srd", [1, 512], F32)
    sones = nc.alloc_sbuf_tensor("sones", [128, 1], F32)
    sonesb = nc.alloc_sbuf_tensor("sonesb", [128, 1], BF16)
    srow1 = nc.alloc_sbuf_tensor("srow1", [1, 128], F32)
    sdelta = nc.alloc_sbuf_tensor("sdelta", [1, 32], F32)
    sdrow = nc.alloc_sbuf_tensor("sdrow", [1, 4, 32], F32)
    sdcol = nc.alloc_sbuf_tensor("sdcol", [32, 2], F32)

    p_s = nc.alloc_psum_tensor("p_s", [128, 512], F32)
    p_s2 = nc.alloc_psum_tensor("p_s2", [128, 512], F32)
    p_o = nc.alloc_psum_tensor("p_o", [128, 512], F32)
    p_d = nc.alloc_psum_tensor("p_d", [1, 512], F32)
    p_bc = nc.alloc_psum_tensor("p_bc", [128, 512], F32)
    p_row = nc.alloc_psum_tensor("p_row", [1, 512], F32)
    p_col = nc.alloc_psum_tensor("p_col", [128, 2], F32)

    sc.dma("sync", sckv[:], ckv.rearrange("(c p) t -> p c t", p=128), ld)
    sc.dma("sync", skr[:], krf, ld)
    sc.dma("sync", smask[:], c_masks.ap().rearrange("(m p) t -> p m t", p=128), ld)
    sc.dma("sync", scos[:], c_cos0.ap(), ld)
    sc.dma("sync", ssin[:], c_sin0.ap(), ld)
    sc.dma("sync", sones[:], ones_col.ap(), ld)
    sc.dma("sync", sonesb[:], ones_colb.ap(), ld)
    sc.dma("sync", srow1[:], ones_row.ap(), ld)
    sc.dma("sync", sdelta[:], delta, ld)
    ld_all = ld.n
    sc.dma("gpsimd", suq[:], wuq.rearrange("(c p) o -> p c o", p=128), ldw)
    sc.dma("gpsimd", sqrw[:], wqr.rearrange("(c p) o -> p c o", p=128), ldw)
    sc.dma("gpsimd", suk[:], wuk.rearrange("(c p) o -> p c o", p=128), ldw)
    sc.dma("gpsimd", suv[:], wuv.rearrange("(c p) o -> p c o", p=128), ldw)
    ldw_all = ldw.n

    sc.ops("vector", [
        lambda v: v.tensor_scalar(sdrow[:, 0, :], sdelta[:], 1.0 / (2.0 * np.pi), MAGIC, ALU.mult, ALU.add),
        lambda v: v.tensor_scalar(sdrow[:, 0, :], sdrow[:, 0, :], MAGIC, None, ALU.subtract),
        lambda v: v.tensor_scalar(sdrow[:, 0, :], sdrow[:, 0, :], 2.0 * np.pi, None, ALU.mult),
        lambda v: v.tensor_sub(sdrow[:, 1, :], sdelta[:], sdrow[:, 0, :]),
        lambda v: v.tensor_scalar_mul(sdrow[:, 2, :], sdrow[:, 1, :], 0.5),
    ], waits=[(ld, ld_all)])
    sc.ops("scalar", [
        lambda s: s.activation(sdrow[:, 1, :], sdrow[:, 1, :], AF.Sin, bias=0.0, scale=1.0),
        lambda s: s.activation(sdrow[:, 2, :], sdrow[:, 2, :], AF.Sin, bias=0.0, scale=1.0),
    ])
    sc.ops("vector", [
        lambda v: v.tensor_mul(sdrow[:, 3, :], sdrow[:, 2, :], sdrow[:, 2, :]),
        lambda v: v.tensor_scalar(sdrow[:, 3, :], sdrow[:, 3, :], -2.0, 1.0, ALU.mult, ALU.add),
    ])
    def tr_ds(t):
        t.matmul(p_col[0:32, 0:1], sdrow[0:1, 3, :], sones[0:1, 0:1], start=True, stop=True)
        return t.matmul(p_col[0:32, 1:2], sdrow[0:1, 1, :], sones[0:1, 0:1], start=True, stop=True)
    sc.st("tensor", tr_ds)
    sc.st("scalar", lambda s: s.copy(sdcol[:], p_col[0:32, 0:2]))
    sstrip = nc.alloc_sbuf_tensor("sstrip", [32, 2, 1024], BF16)
    for st0 in range(0, S, 1024):
        sl = (slice(0, 32), slice(st0, st0 + 1024))
        sc.ops("vector", [
            lambda v, sl=sl: v.tensor_scalar(sstrip[:, 0, :], ssin[sl[0], sl[1]], sdcol[:, 1:2], None, ALU.mult),
            lambda v, sl=sl: v.tensor_scalar(sstrip[:, 1, :], scos[sl[0], sl[1]], sdcol[:, 1:2], None, ALU.mult),
            lambda v, sl=sl: v.tensor_scalar(scos[sl[0], sl[1]], scos[sl[0], sl[1]], sdcol[:, 0:1], None, ALU.mult),
            lambda v, sl=sl: v.tensor_scalar(ssin[sl[0], sl[1]], ssin[sl[0], sl[1]], sdcol[:, 0:1], None, ALU.mult),
            lambda v, sl=sl: v.tensor_sub(scos[sl[0], sl[1]], scos[sl[0], sl[1]], sstrip[:, 0, :]),
            lambda v, sl=sl: v.tensor_add(ssin[sl[0], sl[1]], ssin[sl[0], sl[1]], sstrip[:, 1, :]),
        ])

    for u in range(4):
        for kb8 in range(8):
            accmm(sc, p_s[:], [(suk[:, c, u * 128:(u + 1) * 128], sckv[:, c, kb8 * 512:(kb8 + 1) * 512]) for c in range(4)],
                  waits=[(ldw, ldw_all), (ld, ld_all), (stv, stv.n)])
            sc.st("scalar", lambda s, kb8=kb8: s.copy(sK[:, kb8, :], p_s[:]))
        for kb2 in range(32):
            accmm(sc, p_s[:, 0:128], [(sckv[:, c, kb2 * 128:(kb2 + 1) * 128], suv[:, c, u * 128:(u + 1) * 128]) for c in range(4)])
            sc.st("scalar", lambda s, kb2=kb2: s.copy(sV[:, kb2, :], p_s[:, 0:128]))
        for kb8 in range(8):
            sc.st("vector", lambda v, kb8=kb8: v.tensor_mul(ssq[:], sK[:, kb8, :], sK[:, kb8, :]))
            sc.st("vector", lambda v, kb8=kb8: v.tensor_mul(ssp[:], skr[:, kb8 * 512:(kb8 + 1) * 512], skr[:, kb8 * 512:(kb8 + 1) * 512]))
            accmm(sc, p_row[:], [(sones[:], ssq[:]), (sones[0:64, :], ssp[:])])
            sc.st("vector", lambda v, kb8=kb8: v.tensor_reduce(stiny[0:1, 0, kb8:kb8 + 1], p_row[:], AX.X, ALU.max))

        for qb in range(8):
            par = qb % 2
            sc.dma("gpsimd", scqb[:, par, :, :],
                   cq.rearrange("(c p) t -> p c t", p=128)[:, :, qb * 512:(qb + 1) * 512], ldq)
            ldq_now = ldq.n
            accmm(sc, p_s[:], [(suq[:, c, u * 128:(u + 1) * 128], scqb[:, par, c, :]) for c in range(8)],
                  waits=[(ldq, ldq_now)])
            sc.st("scalar", lambda s, qb=qb: s.copy(sQ[:, qb, :], p_s[:]))
            accmm(sc, p_s2[0:64, :],
                  [(sqrw[:, c, u * 64:(u + 1) * 64], scqb[:, par, c, :]) for c in range(8)],
                  waits=[(ldq, ldq_now)])
            sc.st("scalar", lambda s: s.copy(sqr_raw[:], p_s2[0:64, :]))
            sc.ops("scalar", [
                lambda s: s.activation(ssp[:], sqr_raw[:], AF.Abs, bias=0.0, scale=1.0),
                lambda s: s.activation(ssp[:], ssp[:], AF.Exp, bias=0.0, scale=-1.0),
                lambda s: s.activation(ssp[:], ssp[:], AF.Ln, bias=sones[0:64, :], scale=1.0),
            ])
            sc.ops("vector", [
                lambda v: v.tensor_scalar_max(ssp2[:], sqr_raw[:], 0.0),
                lambda v: v.tensor_add(ssp2[:], ssp2[:], ssp[:]),
            ])
            q0 = qb * 512
            sc.ops("vector", [
                lambda v: v.tensor_copy(smu2[:], ssp2[32:64, :]),
                lambda v, q0=q0: v.tensor_mul(ssp[0:32, :], ssp2[0:32, :], scos[:, q0:q0 + 512]),
                lambda v, q0=q0: v.tensor_mul(smu2b[:], smu2[:], ssin[:, q0:q0 + 512]),
                lambda v, q0=q0: v.tensor_sub(sQr[0:32, q0:q0 + 512], ssp[0:32, :], smu2b[:]),
                lambda v, q0=q0: v.tensor_mul(ssp[0:32, :], ssp2[0:32, :], ssin[:, q0:q0 + 512]),
                lambda v, q0=q0: v.tensor_mul(smu2b[:], smu2[:], scos[:, q0:q0 + 512]),
                lambda v: v.tensor_add(smu2b[:], ssp[0:32, :], smu2b[:]),
                lambda v, q0=q0: v.tensor_copy(sQr[32:64, q0:q0 + 512], smu2b[:]),
            ])
            sc.st("vector", lambda v, qb=qb: v.tensor_mul(ssq[:], sQ[:, qb, :], sQ[:, qb, :]))
            sc.st("vector", lambda v, q0=q0: v.tensor_mul(ssp[:], sQr[:, q0:q0 + 512], sQr[:, q0:q0 + 512]))
            accmm(sc, p_row[:], [(sones[:], ssq[:]), (sones[0:64, :], ssp[:])])
            sc.st("vector", lambda v, qb=qb: v.tensor_reduce(stiny[0:1, 1, qb:qb + 1], p_row[:], AX.X, ALU.max))

        sc.ops("vector", [
            lambda v: v.tensor_reduce(sscal[0:1, 0:1], stiny[0:1, 0, :], AX.X, ALU.max),
            lambda v: v.tensor_reduce(sscal[0:1, 1:2], stiny[0:1, 1, :], AX.X, ALU.max),
            lambda v: v.tensor_mul(sscal[0:1, 2:3], sscal[0:1, 0:1], sscal[0:1, 1:2]),
        ])
        sc.ops("scalar", [
            lambda s: s.activation(sscal[0:1, 3:4], sscal[0:1, 2:3], AF.Ln, bias=0.0, scale=1.0),
            lambda s: s.activation(sscal[0:1, 3:4], sscal[0:1, 3:4], AF.Exp, bias=0.0, scale=0.5),
        ])
        sc.st("vector", lambda v: v.tensor_scalar_mul(sscal[0:1, 4:5], sscal[0:1, 3:4], -SC192))
        sc.st("tensor", lambda t: t.matmul(p_col[:, 0:1], srow1[:], sscal[0:1, 4:5], start=True, stop=True))
        sc.st("scalar", lambda s: s.copy(sbias[:], p_col[:, 0:1]))

        for qc in range(8):
            nkb = 4 * (qc + 1)
            for kb in range(nkb):
                ps = p_s if kb % 2 == 0 else p_s2
                pairs = [(sK[:, kb // 4, (kb % 4) * 128:(kb % 4 + 1) * 128], sQ[:, qc, :]),
                         (skr[:, kb * 128:(kb + 1) * 128], sQr[:, qc * 512:(qc + 1) * 512])]
                accmm(sc, ps[:], pairs)
                if kb // 4 == qc:
                    m = kb % 4
                    sc.st("vector", lambda v, ps=ps, m=m: v.tensor_add(ps[:], ps[:], smask[:, m, :]))
                sc.st("scalar", lambda s, ps=ps, kb=kb: s.activation(
                    sPt[:, kb, :], ps[:], AF.Exp, bias=sbias[:], scale=SC192))
            accmm(sc, p_o[:], [(sV[:, kb, :], sPt[:, kb, :]) for kb in range(nkb)])
            accmm(sc, p_d[:], [(sonesb[:], sPt[:, kb, :]) for kb in range(nkb)])
            sc.st("vector", lambda v: v.reciprocal(srd[:], p_d[:]))
            sc.st("tensor", lambda t: t.matmul(p_bc[:], srow1[:], srd[:], start=True, stop=True))
            sc.st("scalar", lambda s: s.copy(sRec[:], p_bc[:]))
            sc.st("vector", lambda v, qc=qc: v.tensor_mul(sOut[:, qc * 512:(qc + 1) * 512], p_o[:], sRec[:]))
        sc.dma("sync", o_attn[u], sOut[:], stv)

    sc.st("sync", lambda e: None, waits=[(stv, stv.n)])
    sc.run()
    return nc


_L2_CACHE = {}


def run_launch2(inputs, ckv_full, kr_full, cq_full):
    if "nc" not in _L2_CACHE:
        _L2_CACHE["nc"] = build_launch2()
    nc = _L2_CACHE["nc"]
    wuq = np.asarray(inputs["L0_attn_w_uq"], np.float32)
    wqr = np.asarray(inputs["L0_attn_w_qr"], np.float32)
    wuk = np.asarray(inputs["L0_attn_w_uk"], np.float32)
    wuv = np.asarray(inputs["L0_attn_w_uv"], np.float32)
    in_maps = []
    for c in range(NCORES):
        s = c // 4
        hb = (c % 4) * 4
        in_maps.append({
            "ckv": ckv_full[s],
            "krf": kr_full[s],
            "cq": cq_full[s],
            "wuq": np.ascontiguousarray(wuq[:, hb * 128:(hb + 4) * 128]),
            "wqr": np.ascontiguousarray(wqr[:, hb * 64:(hb + 4) * 64]),
            "wuk": np.ascontiguousarray(wuk[:, hb * 128:(hb + 4) * 128]),
            "wuv": np.ascontiguousarray(wuv[:, hb * 128:(hb + 4) * 128]),
            "delta": np.asarray(inputs["L0_attn_delta"], np.float32).reshape(1, 32),
        })
    res = run_bass_kernel_spmd(nc, in_maps, core_ids=list(range(NCORES)))
    return res.results





def build_launch3():
    nc = bass.Bass("TRN2", target_bir_lowering=False)
    x = nc.dram_tensor("x_fm", [D, T], F32, kind="ExternalInput").ap()
    xn = nc.dram_tensor("xn_fm", [D, T], BF16, kind="ExternalInput").ap()
    att = nc.dram_tensor("att_fm", [D, T], BF16, kind="ExternalInput").ap()
    logits = nc.dram_tensor("logits", [T, 8], F32, kind="ExternalInput").ap()
    gbias = nc.dram_tensor("gbias", [1, 8], F32, kind="ExternalInput").ap()
    wout = nc.dram_tensor("wout", [D, D], F32, kind="ExternalInput").ap()
    sgw = nc.dram_tensor("sgw", [D, 5632], F32, kind="ExternalInput").ap()
    sdw = nc.dram_tensor("sdw", [2816, D], F32, kind="ExternalInput").ap()
    upw = nc.dram_tensor("upw", [8, D, 2816], F32, kind="ExternalInput").ap()
    dnw = nc.dram_tensor("dnw", [8, 1408, D], F32, kind="ExternalInput").ap()
    o_x1 = nc.dram_tensor("x1_fm", [D, T], F32, kind="ExternalOutput").ap()

    ones_col = nc.inline_tensor(np.ones((128, 1), np.float32), name="c_ones_col")
    ones_row = nc.inline_tensor(np.ones((1, 128), np.float32), name="c_ones_row")
    id8 = nc.inline_tensor(np.eye(8, dtype=np.float32), name="c_id8")
    id128 = nc.inline_tensor(np.eye(128, dtype=np.float32), name="c_id128")

    sc = Sched(nc)
    ld = sc.sem("ld")
    ldw = sc.sem("ldw")
    stv = sc.sem("stv")

    sxn = nc.alloc_sbuf_tensor("sxn", [128, 16, T], BF16)
    satt = nc.alloc_sbuf_tensor("satt", [128, 16, T], BF16)
    sacc = nc.alloc_sbuf_tensor("sacc", [128, 16, T], F32)
    sh = nc.alloc_sbuf_tensor("sh", [128, 11, T], BF16)
    swu = nc.alloc_sbuf_tensor("swu", [128, 2, 16, 128], BF16)
    swd = nc.alloc_sbuf_tensor("swd", [128, 2, 11, 128], BF16)
    swo = nc.alloc_sbuf_tensor("swo", [128, 2, 16, 128], BF16)
    sxs = nc.alloc_sbuf_tensor("sxs", [128, 2, T], F32)
    ssil = nc.alloc_sbuf_tensor("ssil", [128, 512], F32)
    st2 = nc.alloc_sbuf_tensor("st2", [128, 512], F32)
    slog = nc.alloc_sbuf_tensor("slog", [128, 8, 8], F32)
    ssig = nc.alloc_sbuf_tensor("ssig", [128, 8, 8], F32)
    swe = nc.alloc_sbuf_tensor("swe", [128, 8, 8], F32)
    sm1 = nc.alloc_sbuf_tensor("sm1", [128, 4], F32)
    sBb = nc.alloc_sbuf_tensor("sBb", [128, 8], F32)
    sgb = nc.alloc_sbuf_tensor("sgb", [1, 8], F32)
    swet = nc.alloc_sbuf_tensor("swet", [8, 8, 128], F32)
    sweRow = nc.alloc_sbuf_tensor("sweRow", [1, T], F32)
    sweb = nc.alloc_sbuf_tensor("sweb", [128, T], BF16)
    sones = nc.alloc_sbuf_tensor("sones", [128, 1], F32)
    srow1 = nc.alloc_sbuf_tensor("srow1", [1, 128], F32)
    sid8 = nc.alloc_sbuf_tensor("sid8", [8, 8], F32)
    sid128 = nc.alloc_sbuf_tensor("sid128", [128, 128], F32)

    p_a = nc.alloc_psum_tensor("p_a", [128, 512], F32)
    p_b = nc.alloc_psum_tensor("p_b", [128, 512], F32)
    p_c = nc.alloc_psum_tensor("p_c", [128, 512], F32)
    p_d = nc.alloc_psum_tensor("p_d", [128, 512], F32)
    p_t = nc.alloc_psum_tensor("p_t", [128, 128], F32)
    p_r = nc.alloc_psum_tensor("p_r", [1, 512], F32)

    sc.dma("sync", sxn[:], xn.rearrange("(c p) t -> p c t", p=128), ld)
    sc.dma("sync", satt[:], att.rearrange("(c p) t -> p c t", p=128), ld)
    sc.dma("sync", slog[:], logits.rearrange("(q p) o -> p q o", p=128), ld)
    sc.dma("sync", sgb[:], gbias, ld)
    sc.dma("sync", sones[:], ones_col.ap(), ld)
    sc.dma("sync", srow1[:], ones_row.ap(), ld)
    sc.dma("sync", sid8[:], id8.ap(), ld)
    sc.dma("sync", sid128[:], id128.ap(), ld)
    ld_all = ld.n

    for ti in range(8):
        sc.st("scalar", lambda s, ti=ti: s.activation(ssig[:, ti, :], slog[:, ti, :], AF.Sigmoid, bias=0.0, scale=1.0),
              waits=[(ld, ld_all)])
    sc.st("tensor", lambda t: t.matmul(p_t[:, 0:8], srow1[:], sgb[:], start=True, stop=True))
    sc.st("scalar", lambda s: s.copy(sBb[:], p_t[:, 0:8]))
    for ti in range(8):
        sc.ops("vector", [
            lambda v, ti=ti: v.tensor_add(swe[:, ti, :], ssig[:, ti, :], sBb[:]),
            lambda v, ti=ti: v.tensor_reduce(sm1[:, 0:1], swe[:, ti, :], AX.X, ALU.max),
            lambda v, ti=ti: v.tensor_scalar(sm1[:, 1:2], sm1[:, 0:1], 1.0, None, ALU.mult),
            lambda v, ti=ti: v.tensor_scalar(st2[:, 0:8], swe[:, ti, :], sm1[:, 0:1], None, ALU.is_ge),
            lambda v, ti=ti: v.tensor_scalar(ssil[:, 0:8], st2[:, 0:8], NEG, None, ALU.mult),
            lambda v, ti=ti: v.tensor_add(swe[:, ti, :], swe[:, ti, :], ssil[:, 0:8]),
            lambda v, ti=ti: v.tensor_reduce(sm1[:, 2:3], swe[:, ti, :], AX.X, ALU.max),
            lambda v, ti=ti: v.tensor_scalar(ssil[:, 0:8], swe[:, ti, :], sm1[:, 2:3], None, ALU.is_ge),
            lambda v, ti=ti: v.tensor_add(st2[:, 0:8], st2[:, 0:8], ssil[:, 0:8]),
            lambda v, ti=ti: v.tensor_mul(swe[:, ti, :], st2[:, 0:8], ssig[:, ti, :]),
            lambda v, ti=ti: v.tensor_reduce(sm1[:, 3:4], swe[:, ti, :], AX.X, ALU.add),
            lambda v, ti=ti: v.reciprocal(sm1[:, 3:4], sm1[:, 3:4]),
            lambda v, ti=ti: v.tensor_scalar(swe[:, ti, :], swe[:, ti, :], sm1[:, 3:4], None, ALU.mult),
        ])
    for ti in range(8):
        sc.st("tensor", lambda t, ti=ti: t.transpose(p_t[0:8, 0:128], swe[:, ti, :], sid128[:]))
        sc.st("scalar", lambda s, ti=ti: s.copy(swet[:, ti, :], p_t[0:8, 0:128]))

    for oc in range(16):
        par = oc % 2
        sc.dma("sync", sxs[:, par, :], x.rearrange("(c p) t -> p c t", p=128)[:, oc, :], ld)
        sc.dma("gpsimd", swo[:, par, :, :],
               wout.rearrange("(c p) o -> p c o", p=128)[:, :, oc * 128:(oc + 1) * 128], ldw)
        ldn, lwn = ld.n, ldw.n
        for tt in range(2):
            pp = p_a if tt == 0 else p_b
            accmm(sc, pp[:], [(swo[:, par, hc, :], satt[:, hc, tt * 512:(tt + 1) * 512]) for hc in range(16)],
                  waits=[(ld, ldn), (ldw, lwn)])
        sc.st("vector", lambda v, oc=oc, par=par: v.tensor_add(sacc[:, oc, 0:512], p_a[:], sxs[:, par, 0:512]))
        sc.st("vector", lambda v, oc=oc, par=par: v.tensor_add(sacc[:, oc, 512:1024], p_b[:], sxs[:, par, 512:1024]))


    for half in range(2):
        for j in range(11):
            jj = half * 11 + j
            par = j % 2
            sc.dma("gpsimd", swu[:, par, :, :],
                   sgw.rearrange("(c p) o -> p c o", p=128)[:, :, jj * 128:jj * 128 + 128], ldw)
            sc.dma("gpsimd", swo[:, par, :, :],
                   sgw.rearrange("(c p) o -> p c o", p=128)[:, :, 2816 + jj * 128:2816 + jj * 128 + 128], ldw)
            lwn = ldw.n
            for tt in range(2):
                pg = p_a if tt == 0 else p_b
                pu = p_c if tt == 0 else p_d
                accmm(sc, pu[:], [(swu[:, par, c, :], sxn[:, c, tt * 512:(tt + 1) * 512]) for c in range(16)],
                      waits=[(ldw, lwn)])
                accmm(sc, pg[:], [(swo[:, par, c, :], sxn[:, c, tt * 512:(tt + 1) * 512]) for c in range(16)],
                      waits=[(ldw, lwn)])
                sc.st("scalar", lambda s, pg=pg: s.activation(ssil[:], pg[:], AF.Silu, bias=0.0, scale=1.0))
                sc.st("vector", lambda v, j=j, tt=tt, pu=pu: v.tensor_mul(sh[:, j, tt * 512:(tt + 1) * 512], ssil[:], pu[:]))
        for oc in range(16):
            par = oc % 2
            sc.dma("gpsimd", swd[:, par, :, :],
                   sdw.rearrange("(c p) o -> p c o", p=128)[:, half * 11:(half + 1) * 11, oc * 128:(oc + 1) * 128], ldw)
            lwn = ldw.n
            for tt in range(2):
                pp = p_a if tt == 0 else p_b
                accmm(sc, pp[:], [(swd[:, par, j, :], sh[:, j, tt * 512:(tt + 1) * 512]) for j in range(11)],
                      waits=[(ldw, lwn)])
            sc.st("vector", lambda v, oc=oc: v.tensor_add(sacc[:, oc, 0:512], sacc[:, oc, 0:512], p_a[:]))
            sc.st("vector", lambda v, oc=oc: v.tensor_add(sacc[:, oc, 512:1024], sacc[:, oc, 512:1024], p_b[:]))

    for e in range(8):
        for ti in range(8):
            def row_e(t, e=e, ti=ti):
                return t.matmul(p_r[0:1, 0:128], sid8[:, e:e + 1], swet[:, ti, :], start=True, stop=True)
            sc.st("tensor", row_e)
            sc.st("scalar", lambda s, ti=ti: s.copy(sweRow[:, ti * 128:(ti + 1) * 128], p_r[0:1, 0:128]))
        for tt in range(2):
            sc.st("tensor", lambda t, tt=tt: t.matmul(p_a[:] if tt == 0 else p_b[:], srow1[:], sweRow[:, tt * 512:(tt + 1) * 512], start=True, stop=True))
        sc.st("scalar", lambda s: s.copy(sweb[:, 0:512], p_a[:]))
        sc.st("scalar", lambda s: s.copy(sweb[:, 512:1024], p_b[:]))
        for j in range(11):
            par = j % 2
            sc.dma("gpsimd", swu[:, par, :, :],
                   upw[e].rearrange("(c p) o -> p c o", p=128)[:, :, j * 128:j * 128 + 128], ldw)
            sc.dma("gpsimd", swo[:, par, :, :],
                   upw[e].rearrange("(c p) o -> p c o", p=128)[:, :, 1408 + j * 128:1408 + j * 128 + 128], ldw)
            lwn = ldw.n
            for tt in range(2):
                pg = p_a if tt == 0 else p_b
                pu = p_c if tt == 0 else p_d
                accmm(sc, pg[:], [(swu[:, par, c, :], sxn[:, c, tt * 512:(tt + 1) * 512]) for c in range(16)],
                      waits=[(ldw, lwn)])
                accmm(sc, pu[:], [(swo[:, par, c, :], sxn[:, c, tt * 512:(tt + 1) * 512]) for c in range(16)],
                      waits=[(ldw, lwn)])
                sc.st("scalar", lambda s, pg=pg: s.activation(ssil[:], pg[:], AF.Silu, bias=0.0, scale=1.0))
                sc.st("vector", lambda v, pu=pu: v.tensor_mul(st2[:], ssil[:], pu[:]))
                sc.st("vector", lambda v, j=j, tt=tt: v.tensor_mul(sh[:, j, tt * 512:(tt + 1) * 512], st2[:], sweb[:, tt * 512:(tt + 1) * 512]))
        for oc in range(16):
            par = oc % 2
            sc.dma("gpsimd", swd[:, par, :, :],
                   dnw[e].rearrange("(c p) o -> p c o", p=128)[:, :, oc * 128:(oc + 1) * 128], ldw)
            lwn = ldw.n
            for tt in range(2):
                pp = p_a if tt == 0 else p_b
                accmm(sc, pp[:], [(swd[:, par, j, :], sh[:, j, tt * 512:(tt + 1) * 512]) for j in range(11)],
                      waits=[(ldw, lwn)])
            sc.st("vector", lambda v, oc=oc: v.tensor_add(sacc[:, oc, 0:512], sacc[:, oc, 0:512], p_a[:]))
            sc.st("vector", lambda v, oc=oc: v.tensor_add(sacc[:, oc, 512:1024], sacc[:, oc, 512:1024], p_b[:]))

    sc.dma("sync", o_x1.rearrange("(c p) t -> p c t", p=128), sacc[:], stv)
    sc.st("sync", lambda e: None, waits=[(stv, stv.n)])
    sc.run()
    return nc


_L3_CACHE = {}


def run_launch3(inputs, xs_fm, xn_l, att_l, log_l):
    if "nc" not in _L3_CACHE:
        _L3_CACHE["nc"] = build_launch3()
    nc = _L3_CACHE["nc"]
    sgw_t = np.ascontiguousarray(np.asarray(inputs["L0_ffn_shared_gw"], np.float32).T)
    sdw_t = np.ascontiguousarray(np.asarray(inputs["L0_ffn_shared_dw"], np.float32).T)
    upw_t = np.ascontiguousarray(np.asarray(inputs["L0_ffn_up_w"], np.float32).transpose(0, 2, 1))
    dnw_t = np.ascontiguousarray(np.asarray(inputs["L0_ffn_down_w"], np.float32).transpose(0, 2, 1))
    wout = np.ascontiguousarray(np.asarray(inputs["L0_attn_w_out"], np.float32))
    gb = np.asarray(inputs["L0_ffn_gate_bias"], np.float32).reshape(1, 8)
    in_maps = []
    for c in range(NCORES):
        in_maps.append({
            "x_fm": xs_fm[c], "xn_fm": xn_l[c], "att_fm": att_l[c],
            "logits": log_l[c], "gbias": gb, "wout": wout,
            "sgw": sgw_t, "sdw": sdw_t, "upw": upw_t, "dnw": dnw_t,
        })
    res = run_bass_kernel_spmd(nc, in_maps, core_ids=list(range(NCORES)))
    return res.results





def build_launch4():
    nc = bass.Bass("TRN2", target_bir_lowering=False)
    x1 = nc.dram_tensor("x1_fm", [D, T], F32, kind="ExternalInput").ap()
    wq = nc.dram_tensor("wq", [D, D], F32, kind="ExternalInput").ap()
    wk = nc.dram_tensor("wk", [D, 512], F32, kind="ExternalInput").ap()
    wv = nc.dram_tensor("wv", [D, 512], F32, kind="ExternalInput").ap()
    cos1s = nc.dram_tensor("cos1s", [128, T], F32, kind="ExternalInput").ap()
    sin1s = nc.dram_tensor("sin1s", [128, T], F32, kind="ExternalInput").ap()
    o_xn = nc.dram_tensor("xn1_fm", [D, T], BF16, kind="ExternalOutput").ap()
    o_q = nc.dram_tensor("q_fm", [D, T], BF16, kind="ExternalOutput").ap()
    o_k = nc.dram_tensor("k_fm", [512, T], BF16, kind="ExternalOutput").ap()
    o_v = nc.dram_tensor("v_tm", [T, 512], BF16, kind="ExternalOutput").ap()

    ones_col = nc.inline_tensor(np.ones((128, 1), np.float32), name="c_ones_col")
    ones_row = nc.inline_tensor(np.ones((1, 128), np.float32), name="c_ones_row")
    eps_c = nc.inline_tensor(np.full((1, 1), LN_EPS, np.float32), name="c_eps")

    sc = Sched(nc)
    ld = sc.sem("ld")
    ldw = sc.sem("ldw")
    ldq = sc.sem("ldq")
    stv = sc.sem("stv")

    sx1 = nc.alloc_sbuf_tensor("sx1", [128, 16, 512], F32)
    sxn = nc.alloc_sbuf_tensor("sxn", [128, 16, 512], BF16)
    swk = nc.alloc_sbuf_tensor("swk", [128, 16, 512], BF16)
    swv = nc.alloc_sbuf_tensor("swv", [128, 16, 512], BF16)
    swq = nc.alloc_sbuf_tensor("swq", [128, 2, 16, 128], BF16)
    ssq = nc.alloc_sbuf_tensor("ssq", [128, 2, 512], F32)
    srow_stats = nc.alloc_sbuf_tensor("srow_stats", [1, 2, 512], F32)
    srow_ab = nc.alloc_sbuf_tensor("srow_ab", [1, 2, 512], F32)
    sA = nc.alloc_sbuf_tensor("sA", [128, 512], F32)
    sB = nc.alloc_sbuf_tensor("sB", [128, 512], F32)
    sones = nc.alloc_sbuf_tensor("sones", [128, 1], F32)
    srow1 = nc.alloc_sbuf_tensor("srow1", [1, 128], F32)
    seps = nc.alloc_sbuf_tensor("seps", [1, 1], F32)
    scos = nc.alloc_sbuf_tensor("scos", [128, T], F32)
    ssin = nc.alloc_sbuf_tensor("ssin", [128, T], F32)
    srp = nc.alloc_sbuf_tensor("srp", [128, 512], BF16)
    sdn = nc.alloc_sbuf_tensor("sdn", [64, 512], BF16)
    sup = nc.alloc_sbuf_tensor("sup", [128, 512], BF16)
    st1 = nc.alloc_sbuf_tensor("st1", [128, 512], F32)
    st2b = nc.alloc_sbuf_tensor("st2b", [128, 512], F32)
    sprj = nc.alloc_sbuf_tensor("sprj", [128, 512], BF16)

    p_a = nc.alloc_psum_tensor("p_a", [128, 512], F32)
    p_b = nc.alloc_psum_tensor("p_b", [128, 512], F32)
    p_s1 = nc.alloc_psum_tensor("p_s1", [1, 512], F32)
    p_s2 = nc.alloc_psum_tensor("p_s2", [1, 512], F32)
    p_bc = nc.alloc_psum_tensor("p_bc", [128, 512], F32)
    p_bc2 = nc.alloc_psum_tensor("p_bc2", [128, 512], F32)

    sc.dma("sync", sones[:], ones_col.ap(), ld)
    sc.dma("sync", srow1[:], ones_row.ap(), ld)
    sc.dma("sync", seps[:], eps_c.ap(), ld)
    sc.dma("sync", scos[:], cos1s, ld)
    sc.dma("sync", ssin[:], sin1s, ld)
    ld_all = ld.n
    sc.dma("gpsimd", swk[:], wk.rearrange("(c p) o -> p c o", p=128), ldw)
    sc.dma("gpsimd", swv[:], wv.rearrange("(c p) o -> p c o", p=128), ldw)
    ldw_all = ldw.n

    def rope_block(src128, t0, dst_store):
        c0, c1 = scos[0:64, t0:t0 + 512], scos[64:128, t0:t0 + 512]
        s0, s1 = ssin[0:64, t0:t0 + 512], ssin[64:128, t0:t0 + 512]
        sc.ops("vector", [
            lambda v: v.tensor_copy(sdn[:], src128[64:128, :]),
            lambda v: v.tensor_copy(sup[64:128, :], src128[0:64, :]),
            lambda v: v.tensor_mul(st1[0:64, :], src128[0:64, :], c0),
            lambda v: v.tensor_mul(st2b[0:64, :], sdn[:], s0),
            lambda v: v.tensor_sub(srp[0:64, :], st1[0:64, :], st2b[0:64, :]),
            lambda v: v.tensor_mul(st1[64:128, :], src128[64:128, :], c1),
            lambda v: v.tensor_mul(st2b[64:128, :], sup[64:128, :], s1),
            lambda v: v.tensor_add(srp[64:128, :], st1[64:128, :], st2b[64:128, :]),
        ])
        dst_store()

    for th in range(2):
        t0 = th * 512
        sc.dma("sync", sx1[:], x1.rearrange("(c p) t -> p c t", p=128)[:, :, t0:t0 + 512], ld,
               waits=[(stv, stv.n)])
        ldx = ld.n
        accmm(sc, p_s1[:], [(sones[:], sx1[:, c, :]) for c in range(16)], waits=[(ld, ldx)])
        for c in range(16):
            sc.st("vector", lambda v, c=c: v.tensor_mul(ssq[:, c % 2, :], sx1[:, c, :], sx1[:, c, :]),
                  waits=[(ld, ldx)])
            sc.st("tensor", lambda t, c=c: t.matmul(p_s2[:], sones[:], ssq[:, c % 2, :],
                                                    start=(c == 0), stop=(c == 15)))
        sc.ops("vector", [
            lambda v: v.tensor_scalar_mul(srow_stats[:, 0, :], p_s1[:], 1.0 / 2048.0),
            lambda v: v.tensor_scalar_mul(srow_stats[:, 1, :], p_s2[:], 1.0 / 2048.0),
            lambda v: v.tensor_mul(srow_ab[:, 0, :], srow_stats[:, 0, :], srow_stats[:, 0, :]),
            lambda v: v.tensor_sub(srow_ab[:, 1, :], srow_stats[:, 1, :], srow_ab[:, 0, :]),
        ])
        sc.ops("scalar", [
            lambda s: s.activation(srow_ab[:, 0, :], srow_ab[:, 1, :], AF.Ln, bias=seps[:], scale=1.0),
            lambda s: s.activation(srow_ab[:, 0, :], srow_ab[:, 0, :], AF.Exp, bias=0.0, scale=-0.5),
        ], waits=[ld])
        sc.ops("vector", [
            lambda v: v.tensor_mul(srow_ab[:, 1, :], srow_stats[:, 0, :], srow_ab[:, 0, :]),
            lambda v: v.tensor_scalar_mul(srow_ab[:, 1, :], srow_ab[:, 1, :], -1.0),
        ])
        def bc_ab(t):
            t.matmul(p_bc[:], srow1[:], srow_ab[:, 0, :], start=True, stop=True)
            return t.matmul(p_bc2[:], srow1[:], srow_ab[:, 1, :], start=True, stop=True)
        sc.st("tensor", bc_ab)
        def evac_ab(s):
            s.copy(sA[:], p_bc[:])
            return s.copy(sB[:], p_bc2[:])
        sc.st("scalar", evac_ab)
        xn_ops = []
        for c in range(16):
            xn_ops.append(lambda v, c=c: v.tensor_mul(ssq[:, c % 2, :], sx1[:, c, :], sA[:]))
            xn_ops.append(lambda v, c=c: v.tensor_add(sxn[:, c, :], ssq[:, c % 2, :], sB[:]))
        sc.ops("vector", xn_ops, waits=[(stv, stv.n)])
        sc.dma("sync", o_xn.rearrange("(c p) t -> p c t", p=128)[:, :, t0:t0 + 512], sxn[:], stv)

        for ti in range(4):
            accmm(sc, p_a[:], [(sxn[:, c, ti * 128:(ti + 1) * 128], swv[:, c, :]) for c in range(16)],
                  waits=[(ldw, ldw_all)])
            sc.st("scalar", lambda s: s.copy(sprj[:], p_a[:]), waits=[(stv, stv.n)])
            sc.dma("sync", o_v[t0 + ti * 128:t0 + (ti + 1) * 128, :], sprj[:], stv)

        for g in range(4):
            accmm(sc, p_a[:], [(swk[:, c, g * 128:(g + 1) * 128], sxn[:, c, :]) for c in range(16)],
                  waits=[(ldw, ldw_all)])
            sc.st("scalar", lambda s: s.copy(sprj[:], p_a[:]), waits=[(stv, stv.n)])
            def store_k(g=g, t0=t0):
                sc.dma("sync", o_k.rearrange("(g p) t -> p g t", p=128)[:, g, t0:t0 + 512], srp[:], stv)
            rope_block(sprj[:], t0, store_k)

        for h in range(16):
            par = h % 2
            sc.dma("gpsimd", swq[:, par, :, :],
                   wq.rearrange("(c p) o -> p c o", p=128)[:, :, h * 128:(h + 1) * 128], ldq)
            lqn = ldq.n
            accmm(sc, p_a[:], [(swq[:, par, c, :], sxn[:, c, :]) for c in range(16)],
                  waits=[(ldq, lqn)])
            sc.st("scalar", lambda s: s.copy(sprj[:], p_a[:]), waits=[(stv, stv.n)])
            def store_q(h=h, t0=t0):
                sc.dma("sync", o_q.rearrange("(h p) t -> p h t", p=128)[:, h, t0:t0 + 512], srp[:], stv)
            rope_block(sprj[:], t0, store_q)

    sc.st("sync", lambda e: None, waits=[(stv, stv.n)])
    sc.run()
    return nc


_L4_CACHE = {}


def run_launch4(inputs, x1_l):
    if "nc" not in _L4_CACHE:
        _L4_CACHE["nc"] = build_launch4()
    nc = _L4_CACHE["nc"]
    cos1, sin1 = rope1_tables()
    wq = np.ascontiguousarray(np.asarray(inputs["L1_attn_Wq"], np.float32))
    wk = np.ascontiguousarray(np.asarray(inputs["L1_attn_Wk"], np.float32))
    wv = np.ascontiguousarray(np.asarray(inputs["L1_attn_Wv"], np.float32))
    in_maps = []
    for c in range(NCORES):
        pos0 = (c % 4) * T
        in_maps.append({
            "x1_fm": x1_l[c], "wq": wq, "wk": wk, "wv": wv,
            "cos1s": np.ascontiguousarray(cos1[:, pos0:pos0 + T]),
            "sin1s": np.ascontiguousarray(sin1[:, pos0:pos0 + T]),
        })
    res = run_bass_kernel_spmd(nc, in_maps, core_ids=list(range(NCORES)))
    return res.results



SC128 = 1.0 / np.sqrt(128.0)


def build_launch5():
    nc = bass.Bass("TRN2", target_bir_lowering=False)
    qp = nc.dram_tensor("qp", [512, S], BF16, kind="ExternalInput").ap()
    kg = nc.dram_tensor("kg", [128, S], BF16, kind="ExternalInput").ap()
    vg = nc.dram_tensor("vg", [S, 128], BF16, kind="ExternalInput").ap()
    hsp = nc.dram_tensor("hsp", [1, 4], F32, kind="ExternalInput").ap()
    o_attn = nc.dram_tensor("attn", [4, 128, S], BF16, kind="ExternalOutput").ap()

    masks = np.where((np.arange(4)[:, None, None] * 128 + np.arange(128)[None, :, None])
                     > np.arange(512)[None, None, :], NEG, 0.0).astype(np.float32)
    c_masks = nc.inline_tensor(masks.reshape(4 * 128, 512), name="c_masks")
    ones_col = nc.inline_tensor(np.ones((128, 1), np.float32), name="c_ones_col")
    ones_colb = nc.inline_tensor(np.ones((128, 1), np.float32).astype(BF16NP), name="c_ones_colb")
    ones_row = nc.inline_tensor(np.ones((1, 128), np.float32), name="c_ones_row")

    sc = Sched(nc)
    ld = sc.sem("ld")
    ldq = sc.sem("ldq")
    stv = sc.sem("stv")

    sKg = nc.alloc_sbuf_tensor("sKg", [128, S], BF16)
    sV = nc.alloc_sbuf_tensor("sV", [128, 32, 128], BF16)
    sQ = nc.alloc_sbuf_tensor("sQ", [128, S], BF16)
    sPt = nc.alloc_sbuf_tensor("sPt", [128, 32, 512], BF16)
    sOut = nc.alloc_sbuf_tensor("sOut", [128, S], BF16)
    smask = nc.alloc_sbuf_tensor("smask", [128, 4, 512], F32)
    ssq = nc.alloc_sbuf_tensor("ssq", [128, 512], F32)
    stiny = nc.alloc_sbuf_tensor("stiny", [1, 2, 8], F32)
    sscal = nc.alloc_sbuf_tensor("sscal", [1, 8], F32)
    sbias = nc.alloc_sbuf_tensor("sbias", [128, 1], F32)
    sRec = nc.alloc_sbuf_tensor("sRec", [128, 512], F32)
    srd = nc.alloc_sbuf_tensor("srd", [1, 512], F32)
    sones = nc.alloc_sbuf_tensor("sones", [128, 1], F32)
    sonesb = nc.alloc_sbuf_tensor("sonesb", [128, 1], BF16)
    srow1 = nc.alloc_sbuf_tensor("srow1", [1, 128], F32)
    shs = nc.alloc_sbuf_tensor("shs", [1, 4], F32)

    p_s = nc.alloc_psum_tensor("p_s", [128, 512], F32)
    p_s2 = nc.alloc_psum_tensor("p_s2", [128, 512], F32)
    p_o = nc.alloc_psum_tensor("p_o", [128, 512], F32)
    p_d = nc.alloc_psum_tensor("p_d", [1, 512], F32)
    p_bc = nc.alloc_psum_tensor("p_bc", [128, 512], F32)
    p_row = nc.alloc_psum_tensor("p_row", [1, 512], F32)
    p_col = nc.alloc_psum_tensor("p_col", [128, 2], F32)

    sc.dma("sync", sKg[:], kg, ld)
    sc.dma("sync", sV[:], vg.rearrange("(b p) d -> p b d", p=128), ld)
    sc.dma("sync", smask[:], c_masks.ap().rearrange("(m p) t -> p m t", p=128), ld)
    sc.dma("sync", sones[:], ones_col.ap(), ld)
    sc.dma("sync", sonesb[:], ones_colb.ap(), ld)
    sc.dma("sync", srow1[:], ones_row.ap(), ld)
    sc.dma("sync", shs[:], hsp, ld)
    ld_all = ld.n

    for kb8 in range(8):
        sc.st("vector", lambda v, kb8=kb8: v.tensor_mul(ssq[:], sKg[:, kb8 * 512:(kb8 + 1) * 512], sKg[:, kb8 * 512:(kb8 + 1) * 512]),
              waits=[(ld, ld_all)])
        accmm(sc, p_row[:], [(sones[:], ssq[:])])
        sc.st("vector", lambda v, kb8=kb8: v.tensor_reduce(stiny[0:1, 0, kb8:kb8 + 1], p_row[:], AX.X, ALU.max))

    for u in range(4):
        sc.dma("sync", sQ[:], qp[u * 128:(u + 1) * 128, :], ldq, waits=[(stv, stv.n)])
        lqn = ldq.n
        for qb in range(8):
            sc.st("vector", lambda v, qb=qb: v.tensor_mul(ssq[:], sQ[:, qb * 512:(qb + 1) * 512], sQ[:, qb * 512:(qb + 1) * 512]),
                  waits=[(ldq, lqn)])
            accmm(sc, p_row[:], [(sones[:], ssq[:])])
            sc.st("vector", lambda v, qb=qb: v.tensor_reduce(stiny[0:1, 1, qb:qb + 1], p_row[:], AX.X, ALU.max))
        sc.ops("vector", [
            lambda v: v.tensor_reduce(sscal[0:1, 0:1], stiny[0:1, 0, :], AX.X, ALU.max),
            lambda v: v.tensor_reduce(sscal[0:1, 1:2], stiny[0:1, 1, :], AX.X, ALU.max),
            lambda v: v.tensor_mul(sscal[0:1, 2:3], sscal[0:1, 0:1], sscal[0:1, 1:2]),
        ])
        sc.ops("scalar", [
            lambda s: s.activation(sscal[0:1, 3:4], sscal[0:1, 2:3], AF.Ln, bias=0.0, scale=1.0),
            lambda s: s.activation(sscal[0:1, 3:4], sscal[0:1, 3:4], AF.Exp, bias=0.0, scale=0.5),
        ])
        sc.st("vector", lambda v: v.tensor_scalar_mul(sscal[0:1, 4:5], sscal[0:1, 3:4], -SC128))
        sc.st("tensor", lambda t: t.matmul(p_col[:, 0:1], srow1[:], sscal[0:1, 4:5], start=True, stop=True))
        sc.st("scalar", lambda s: s.copy(sbias[:], p_col[:, 0:1]))

        for qc in range(8):
            nkb = 4 * (qc + 1)
            for kb in range(nkb):
                ps = p_s if kb % 2 == 0 else p_s2
                accmm(sc, ps[:], [(sKg[:, kb * 128:(kb + 1) * 128], sQ[:, qc * 512:(qc + 1) * 512])])
                if kb // 4 == qc:
                    m = kb % 4
                    sc.st("vector", lambda v, ps=ps, m=m: v.tensor_add(ps[:], ps[:], smask[:, m, :]))
                sc.st("scalar", lambda s, ps=ps, kb=kb: s.activation(
                    sPt[:, kb, :], ps[:], AF.Exp, bias=sbias[:], scale=SC128))
            accmm(sc, p_o[:], [(sV[:, kb, :], sPt[:, kb, :]) for kb in range(nkb)])
            accmm(sc, p_d[:], [(sonesb[:], sPt[:, kb, :]) for kb in range(nkb)])
            sc.st("vector", lambda v: v.reciprocal(srd[:], p_d[:]))
            sc.st("vector", lambda v, u=u: v.tensor_scalar(srd[:], srd[:], shs[0:1, u:u + 1], None, ALU.mult))
            sc.st("tensor", lambda t: t.matmul(p_bc[:], srow1[:], srd[:], start=True, stop=True))
            sc.st("scalar", lambda s: s.copy(sRec[:], p_bc[:]))
            sc.st("vector", lambda v, qc=qc: v.tensor_mul(sOut[:, qc * 512:(qc + 1) * 512], p_o[:], sRec[:]))
        sc.dma("sync", o_attn[u], sOut[:], stv)

    sc.st("sync", lambda e: None, waits=[(stv, stv.n)])
    sc.run()
    return nc


_L5_CACHE = {}


def run_launch5(inputs, q_full, k_full, v_full):
    if "nc" not in _L5_CACHE:
        _L5_CACHE["nc"] = build_launch5()
    nc = _L5_CACHE["nc"]
    hs = np.asarray(inputs["L1_attn_head_scale"], np.float32).reshape(16)
    in_maps = []
    for c in range(NCORES):
        s = c // 4
        g = c % 4
        in_maps.append({
            "qp": np.ascontiguousarray(q_full[s][g * 512:(g + 1) * 512, :]),
            "kg": np.ascontiguousarray(k_full[s][g * 128:(g + 1) * 128, :]),
            "vg": np.ascontiguousarray(v_full[s][:, g * 128:(g + 1) * 128]),
            "hsp": np.ascontiguousarray(hs[g * 4:(g + 1) * 4].reshape(1, 4)),
        })
    res = run_bass_kernel_spmd(nc, in_maps, core_ids=list(range(NCORES)))
    return res.results





RT2048 = float(np.sqrt(2048.0))
RT8192 = float(np.sqrt(8192.0))


def _hadamard128():
    h = np.array([[1.0]], np.float32)
    for _ in range(7):
        h = np.block([[h, h], [h, -h]])
    return h


def build_launch6():
    nc = bass.Bass("TRN2", target_bir_lowering=False)
    x1 = nc.dram_tensor("x1_fm", [D, T], F32, kind="ExternalInput").ap()
    xn1 = nc.dram_tensor("xn1_fm", [D, T], BF16, kind="ExternalInput").ap()
    att = nc.dram_tensor("att_fm", [D, T], BF16, kind="ExternalInput").ap()
    wo = nc.dram_tensor("wo", [D, D], F32, kind="ExternalInput").ap()
    wg = nc.dram_tensor("wg", [D, 8192], F32, kind="ExternalInput").ap()
    wu = nc.dram_tensor("wu", [D, 8192], F32, kind="ExternalInput").ap()
    wd = nc.dram_tensor("wd", [8192, D], F32, kind="ExternalInput").ap()
    o_x2 = nc.dram_tensor("x2_fm", [D, T], F32, kind="ExternalOutput").ap()
    h2scr = nc.dram_tensor("h2scr", [64, 128, T], BF16, kind="Internal").ap()

    ones_col = nc.inline_tensor(np.ones((128, 1), np.float32), name="c_ones_col")
    ones_row = nc.inline_tensor(np.ones((1, 128), np.float32), name="c_ones_row")
    magic_col = nc.inline_tensor(np.full((128, 1), MAGIC, np.float32), name="c_magic")
    h128 = nc.inline_tensor(_hadamard128().astype(BF16NP), name="c_h128")
    id128 = nc.inline_tensor(np.eye(128, dtype=np.float32), name="c_id128")

    sc = Sched(nc)
    ld = sc.sem("ld")
    ldw = sc.sem("ldw")
    sth = sc.sem("sth")
    ldh = sc.sem("ldh")
    stv = sc.sem("stv")

    sxn = nc.alloc_sbuf_tensor("sxn", [128, 16, T], BF16)
    sU = nc.alloc_sbuf_tensor("sU", [128, 64, 512], BF16)
    sF1 = nc.alloc_sbuf_tensor("sF1", [128, 64, 128], BF16)
    sF2 = nc.alloc_sbuf_tensor("sF2", [128, 64, 128], BF16)
    swb = nc.alloc_sbuf_tensor("swb", [128, 16, 256], BF16)
    swdb = nc.alloc_sbuf_tensor("swdb", [128, 64, 128], BF16)
    sTg = nc.alloc_sbuf_tensor("sTg", [128, 16, 128], BF16)
    sTu = nc.alloc_sbuf_tensor("sTu", [128, 16, 128], BF16)
    sqz = nc.alloc_sbuf_tensor("sqz", [128, 2048], F32)
    sgt = nc.alloc_sbuf_tensor("sgt", [128, 512], F32)
    sut = nc.alloc_sbuf_tensor("sut", [128, 512], F32)
    ssl = nc.alloc_sbuf_tensor("ssl", [128, 512], F32)
    sh2 = nc.alloc_sbuf_tensor("sh2", [128, 512], BF16)
    sMxF = nc.alloc_sbuf_tensor("sMxF", [128, 512], F32)
    sAbs = nc.alloc_sbuf_tensor("sAbs", [128, 512], F32)
    sMxC = nc.alloc_sbuf_tensor("sMxC", [128, 4], F32)
    srowm = nc.alloc_sbuf_tensor("srowm", [1, 8], F32)
    srowq = nc.alloc_sbuf_tensor("srowq", [1, 2, 512], F32)
    sQs = nc.alloc_sbuf_tensor("sQs", [128, 1, 512], BF16)
    sRg = nc.alloc_sbuf_tensor("sRg", [128, 512], F32)
    sRu = nc.alloc_sbuf_tensor("sRu", [128, 512], F32)
    sRd = nc.alloc_sbuf_tensor("sRd", [128, 512], F32)
    sAcc = nc.alloc_sbuf_tensor("sAcc", [128, 3], F32)
    sInv = nc.alloc_sbuf_tensor("sInv", [128, 3], F32)
    sScl = nc.alloc_sbuf_tensor("sScl", [128, 3], F32)
    sredc = nc.alloc_sbuf_tensor("sredc", [128, 1], F32)
    sones = nc.alloc_sbuf_tensor("sones", [128, 1], F32)
    srow1 = nc.alloc_sbuf_tensor("srow1", [1, 128], F32)
    smgc = nc.alloc_sbuf_tensor("smgc", [128, 1], F32)
    sH = nc.alloc_sbuf_tensor("sH", [128, 128], BF16)
    sid = nc.alloc_sbuf_tensor("sid", [128, 128], F32)
    sx1s = nc.alloc_sbuf_tensor("sx1s", [128, T], F32)
    sot = nc.alloc_sbuf_tensor("sot", [128, T], F32)

    p_a = nc.alloc_psum_tensor("p_a", [128, 512], F32)
    p_b = nc.alloc_psum_tensor("p_b", [128, 512], F32)
    p_t = nc.alloc_psum_tensor("p_t", [128, 128], F32)
    p_r = nc.alloc_psum_tensor("p_r", [1, 512], F32)
    p_bc = nc.alloc_psum_tensor("p_bc", [128, 512], F32)

    sxq = sU[:, 32:48, :]
    sc.dma("sync", sxn[:], xn1.rearrange("(c p) t -> p c t", p=128), ld)
    sc.dma("sync", sones[:], ones_col.ap(), ld)
    sc.dma("sync", srow1[:], ones_row.ap(), ld)
    sc.dma("sync", smgc[:], magic_col.ap(), ld)
    sc.dma("sync", sH[:], h128.ap(), ld)
    sc.dma("sync", sid[:], id128.ap(), ld)
    ld_all = ld.n

    scan_view = sU[:].rearrange("p a b -> p (a b)")
    first = [True, True, True]
    def scan_tiles(m, dram_tiles, width):
        for i, ap in enumerate(dram_tiles):
            par = i % 2
            dst = scan_view[:, par * 16384:par * 16384 + width]
            sc.dma("gpsimd", dst, ap, ldw)
            lwn = ldw.n
            if first[m]:
                sc.st("vector", lambda v, m=m, dst=dst: v.tensor_reduce(
                    sAcc[:, m:m + 1], dst, AX.X, ALU.add, apply_absolute_value=True),
                    waits=[(ldw, lwn)])
                first[m] = False
            else:
                sc.st("vector", lambda v, dst=dst: v.tensor_reduce(
                    sredc[:], dst, AX.X, ALU.add, apply_absolute_value=True),
                    waits=[(ldw, lwn)])
                sc.st("vector", lambda v, m=m: v.tensor_add(sAcc[:, m:m + 1], sAcc[:, m:m + 1], sredc[:]))
    wg_r = wg.rearrange("(c p) o -> p c o", p=128)
    wu_r = wu.rearrange("(c p) o -> p c o", p=128)
    wd_r = wd.rearrange("(c p) o -> p c o", p=128)
    scan_tiles(0, [wg_r[:, c, :] for c in range(16)], 8192)
    scan_tiles(1, [wu_r[:, c, :] for c in range(16)], 8192)
    scan_tiles(2, [wd_r[:, c, :] for c in range(64)], 2048)
    accmm(sc, p_t[0:1, 0:3], [(sones[:], sAcc[:])])
    sc.st("scalar", lambda s: s.copy(srowm[0:1, 0:3], p_t[0:1, 0:3]))
    sc.ops("vector", [
        lambda v: v.tensor_scalar(srowm[0:1, 0:3], srowm[0:1, 0:3], 1.0 / 16777216.0, 1e-5, ALU.mult, ALU.max),
        lambda v: v.reciprocal(srowm[0:1, 3:6], srowm[0:1, 0:3]),
    ])
    for m in range(3):
        sc.st("tensor", lambda t, m=m: t.matmul(p_t[:, 0:1], srow1[:], srowm[0:1, 3 + m:4 + m], start=True, stop=True))
        sc.st("scalar", lambda s, m=m: s.copy(sInv[:, m:m + 1], p_t[:, 0:1]))
        sc.st("tensor", lambda t, m=m: t.matmul(p_t[:, 0:1], srow1[:], srowm[0:1, m:m + 1], start=True, stop=True))
        sc.st("scalar", lambda s, m=m: s.copy(sScl[:, m:m + 1], p_t[:, 0:1]))

    satt = sU[:].rearrange("p a b -> p (a b)")[:, 0:16384].rearrange("p (c t) -> p c t", t=T)
    sc.dma("sync", satt, att.rearrange("(c p) t -> p c t", p=128), ld)
    ld_att = ld.n
    for oc in range(16):
        sc.dma("sync", sx1s[:], x1.rearrange("(c p) t -> p c t", p=128)[:, oc, :], ld,
               waits=[(stv, stv.n)])
        sc.dma("gpsimd", swb[:, :, 0:128],
               wo.rearrange("(c p) o -> p c o", p=128)[:, :, oc * 128:(oc + 1) * 128], ldw)
        ldn, lwn = ld.n, ldw.n
        for tt in range(2):
            pp = p_a if tt == 0 else p_b
            accmm(sc, pp[:], [(swb[:, hc, 0:128], satt[:, hc, tt * 512:(tt + 1) * 512]) for hc in range(16)],
                  waits=[(ld, ldn), (ldw, lwn)])
        sc.st("vector", lambda v: v.tensor_add(sot[:, 0:512], p_a[:], sx1s[:, 0:512]))
        sc.st("vector", lambda v: v.tensor_add(sot[:, 512:1024], p_b[:], sx1s[:, 512:1024]))
        sc.dma("sync", o_x2.rearrange("(c p) t -> p c t", p=128)[:, oc, :], sot[:], stv)
    stv_phaseA = stv.n

    def butterfly_stages(src3, bufA3, bufB3, nch, fd):
        nst = int(np.log2(nch))
        cur = src3
        bufs = [bufA3, bufB3]
        for st_i in range(nst):
            h = 1 << st_i
            g = nch // (2 * h)
            out = bufs[st_i % 2]
            ina = cur.rearrange("p (g two h) f -> p g two h f", two=2, h=h)
            outa = out.rearrange("p (g two h) f -> p g two h f", two=2, h=h)
            sc.st("vector", lambda v, ina=ina, outa=outa: v.tensor_add(
                outa[:, :, 0, :, :], ina[:, :, 0, :, :], ina[:, :, 1, :, :]))
            sc.st("vector", lambda v, ina=ina, outa=outa: v.tensor_sub(
                outa[:, :, 1, :, :], ina[:, :, 0, :, :], ina[:, :, 1, :, :]))
            cur = out
        return cur

    for th in range(2):
        t0 = th * 512
        fA = sF1[:].rearrange("p a b -> p (a b)").rearrange("p (c t) -> p c t", t=512)
        fB = sF2[:].rearrange("p a b -> p (a b)").rearrange("p (c t) -> p c t", t=512)
        fin = butterfly_stages(sxn[:, :, t0:t0 + 512], fA, fB, 16, 512)
        sfv = fA
        for c in range(16):
            sc.st("tensor", lambda t, c=c, fin=fin: t.matmul(p_a[:], sH[:], fin[:, c, :], start=True, stop=True))
            sc.st("scalar", lambda s, c=c: s.activation(sAbs[:], p_a[:], AF.Abs, bias=0.0, scale=1.0))
            sc.st("scalar", lambda s, c=c, sfv=sfv: s.copy(sfv[:, c, :], p_a[:]))
            if c == 0:
                sc.st("vector", lambda v: v.tensor_copy(sMxF[:], sAbs[:]))
            else:
                sc.st("vector", lambda v: v.tensor_max(sMxF[:], sMxF[:], sAbs[:]))
        for ti in range(4):
            sc.st("tensor", lambda t, ti=ti: t.transpose(p_t[:], sMxF[:, ti * 128:(ti + 1) * 128], sid[:]))
            sc.st("vector", lambda v, ti=ti: v.tensor_reduce(sMxC[:, ti:ti + 1], p_t[:], AX.X, ALU.max))
        for ti in range(4):
            sc.st("tensor", lambda t, ti=ti: t.transpose(p_t[0:1, 0:128], sMxC[:, ti:ti + 1], sid[:]))
            sc.st("scalar", lambda s, ti=ti: s.copy(srowq[0:1, 0, ti * 128:(ti + 1) * 128], p_t[0:1, 0:128]))
        sc.ops("vector", [
            lambda v: v.tensor_scalar_max(srowq[:, 0, :], srowq[:, 0, :], 1e-5 * RT2048),
            lambda v: v.tensor_scalar_mul(srowq[:, 1, :], srowq[:, 0, :], 1.0 / (127.0 * RT2048)),
            lambda v: v.reciprocal(srowq[:, 0, :], srowq[:, 0, :]),
            lambda v: v.tensor_scalar_mul(srowq[:, 0, :], srowq[:, 0, :], 127.0),
        ])
        sc.st("tensor", lambda t: t.matmul(p_bc[:], srow1[:], srowq[0:1, 0, :], start=True, stop=True))
        sc.st("scalar", lambda s: s.copy(sQs[:, 0, :], p_bc[:]))
        sc.st("tensor", lambda t: t.matmul(p_bc[:], srow1[:], srowq[0:1, 1, :], start=True, stop=True))
        sc.st("vector", lambda v: v.tensor_scalar(sRg[:], p_bc[:], sScl[:, 0:1], None, ALU.mult))
        sc.st("vector", lambda v: v.tensor_scalar(sRu[:], p_bc[:], sScl[:, 1:2], None, ALU.mult))
        for g4 in range(4):
            cs = slice(g4 * 4, g4 * 4 + 4)
            qv = sqz[:].rearrange("p (c t) -> p c t", t=512)[:, 0:4, :]
            def qmul(v, cs=cs, qv=qv, sfv=sfv):
                a, b = bass.broadcast_tensor_aps(sfv[:, cs, :], sQs[:])
                return v.tensor_tensor(qv, a, b, ALU.mult)
            sc.st("vector", qmul)
            sc.st("scalar", lambda s, qv=qv: s.activation(qv, qv, AF.Identity, bias=smgc[:], scale=1.0))
            sc.st("vector", lambda v, qv=qv: v.tensor_scalar(qv, qv, MAGIC, 127.0, ALU.subtract, ALU.min))
            sc.st("vector", lambda v, cs=cs, qv=qv: v.tensor_scalar(sxq[:, cs, :], qv, -128.0, None, ALU.max))

        sqzv = sqz[:].rearrange("p (c o) -> p c o", o=128)
        for hc in range(64):
            o0 = hc * 128
            for (wsrc, dstT, m) in ((wg_r, sTg, 0), (wu_r, sTu, 1)):
                sc.dma("sync", sqzv, wsrc[:, :, o0:o0 + 128], ldw)
                lwn = ldw.n
                dv = dstT[:].rearrange("p c o -> p (c o)")
                sc.st("scalar", lambda s, m=m: s.activation(
                    sqz[:], sqz[:], AF.Identity, bias=smgc[:], scale=sInv[:, m:m + 1]),
                    waits=[(ldw, lwn)])
                sc.st("vector", lambda v: v.tensor_scalar(sqz[:], sqz[:], MAGIC, 1.0, ALU.subtract, ALU.min))
                sc.st("vector", lambda v, dv=dv: v.tensor_scalar(dv, sqz[:], -1.0, None, ALU.max))
            accmm(sc, p_a[:], [(sTg[:, c, 0:128], sxq[:, c, :]) for c in range(16)])
            accmm(sc, p_b[:], [(sTu[:, c, 0:128], sxq[:, c, :]) for c in range(16)])
            sc.st("vector", lambda v: v.tensor_mul(sgt[:], p_a[:], sRg[:]))
            sc.st("scalar", lambda s: s.activation(ssl[:], sgt[:], AF.Silu, bias=0.0, scale=1.0))
            sc.st("vector", lambda v: v.tensor_mul(sut[:], p_b[:], sRu[:]))
            sc.st("vector", lambda v: v.tensor_mul(sh2[:], ssl[:], sut[:]), waits=[(sth, sth.n)])
            sc.dma("sync", h2scr.rearrange("h p t -> p h t")[:, hc, t0:t0 + 512], sh2[:], sth)

        for tq in range(4):
            tq0 = t0 + tq * 128
            sc.dma("sync", sF1[:], h2scr.rearrange("h p t -> p h t")[:, :, tq0:tq0 + 128], ldh,
                   waits=[(sth, sth.n)])
            ldh_n = ldh.n
            sc.st("vector", lambda v: v.engine_nop(), waits=[(ldh, ldh_n)])
            fin8 = butterfly_stages(sF1[:], sF2[:], sF1[:], 64, 128)
            other = sF1[:] if fin8.tensor.name == "sF2" else sF2[:]
            for g4 in range(16):
                sc.st("tensor", lambda t, g4=g4, fin8=fin8: t.matmul(
                    p_a[:], sH[:], fin8[:, g4 * 4:(g4 + 1) * 4, :].rearrange("p c t -> p (c t)"),
                    start=True, stop=True))
                sc.st("scalar", lambda s: s.activation(sAbs[:], p_a[:], AF.Abs, bias=0.0, scale=1.0))
                sc.st("scalar", lambda s, g4=g4, other=other: s.copy(
                    other[:, g4 * 4:(g4 + 1) * 4, :].rearrange("p c t -> p (c t)"), p_a[:]))
                if g4 == 0:
                    sc.st("vector", lambda v: v.tensor_copy(sMxF[:], sAbs[:]))
                else:
                    sc.st("vector", lambda v: v.tensor_max(sMxF[:], sMxF[:], sAbs[:]))
            sc.ops("vector", [
                lambda v: v.tensor_max(sMxF[:, 0:128], sMxF[:, 0:128], sMxF[:, 128:256]),
                lambda v: v.tensor_max(sMxF[:, 256:384], sMxF[:, 256:384], sMxF[:, 384:512]),
                lambda v: v.tensor_max(sMxF[:, 0:128], sMxF[:, 0:128], sMxF[:, 256:384]),
            ])
            sc.st("tensor", lambda t: t.transpose(p_t[:], sMxF[:, 0:128], sid[:]))
            sc.st("vector", lambda v: v.tensor_reduce(sMxC[:, 0:1], p_t[:], AX.X, ALU.max))
            sc.st("tensor", lambda t: t.transpose(p_t[0:1, 0:128], sMxC[:, 0:1], sid[:]))
            sc.st("scalar", lambda s: s.copy(srowq[0:1, 0, 0:128], p_t[0:1, 0:128]))
            sc.ops("vector", [
                lambda v: v.tensor_scalar_max(srowq[:, 0, 0:128], srowq[:, 0, 0:128], 1e-5 * RT8192),
                lambda v: v.tensor_scalar_mul(srowq[:, 1, 0:128], srowq[:, 0, 0:128], 1.0 / (127.0 * RT8192)),
                lambda v: v.reciprocal(srowq[:, 0, 0:128], srowq[:, 0, 0:128]),
                lambda v: v.tensor_scalar_mul(srowq[:, 0, 0:128], srowq[:, 0, 0:128], 127.0),
            ])
            sc.st("tensor", lambda t: t.matmul(p_bc[:, 0:128], srow1[:], srowq[0:1, 0, 0:128], start=True, stop=True))
            sc.st("scalar", lambda s: s.copy(sQs[:, 0, 0:128], p_bc[:, 0:128]))
            sc.st("tensor", lambda t: t.matmul(p_bc[:, 0:128], srow1[:], srowq[0:1, 1, 0:128], start=True, stop=True))
            sc.st("vector", lambda v, tq=tq: v.tensor_scalar(sRd[:, tq * 128:(tq + 1) * 128], p_bc[:, 0:128], sScl[:, 2:3], None, ALU.mult))
            for g16 in range(4):
                cs = slice(g16 * 16, g16 * 16 + 16)
                qv = sqz[:].rearrange("p (c t) -> p c t", t=128)[:, 0:16, :]
                def qmul8(v, cs=cs, qv=qv, other=other):
                    a, b = bass.broadcast_tensor_aps(other[:, cs, :], sQs[:, :, 0:128])
                    return v.tensor_tensor(qv, a, b, ALU.mult)
                sc.st("vector", qmul8)
                sc.st("scalar", lambda s, qv=qv: s.activation(qv, qv, AF.Identity, bias=smgc[:], scale=1.0))
                sc.st("vector", lambda v, qv=qv: v.tensor_scalar(qv, qv, MAGIC, 127.0, ALU.subtract, ALU.min))
                sc.st("vector", lambda v, cs=cs, qv=qv, tq=tq: v.tensor_scalar(
                    sU[:, cs, tq * 128:(tq + 1) * 128], qv, -128.0, None, ALU.max))

        for oc in range(16):
            for qt in range(4):
                view = swdb[:, qt * 16:(qt + 1) * 16, :].rearrange("p c o -> p (c o)")
                sc.dma("sync", sqzv, wd_r[:, qt * 16:(qt + 1) * 16, oc * 128:(oc + 1) * 128], ldw)
                lwn = ldw.n
                sc.st("scalar", lambda s: s.activation(
                    sqz[:], sqz[:], AF.Identity, bias=smgc[:], scale=sInv[:, 2:3]),
                    waits=[(ldw, lwn)])
                sc.st("vector", lambda v: v.tensor_scalar(sqz[:], sqz[:], MAGIC, 1.0, ALU.subtract, ALU.min))
                sc.st("vector", lambda v, view=view: v.tensor_scalar(view, sqz[:], -1.0, None, ALU.max))
            accmm(sc, p_a[:], [(swdb[:, c, :], sU[:, c, :]) for c in range(64)])
            sc.st("vector", lambda v: v.tensor_mul(sot[:, 0:512], p_a[:], sRd[:]), waits=[(stv, stv.n)])
            sc.dma("gpsimd", o_x2.rearrange("(c p) t -> p c t", p=128)[:, oc, t0:t0 + 512],
                   sot[:, 0:512], stv, waits=[(stv, stv_phaseA)], accum=True)

    sc.st("sync", lambda e: None, waits=[(stv, stv.n), (sth, sth.n)])
    sc.run()
    return nc


_L6_CACHE = {}


def run_launch6(inputs, x1_l, xn1_l, att1_l):
    if "nc" not in _L6_CACHE:
        _L6_CACHE["nc"] = build_launch6()
    nc = _L6_CACHE["nc"]
    wo = np.ascontiguousarray(np.asarray(inputs["L1_attn_Wo"], np.float32))
    wg_t = np.ascontiguousarray(np.asarray(inputs["L1_ffn_w_gate"], np.float32).T)
    wu_t = np.ascontiguousarray(np.asarray(inputs["L1_ffn_w_up"], np.float32).T)
    wd_t = np.ascontiguousarray(np.asarray(inputs["L1_ffn_w_down"], np.float32).T)
    in_maps = []
    for c in range(NCORES):
        in_maps.append({
            "x1_fm": x1_l[c], "xn1_fm": xn1_l[c], "att_fm": att1_l[c],
            "wo": wo, "wg": wg_t, "wu": wu_t, "wd": wd_t,
        })
    res = run_bass_kernel_spmd(nc, in_maps, core_ids=list(range(NCORES)))
    return res.results


def kernel(**inputs):
    x = np.ascontiguousarray(np.asarray(inputs["L0_attn_x"], np.float32).reshape(B * S, D))
    xs_fm = [np.ascontiguousarray(x[c * T:(c + 1) * T].T) for c in range(NCORES)]

    r1 = run_launch1(inputs, xs_fm)
    ckv_full = [np.ascontiguousarray(np.concatenate([r1[4 * s + j]["ckv_fm"] for j in range(4)], axis=1)) for s in range(2)]
    kr_full = [np.ascontiguousarray(np.concatenate([r1[4 * s + j]["kr_fm"] for j in range(4)], axis=1)) for s in range(2)]
    cq_full = [np.ascontiguousarray(np.concatenate([r1[4 * s + j]["cq_fm"] for j in range(4)], axis=1)) for s in range(2)]

    r2 = run_launch2(inputs, ckv_full, kr_full, cq_full)
    att_seq = [np.concatenate([np.asarray(r2[4 * s + j]["attn"]).reshape(512, S) for j in range(4)], axis=0)
               for s in range(2)]
    att_l = [np.ascontiguousarray(att_seq[c // 4][:, (c % 4) * T:(c % 4 + 1) * T]) for c in range(NCORES)]
    xn_l = [r1[c]["xn_fm"] for c in range(NCORES)]
    log_l = [r1[c]["logits"] for c in range(NCORES)]

    r3 = run_launch3(inputs, xs_fm, xn_l, att_l, log_l)
    x1_l = [np.ascontiguousarray(r3[c]["x1_fm"]) for c in range(NCORES)]

    r4 = run_launch4(inputs, x1_l)
    q_full = [np.ascontiguousarray(np.concatenate([np.asarray(r4[4 * s + j]["q_fm"]) for j in range(4)], axis=1)) for s in range(2)]
    k_full = [np.ascontiguousarray(np.concatenate([np.asarray(r4[4 * s + j]["k_fm"]) for j in range(4)], axis=1)) for s in range(2)]
    v_full = [np.ascontiguousarray(np.concatenate([np.asarray(r4[4 * s + j]["v_tm"]) for j in range(4)], axis=0)) for s in range(2)]
    xn1_l = [r4[c]["xn1_fm"] for c in range(NCORES)]

    r5 = run_launch5(inputs, q_full, k_full, v_full)
    att1_seq = [np.concatenate([np.asarray(r5[4 * s + j]["attn"]).reshape(512, S) for j in range(4)], axis=0)
                for s in range(2)]
    att1_l = [np.ascontiguousarray(att1_seq[c // 4][:, (c % 4) * T:(c % 4 + 1) * T]) for c in range(NCORES)]

    r6 = run_launch6(inputs, x1_l, xn1_l, att1_l)
    x2 = np.concatenate([np.asarray(r6[c]["x2_fm"], np.float32).T for c in range(NCORES)], axis=0)
    return np.ascontiguousarray(x2.reshape(B, S, D))


# version 03
# speedup vs baseline: 1.0625x; 1.0585x over previous
import numpy as np
import ml_dtypes
import concourse.bass as bass
import concourse.mybir as mybir
from concourse.bass_utils import run_bass_kernel_spmd

F32 = mybir.dt.float32
BF16 = mybir.dt.bfloat16
AF = mybir.ActivationFunctionType
ALU = mybir.AluOpType
AX = mybir.AxisListType
BF16NP = ml_dtypes.bfloat16

B, S, D = 2, 4096, 2048
NCORES = 8
T = 1024
NEG = -1.0e30
MAGIC = 12582912.0
LN_EPS = 1e-5


class Sem:
    __slots__ = ("name", "n", "h")

    def __init__(self, name):
        self.name, self.n, self.h = name, 0, None


class Sched:
    def __init__(self, nc):
        self.nc = nc
        self.steps = []
        self.sems = {}
        self.cmp = self.sem("cmp")
        self.pe = self.sem("pe")

    def sem(self, name):
        s = self.sems.get(name)
        if s is None:
            s = Sem(name)
            self.sems[name] = s
        return s

    def st(self, eng, fn, waits=(), incs=(), auto=True, engine_inc=True):
        ws = []
        if auto:
            if self.cmp.n:
                ws.append((self.cmp, self.cmp.n))
            if self.pe.n:
                ws.append((self.pe, self.pe.n))
        for w in waits:
            ws.append((w, w.n) if isinstance(w, Sem) else w)
        cs = [(c, 1) if isinstance(c, Sem) else c for c in incs]
        if engine_inc:
            cs.append((self.pe, 1) if eng == "tensor" else (self.cmp, 1))
        for (sm, n) in cs:
            sm.n += n
        self.steps.append((eng, fn, ws, cs))

    def ops(self, eng, fns, waits=()):
        for i, fn in enumerate(fns):
            self.st(eng, fn, waits=waits if i == 0 else ())

    def dma(self, eng, out_ap, in_ap, sem, waits=(), accum=False):
        def fn(e, out_ap=out_ap, in_ap=in_ap, accum=accum):
            if accum:
                return e.dma_start(out=out_ap, in_=in_ap, accum_op=mybir.AluOpType.add)
            return e.dma_start(out=out_ap, in_=in_ap)
        self.st(eng, fn, waits=waits, incs=[(sem, 16)], engine_inc=False)

    def run(self):
        nc = self.nc
        for s in self.sems.values():
            s.h = nc.alloc_semaphore(s.name)
        with nc.Block() as block:
            for engname in ("sync", "gpsimd", "tensor", "vector", "scalar"):
                mine = [s for s in self.steps if s[0] == engname]
                if not mine:
                    continue

                def go(eng, mine=mine):
                    for _, fn, ws, cs in mine:
                        for (sm, v) in ws:
                            if v > 0:
                                eng.wait_ge(sm.h, v)
                        last = fn(eng)
                        for (sm, n) in cs:
                            tgt = last if last is not None else eng.nop()
                            tgt.then_inc(sm.h, n)
                            last = None
                getattr(block, engname)(go)


def accmm(sc, psum_ap, pairs, waits=()):
    def fn(t):
        last = None
        n = len(pairs)
        for i, (l, r) in enumerate(pairs):
            last = t.matmul(psum_ap, l, r, start=(i == 0), stop=(i == n - 1))
        return last
    sc.st("tensor", fn, waits=waits)




def emit_flash_pipelined(sc, qc_range, nkb_of, mm_pairs, diag_info, exp_out, exp_bias,
                         exp_scale, pv_pairs, den_pairs, recip_row, bcast_row_mm,
                         evac_rec, final_mul, banks):
    cmp, pe = sc.cmp, sc.pe
    exp_mark = {0: cmp.n, 1: cmp.n}
    entry = (cmp.n, pe.n)
    first = True
    for qc in qc_range:
        nkb = nkb_of(qc)
        for kb in range(nkb):
            bi = kb % 2
            ps = banks[bi]
            ws = [(cmp, exp_mark[bi])]
            if first:
                ws = [(cmp, entry[0]), (pe, entry[1])]
                first = False
            def fn(t, qc=qc, kb=kb, ps=ps):
                last = None
                pairs = mm_pairs(qc, kb, ps)
                n = len(pairs)
                for i, (l, r) in enumerate(pairs):
                    last = t.matmul(ps[:], l, r, start=(i == 0), stop=(i == n - 1))
                return last
            sc.st("tensor", fn, waits=ws, auto=False)
            mm_mark = pe.n
            d = diag_info(qc, kb)
            if d is not None:
                sc.st("vector", lambda v, ps=ps, d=d: v.tensor_add(ps[:], ps[:], d),
                      waits=[(pe, mm_mark)], auto=False)
            sc.st("scalar", lambda s, ps=ps, qc=qc, kb=kb: s.activation(
                exp_out(qc, kb), ps[:], AF.Exp, bias=exp_bias, scale=exp_scale),
                waits=[(pe, mm_mark)] + ([(cmp, cmp.n)] if d is not None else []),
                auto=False)
            exp_mark[bi] = cmp.n
        after_exps = cmp.n
        def pv(t, qc=qc, nkb=nkb):
            last = None
            pairs = pv_pairs(qc, nkb)
            n = len(pairs)
            for i, (l, r) in enumerate(pairs):
                last = t.matmul(banks[2][:], l, r, start=(i == 0), stop=(i == n - 1))
            return last
        sc.st("tensor", pv, waits=[(cmp, after_exps)], auto=False)
        def den(t, qc=qc, nkb=nkb):
            last = None
            pairs = den_pairs(qc, nkb)
            n = len(pairs)
            for i, (l, r) in enumerate(pairs):
                last = t.matmul(banks[3][:], l, r, start=(i == 0), stop=(i == n - 1))
            return last
        sc.st("tensor", den, waits=[], auto=False)
        den_mark = pe.n
        sc.st("vector", lambda v: recip_row(v), waits=[(pe, den_mark)], auto=False)
        rec_mark = cmp.n
        sc.st("tensor", lambda t: bcast_row_mm(t), waits=[(cmp, rec_mark)], auto=False)
        bc_mark = pe.n
        sc.st("scalar", lambda s: evac_rec(s), waits=[(pe, bc_mark)], auto=False)
        ev_mark = cmp.n
        sc.st("vector", lambda v, qc=qc: final_mul(v, qc), waits=[(cmp, ev_mark)], auto=False)



def pope_tables():
    half = 32
    theta = 1.0 / (10000.0 ** (2.0 * np.arange(half, dtype=np.float64) / 64.0))
    ang = np.arange(S, dtype=np.float64)[:, None] * theta[None, :]
    return (np.cos(ang).astype(np.float32).T.copy(),
            np.sin(ang).astype(np.float32).T.copy())


def rope1_tables():
    inv = 1.0 / (10000.0 ** (np.arange(0, 128, 2, dtype=np.float64) / 128.0))
    fr = np.arange(S, dtype=np.float64)[:, None] * inv[None, :]
    emb = np.concatenate([fr, fr], axis=1)
    return (np.cos(emb).astype(np.float32).T.copy(),
            np.sin(emb).astype(np.float32).T.copy())



def build_launch1():
    nc = bass.Bass("TRN2", target_bir_lowering=False)
    x = nc.dram_tensor("x_fm", [D, T], F32, kind="ExternalInput").ap()
    w_dkv = nc.dram_tensor("w_dkv", [D, 512], F32, kind="ExternalInput").ap()
    w_dq = nc.dram_tensor("w_dq", [D, 1024], F32, kind="ExternalInput").ap()
    w_kr = nc.dram_tensor("w_kr", [D, 64], F32, kind="ExternalInput").ap()
    gw_t = nc.dram_tensor("gw_t", [D, 8], F32, kind="ExternalInput").ap()
    delta = nc.dram_tensor("delta", [1, 32], F32, kind="ExternalInput").ap()
    cos0s = nc.dram_tensor("cos0s", [32, T], F32, kind="ExternalInput").ap()
    sin0s = nc.dram_tensor("sin0s", [32, T], F32, kind="ExternalInput").ap()

    o_xn = nc.dram_tensor("xn_fm", [D, T], BF16, kind="ExternalOutput").ap()
    o_ckv = nc.dram_tensor("ckv_fm", [512, T], BF16, kind="ExternalOutput").ap()
    o_cq = nc.dram_tensor("cq_fm", [1024, T], BF16, kind="ExternalOutput").ap()
    o_kr = nc.dram_tensor("kr_fm", [64, T], BF16, kind="ExternalOutput").ap()
    o_log = nc.dram_tensor("logits", [T, 8], F32, kind="ExternalOutput").ap()

    sc = Sched(nc)
    ld = sc.sem("ld")
    ldw = sc.sem("ldw")
    stv = sc.sem("stv")

    sx = nc.alloc_sbuf_tensor("sx", [128, 16, T], F32)
    sxn = nc.alloc_sbuf_tensor("sxn", [128, 16, T], BF16)
    swdkv = nc.alloc_sbuf_tensor("swdkv", [128, 16, 512], BF16)
    swdq = nc.alloc_sbuf_tensor("swdq", [128, 16, 512], BF16)
    swkr = nc.alloc_sbuf_tensor("swkr", [128, 16, 64], BF16)
    sgw = nc.alloc_sbuf_tensor("sgw", [128, 16, 8], F32)
    ssq = nc.alloc_sbuf_tensor("ssq", [128, 2, 512], F32)
    sones = nc.alloc_sbuf_tensor("sones", [128, 1], F32)
    srow1 = nc.alloc_sbuf_tensor("srow1", [1, 128], F32)
    srow_stats = nc.alloc_sbuf_tensor("srow_stats", [1, 2, 512], F32)
    srow_ab = nc.alloc_sbuf_tensor("srow_ab", [1, 2, 512], F32)
    sA = nc.alloc_sbuf_tensor("sA", [128, 512], F32)
    sB = nc.alloc_sbuf_tensor("sB", [128, 512], F32)
    s_mr_col = nc.alloc_sbuf_tensor("s_mr_col", [128, 2], F32)
    sSrow = nc.alloc_sbuf_tensor("sSrow", [1, 8], F32)
    sSb = nc.alloc_sbuf_tensor("sSb", [128, 8], F32)
    slog = nc.alloc_sbuf_tensor("slog", [128, 8, 8], F32)
    sckv = nc.alloc_sbuf_tensor("sckv", [128, 4, T], BF16)
    scq = nc.alloc_sbuf_tensor("scq", [128, 8, T], BF16)
    skr_raw = nc.alloc_sbuf_tensor("skr_raw", [64, T], F32)
    skr = nc.alloc_sbuf_tensor("skr", [64, T], BF16)
    ssp = nc.alloc_sbuf_tensor("ssp", [64, T], F32)
    ssp2 = nc.alloc_sbuf_tensor("ssp2", [64, T], F32)
    sdelta = nc.alloc_sbuf_tensor("sdelta", [1, 32], F32)
    sdrow = nc.alloc_sbuf_tensor("sdrow", [1, 4, 32], F32)
    sdcol = nc.alloc_sbuf_tensor("sdcol", [32, 2], F32)
    scos0 = nc.alloc_sbuf_tensor("scos0", [32, T], F32)
    ssin0 = nc.alloc_sbuf_tensor("ssin0", [32, T], F32)
    seps = nc.alloc_sbuf_tensor("seps", [1, 1], F32)
    smu2 = nc.alloc_sbuf_tensor("smu2", [32, T], F32)
    smu2b = nc.alloc_sbuf_tensor("smu2b", [32, T], F32)

    ones_col = nc.inline_tensor(np.ones((128, 1), np.float32), name="c_ones_col")
    ones_row = nc.inline_tensor(np.ones((1, 128), np.float32), name="c_ones_row")
    eps_c = nc.inline_tensor(np.full((1, 1), LN_EPS, np.float32), name="c_eps")

    p_big = nc.alloc_psum_tensor("p_big", [128, 512], F32)
    p_big2 = nc.alloc_psum_tensor("p_big2", [128, 512], F32)
    p_s1 = nc.alloc_psum_tensor("p_s1", [1, 512], F32)
    p_s2 = nc.alloc_psum_tensor("p_s2", [1, 512], F32)
    p_bc = nc.alloc_psum_tensor("p_bc", [128, 512], F32)
    p_bc2 = nc.alloc_psum_tensor("p_bc2", [128, 512], F32)
    p_col = nc.alloc_psum_tensor("p_col", [128, 2], F32)
    p_row8 = nc.alloc_psum_tensor("p_row8", [1, 8], F32)

    sc.dma("sync", sx[:], x.rearrange("(c p) t -> p c t", p=128), ld)
    sc.dma("sync", sones[:], ones_col.ap(), ld)
    sc.dma("sync", srow1[:], ones_row.ap(), ld)
    sc.dma("sync", sdelta[:], delta, ld)
    sc.dma("sync", scos0[:], cos0s, ld)
    sc.dma("sync", ssin0[:], sin0s, ld)
    sc.dma("sync", sgw[:], gw_t.rearrange("(c p) o -> p c o", p=128), ld)
    sc.dma("sync", seps[:], eps_c.ap(), ld)
    ld_all = ld.n
    sc.dma("gpsimd", swdkv[:], w_dkv.rearrange("(c p) o -> p c o", p=128), ldw)
    sc.dma("gpsimd", swdq[:], w_dq.rearrange("(c p) o -> p c o", p=128)[:, :, 0:512], ldw)
    sc.dma("gpsimd", swkr[:], w_kr.rearrange("(c p) o -> p c o", p=128), ldw)
    ldw_all = ldw.n

    accmm(sc, p_row8[:], [(sones[:], sgw[:, c, :]) for c in range(16)],
          waits=[(ld, ld_all)])
    sc.st("scalar", lambda s: s.copy(sSrow[:], p_row8[:]))
    sc.st("tensor", lambda t: t.matmul(p_bc[:, 0:8], srow1[:], sSrow[:], start=True, stop=True))
    sc.st("scalar", lambda s: s.copy(sSb[:], p_bc[:, 0:8]))

    for tt in range(2):
        t0 = tt * 512
        accmm(sc, p_s1[:], [(sones[:], sx[:, c, t0:t0 + 512]) for c in range(16)],
              waits=[(ld, ld_all)])
        for c in range(16):
            sl = c % 2
            sc.st("vector",
                  lambda v, c=c, sl=sl, t0=t0:
                  v.tensor_mul(ssq[:, sl, :], sx[:, c, t0:t0 + 512], sx[:, c, t0:t0 + 512]),
                  waits=[(ld, ld_all)])
            sc.st("tensor",
                  lambda t, c=c, sl=sl:
                  t.matmul(p_s2[:], sones[:], ssq[:, sl, :], start=(c == 0), stop=(c == 15)))

        sc.ops("vector", [
            lambda v: v.tensor_scalar_mul(srow_stats[:, 0, :], p_s1[:], 1.0 / 2048.0),
            lambda v: v.tensor_scalar_mul(srow_stats[:, 1, :], p_s2[:], 1.0 / 2048.0),
            lambda v: v.tensor_mul(srow_ab[:, 0, :], srow_stats[:, 0, :], srow_stats[:, 0, :]),
            lambda v: v.tensor_sub(srow_ab[:, 1, :], srow_stats[:, 1, :], srow_ab[:, 0, :]),
        ])

        sc.ops("scalar", [
            lambda s: s.activation(srow_ab[:, 0, :], srow_ab[:, 1, :], AF.Ln, bias=seps[:], scale=1.0),
            lambda s: s.activation(srow_ab[:, 0, :], srow_ab[:, 0, :], AF.Exp, bias=0.0, scale=-0.5),
        ], waits=[(ld, ld_all)])

        sc.ops("vector", [
            lambda v: v.tensor_mul(srow_ab[:, 1, :], srow_stats[:, 0, :], srow_ab[:, 0, :]),
            lambda v: v.tensor_scalar_mul(srow_ab[:, 1, :], srow_ab[:, 1, :], -1.0),
        ])

        def bc_ab(t, tt=tt):
            t.matmul(p_bc[:], srow1[:], srow_ab[:, 0, :], start=True, stop=True)
            return t.matmul(p_bc2[:], srow1[:], srow_ab[:, 1, :], start=True, stop=True)
        sc.st("tensor", bc_ab)

        def evac_ab(s):
            s.copy(sA[:], p_bc[:])
            return s.copy(sB[:], p_bc2[:])
        sc.st("scalar", evac_ab)

        xn_ops = []
        for c in range(16):
            xn_ops.append(lambda v, c=c, t0=t0: v.tensor_mul(ssq[:, c % 2, :], sx[:, c, t0:t0 + 512], sA[:]))
            xn_ops.append(lambda v, c=c, t0=t0: v.tensor_add(sxn[:, c, t0:t0 + 512], ssq[:, c % 2, :], sB[:]))
        sc.ops("vector", xn_ops)

        for q in range(4):
            ti = tt * 4 + q
            tq = ti * 128
            accmm(sc, p_big[:, 0:8],
                  [(sx[:, c, tq:tq + 128], sgw[:, c, :]) for c in range(16)],
                  waits=[(ld, ld_all)])

            def tr_mr(t, off=q * 128):
                t.matmul(p_col[:, 0:1], srow_stats[0:1, 0, off:off + 128], sones[0:1, 0:1],
                         start=True, stop=True)
                return t.matmul(p_col[:, 1:2], srow_ab[0:1, 0, off:off + 128], sones[0:1, 0:1],
                                start=True, stop=True)
            sc.st("tensor", tr_mr)
            sc.st("scalar", lambda s: s.copy(s_mr_col[:], p_col[:, 0:2]))

            sc.ops("vector", [
                lambda v, ti=ti: v.tensor_scalar(slog[:, ti, :], sSb[:], s_mr_col[:, 0:1], None, ALU.mult),
                lambda v, ti=ti: v.tensor_sub(slog[:, ti, :], p_big[:, 0:8], slog[:, ti, :]),
                lambda v, ti=ti: v.tensor_scalar(slog[:, ti, :], slog[:, ti, :], s_mr_col[:, 1:2], None, ALU.mult),
            ])

    sc.dma("sync", o_xn.rearrange("(c p) t -> p c t", p=128), sxn[:], stv)
    sc.dma("sync", o_log.rearrange("(q p) o -> p q o", p=128), slog[:], stv)

    ldw2 = sc.sem("ldw2")
    for (wt, outt, nout, wsem, wval) in ((swdkv, sckv, 4, ldw, ldw_all),
                                         (swdq, scq, 4, ldw, ldw_all)):
        for oc in range(nout):
            for tt in range(2):
                t0 = tt * 512
                pb = p_big if (oc + tt) % 2 == 0 else p_big2
                accmm(sc, pb[:],
                      [(wt[:, c, oc * 128:(oc + 1) * 128], sxn[:, c, t0:t0 + 512]) for c in range(16)],
                      waits=[(wsem, wval)])
                sc.st("scalar",
                      lambda s, outt=outt, oc=oc, t0=t0, pb=pb:
                      s.copy(outt[:, oc, t0:t0 + 512], pb[:]))
    sc.dma("gpsimd", swdq[:], w_dq.rearrange("(c p) o -> p c o", p=128)[:, :, 512:1024], ldw2)
    for oc in range(4):
        for tt in range(2):
            t0 = tt * 512
            pb = p_big if (oc + tt) % 2 == 0 else p_big2
            accmm(sc, pb[:],
                  [(swdq[:, c, oc * 128:(oc + 1) * 128], sxn[:, c, t0:t0 + 512]) for c in range(16)],
                  waits=[(ldw2, 16)])
            sc.st("scalar",
                  lambda s, oc=oc, t0=t0, pb=pb:
                  s.copy(scq[:, 4 + oc, t0:t0 + 512], pb[:]))
    for tt in range(2):
        t0 = tt * 512
        pb = p_big if tt % 2 == 0 else p_big2
        accmm(sc, pb[0:64, :],
              [(swkr[:, c, :], sxn[:, c, t0:t0 + 512]) for c in range(16)],
              waits=[(ldw, ldw_all)])
        sc.st("scalar", lambda s, t0=t0, pb=pb: s.copy(skr_raw[:, t0:t0 + 512], pb[0:64, :]))

    sc.dma("sync", o_ckv.rearrange("(c p) t -> p c t", p=128), sckv[:], stv)
    sc.dma("sync", o_cq.rearrange("(c p) t -> p c t", p=128), scq[:], stv)

    sc.ops("vector", [
        lambda v: v.tensor_scalar(sdrow[:, 0, :], sdelta[:], 1.0 / (2.0 * np.pi), MAGIC, ALU.mult, ALU.add),
        lambda v: v.tensor_scalar(sdrow[:, 0, :], sdrow[:, 0, :], MAGIC, None, ALU.subtract),
        lambda v: v.tensor_scalar(sdrow[:, 0, :], sdrow[:, 0, :], 2.0 * np.pi, None, ALU.mult),
        lambda v: v.tensor_sub(sdrow[:, 1, :], sdelta[:], sdrow[:, 0, :]),
        lambda v: v.tensor_scalar_mul(sdrow[:, 2, :], sdrow[:, 1, :], 0.5),
    ], waits=[(ld, ld_all)])

    sc.ops("scalar", [
        lambda s: s.activation(sdrow[:, 1, :], sdrow[:, 1, :], AF.Sin, bias=0.0, scale=1.0),
        lambda s: s.activation(sdrow[:, 2, :], sdrow[:, 2, :], AF.Sin, bias=0.0, scale=1.0),
    ])

    sc.ops("vector", [
        lambda v: v.tensor_mul(sdrow[:, 3, :], sdrow[:, 2, :], sdrow[:, 2, :]),
        lambda v: v.tensor_scalar(sdrow[:, 3, :], sdrow[:, 3, :], -2.0, 1.0, ALU.mult, ALU.add),
    ])

    def tr_ds(t):
        t.matmul(p_col[0:32, 0:1], sdrow[0:1, 3, :], sones[0:1, 0:1], start=True, stop=True)
        return t.matmul(p_col[0:32, 1:2], sdrow[0:1, 1, :], sones[0:1, 0:1], start=True, stop=True)
    sc.st("tensor", tr_ds)
    sc.st("scalar", lambda s: s.copy(sdcol[:], p_col[0:32, 0:2]))

    sc.ops("vector", [
        lambda v: v.tensor_scalar(ssp[0:32, :], ssin0[:], sdcol[:, 1:2], None, ALU.mult),
        lambda v: v.tensor_scalar(ssp2[0:32, :], scos0[:], sdcol[:, 1:2], None, ALU.mult),
        lambda v: v.tensor_scalar(scos0[:], scos0[:], sdcol[:, 0:1], None, ALU.mult),
        lambda v: v.tensor_scalar(ssin0[:], ssin0[:], sdcol[:, 0:1], None, ALU.mult),
        lambda v: v.tensor_sub(scos0[:], scos0[:], ssp[0:32, :]),
        lambda v: v.tensor_add(ssin0[:], ssin0[:], ssp2[0:32, :]),
    ])

    sc.ops("scalar", [
        lambda s: s.activation(ssp[:], skr_raw[:], AF.Abs, bias=0.0, scale=1.0),
        lambda s: s.activation(ssp[:], ssp[:], AF.Exp, bias=0.0, scale=-1.0),
        lambda s: s.activation(ssp[:], ssp[:], AF.Ln, bias=sones[0:64, :], scale=1.0),
    ])

    sc.ops("vector", [
        lambda v: v.tensor_scalar_max(ssp2[:], skr_raw[:], 0.0),
        lambda v: v.tensor_add(ssp2[:], ssp2[:], ssp[:]),
    ])

    sc.ops("vector", [
        lambda v: v.tensor_copy(smu2[:], ssp2[32:64, :]),
        lambda v: v.tensor_mul(ssp[0:32, :], ssp2[0:32, :], scos0[:]),
        lambda v: v.tensor_mul(smu2b[:], smu2[:], ssin0[:]),
        lambda v: v.tensor_sub(skr[0:32, :], ssp[0:32, :], smu2b[:]),
        lambda v: v.tensor_mul(ssp[0:32, :], ssp2[0:32, :], ssin0[:]),
        lambda v: v.tensor_mul(smu2b[:], smu2[:], scos0[:]),
        lambda v: v.tensor_add(smu2b[:], ssp[0:32, :], smu2b[:]),
        lambda v: v.tensor_copy(skr[32:64, :], smu2b[:]),
    ])

    sc.dma("sync", o_kr, skr[:], stv)
    sc.st("sync", lambda e: None, waits=[(stv, stv.n)])
    sc.run()
    return nc


_L1_CACHE = {}


def run_launch1(inputs, xs_fm):
    if "nc" not in _L1_CACHE:
        _L1_CACHE["nc"] = build_launch1()
    nc = _L1_CACHE["nc"]
    cos0, sin0 = pope_tables()
    in_maps = []
    for c in range(NCORES):
        pos0 = (c % 4) * T
        in_maps.append({
            "x_fm": xs_fm[c],
            "w_dkv": np.ascontiguousarray(inputs["L0_attn_w_dkv"], np.float32),
            "w_dq": np.ascontiguousarray(inputs["L0_attn_w_dq"], np.float32),
            "w_kr": np.ascontiguousarray(inputs["L0_attn_w_kr"], np.float32),
            "gw_t": np.ascontiguousarray(np.asarray(inputs["L0_ffn_gate_w"], np.float32).T),
            "delta": np.asarray(inputs["L0_attn_delta"], np.float32).reshape(1, 32),
            "cos0s": np.ascontiguousarray(cos0[:, pos0:pos0 + T]),
            "sin0s": np.ascontiguousarray(sin0[:, pos0:pos0 + T]),
        })
    res = run_bass_kernel_spmd(nc, in_maps, core_ids=list(range(NCORES)))
    return res.results





SC192 = 1.0 / np.sqrt(192.0)


def build_launch2():
    nc = bass.Bass("TRN2", target_bir_lowering=False)
    ckv = nc.dram_tensor("ckv", [512, S], BF16, kind="ExternalInput").ap()
    krf = nc.dram_tensor("krf", [64, S], BF16, kind="ExternalInput").ap()
    cq = nc.dram_tensor("cq", [1024, S], BF16, kind="ExternalInput").ap()
    wuq = nc.dram_tensor("wuq", [1024, 512], F32, kind="ExternalInput").ap()
    wqr = nc.dram_tensor("wqr", [1024, 256], F32, kind="ExternalInput").ap()
    wuk = nc.dram_tensor("wuk", [512, 512], F32, kind="ExternalInput").ap()
    wuv = nc.dram_tensor("wuv", [512, 512], F32, kind="ExternalInput").ap()
    delta = nc.dram_tensor("delta", [1, 32], F32, kind="ExternalInput").ap()
    o_attn = nc.dram_tensor("attn", [4, 128, S], BF16, kind="ExternalOutput").ap()

    cos0, sin0 = pope_tables()
    c_cos0 = nc.inline_tensor(cos0.astype(BF16NP), name="c_cos0")
    c_sin0 = nc.inline_tensor(sin0.astype(BF16NP), name="c_sin0")
    masks = np.where((np.arange(4)[:, None, None] * 128 + np.arange(128)[None, :, None])
                     > np.arange(512)[None, None, :], NEG, 0.0).astype(np.float32)
    c_masks = nc.inline_tensor(masks.reshape(4 * 128, 512), name="c_masks")
    ones_col = nc.inline_tensor(np.ones((128, 1), np.float32), name="c_ones_col")
    ones_colb = nc.inline_tensor(np.ones((128, 1), np.float32).astype(BF16NP), name="c_ones_colb")
    ones_row = nc.inline_tensor(np.ones((1, 128), np.float32), name="c_ones_row")

    sc = Sched(nc)
    ld = sc.sem("ld")
    ldw = sc.sem("ldw")
    ldq = sc.sem("ldq")
    stv = sc.sem("stv")

    sckv = nc.alloc_sbuf_tensor("sckv", [128, 4, S], BF16)
    skr = nc.alloc_sbuf_tensor("skr", [64, S], BF16)
    scqb = nc.alloc_sbuf_tensor("scqb", [128, 2, 8, 512], BF16)
    suq = nc.alloc_sbuf_tensor("suq", [128, 8, 512], BF16)
    sqrw = nc.alloc_sbuf_tensor("sqrw", [128, 8, 256], BF16)
    suk = nc.alloc_sbuf_tensor("suk", [128, 4, 512], BF16)
    suv = nc.alloc_sbuf_tensor("suv", [128, 4, 512], BF16)
    sK = nc.alloc_sbuf_tensor("sK", [128, 8, 512], BF16)
    sV = nc.alloc_sbuf_tensor("sV", [128, 32, 128], BF16)
    sQ = nc.alloc_sbuf_tensor("sQ", [128, 8, 512], BF16)
    sQr = nc.alloc_sbuf_tensor("sQr", [64, S], BF16)
    sqr_raw = nc.alloc_sbuf_tensor("sqr_raw", [64, 512], F32)
    ssp = nc.alloc_sbuf_tensor("ssp", [64, 512], F32)
    ssp2 = nc.alloc_sbuf_tensor("ssp2", [64, 512], F32)
    smu2 = nc.alloc_sbuf_tensor("smu2", [32, 512], F32)
    smu2b = nc.alloc_sbuf_tensor("smu2b", [32, 512], F32)
    sPt = nc.alloc_sbuf_tensor("sPt", [128, 32, 512], BF16)
    sOut = nc.alloc_sbuf_tensor("sOut", [128, S], BF16)
    smask = nc.alloc_sbuf_tensor("smask", [128, 4, 512], F32)
    scos = nc.alloc_sbuf_tensor("scos", [32, S], BF16)
    ssin = nc.alloc_sbuf_tensor("ssin", [32, S], BF16)
    ssq = nc.alloc_sbuf_tensor("ssq", [128, 512], F32)
    stiny = nc.alloc_sbuf_tensor("stiny", [1, 2, 8], F32)
    sscal = nc.alloc_sbuf_tensor("sscal", [1, 8], F32)
    sbias = nc.alloc_sbuf_tensor("sbias", [128, 1], F32)
    sRec = nc.alloc_sbuf_tensor("sRec", [128, 512], F32)
    srd = nc.alloc_sbuf_tensor("srd", [1, 512], F32)
    sones = nc.alloc_sbuf_tensor("sones", [128, 1], F32)
    sonesb = nc.alloc_sbuf_tensor("sonesb", [128, 1], BF16)
    srow1 = nc.alloc_sbuf_tensor("srow1", [1, 128], F32)
    sdelta = nc.alloc_sbuf_tensor("sdelta", [1, 32], F32)
    sdrow = nc.alloc_sbuf_tensor("sdrow", [1, 4, 32], F32)
    sdcol = nc.alloc_sbuf_tensor("sdcol", [32, 2], F32)

    p_s = nc.alloc_psum_tensor("p_s", [128, 512], F32)
    p_s2 = nc.alloc_psum_tensor("p_s2", [128, 512], F32)
    p_o = nc.alloc_psum_tensor("p_o", [128, 512], F32)
    p_d = nc.alloc_psum_tensor("p_d", [1, 512], F32)
    p_bc = nc.alloc_psum_tensor("p_bc", [128, 512], F32)
    p_row = nc.alloc_psum_tensor("p_row", [1, 512], F32)
    p_col = nc.alloc_psum_tensor("p_col", [128, 2], F32)

    sc.dma("sync", sckv[:], ckv.rearrange("(c p) t -> p c t", p=128), ld)
    sc.dma("sync", skr[:], krf, ld)
    sc.dma("sync", smask[:], c_masks.ap().rearrange("(m p) t -> p m t", p=128), ld)
    sc.dma("sync", scos[:], c_cos0.ap(), ld)
    sc.dma("sync", ssin[:], c_sin0.ap(), ld)
    sc.dma("sync", sones[:], ones_col.ap(), ld)
    sc.dma("sync", sonesb[:], ones_colb.ap(), ld)
    sc.dma("sync", srow1[:], ones_row.ap(), ld)
    sc.dma("sync", sdelta[:], delta, ld)
    ld_all = ld.n
    sc.dma("gpsimd", suq[:], wuq.rearrange("(c p) o -> p c o", p=128), ldw)
    sc.dma("gpsimd", sqrw[:], wqr.rearrange("(c p) o -> p c o", p=128), ldw)
    sc.dma("gpsimd", suk[:], wuk.rearrange("(c p) o -> p c o", p=128), ldw)
    sc.dma("gpsimd", suv[:], wuv.rearrange("(c p) o -> p c o", p=128), ldw)
    ldw_all = ldw.n

    sc.ops("vector", [
        lambda v: v.tensor_scalar(sdrow[:, 0, :], sdelta[:], 1.0 / (2.0 * np.pi), MAGIC, ALU.mult, ALU.add),
        lambda v: v.tensor_scalar(sdrow[:, 0, :], sdrow[:, 0, :], MAGIC, None, ALU.subtract),
        lambda v: v.tensor_scalar(sdrow[:, 0, :], sdrow[:, 0, :], 2.0 * np.pi, None, ALU.mult),
        lambda v: v.tensor_sub(sdrow[:, 1, :], sdelta[:], sdrow[:, 0, :]),
        lambda v: v.tensor_scalar_mul(sdrow[:, 2, :], sdrow[:, 1, :], 0.5),
    ], waits=[(ld, ld_all)])
    sc.ops("scalar", [
        lambda s: s.activation(sdrow[:, 1, :], sdrow[:, 1, :], AF.Sin, bias=0.0, scale=1.0),
        lambda s: s.activation(sdrow[:, 2, :], sdrow[:, 2, :], AF.Sin, bias=0.0, scale=1.0),
    ])
    sc.ops("vector", [
        lambda v: v.tensor_mul(sdrow[:, 3, :], sdrow[:, 2, :], sdrow[:, 2, :]),
        lambda v: v.tensor_scalar(sdrow[:, 3, :], sdrow[:, 3, :], -2.0, 1.0, ALU.mult, ALU.add),
    ])
    def tr_ds(t):
        t.matmul(p_col[0:32, 0:1], sdrow[0:1, 3, :], sones[0:1, 0:1], start=True, stop=True)
        return t.matmul(p_col[0:32, 1:2], sdrow[0:1, 1, :], sones[0:1, 0:1], start=True, stop=True)
    sc.st("tensor", tr_ds)
    sc.st("scalar", lambda s: s.copy(sdcol[:], p_col[0:32, 0:2]))
    sstrip = nc.alloc_sbuf_tensor("sstrip", [32, 2, 1024], BF16)
    for st0 in range(0, S, 1024):
        sl = (slice(0, 32), slice(st0, st0 + 1024))
        sc.ops("vector", [
            lambda v, sl=sl: v.tensor_scalar(sstrip[:, 0, :], ssin[sl[0], sl[1]], sdcol[:, 1:2], None, ALU.mult),
            lambda v, sl=sl: v.tensor_scalar(sstrip[:, 1, :], scos[sl[0], sl[1]], sdcol[:, 1:2], None, ALU.mult),
            lambda v, sl=sl: v.tensor_scalar(scos[sl[0], sl[1]], scos[sl[0], sl[1]], sdcol[:, 0:1], None, ALU.mult),
            lambda v, sl=sl: v.tensor_scalar(ssin[sl[0], sl[1]], ssin[sl[0], sl[1]], sdcol[:, 0:1], None, ALU.mult),
            lambda v, sl=sl: v.tensor_sub(scos[sl[0], sl[1]], scos[sl[0], sl[1]], sstrip[:, 0, :]),
            lambda v, sl=sl: v.tensor_add(ssin[sl[0], sl[1]], ssin[sl[0], sl[1]], sstrip[:, 1, :]),
        ])

    for u in range(4):
        for kb8 in range(8):
            accmm(sc, p_s[:], [(suk[:, c, u * 128:(u + 1) * 128], sckv[:, c, kb8 * 512:(kb8 + 1) * 512]) for c in range(4)],
                  waits=[(ldw, ldw_all), (ld, ld_all), (stv, stv.n)])
            sc.st("scalar", lambda s, kb8=kb8: s.copy(sK[:, kb8, :], p_s[:]))
        for kb2 in range(32):
            accmm(sc, p_s[:, 0:128], [(sckv[:, c, kb2 * 128:(kb2 + 1) * 128], suv[:, c, u * 128:(u + 1) * 128]) for c in range(4)])
            sc.st("scalar", lambda s, kb2=kb2: s.copy(sV[:, kb2, :], p_s[:, 0:128]))
        for kb8 in range(8):
            sc.st("vector", lambda v, kb8=kb8: v.tensor_mul(ssq[:], sK[:, kb8, :], sK[:, kb8, :]))
            sc.st("vector", lambda v, kb8=kb8: v.tensor_mul(ssp[:], skr[:, kb8 * 512:(kb8 + 1) * 512], skr[:, kb8 * 512:(kb8 + 1) * 512]))
            accmm(sc, p_row[:], [(sones[:], ssq[:]), (sones[0:64, :], ssp[:])])
            sc.st("vector", lambda v, kb8=kb8: v.tensor_reduce(stiny[0:1, 0, kb8:kb8 + 1], p_row[:], AX.X, ALU.max))

        for qb in range(8):
            par = qb % 2
            sc.dma("gpsimd", scqb[:, par, :, :],
                   cq.rearrange("(c p) t -> p c t", p=128)[:, :, qb * 512:(qb + 1) * 512], ldq)
            ldq_now = ldq.n
            accmm(sc, p_s[:], [(suq[:, c, u * 128:(u + 1) * 128], scqb[:, par, c, :]) for c in range(8)],
                  waits=[(ldq, ldq_now)])
            sc.st("scalar", lambda s, qb=qb: s.copy(sQ[:, qb, :], p_s[:]))
            accmm(sc, p_s2[0:64, :],
                  [(sqrw[:, c, u * 64:(u + 1) * 64], scqb[:, par, c, :]) for c in range(8)],
                  waits=[(ldq, ldq_now)])
            sc.st("scalar", lambda s: s.copy(sqr_raw[:], p_s2[0:64, :]))
            sc.ops("scalar", [
                lambda s: s.activation(ssp[:], sqr_raw[:], AF.Abs, bias=0.0, scale=1.0),
                lambda s: s.activation(ssp[:], ssp[:], AF.Exp, bias=0.0, scale=-1.0),
                lambda s: s.activation(ssp[:], ssp[:], AF.Ln, bias=sones[0:64, :], scale=1.0),
            ])
            sc.ops("vector", [
                lambda v: v.tensor_scalar_max(ssp2[:], sqr_raw[:], 0.0),
                lambda v: v.tensor_add(ssp2[:], ssp2[:], ssp[:]),
            ])
            q0 = qb * 512
            sc.ops("vector", [
                lambda v: v.tensor_copy(smu2[:], ssp2[32:64, :]),
                lambda v, q0=q0: v.tensor_mul(ssp[0:32, :], ssp2[0:32, :], scos[:, q0:q0 + 512]),
                lambda v, q0=q0: v.tensor_mul(smu2b[:], smu2[:], ssin[:, q0:q0 + 512]),
                lambda v, q0=q0: v.tensor_sub(sQr[0:32, q0:q0 + 512], ssp[0:32, :], smu2b[:]),
                lambda v, q0=q0: v.tensor_mul(ssp[0:32, :], ssp2[0:32, :], ssin[:, q0:q0 + 512]),
                lambda v, q0=q0: v.tensor_mul(smu2b[:], smu2[:], scos[:, q0:q0 + 512]),
                lambda v: v.tensor_add(smu2b[:], ssp[0:32, :], smu2b[:]),
                lambda v, q0=q0: v.tensor_copy(sQr[32:64, q0:q0 + 512], smu2b[:]),
            ])
            sc.st("vector", lambda v, qb=qb: v.tensor_mul(ssq[:], sQ[:, qb, :], sQ[:, qb, :]))
            sc.st("vector", lambda v, q0=q0: v.tensor_mul(ssp[:], sQr[:, q0:q0 + 512], sQr[:, q0:q0 + 512]))
            accmm(sc, p_row[:], [(sones[:], ssq[:]), (sones[0:64, :], ssp[:])])
            sc.st("vector", lambda v, qb=qb: v.tensor_reduce(stiny[0:1, 1, qb:qb + 1], p_row[:], AX.X, ALU.max))

        sc.ops("vector", [
            lambda v: v.tensor_reduce(sscal[0:1, 0:1], stiny[0:1, 0, :], AX.X, ALU.max),
            lambda v: v.tensor_reduce(sscal[0:1, 1:2], stiny[0:1, 1, :], AX.X, ALU.max),
            lambda v: v.tensor_mul(sscal[0:1, 2:3], sscal[0:1, 0:1], sscal[0:1, 1:2]),
        ])
        sc.ops("scalar", [
            lambda s: s.activation(sscal[0:1, 3:4], sscal[0:1, 2:3], AF.Ln, bias=0.0, scale=1.0),
            lambda s: s.activation(sscal[0:1, 3:4], sscal[0:1, 3:4], AF.Exp, bias=0.0, scale=0.5),
        ])
        sc.st("vector", lambda v: v.tensor_scalar_mul(sscal[0:1, 4:5], sscal[0:1, 3:4], -SC192))
        sc.st("tensor", lambda t: t.matmul(p_col[:, 0:1], srow1[:], sscal[0:1, 4:5], start=True, stop=True))
        sc.st("scalar", lambda s: s.copy(sbias[:], p_col[:, 0:1]))

        emit_flash_pipelined(
            sc, range(8), lambda qc: 4 * (qc + 1),
            lambda qc, kb, ps: [(sK[:, kb // 4, (kb % 4) * 128:(kb % 4 + 1) * 128], sQ[:, qc, :]),
                                (skr[:, kb * 128:(kb + 1) * 128], sQr[:, qc * 512:(qc + 1) * 512])],
            lambda qc, kb: smask[:, kb % 4, :] if kb // 4 == qc else None,
            lambda qc, kb: sPt[:, kb, :], sbias[:], SC192,
            lambda qc, nkb: [(sV[:, kb, :], sPt[:, kb, :]) for kb in range(nkb)],
            lambda qc, nkb: [(sonesb[:], sPt[:, kb, :]) for kb in range(nkb)],
            lambda v: v.reciprocal(srd[:], p_d[:]),
            lambda t: t.matmul(p_bc[:], srow1[:], srd[:], start=True, stop=True),
            lambda s: s.copy(sRec[:], p_bc[:]),
            lambda v, qc: v.tensor_mul(sOut[:, qc * 512:(qc + 1) * 512], p_o[:], sRec[:]),
            (p_s, p_s2, p_o, p_d))
        sc.dma("sync", o_attn[u], sOut[:], stv)

    sc.st("sync", lambda e: None, waits=[(stv, stv.n)])
    sc.run()
    return nc


_L2_CACHE = {}


def run_launch2(inputs, ckv_full, kr_full, cq_full):
    if "nc" not in _L2_CACHE:
        _L2_CACHE["nc"] = build_launch2()
    nc = _L2_CACHE["nc"]
    wuq = np.asarray(inputs["L0_attn_w_uq"], np.float32)
    wqr = np.asarray(inputs["L0_attn_w_qr"], np.float32)
    wuk = np.asarray(inputs["L0_attn_w_uk"], np.float32)
    wuv = np.asarray(inputs["L0_attn_w_uv"], np.float32)
    in_maps = []
    for c in range(NCORES):
        s = c // 4
        hb = (c % 4) * 4
        in_maps.append({
            "ckv": ckv_full[s],
            "krf": kr_full[s],
            "cq": cq_full[s],
            "wuq": np.ascontiguousarray(wuq[:, hb * 128:(hb + 4) * 128]),
            "wqr": np.ascontiguousarray(wqr[:, hb * 64:(hb + 4) * 64]),
            "wuk": np.ascontiguousarray(wuk[:, hb * 128:(hb + 4) * 128]),
            "wuv": np.ascontiguousarray(wuv[:, hb * 128:(hb + 4) * 128]),
            "delta": np.asarray(inputs["L0_attn_delta"], np.float32).reshape(1, 32),
        })
    res = run_bass_kernel_spmd(nc, in_maps, core_ids=list(range(NCORES)))
    return res.results





def build_launch3():
    nc = bass.Bass("TRN2", target_bir_lowering=False)
    x = nc.dram_tensor("x_fm", [D, T], F32, kind="ExternalInput").ap()
    xn = nc.dram_tensor("xn_fm", [D, T], BF16, kind="ExternalInput").ap()
    att = nc.dram_tensor("att_fm", [D, T], BF16, kind="ExternalInput").ap()
    logits = nc.dram_tensor("logits", [T, 8], F32, kind="ExternalInput").ap()
    gbias = nc.dram_tensor("gbias", [1, 8], F32, kind="ExternalInput").ap()
    wout = nc.dram_tensor("wout", [D, D], F32, kind="ExternalInput").ap()
    sgw = nc.dram_tensor("sgw", [D, 5632], F32, kind="ExternalInput").ap()
    sdw = nc.dram_tensor("sdw", [2816, D], F32, kind="ExternalInput").ap()
    upw = nc.dram_tensor("upw", [8, D, 2816], F32, kind="ExternalInput").ap()
    dnw = nc.dram_tensor("dnw", [8, 1408, D], F32, kind="ExternalInput").ap()
    o_x1 = nc.dram_tensor("x1_fm", [D, T], F32, kind="ExternalOutput").ap()

    ones_col = nc.inline_tensor(np.ones((128, 1), np.float32), name="c_ones_col")
    ones_row = nc.inline_tensor(np.ones((1, 128), np.float32), name="c_ones_row")
    id8 = nc.inline_tensor(np.eye(8, dtype=np.float32), name="c_id8")
    id128 = nc.inline_tensor(np.eye(128, dtype=np.float32), name="c_id128")

    sc = Sched(nc)
    ld = sc.sem("ld")
    ldw = sc.sem("ldw")
    stv = sc.sem("stv")

    sxn = nc.alloc_sbuf_tensor("sxn", [128, 16, T], BF16)
    satt = nc.alloc_sbuf_tensor("satt", [128, 16, T], BF16)
    sacc = nc.alloc_sbuf_tensor("sacc", [128, 16, T], F32)
    sh = nc.alloc_sbuf_tensor("sh", [128, 11, T], BF16)
    swu = nc.alloc_sbuf_tensor("swu", [128, 2, 16, 128], BF16)
    swd = nc.alloc_sbuf_tensor("swd", [128, 2, 11, 128], BF16)
    swo = nc.alloc_sbuf_tensor("swo", [128, 2, 16, 128], BF16)
    sxs = nc.alloc_sbuf_tensor("sxs", [128, 2, T], F32)
    ssil = nc.alloc_sbuf_tensor("ssil", [128, 512], F32)
    st2 = nc.alloc_sbuf_tensor("st2", [128, 512], F32)
    slog = nc.alloc_sbuf_tensor("slog", [128, 8, 8], F32)
    ssig = nc.alloc_sbuf_tensor("ssig", [128, 8, 8], F32)
    swe = nc.alloc_sbuf_tensor("swe", [128, 8, 8], F32)
    sm1 = nc.alloc_sbuf_tensor("sm1", [128, 4], F32)
    sBb = nc.alloc_sbuf_tensor("sBb", [128, 8], F32)
    sgb = nc.alloc_sbuf_tensor("sgb", [1, 8], F32)
    swet = nc.alloc_sbuf_tensor("swet", [8, 8, 128], F32)
    sweRow = nc.alloc_sbuf_tensor("sweRow", [1, T], F32)
    sweb = nc.alloc_sbuf_tensor("sweb", [128, T], BF16)
    sones = nc.alloc_sbuf_tensor("sones", [128, 1], F32)
    srow1 = nc.alloc_sbuf_tensor("srow1", [1, 128], F32)
    sid8 = nc.alloc_sbuf_tensor("sid8", [8, 8], F32)
    sid128 = nc.alloc_sbuf_tensor("sid128", [128, 128], F32)

    p_a = nc.alloc_psum_tensor("p_a", [128, 512], F32)
    p_b = nc.alloc_psum_tensor("p_b", [128, 512], F32)
    p_c = nc.alloc_psum_tensor("p_c", [128, 512], F32)
    p_d = nc.alloc_psum_tensor("p_d", [128, 512], F32)
    p_t = nc.alloc_psum_tensor("p_t", [128, 128], F32)
    p_r = nc.alloc_psum_tensor("p_r", [1, 512], F32)

    sc.dma("sync", sxn[:], xn.rearrange("(c p) t -> p c t", p=128), ld)
    sc.dma("sync", satt[:], att.rearrange("(c p) t -> p c t", p=128), ld)
    sc.dma("sync", slog[:], logits.rearrange("(q p) o -> p q o", p=128), ld)
    sc.dma("sync", sgb[:], gbias, ld)
    sc.dma("sync", sones[:], ones_col.ap(), ld)
    sc.dma("sync", srow1[:], ones_row.ap(), ld)
    sc.dma("sync", sid8[:], id8.ap(), ld)
    sc.dma("sync", sid128[:], id128.ap(), ld)
    ld_all = ld.n

    for ti in range(8):
        sc.st("scalar", lambda s, ti=ti: s.activation(ssig[:, ti, :], slog[:, ti, :], AF.Sigmoid, bias=0.0, scale=1.0),
              waits=[(ld, ld_all)])
    sc.st("tensor", lambda t: t.matmul(p_t[:, 0:8], srow1[:], sgb[:], start=True, stop=True))
    sc.st("scalar", lambda s: s.copy(sBb[:], p_t[:, 0:8]))
    for ti in range(8):
        sc.ops("vector", [
            lambda v, ti=ti: v.tensor_add(swe[:, ti, :], ssig[:, ti, :], sBb[:]),
            lambda v, ti=ti: v.tensor_reduce(sm1[:, 0:1], swe[:, ti, :], AX.X, ALU.max),
            lambda v, ti=ti: v.tensor_scalar(sm1[:, 1:2], sm1[:, 0:1], 1.0, None, ALU.mult),
            lambda v, ti=ti: v.tensor_scalar(st2[:, 0:8], swe[:, ti, :], sm1[:, 0:1], None, ALU.is_ge),
            lambda v, ti=ti: v.tensor_scalar(ssil[:, 0:8], st2[:, 0:8], NEG, None, ALU.mult),
            lambda v, ti=ti: v.tensor_add(swe[:, ti, :], swe[:, ti, :], ssil[:, 0:8]),
            lambda v, ti=ti: v.tensor_reduce(sm1[:, 2:3], swe[:, ti, :], AX.X, ALU.max),
            lambda v, ti=ti: v.tensor_scalar(ssil[:, 0:8], swe[:, ti, :], sm1[:, 2:3], None, ALU.is_ge),
            lambda v, ti=ti: v.tensor_add(st2[:, 0:8], st2[:, 0:8], ssil[:, 0:8]),
            lambda v, ti=ti: v.tensor_mul(swe[:, ti, :], st2[:, 0:8], ssig[:, ti, :]),
            lambda v, ti=ti: v.tensor_reduce(sm1[:, 3:4], swe[:, ti, :], AX.X, ALU.add),
            lambda v, ti=ti: v.reciprocal(sm1[:, 3:4], sm1[:, 3:4]),
            lambda v, ti=ti: v.tensor_scalar(swe[:, ti, :], swe[:, ti, :], sm1[:, 3:4], None, ALU.mult),
        ])
    for ti in range(8):
        sc.st("tensor", lambda t, ti=ti: t.transpose(p_t[0:8, 0:128], swe[:, ti, :], sid128[:]))
        sc.st("scalar", lambda s, ti=ti: s.copy(swet[:, ti, :], p_t[0:8, 0:128]))

    for oc in range(16):
        par = oc % 2
        sc.dma("sync", sxs[:, par, :], x.rearrange("(c p) t -> p c t", p=128)[:, oc, :], ld)
        sc.dma("gpsimd", swo[:, par, :, :],
               wout.rearrange("(c p) o -> p c o", p=128)[:, :, oc * 128:(oc + 1) * 128], ldw)
        ldn, lwn = ld.n, ldw.n
        for tt in range(2):
            pp = p_a if tt == 0 else p_b
            accmm(sc, pp[:], [(swo[:, par, hc, :], satt[:, hc, tt * 512:(tt + 1) * 512]) for hc in range(16)],
                  waits=[(ld, ldn), (ldw, lwn)])
        sc.st("vector", lambda v, oc=oc, par=par: v.tensor_add(sacc[:, oc, 0:512], p_a[:], sxs[:, par, 0:512]))
        sc.st("vector", lambda v, oc=oc, par=par: v.tensor_add(sacc[:, oc, 512:1024], p_b[:], sxs[:, par, 512:1024]))


    for half in range(2):
        for j in range(11):
            jj = half * 11 + j
            par = j % 2
            sc.dma("gpsimd", swu[:, par, :, :],
                   sgw.rearrange("(c p) o -> p c o", p=128)[:, :, jj * 128:jj * 128 + 128], ldw)
            sc.dma("gpsimd", swo[:, par, :, :],
                   sgw.rearrange("(c p) o -> p c o", p=128)[:, :, 2816 + jj * 128:2816 + jj * 128 + 128], ldw)
            lwn = ldw.n
            for tt in range(2):
                pg = p_a if tt == 0 else p_b
                pu = p_c if tt == 0 else p_d
                accmm(sc, pu[:], [(swu[:, par, c, :], sxn[:, c, tt * 512:(tt + 1) * 512]) for c in range(16)],
                      waits=[(ldw, lwn)])
                accmm(sc, pg[:], [(swo[:, par, c, :], sxn[:, c, tt * 512:(tt + 1) * 512]) for c in range(16)],
                      waits=[(ldw, lwn)])
                sc.st("scalar", lambda s, pg=pg: s.activation(ssil[:], pg[:], AF.Silu, bias=0.0, scale=1.0))
                sc.st("vector", lambda v, j=j, tt=tt, pu=pu: v.tensor_mul(sh[:, j, tt * 512:(tt + 1) * 512], ssil[:], pu[:]))
        for oc in range(16):
            par = oc % 2
            sc.dma("gpsimd", swd[:, par, :, :],
                   sdw.rearrange("(c p) o -> p c o", p=128)[:, half * 11:(half + 1) * 11, oc * 128:(oc + 1) * 128], ldw)
            lwn = ldw.n
            for tt in range(2):
                pp = p_a if tt == 0 else p_b
                accmm(sc, pp[:], [(swd[:, par, j, :], sh[:, j, tt * 512:(tt + 1) * 512]) for j in range(11)],
                      waits=[(ldw, lwn)])
            sc.st("vector", lambda v, oc=oc: v.tensor_add(sacc[:, oc, 0:512], sacc[:, oc, 0:512], p_a[:]))
            sc.st("vector", lambda v, oc=oc: v.tensor_add(sacc[:, oc, 512:1024], sacc[:, oc, 512:1024], p_b[:]))

    for e in range(8):
        for ti in range(8):
            def row_e(t, e=e, ti=ti):
                return t.matmul(p_r[0:1, 0:128], sid8[:, e:e + 1], swet[:, ti, :], start=True, stop=True)
            sc.st("tensor", row_e)
            sc.st("scalar", lambda s, ti=ti: s.copy(sweRow[:, ti * 128:(ti + 1) * 128], p_r[0:1, 0:128]))
        for tt in range(2):
            sc.st("tensor", lambda t, tt=tt: t.matmul(p_a[:] if tt == 0 else p_b[:], srow1[:], sweRow[:, tt * 512:(tt + 1) * 512], start=True, stop=True))
        sc.st("scalar", lambda s: s.copy(sweb[:, 0:512], p_a[:]))
        sc.st("scalar", lambda s: s.copy(sweb[:, 512:1024], p_b[:]))
        for j in range(11):
            par = j % 2
            sc.dma("gpsimd", swu[:, par, :, :],
                   upw[e].rearrange("(c p) o -> p c o", p=128)[:, :, j * 128:j * 128 + 128], ldw)
            sc.dma("gpsimd", swo[:, par, :, :],
                   upw[e].rearrange("(c p) o -> p c o", p=128)[:, :, 1408 + j * 128:1408 + j * 128 + 128], ldw)
            lwn = ldw.n
            for tt in range(2):
                pg = p_a if tt == 0 else p_b
                pu = p_c if tt == 0 else p_d
                accmm(sc, pg[:], [(swu[:, par, c, :], sxn[:, c, tt * 512:(tt + 1) * 512]) for c in range(16)],
                      waits=[(ldw, lwn)])
                accmm(sc, pu[:], [(swo[:, par, c, :], sxn[:, c, tt * 512:(tt + 1) * 512]) for c in range(16)],
                      waits=[(ldw, lwn)])
                sc.st("scalar", lambda s, pg=pg: s.activation(ssil[:], pg[:], AF.Silu, bias=0.0, scale=1.0))
                sc.st("vector", lambda v, pu=pu: v.tensor_mul(st2[:], ssil[:], pu[:]))
                sc.st("vector", lambda v, j=j, tt=tt: v.tensor_mul(sh[:, j, tt * 512:(tt + 1) * 512], st2[:], sweb[:, tt * 512:(tt + 1) * 512]))
        for oc in range(16):
            par = oc % 2
            sc.dma("gpsimd", swd[:, par, :, :],
                   dnw[e].rearrange("(c p) o -> p c o", p=128)[:, :, oc * 128:(oc + 1) * 128], ldw)
            lwn = ldw.n
            for tt in range(2):
                pp = p_a if tt == 0 else p_b
                accmm(sc, pp[:], [(swd[:, par, j, :], sh[:, j, tt * 512:(tt + 1) * 512]) for j in range(11)],
                      waits=[(ldw, lwn)])
            sc.st("vector", lambda v, oc=oc: v.tensor_add(sacc[:, oc, 0:512], sacc[:, oc, 0:512], p_a[:]))
            sc.st("vector", lambda v, oc=oc: v.tensor_add(sacc[:, oc, 512:1024], sacc[:, oc, 512:1024], p_b[:]))

    sc.dma("sync", o_x1.rearrange("(c p) t -> p c t", p=128), sacc[:], stv)
    sc.st("sync", lambda e: None, waits=[(stv, stv.n)])
    sc.run()
    return nc


_L3_CACHE = {}


def run_launch3(inputs, xs_fm, xn_l, att_l, log_l):
    if "nc" not in _L3_CACHE:
        _L3_CACHE["nc"] = build_launch3()
    nc = _L3_CACHE["nc"]
    sgw_t = np.ascontiguousarray(np.asarray(inputs["L0_ffn_shared_gw"], np.float32).T)
    sdw_t = np.ascontiguousarray(np.asarray(inputs["L0_ffn_shared_dw"], np.float32).T)
    upw_t = np.ascontiguousarray(np.asarray(inputs["L0_ffn_up_w"], np.float32).transpose(0, 2, 1))
    dnw_t = np.ascontiguousarray(np.asarray(inputs["L0_ffn_down_w"], np.float32).transpose(0, 2, 1))
    wout = np.ascontiguousarray(np.asarray(inputs["L0_attn_w_out"], np.float32))
    gb = np.asarray(inputs["L0_ffn_gate_bias"], np.float32).reshape(1, 8)
    in_maps = []
    for c in range(NCORES):
        in_maps.append({
            "x_fm": xs_fm[c], "xn_fm": xn_l[c], "att_fm": att_l[c],
            "logits": log_l[c], "gbias": gb, "wout": wout,
            "sgw": sgw_t, "sdw": sdw_t, "upw": upw_t, "dnw": dnw_t,
        })
    res = run_bass_kernel_spmd(nc, in_maps, core_ids=list(range(NCORES)))
    return res.results





def build_launch4():
    nc = bass.Bass("TRN2", target_bir_lowering=False)
    x1 = nc.dram_tensor("x1_fm", [D, T], F32, kind="ExternalInput").ap()
    wq = nc.dram_tensor("wq", [D, D], F32, kind="ExternalInput").ap()
    wk = nc.dram_tensor("wk", [D, 512], F32, kind="ExternalInput").ap()
    wv = nc.dram_tensor("wv", [D, 512], F32, kind="ExternalInput").ap()
    cos1s = nc.dram_tensor("cos1s", [128, T], F32, kind="ExternalInput").ap()
    sin1s = nc.dram_tensor("sin1s", [128, T], F32, kind="ExternalInput").ap()
    o_xn = nc.dram_tensor("xn1_fm", [D, T], BF16, kind="ExternalOutput").ap()
    o_q = nc.dram_tensor("q_fm", [D, T], BF16, kind="ExternalOutput").ap()
    o_k = nc.dram_tensor("k_fm", [512, T], BF16, kind="ExternalOutput").ap()
    o_v = nc.dram_tensor("v_tm", [T, 512], BF16, kind="ExternalOutput").ap()

    ones_col = nc.inline_tensor(np.ones((128, 1), np.float32), name="c_ones_col")
    ones_row = nc.inline_tensor(np.ones((1, 128), np.float32), name="c_ones_row")
    eps_c = nc.inline_tensor(np.full((1, 1), LN_EPS, np.float32), name="c_eps")

    sc = Sched(nc)
    ld = sc.sem("ld")
    ldw = sc.sem("ldw")
    ldq = sc.sem("ldq")
    stv = sc.sem("stv")

    sx1 = nc.alloc_sbuf_tensor("sx1", [128, 16, 512], F32)
    sxn = nc.alloc_sbuf_tensor("sxn", [128, 16, 512], BF16)
    swk = nc.alloc_sbuf_tensor("swk", [128, 16, 512], BF16)
    swv = nc.alloc_sbuf_tensor("swv", [128, 16, 512], BF16)
    swq = nc.alloc_sbuf_tensor("swq", [128, 2, 16, 128], BF16)
    ssq = nc.alloc_sbuf_tensor("ssq", [128, 2, 512], F32)
    srow_stats = nc.alloc_sbuf_tensor("srow_stats", [1, 2, 512], F32)
    srow_ab = nc.alloc_sbuf_tensor("srow_ab", [1, 2, 512], F32)
    sA = nc.alloc_sbuf_tensor("sA", [128, 512], F32)
    sB = nc.alloc_sbuf_tensor("sB", [128, 512], F32)
    sones = nc.alloc_sbuf_tensor("sones", [128, 1], F32)
    srow1 = nc.alloc_sbuf_tensor("srow1", [1, 128], F32)
    seps = nc.alloc_sbuf_tensor("seps", [1, 1], F32)
    scos = nc.alloc_sbuf_tensor("scos", [128, T], F32)
    ssin = nc.alloc_sbuf_tensor("ssin", [128, T], F32)
    srp = nc.alloc_sbuf_tensor("srp", [128, 512], BF16)
    sdn = nc.alloc_sbuf_tensor("sdn", [64, 512], BF16)
    sup = nc.alloc_sbuf_tensor("sup", [128, 512], BF16)
    st1 = nc.alloc_sbuf_tensor("st1", [128, 512], F32)
    st2b = nc.alloc_sbuf_tensor("st2b", [128, 512], F32)
    sprj = nc.alloc_sbuf_tensor("sprj", [128, 512], BF16)

    p_a = nc.alloc_psum_tensor("p_a", [128, 512], F32)
    p_b = nc.alloc_psum_tensor("p_b", [128, 512], F32)
    p_s1 = nc.alloc_psum_tensor("p_s1", [1, 512], F32)
    p_s2 = nc.alloc_psum_tensor("p_s2", [1, 512], F32)
    p_bc = nc.alloc_psum_tensor("p_bc", [128, 512], F32)
    p_bc2 = nc.alloc_psum_tensor("p_bc2", [128, 512], F32)

    sc.dma("sync", sones[:], ones_col.ap(), ld)
    sc.dma("sync", srow1[:], ones_row.ap(), ld)
    sc.dma("sync", seps[:], eps_c.ap(), ld)
    sc.dma("sync", scos[:], cos1s, ld)
    sc.dma("sync", ssin[:], sin1s, ld)
    ld_all = ld.n
    sc.dma("gpsimd", swk[:], wk.rearrange("(c p) o -> p c o", p=128), ldw)
    sc.dma("gpsimd", swv[:], wv.rearrange("(c p) o -> p c o", p=128), ldw)
    ldw_all = ldw.n

    def rope_block(src128, t0, dst_store):
        c0, c1 = scos[0:64, t0:t0 + 512], scos[64:128, t0:t0 + 512]
        s0, s1 = ssin[0:64, t0:t0 + 512], ssin[64:128, t0:t0 + 512]
        sc.ops("vector", [
            lambda v: v.tensor_copy(sdn[:], src128[64:128, :]),
            lambda v: v.tensor_copy(sup[64:128, :], src128[0:64, :]),
            lambda v: v.tensor_mul(st1[0:64, :], src128[0:64, :], c0),
            lambda v: v.tensor_mul(st2b[0:64, :], sdn[:], s0),
            lambda v: v.tensor_sub(srp[0:64, :], st1[0:64, :], st2b[0:64, :]),
            lambda v: v.tensor_mul(st1[64:128, :], src128[64:128, :], c1),
            lambda v: v.tensor_mul(st2b[64:128, :], sup[64:128, :], s1),
            lambda v: v.tensor_add(srp[64:128, :], st1[64:128, :], st2b[64:128, :]),
        ])
        dst_store()

    for th in range(2):
        t0 = th * 512
        sc.dma("sync", sx1[:], x1.rearrange("(c p) t -> p c t", p=128)[:, :, t0:t0 + 512], ld,
               waits=[(stv, stv.n)])
        ldx = ld.n
        accmm(sc, p_s1[:], [(sones[:], sx1[:, c, :]) for c in range(16)], waits=[(ld, ldx)])
        for c in range(16):
            sc.st("vector", lambda v, c=c: v.tensor_mul(ssq[:, c % 2, :], sx1[:, c, :], sx1[:, c, :]),
                  waits=[(ld, ldx)])
            sc.st("tensor", lambda t, c=c: t.matmul(p_s2[:], sones[:], ssq[:, c % 2, :],
                                                    start=(c == 0), stop=(c == 15)))
        sc.ops("vector", [
            lambda v: v.tensor_scalar_mul(srow_stats[:, 0, :], p_s1[:], 1.0 / 2048.0),
            lambda v: v.tensor_scalar_mul(srow_stats[:, 1, :], p_s2[:], 1.0 / 2048.0),
            lambda v: v.tensor_mul(srow_ab[:, 0, :], srow_stats[:, 0, :], srow_stats[:, 0, :]),
            lambda v: v.tensor_sub(srow_ab[:, 1, :], srow_stats[:, 1, :], srow_ab[:, 0, :]),
        ])
        sc.ops("scalar", [
            lambda s: s.activation(srow_ab[:, 0, :], srow_ab[:, 1, :], AF.Ln, bias=seps[:], scale=1.0),
            lambda s: s.activation(srow_ab[:, 0, :], srow_ab[:, 0, :], AF.Exp, bias=0.0, scale=-0.5),
        ], waits=[ld])
        sc.ops("vector", [
            lambda v: v.tensor_mul(srow_ab[:, 1, :], srow_stats[:, 0, :], srow_ab[:, 0, :]),
            lambda v: v.tensor_scalar_mul(srow_ab[:, 1, :], srow_ab[:, 1, :], -1.0),
        ])
        def bc_ab(t):
            t.matmul(p_bc[:], srow1[:], srow_ab[:, 0, :], start=True, stop=True)
            return t.matmul(p_bc2[:], srow1[:], srow_ab[:, 1, :], start=True, stop=True)
        sc.st("tensor", bc_ab)
        def evac_ab(s):
            s.copy(sA[:], p_bc[:])
            return s.copy(sB[:], p_bc2[:])
        sc.st("scalar", evac_ab)
        xn_ops = []
        for c in range(16):
            xn_ops.append(lambda v, c=c: v.tensor_mul(ssq[:, c % 2, :], sx1[:, c, :], sA[:]))
            xn_ops.append(lambda v, c=c: v.tensor_add(sxn[:, c, :], ssq[:, c % 2, :], sB[:]))
        sc.ops("vector", xn_ops, waits=[(stv, stv.n)])
        sc.dma("sync", o_xn.rearrange("(c p) t -> p c t", p=128)[:, :, t0:t0 + 512], sxn[:], stv)

        for ti in range(4):
            accmm(sc, p_a[:], [(sxn[:, c, ti * 128:(ti + 1) * 128], swv[:, c, :]) for c in range(16)],
                  waits=[(ldw, ldw_all)])
            sc.st("scalar", lambda s: s.copy(sprj[:], p_a[:]), waits=[(stv, stv.n)])
            sc.dma("sync", o_v[t0 + ti * 128:t0 + (ti + 1) * 128, :], sprj[:], stv)

        for g in range(4):
            accmm(sc, p_a[:], [(swk[:, c, g * 128:(g + 1) * 128], sxn[:, c, :]) for c in range(16)],
                  waits=[(ldw, ldw_all)])
            sc.st("scalar", lambda s: s.copy(sprj[:], p_a[:]), waits=[(stv, stv.n)])
            def store_k(g=g, t0=t0):
                sc.dma("sync", o_k.rearrange("(g p) t -> p g t", p=128)[:, g, t0:t0 + 512], srp[:], stv)
            rope_block(sprj[:], t0, store_k)

        for h in range(16):
            par = h % 2
            sc.dma("gpsimd", swq[:, par, :, :],
                   wq.rearrange("(c p) o -> p c o", p=128)[:, :, h * 128:(h + 1) * 128], ldq)
            lqn = ldq.n
            accmm(sc, p_a[:], [(swq[:, par, c, :], sxn[:, c, :]) for c in range(16)],
                  waits=[(ldq, lqn)])
            sc.st("scalar", lambda s: s.copy(sprj[:], p_a[:]), waits=[(stv, stv.n)])
            def store_q(h=h, t0=t0):
                sc.dma("sync", o_q.rearrange("(h p) t -> p h t", p=128)[:, h, t0:t0 + 512], srp[:], stv)
            rope_block(sprj[:], t0, store_q)

    sc.st("sync", lambda e: None, waits=[(stv, stv.n)])
    sc.run()
    return nc


_L4_CACHE = {}


def run_launch4(inputs, x1_l):
    if "nc" not in _L4_CACHE:
        _L4_CACHE["nc"] = build_launch4()
    nc = _L4_CACHE["nc"]
    cos1, sin1 = rope1_tables()
    wq = np.ascontiguousarray(np.asarray(inputs["L1_attn_Wq"], np.float32))
    wk = np.ascontiguousarray(np.asarray(inputs["L1_attn_Wk"], np.float32))
    wv = np.ascontiguousarray(np.asarray(inputs["L1_attn_Wv"], np.float32))
    in_maps = []
    for c in range(NCORES):
        pos0 = (c % 4) * T
        in_maps.append({
            "x1_fm": x1_l[c], "wq": wq, "wk": wk, "wv": wv,
            "cos1s": np.ascontiguousarray(cos1[:, pos0:pos0 + T]),
            "sin1s": np.ascontiguousarray(sin1[:, pos0:pos0 + T]),
        })
    res = run_bass_kernel_spmd(nc, in_maps, core_ids=list(range(NCORES)))
    return res.results



SC128 = 1.0 / np.sqrt(128.0)


def build_launch5():
    nc = bass.Bass("TRN2", target_bir_lowering=False)
    qp = nc.dram_tensor("qp", [512, S], BF16, kind="ExternalInput").ap()
    kg = nc.dram_tensor("kg", [128, S], BF16, kind="ExternalInput").ap()
    vg = nc.dram_tensor("vg", [S, 128], BF16, kind="ExternalInput").ap()
    hsp = nc.dram_tensor("hsp", [1, 4], F32, kind="ExternalInput").ap()
    o_attn = nc.dram_tensor("attn", [4, 128, S], BF16, kind="ExternalOutput").ap()

    masks = np.where((np.arange(4)[:, None, None] * 128 + np.arange(128)[None, :, None])
                     > np.arange(512)[None, None, :], NEG, 0.0).astype(np.float32)
    c_masks = nc.inline_tensor(masks.reshape(4 * 128, 512), name="c_masks")
    ones_col = nc.inline_tensor(np.ones((128, 1), np.float32), name="c_ones_col")
    ones_colb = nc.inline_tensor(np.ones((128, 1), np.float32).astype(BF16NP), name="c_ones_colb")
    ones_row = nc.inline_tensor(np.ones((1, 128), np.float32), name="c_ones_row")

    sc = Sched(nc)
    ld = sc.sem("ld")
    ldq = sc.sem("ldq")
    stv = sc.sem("stv")

    sKg = nc.alloc_sbuf_tensor("sKg", [128, S], BF16)
    sV = nc.alloc_sbuf_tensor("sV", [128, 32, 128], BF16)
    sQ = nc.alloc_sbuf_tensor("sQ", [128, S], BF16)
    sPt = nc.alloc_sbuf_tensor("sPt", [128, 32, 512], BF16)
    sOut = nc.alloc_sbuf_tensor("sOut", [128, S], BF16)
    smask = nc.alloc_sbuf_tensor("smask", [128, 4, 512], F32)
    ssq = nc.alloc_sbuf_tensor("ssq", [128, 512], F32)
    stiny = nc.alloc_sbuf_tensor("stiny", [1, 2, 8], F32)
    sscal = nc.alloc_sbuf_tensor("sscal", [1, 8], F32)
    sbias = nc.alloc_sbuf_tensor("sbias", [128, 1], F32)
    sRec = nc.alloc_sbuf_tensor("sRec", [128, 512], F32)
    srd = nc.alloc_sbuf_tensor("srd", [1, 512], F32)
    sones = nc.alloc_sbuf_tensor("sones", [128, 1], F32)
    sonesb = nc.alloc_sbuf_tensor("sonesb", [128, 1], BF16)
    srow1 = nc.alloc_sbuf_tensor("srow1", [1, 128], F32)
    shs = nc.alloc_sbuf_tensor("shs", [1, 4], F32)

    p_s = nc.alloc_psum_tensor("p_s", [128, 512], F32)
    p_s2 = nc.alloc_psum_tensor("p_s2", [128, 512], F32)
    p_o = nc.alloc_psum_tensor("p_o", [128, 512], F32)
    p_d = nc.alloc_psum_tensor("p_d", [1, 512], F32)
    p_bc = nc.alloc_psum_tensor("p_bc", [128, 512], F32)
    p_row = nc.alloc_psum_tensor("p_row", [1, 512], F32)
    p_col = nc.alloc_psum_tensor("p_col", [128, 2], F32)

    sc.dma("sync", sKg[:], kg, ld)
    sc.dma("sync", sV[:], vg.rearrange("(b p) d -> p b d", p=128), ld)
    sc.dma("sync", smask[:], c_masks.ap().rearrange("(m p) t -> p m t", p=128), ld)
    sc.dma("sync", sones[:], ones_col.ap(), ld)
    sc.dma("sync", sonesb[:], ones_colb.ap(), ld)
    sc.dma("sync", srow1[:], ones_row.ap(), ld)
    sc.dma("sync", shs[:], hsp, ld)
    ld_all = ld.n

    for kb8 in range(8):
        sc.st("vector", lambda v, kb8=kb8: v.tensor_mul(ssq[:], sKg[:, kb8 * 512:(kb8 + 1) * 512], sKg[:, kb8 * 512:(kb8 + 1) * 512]),
              waits=[(ld, ld_all)])
        accmm(sc, p_row[:], [(sones[:], ssq[:])])
        sc.st("vector", lambda v, kb8=kb8: v.tensor_reduce(stiny[0:1, 0, kb8:kb8 + 1], p_row[:], AX.X, ALU.max))

    for u in range(4):
        sc.dma("sync", sQ[:], qp[u * 128:(u + 1) * 128, :], ldq, waits=[(stv, stv.n)])
        lqn = ldq.n
        for qb in range(8):
            sc.st("vector", lambda v, qb=qb: v.tensor_mul(ssq[:], sQ[:, qb * 512:(qb + 1) * 512], sQ[:, qb * 512:(qb + 1) * 512]),
                  waits=[(ldq, lqn)])
            accmm(sc, p_row[:], [(sones[:], ssq[:])])
            sc.st("vector", lambda v, qb=qb: v.tensor_reduce(stiny[0:1, 1, qb:qb + 1], p_row[:], AX.X, ALU.max))
        sc.ops("vector", [
            lambda v: v.tensor_reduce(sscal[0:1, 0:1], stiny[0:1, 0, :], AX.X, ALU.max),
            lambda v: v.tensor_reduce(sscal[0:1, 1:2], stiny[0:1, 1, :], AX.X, ALU.max),
            lambda v: v.tensor_mul(sscal[0:1, 2:3], sscal[0:1, 0:1], sscal[0:1, 1:2]),
        ])
        sc.ops("scalar", [
            lambda s: s.activation(sscal[0:1, 3:4], sscal[0:1, 2:3], AF.Ln, bias=0.0, scale=1.0),
            lambda s: s.activation(sscal[0:1, 3:4], sscal[0:1, 3:4], AF.Exp, bias=0.0, scale=0.5),
        ])
        sc.st("vector", lambda v: v.tensor_scalar_mul(sscal[0:1, 4:5], sscal[0:1, 3:4], -SC128))
        sc.st("tensor", lambda t: t.matmul(p_col[:, 0:1], srow1[:], sscal[0:1, 4:5], start=True, stop=True))
        sc.st("scalar", lambda s: s.copy(sbias[:], p_col[:, 0:1]))

        def recip_hs(v, u=u):
            v.reciprocal(srd[:], p_d[:])
            return v.tensor_scalar(srd[:], srd[:], shs[0:1, u:u + 1], None, ALU.mult)
        emit_flash_pipelined(
            sc, range(8), lambda qc: 4 * (qc + 1),
            lambda qc, kb, ps: [(sKg[:, kb * 128:(kb + 1) * 128], sQ[:, qc * 512:(qc + 1) * 512])],
            lambda qc, kb: smask[:, kb % 4, :] if kb // 4 == qc else None,
            lambda qc, kb: sPt[:, kb, :], sbias[:], SC128,
            lambda qc, nkb: [(sV[:, kb, :], sPt[:, kb, :]) for kb in range(nkb)],
            lambda qc, nkb: [(sonesb[:], sPt[:, kb, :]) for kb in range(nkb)],
            recip_hs,
            lambda t: t.matmul(p_bc[:], srow1[:], srd[:], start=True, stop=True),
            lambda s: s.copy(sRec[:], p_bc[:]),
            lambda v, qc: v.tensor_mul(sOut[:, qc * 512:(qc + 1) * 512], p_o[:], sRec[:]),
            (p_s, p_s2, p_o, p_d))
        sc.dma("sync", o_attn[u], sOut[:], stv)

    sc.st("sync", lambda e: None, waits=[(stv, stv.n)])
    sc.run()
    return nc


_L5_CACHE = {}


def run_launch5(inputs, q_full, k_full, v_full):
    if "nc" not in _L5_CACHE:
        _L5_CACHE["nc"] = build_launch5()
    nc = _L5_CACHE["nc"]
    hs = np.asarray(inputs["L1_attn_head_scale"], np.float32).reshape(16)
    in_maps = []
    for c in range(NCORES):
        s = c // 4
        g = c % 4
        in_maps.append({
            "qp": np.ascontiguousarray(q_full[s][g * 512:(g + 1) * 512, :]),
            "kg": np.ascontiguousarray(k_full[s][g * 128:(g + 1) * 128, :]),
            "vg": np.ascontiguousarray(v_full[s][:, g * 128:(g + 1) * 128]),
            "hsp": np.ascontiguousarray(hs[g * 4:(g + 1) * 4].reshape(1, 4)),
        })
    res = run_bass_kernel_spmd(nc, in_maps, core_ids=list(range(NCORES)))
    return res.results





RT2048 = float(np.sqrt(2048.0))
RT8192 = float(np.sqrt(8192.0))


def _hadamard128():
    h = np.array([[1.0]], np.float32)
    for _ in range(7):
        h = np.block([[h, h], [h, -h]])
    return h


def build_launch6():
    nc = bass.Bass("TRN2", target_bir_lowering=False)
    x1 = nc.dram_tensor("x1_fm", [D, T], F32, kind="ExternalInput").ap()
    xn1 = nc.dram_tensor("xn1_fm", [D, T], BF16, kind="ExternalInput").ap()
    att = nc.dram_tensor("att_fm", [D, T], BF16, kind="ExternalInput").ap()
    wo = nc.dram_tensor("wo", [D, D], F32, kind="ExternalInput").ap()
    wg = nc.dram_tensor("wg", [D, 8192], F32, kind="ExternalInput").ap()
    wu = nc.dram_tensor("wu", [D, 8192], F32, kind="ExternalInput").ap()
    wd = nc.dram_tensor("wd", [8192, D], F32, kind="ExternalInput").ap()
    o_x2 = nc.dram_tensor("x2_fm", [D, T], F32, kind="ExternalOutput").ap()
    h2scr = nc.dram_tensor("h2scr", [64, 128, T], BF16, kind="Internal").ap()

    ones_col = nc.inline_tensor(np.ones((128, 1), np.float32), name="c_ones_col")
    ones_row = nc.inline_tensor(np.ones((1, 128), np.float32), name="c_ones_row")
    magic_col = nc.inline_tensor(np.full((128, 1), MAGIC, np.float32), name="c_magic")
    h128 = nc.inline_tensor(_hadamard128().astype(BF16NP), name="c_h128")
    id128 = nc.inline_tensor(np.eye(128, dtype=np.float32), name="c_id128")

    sc = Sched(nc)
    ld = sc.sem("ld")
    ldw = sc.sem("ldw")
    sth = sc.sem("sth")
    ldh = sc.sem("ldh")
    stv = sc.sem("stv")

    sxn = nc.alloc_sbuf_tensor("sxn", [128, 16, T], BF16)
    sU = nc.alloc_sbuf_tensor("sU", [128, 64, 512], BF16)
    sF1 = nc.alloc_sbuf_tensor("sF1", [128, 64, 128], BF16)
    sF2 = nc.alloc_sbuf_tensor("sF2", [128, 64, 128], BF16)
    swb = nc.alloc_sbuf_tensor("swb", [128, 16, 256], BF16)
    swdb = nc.alloc_sbuf_tensor("swdb", [128, 64, 128], BF16)
    sTg = nc.alloc_sbuf_tensor("sTg", [128, 16, 128], BF16)
    sTu = nc.alloc_sbuf_tensor("sTu", [128, 16, 128], BF16)
    sqz = nc.alloc_sbuf_tensor("sqz", [128, 2048], F32)
    sgt = nc.alloc_sbuf_tensor("sgt", [128, 512], F32)
    sut = nc.alloc_sbuf_tensor("sut", [128, 512], F32)
    ssl = nc.alloc_sbuf_tensor("ssl", [128, 512], F32)
    sh2 = nc.alloc_sbuf_tensor("sh2", [128, 512], BF16)
    sMxF = nc.alloc_sbuf_tensor("sMxF", [128, 512], F32)
    sAbs = nc.alloc_sbuf_tensor("sAbs", [128, 512], F32)
    sMxC = nc.alloc_sbuf_tensor("sMxC", [128, 4], F32)
    srowm = nc.alloc_sbuf_tensor("srowm", [1, 8], F32)
    srowq = nc.alloc_sbuf_tensor("srowq", [1, 2, 512], F32)
    sQs = nc.alloc_sbuf_tensor("sQs", [128, 1, 512], BF16)
    sRg = nc.alloc_sbuf_tensor("sRg", [128, 512], F32)
    sRu = nc.alloc_sbuf_tensor("sRu", [128, 512], F32)
    sRd = nc.alloc_sbuf_tensor("sRd", [128, 512], F32)
    sAcc = nc.alloc_sbuf_tensor("sAcc", [128, 3], F32)
    sInv = nc.alloc_sbuf_tensor("sInv", [128, 3], F32)
    sScl = nc.alloc_sbuf_tensor("sScl", [128, 3], F32)
    sredc = nc.alloc_sbuf_tensor("sredc", [128, 1], F32)
    sones = nc.alloc_sbuf_tensor("sones", [128, 1], F32)
    srow1 = nc.alloc_sbuf_tensor("srow1", [1, 128], F32)
    smgc = nc.alloc_sbuf_tensor("smgc", [128, 1], F32)
    sH = nc.alloc_sbuf_tensor("sH", [128, 128], BF16)
    sid = nc.alloc_sbuf_tensor("sid", [128, 128], F32)
    sx1s = nc.alloc_sbuf_tensor("sx1s", [128, T], F32)
    sot = nc.alloc_sbuf_tensor("sot", [128, T], F32)

    p_a = nc.alloc_psum_tensor("p_a", [128, 512], F32)
    p_b = nc.alloc_psum_tensor("p_b", [128, 512], F32)
    p_t = nc.alloc_psum_tensor("p_t", [128, 128], F32)
    p_r = nc.alloc_psum_tensor("p_r", [1, 512], F32)
    p_bc = nc.alloc_psum_tensor("p_bc", [128, 512], F32)

    sxq = sU[:, 32:48, :]
    sc.dma("sync", sxn[:], xn1.rearrange("(c p) t -> p c t", p=128), ld)
    sc.dma("sync", sones[:], ones_col.ap(), ld)
    sc.dma("sync", srow1[:], ones_row.ap(), ld)
    sc.dma("sync", smgc[:], magic_col.ap(), ld)
    sc.dma("sync", sH[:], h128.ap(), ld)
    sc.dma("sync", sid[:], id128.ap(), ld)
    ld_all = ld.n

    scan_view = sU[:].rearrange("p a b -> p (a b)")
    first = [True, True, True]
    def scan_tiles(m, dram_tiles, width):
        for i, ap in enumerate(dram_tiles):
            par = i % 2
            dst = scan_view[:, par * 16384:par * 16384 + width]
            sc.dma("gpsimd", dst, ap, ldw)
            lwn = ldw.n
            if first[m]:
                sc.st("vector", lambda v, m=m, dst=dst: v.tensor_reduce(
                    sAcc[:, m:m + 1], dst, AX.X, ALU.add, apply_absolute_value=True),
                    waits=[(ldw, lwn)])
                first[m] = False
            else:
                sc.st("vector", lambda v, dst=dst: v.tensor_reduce(
                    sredc[:], dst, AX.X, ALU.add, apply_absolute_value=True),
                    waits=[(ldw, lwn)])
                sc.st("vector", lambda v, m=m: v.tensor_add(sAcc[:, m:m + 1], sAcc[:, m:m + 1], sredc[:]))
    wg_r = wg.rearrange("(c p) o -> p c o", p=128)
    wu_r = wu.rearrange("(c p) o -> p c o", p=128)
    wd_r = wd.rearrange("(c p) o -> p c o", p=128)
    scan_tiles(0, [wg_r[:, c, :] for c in range(16)], 8192)
    scan_tiles(1, [wu_r[:, c, :] for c in range(16)], 8192)
    scan_tiles(2, [wd_r[:, c, :] for c in range(64)], 2048)
    accmm(sc, p_t[0:1, 0:3], [(sones[:], sAcc[:])])
    sc.st("scalar", lambda s: s.copy(srowm[0:1, 0:3], p_t[0:1, 0:3]))
    sc.ops("vector", [
        lambda v: v.tensor_scalar(srowm[0:1, 0:3], srowm[0:1, 0:3], 1.0 / 16777216.0, 1e-5, ALU.mult, ALU.max),
        lambda v: v.reciprocal(srowm[0:1, 3:6], srowm[0:1, 0:3]),
    ])
    for m in range(3):
        sc.st("tensor", lambda t, m=m: t.matmul(p_t[:, 0:1], srow1[:], srowm[0:1, 3 + m:4 + m], start=True, stop=True))
        sc.st("scalar", lambda s, m=m: s.copy(sInv[:, m:m + 1], p_t[:, 0:1]))
        sc.st("tensor", lambda t, m=m: t.matmul(p_t[:, 0:1], srow1[:], srowm[0:1, m:m + 1], start=True, stop=True))
        sc.st("scalar", lambda s, m=m: s.copy(sScl[:, m:m + 1], p_t[:, 0:1]))

    satt = sU[:].rearrange("p a b -> p (a b)")[:, 0:16384].rearrange("p (c t) -> p c t", t=T)
    sc.dma("sync", satt, att.rearrange("(c p) t -> p c t", p=128), ld)
    ld_att = ld.n
    for oc in range(16):
        sc.dma("sync", sx1s[:], x1.rearrange("(c p) t -> p c t", p=128)[:, oc, :], ld,
               waits=[(stv, stv.n)])
        sc.dma("gpsimd", swb[:, :, 0:128],
               wo.rearrange("(c p) o -> p c o", p=128)[:, :, oc * 128:(oc + 1) * 128], ldw)
        ldn, lwn = ld.n, ldw.n
        for tt in range(2):
            pp = p_a if tt == 0 else p_b
            accmm(sc, pp[:], [(swb[:, hc, 0:128], satt[:, hc, tt * 512:(tt + 1) * 512]) for hc in range(16)],
                  waits=[(ld, ldn), (ldw, lwn)])
        sc.st("vector", lambda v: v.tensor_add(sot[:, 0:512], p_a[:], sx1s[:, 0:512]))
        sc.st("vector", lambda v: v.tensor_add(sot[:, 512:1024], p_b[:], sx1s[:, 512:1024]))
        sc.dma("sync", o_x2.rearrange("(c p) t -> p c t", p=128)[:, oc, :], sot[:], stv)
    stv_phaseA = stv.n

    def butterfly_stages(src3, bufA3, bufB3, nch, fd):
        nst = int(np.log2(nch))
        cur = src3
        bufs = [bufA3, bufB3]
        for st_i in range(nst):
            h = 1 << st_i
            g = nch // (2 * h)
            out = bufs[st_i % 2]
            ina = cur.rearrange("p (g two h) f -> p g two h f", two=2, h=h)
            outa = out.rearrange("p (g two h) f -> p g two h f", two=2, h=h)
            sc.st("vector", lambda v, ina=ina, outa=outa: v.tensor_add(
                outa[:, :, 0, :, :], ina[:, :, 0, :, :], ina[:, :, 1, :, :]))
            sc.st("vector", lambda v, ina=ina, outa=outa: v.tensor_sub(
                outa[:, :, 1, :, :], ina[:, :, 0, :, :], ina[:, :, 1, :, :]))
            cur = out
        return cur

    for th in range(2):
        t0 = th * 512
        fA = sF1[:].rearrange("p a b -> p (a b)").rearrange("p (c t) -> p c t", t=512)
        fB = sF2[:].rearrange("p a b -> p (a b)").rearrange("p (c t) -> p c t", t=512)
        fin = butterfly_stages(sxn[:, :, t0:t0 + 512], fA, fB, 16, 512)
        sfv = fA
        for c in range(16):
            sc.st("tensor", lambda t, c=c, fin=fin: t.matmul(p_a[:], sH[:], fin[:, c, :], start=True, stop=True))
            sc.st("scalar", lambda s, c=c: s.activation(sAbs[:], p_a[:], AF.Abs, bias=0.0, scale=1.0))
            sc.st("scalar", lambda s, c=c, sfv=sfv: s.copy(sfv[:, c, :], p_a[:]))
            if c == 0:
                sc.st("vector", lambda v: v.tensor_copy(sMxF[:], sAbs[:]))
            else:
                sc.st("vector", lambda v: v.tensor_max(sMxF[:], sMxF[:], sAbs[:]))
        for ti in range(4):
            sc.st("tensor", lambda t, ti=ti: t.transpose(p_t[:], sMxF[:, ti * 128:(ti + 1) * 128], sid[:]))
            sc.st("vector", lambda v, ti=ti: v.tensor_reduce(sMxC[:, ti:ti + 1], p_t[:], AX.X, ALU.max))
        for ti in range(4):
            sc.st("tensor", lambda t, ti=ti: t.transpose(p_t[0:1, 0:128], sMxC[:, ti:ti + 1], sid[:]))
            sc.st("scalar", lambda s, ti=ti: s.copy(srowq[0:1, 0, ti * 128:(ti + 1) * 128], p_t[0:1, 0:128]))
        sc.ops("vector", [
            lambda v: v.tensor_scalar_max(srowq[:, 0, :], srowq[:, 0, :], 1e-5 * RT2048),
            lambda v: v.tensor_scalar_mul(srowq[:, 1, :], srowq[:, 0, :], 1.0 / (127.0 * RT2048)),
            lambda v: v.reciprocal(srowq[:, 0, :], srowq[:, 0, :]),
            lambda v: v.tensor_scalar_mul(srowq[:, 0, :], srowq[:, 0, :], 127.0),
        ])
        sc.st("tensor", lambda t: t.matmul(p_bc[:], srow1[:], srowq[0:1, 0, :], start=True, stop=True))
        sc.st("scalar", lambda s: s.copy(sQs[:, 0, :], p_bc[:]))
        sc.st("tensor", lambda t: t.matmul(p_bc[:], srow1[:], srowq[0:1, 1, :], start=True, stop=True))
        sc.st("vector", lambda v: v.tensor_scalar(sRg[:], p_bc[:], sScl[:, 0:1], None, ALU.mult))
        sc.st("vector", lambda v: v.tensor_scalar(sRu[:], p_bc[:], sScl[:, 1:2], None, ALU.mult))
        for g4 in range(4):
            cs = slice(g4 * 4, g4 * 4 + 4)
            qv = sqz[:].rearrange("p (c t) -> p c t", t=512)[:, 0:4, :]
            def qmul(v, cs=cs, qv=qv, sfv=sfv):
                a, b = bass.broadcast_tensor_aps(sfv[:, cs, :], sQs[:])
                return v.tensor_tensor(qv, a, b, ALU.mult)
            sc.st("vector", qmul)
            sc.st("scalar", lambda s, qv=qv: s.activation(qv, qv, AF.Identity, bias=smgc[:], scale=1.0))
            sc.st("vector", lambda v, qv=qv: v.tensor_scalar(qv, qv, MAGIC, 127.0, ALU.subtract, ALU.min))
            sc.st("vector", lambda v, cs=cs, qv=qv: v.tensor_scalar(sxq[:, cs, :], qv, -128.0, None, ALU.max))

        sqzv = sqz[:].rearrange("p (c o) -> p c o", o=128)
        for hc in range(64):
            o0 = hc * 128
            for (wsrc, dstT, m) in ((wg_r, sTg, 0), (wu_r, sTu, 1)):
                sc.dma("sync", sqzv, wsrc[:, :, o0:o0 + 128], ldw)
                lwn = ldw.n
                dv = dstT[:].rearrange("p c o -> p (c o)")
                sc.st("scalar", lambda s, m=m: s.activation(
                    sqz[:], sqz[:], AF.Identity, bias=smgc[:], scale=sInv[:, m:m + 1]),
                    waits=[(ldw, lwn)])
                sc.st("vector", lambda v: v.tensor_scalar(sqz[:], sqz[:], MAGIC, 1.0, ALU.subtract, ALU.min))
                sc.st("vector", lambda v, dv=dv: v.tensor_scalar(dv, sqz[:], -1.0, None, ALU.max))
            accmm(sc, p_a[:], [(sTg[:, c, 0:128], sxq[:, c, :]) for c in range(16)])
            accmm(sc, p_b[:], [(sTu[:, c, 0:128], sxq[:, c, :]) for c in range(16)])
            sc.st("vector", lambda v: v.tensor_mul(sgt[:], p_a[:], sRg[:]))
            sc.st("scalar", lambda s: s.activation(ssl[:], sgt[:], AF.Silu, bias=0.0, scale=1.0))
            sc.st("vector", lambda v: v.tensor_mul(sut[:], p_b[:], sRu[:]))
            sc.st("vector", lambda v: v.tensor_mul(sh2[:], ssl[:], sut[:]), waits=[(sth, sth.n)])
            sc.dma("sync", h2scr.rearrange("h p t -> p h t")[:, hc, t0:t0 + 512], sh2[:], sth)

        for tq in range(4):
            tq0 = t0 + tq * 128
            sc.dma("sync", sF1[:], h2scr.rearrange("h p t -> p h t")[:, :, tq0:tq0 + 128], ldh,
                   waits=[(sth, sth.n)])
            ldh_n = ldh.n
            sc.st("vector", lambda v: v.engine_nop(), waits=[(ldh, ldh_n)])
            fin8 = butterfly_stages(sF1[:], sF2[:], sF1[:], 64, 128)
            other = sF1[:] if fin8.tensor.name == "sF2" else sF2[:]
            for g4 in range(16):
                sc.st("tensor", lambda t, g4=g4, fin8=fin8: t.matmul(
                    p_a[:], sH[:], fin8[:, g4 * 4:(g4 + 1) * 4, :].rearrange("p c t -> p (c t)"),
                    start=True, stop=True))
                sc.st("scalar", lambda s: s.activation(sAbs[:], p_a[:], AF.Abs, bias=0.0, scale=1.0))
                sc.st("scalar", lambda s, g4=g4, other=other: s.copy(
                    other[:, g4 * 4:(g4 + 1) * 4, :].rearrange("p c t -> p (c t)"), p_a[:]))
                if g4 == 0:
                    sc.st("vector", lambda v: v.tensor_copy(sMxF[:], sAbs[:]))
                else:
                    sc.st("vector", lambda v: v.tensor_max(sMxF[:], sMxF[:], sAbs[:]))
            sc.ops("vector", [
                lambda v: v.tensor_max(sMxF[:, 0:128], sMxF[:, 0:128], sMxF[:, 128:256]),
                lambda v: v.tensor_max(sMxF[:, 256:384], sMxF[:, 256:384], sMxF[:, 384:512]),
                lambda v: v.tensor_max(sMxF[:, 0:128], sMxF[:, 0:128], sMxF[:, 256:384]),
            ])
            sc.st("tensor", lambda t: t.transpose(p_t[:], sMxF[:, 0:128], sid[:]))
            sc.st("vector", lambda v: v.tensor_reduce(sMxC[:, 0:1], p_t[:], AX.X, ALU.max))
            sc.st("tensor", lambda t: t.transpose(p_t[0:1, 0:128], sMxC[:, 0:1], sid[:]))
            sc.st("scalar", lambda s: s.copy(srowq[0:1, 0, 0:128], p_t[0:1, 0:128]))
            sc.ops("vector", [
                lambda v: v.tensor_scalar_max(srowq[:, 0, 0:128], srowq[:, 0, 0:128], 1e-5 * RT8192),
                lambda v: v.tensor_scalar_mul(srowq[:, 1, 0:128], srowq[:, 0, 0:128], 1.0 / (127.0 * RT8192)),
                lambda v: v.reciprocal(srowq[:, 0, 0:128], srowq[:, 0, 0:128]),
                lambda v: v.tensor_scalar_mul(srowq[:, 0, 0:128], srowq[:, 0, 0:128], 127.0),
            ])
            sc.st("tensor", lambda t: t.matmul(p_bc[:, 0:128], srow1[:], srowq[0:1, 0, 0:128], start=True, stop=True))
            sc.st("scalar", lambda s: s.copy(sQs[:, 0, 0:128], p_bc[:, 0:128]))
            sc.st("tensor", lambda t: t.matmul(p_bc[:, 0:128], srow1[:], srowq[0:1, 1, 0:128], start=True, stop=True))
            sc.st("vector", lambda v, tq=tq: v.tensor_scalar(sRd[:, tq * 128:(tq + 1) * 128], p_bc[:, 0:128], sScl[:, 2:3], None, ALU.mult))
            for g16 in range(4):
                cs = slice(g16 * 16, g16 * 16 + 16)
                qv = sqz[:].rearrange("p (c t) -> p c t", t=128)[:, 0:16, :]
                def qmul8(v, cs=cs, qv=qv, other=other):
                    a, b = bass.broadcast_tensor_aps(other[:, cs, :], sQs[:, :, 0:128])
                    return v.tensor_tensor(qv, a, b, ALU.mult)
                sc.st("vector", qmul8)
                sc.st("scalar", lambda s, qv=qv: s.activation(qv, qv, AF.Identity, bias=smgc[:], scale=1.0))
                sc.st("vector", lambda v, qv=qv: v.tensor_scalar(qv, qv, MAGIC, 127.0, ALU.subtract, ALU.min))
                sc.st("vector", lambda v, cs=cs, qv=qv, tq=tq: v.tensor_scalar(
                    sU[:, cs, tq * 128:(tq + 1) * 128], qv, -128.0, None, ALU.max))

        for oc in range(16):
            for qt in range(4):
                view = swdb[:, qt * 16:(qt + 1) * 16, :].rearrange("p c o -> p (c o)")
                sc.dma("sync", sqzv, wd_r[:, qt * 16:(qt + 1) * 16, oc * 128:(oc + 1) * 128], ldw)
                lwn = ldw.n
                sc.st("scalar", lambda s: s.activation(
                    sqz[:], sqz[:], AF.Identity, bias=smgc[:], scale=sInv[:, 2:3]),
                    waits=[(ldw, lwn)])
                sc.st("vector", lambda v: v.tensor_scalar(sqz[:], sqz[:], MAGIC, 1.0, ALU.subtract, ALU.min))
                sc.st("vector", lambda v, view=view: v.tensor_scalar(view, sqz[:], -1.0, None, ALU.max))
            accmm(sc, p_a[:], [(swdb[:, c, :], sU[:, c, :]) for c in range(64)])
            sc.st("vector", lambda v: v.tensor_mul(sot[:, 0:512], p_a[:], sRd[:]), waits=[(stv, stv.n)])
            sc.dma("gpsimd", o_x2.rearrange("(c p) t -> p c t", p=128)[:, oc, t0:t0 + 512],
                   sot[:, 0:512], stv, waits=[(stv, stv_phaseA)], accum=True)

    sc.st("sync", lambda e: None, waits=[(stv, stv.n), (sth, sth.n)])
    sc.run()
    return nc


_L6_CACHE = {}


def run_launch6(inputs, x1_l, xn1_l, att1_l):
    if "nc" not in _L6_CACHE:
        _L6_CACHE["nc"] = build_launch6()
    nc = _L6_CACHE["nc"]
    wo = np.ascontiguousarray(np.asarray(inputs["L1_attn_Wo"], np.float32))
    wg_t = np.ascontiguousarray(np.asarray(inputs["L1_ffn_w_gate"], np.float32).T)
    wu_t = np.ascontiguousarray(np.asarray(inputs["L1_ffn_w_up"], np.float32).T)
    wd_t = np.ascontiguousarray(np.asarray(inputs["L1_ffn_w_down"], np.float32).T)
    in_maps = []
    for c in range(NCORES):
        in_maps.append({
            "x1_fm": x1_l[c], "xn1_fm": xn1_l[c], "att_fm": att1_l[c],
            "wo": wo, "wg": wg_t, "wu": wu_t, "wd": wd_t,
        })
    res = run_bass_kernel_spmd(nc, in_maps, core_ids=list(range(NCORES)))
    return res.results




def kernel(**inputs):
    x = np.ascontiguousarray(np.asarray(inputs["L0_attn_x"], np.float32).reshape(B * S, D))
    xs_fm = [np.ascontiguousarray(x[c * T:(c + 1) * T].T) for c in range(NCORES)]

    r1 = run_launch1(inputs, xs_fm)
    ckv_full = [np.ascontiguousarray(np.concatenate([r1[4 * s + j]["ckv_fm"] for j in range(4)], axis=1)) for s in range(2)]
    kr_full = [np.ascontiguousarray(np.concatenate([r1[4 * s + j]["kr_fm"] for j in range(4)], axis=1)) for s in range(2)]
    cq_full = [np.ascontiguousarray(np.concatenate([r1[4 * s + j]["cq_fm"] for j in range(4)], axis=1)) for s in range(2)]

    r2 = run_launch2(inputs, ckv_full, kr_full, cq_full)
    att_seq = [np.concatenate([np.asarray(r2[4 * s + j]["attn"]).reshape(512, S) for j in range(4)], axis=0)
               for s in range(2)]
    att_l = [np.ascontiguousarray(att_seq[c // 4][:, (c % 4) * T:(c % 4 + 1) * T]) for c in range(NCORES)]
    xn_l = [r1[c]["xn_fm"] for c in range(NCORES)]
    log_l = [r1[c]["logits"] for c in range(NCORES)]

    r3 = run_launch3(inputs, xs_fm, xn_l, att_l, log_l)
    x1_l = [np.ascontiguousarray(r3[c]["x1_fm"]) for c in range(NCORES)]

    r4 = run_launch4(inputs, x1_l)
    q_full = [np.ascontiguousarray(np.concatenate([np.asarray(r4[4 * s + j]["q_fm"]) for j in range(4)], axis=1)) for s in range(2)]
    k_full = [np.ascontiguousarray(np.concatenate([np.asarray(r4[4 * s + j]["k_fm"]) for j in range(4)], axis=1)) for s in range(2)]
    v_full = [np.ascontiguousarray(np.concatenate([np.asarray(r4[4 * s + j]["v_tm"]) for j in range(4)], axis=0)) for s in range(2)]
    xn1_l = [r4[c]["xn1_fm"] for c in range(NCORES)]

    r5 = run_launch5(inputs, q_full, k_full, v_full)
    att1_seq = [np.concatenate([np.asarray(r5[4 * s + j]["attn"]).reshape(512, S) for j in range(4)], axis=0)
                for s in range(2)]
    att1_l = [np.ascontiguousarray(att1_seq[c // 4][:, (c % 4) * T:(c % 4 + 1) * T]) for c in range(NCORES)]

    r6 = run_launch6(inputs, x1_l, xn1_l, att1_l)
    x2 = np.concatenate([np.asarray(r6[c]["x2_fm"], np.float32).T for c in range(NCORES)], axis=0)
    return np.ascontiguousarray(x2.reshape(B, S, D))


# version 06 (final)
# speedup vs baseline: 1.2427x; 1.1695x over previous
import numpy as np
import ml_dtypes
import concourse.bass as bass
import concourse.mybir as mybir
from concourse.bass_utils import run_bass_kernel_spmd

F32 = mybir.dt.float32
BF16 = mybir.dt.bfloat16
AF = mybir.ActivationFunctionType
ALU = mybir.AluOpType
AX = mybir.AxisListType
BF16NP = ml_dtypes.bfloat16

B, S, D = 2, 4096, 2048
NCORES = 8
T = 1024
NEG = -1.0e30
MAGIC = 12582912.0
LN_EPS = 1e-5


class Sem:
    __slots__ = ("name", "n", "h")

    def __init__(self, name):
        self.name, self.n, self.h = name, 0, None


class Sched:
    def __init__(self, nc):
        self.nc = nc
        self.steps = []
        self.sems = {}
        self.cmp = self.sem("cmp")
        self.pe = self.sem("pe")

    def sem(self, name):
        s = self.sems.get(name)
        if s is None:
            s = Sem(name)
            self.sems[name] = s
        return s

    FORCE_AUTO = False
    AUTO_SECTS = {"scan"}

    def st(self, eng, fn, waits=(), incs=(), auto=True, engine_inc=True):
        if Sched.FORCE_AUTO or getattr(self, "cur_sect", None) in Sched.AUTO_SECTS:
            auto = True
        ws = []
        if auto:
            if self.cmp.n:
                ws.append((self.cmp, self.cmp.n))
            if self.pe.n:
                ws.append((self.pe, self.pe.n))
        for w in waits:
            ws.append((w, w.n) if isinstance(w, Sem) else w)
        cs = [(c, 1) if isinstance(c, Sem) else c for c in incs]
        if engine_inc:
            cs.append((self.pe, 1) if eng == "tensor" else (self.cmp, 1))
        for (sm, n) in cs:
            sm.n += n
        self.steps.append((eng, fn, ws, cs))

    def ops(self, eng, fns, waits=()):
        for i, fn in enumerate(fns):
            self.st(eng, fn, waits=waits if i == 0 else ())

    def dma(self, eng, out_ap, in_ap, sem, waits=(), accum=False):
        def fn(e, out_ap=out_ap, in_ap=in_ap, accum=accum):
            if accum:
                return e.dma_start(out=out_ap, in_=in_ap, accum_op=mybir.AluOpType.add)
            return e.dma_start(out=out_ap, in_=in_ap)
        self.st(eng, fn, waits=waits, incs=[(sem, 16)], engine_inc=False)

    def run(self):
        nc = self.nc
        for s in self.sems.values():
            s.h = nc.alloc_semaphore(s.name)
        with nc.Block() as block:
            for engname in ("sync", "gpsimd", "tensor", "vector", "scalar"):
                mine = [s for s in self.steps if s[0] == engname]
                if not mine:
                    continue

                def go(eng, mine=mine):
                    for _, fn, ws, cs in mine:
                        for (sm, v) in ws:
                            if v > 0:
                                eng.wait_ge(sm.h, v)
                        last = fn(eng)
                        for (sm, n) in cs:
                            tgt = last if last is not None else eng.nop()
                            tgt.then_inc(sm.h, n)
                            last = None
                getattr(block, engname)(go)


def accmm(sc, psum_ap, pairs, waits=()):
    def fn(t):
        last = None
        n = len(pairs)
        for i, (l, r) in enumerate(pairs):
            last = t.matmul(psum_ap, l, r, start=(i == 0), stop=(i == n - 1))
        return last
    sc.st("tensor", fn, waits=waits)




def emit_flash_pipelined(sc, qc_range, nkb_of, mm_pairs, diag_info, exp_out, exp_bias,
                         exp_scale, pv_pairs, den_pairs, recip_row, bcast_row_mm,
                         evac_rec, final_mul, banks):
    cmp, pe = sc.cmp, sc.pe
    exp_mark = {0: cmp.n, 1: cmp.n}
    entry = (cmp.n, pe.n)
    first = True
    for qc in qc_range:
        nkb = nkb_of(qc)
        for kb in range(nkb):
            bi = kb % 2
            ps = banks[bi]
            ws = [(cmp, exp_mark[bi])]
            if first:
                ws = [(cmp, entry[0]), (pe, entry[1])]
                first = False
            def fn(t, qc=qc, kb=kb, ps=ps):
                last = None
                pairs = mm_pairs(qc, kb, ps)
                n = len(pairs)
                for i, (l, r) in enumerate(pairs):
                    last = t.matmul(ps[:], l, r, start=(i == 0), stop=(i == n - 1))
                return last
            sc.st("tensor", fn, waits=ws, auto=False)
            mm_mark = pe.n
            d = diag_info(qc, kb)
            if d is not None:
                sc.st("vector", lambda v, ps=ps, d=d: v.tensor_add(ps[:], ps[:], d),
                      waits=[(pe, mm_mark)], auto=False)
            sc.st("scalar", lambda s, ps=ps, qc=qc, kb=kb: s.activation(
                exp_out(qc, kb), ps[:], AF.Exp, bias=exp_bias, scale=exp_scale),
                waits=[(pe, mm_mark)] + ([(cmp, cmp.n)] if d is not None else []),
                auto=False)
            exp_mark[bi] = cmp.n
        after_exps = cmp.n
        def pv(t, qc=qc, nkb=nkb):
            last = None
            pairs = pv_pairs(qc, nkb)
            n = len(pairs)
            for i, (l, r) in enumerate(pairs):
                last = t.matmul(banks[2][:], l, r, start=(i == 0), stop=(i == n - 1))
            return last
        sc.st("tensor", pv, waits=[(cmp, after_exps)], auto=False)
        def den(t, qc=qc, nkb=nkb):
            last = None
            pairs = den_pairs(qc, nkb)
            n = len(pairs)
            for i, (l, r) in enumerate(pairs):
                last = t.matmul(banks[3][:], l, r, start=(i == 0), stop=(i == n - 1))
            return last
        sc.st("tensor", den, waits=[], auto=False)
        den_mark = pe.n
        sc.st("vector", lambda v: recip_row(v), waits=[(pe, den_mark)], auto=False)
        rec_mark = cmp.n
        sc.st("tensor", lambda t: bcast_row_mm(t), waits=[(cmp, rec_mark)], auto=False)
        bc_mark = pe.n
        sc.st("scalar", lambda s: evac_rec(s), waits=[(pe, bc_mark)], auto=False)
        ev_mark = cmp.n
        sc.st("vector", lambda v, qc=qc: final_mul(v, qc), waits=[(cmp, ev_mark)], auto=False)



def pope_tables():
    half = 32
    theta = 1.0 / (10000.0 ** (2.0 * np.arange(half, dtype=np.float64) / 64.0))
    ang = np.arange(S, dtype=np.float64)[:, None] * theta[None, :]
    return (np.cos(ang).astype(np.float32).T.copy(),
            np.sin(ang).astype(np.float32).T.copy())


def rope1_tables():
    inv = 1.0 / (10000.0 ** (np.arange(0, 128, 2, dtype=np.float64) / 128.0))
    fr = np.arange(S, dtype=np.float64)[:, None] * inv[None, :]
    emb = np.concatenate([fr, fr], axis=1)
    return (np.cos(emb).astype(np.float32).T.copy(),
            np.sin(emb).astype(np.float32).T.copy())



def build_launch1():
    nc = bass.Bass("TRN2", target_bir_lowering=False)
    x = nc.dram_tensor("x_fm", [D, T], F32, kind="ExternalInput").ap()
    w_dkv = nc.dram_tensor("w_dkv", [D, 512], F32, kind="ExternalInput").ap()
    w_dq = nc.dram_tensor("w_dq", [D, 1024], F32, kind="ExternalInput").ap()
    w_kr = nc.dram_tensor("w_kr", [D, 64], F32, kind="ExternalInput").ap()
    gw_t = nc.dram_tensor("gw_t", [D, 8], F32, kind="ExternalInput").ap()
    delta = nc.dram_tensor("delta", [1, 32], F32, kind="ExternalInput").ap()
    cos0s = nc.dram_tensor("cos0s", [32, T], F32, kind="ExternalInput").ap()
    sin0s = nc.dram_tensor("sin0s", [32, T], F32, kind="ExternalInput").ap()

    o_xn = nc.dram_tensor("xn_fm", [D, T], BF16, kind="ExternalOutput").ap()
    o_ckv = nc.dram_tensor("ckv_fm", [512, T], BF16, kind="ExternalOutput").ap()
    o_cq = nc.dram_tensor("cq_fm", [1024, T], BF16, kind="ExternalOutput").ap()
    o_kr = nc.dram_tensor("kr_fm", [64, T], BF16, kind="ExternalOutput").ap()
    o_log = nc.dram_tensor("logits", [T, 8], F32, kind="ExternalOutput").ap()

    sc = Sched(nc)
    ld = sc.sem("ld")
    ldw = sc.sem("ldw")
    stv = sc.sem("stv")

    sx = nc.alloc_sbuf_tensor("sx", [128, 16, T], F32)
    sxn = nc.alloc_sbuf_tensor("sxn", [128, 16, T], BF16)
    swdkv = nc.alloc_sbuf_tensor("swdkv", [128, 16, 512], BF16)
    swdq = nc.alloc_sbuf_tensor("swdq", [128, 16, 512], BF16)
    swkr = nc.alloc_sbuf_tensor("swkr", [128, 16, 64], BF16)
    sgw = nc.alloc_sbuf_tensor("sgw", [128, 16, 8], F32)
    ssq = nc.alloc_sbuf_tensor("ssq", [128, 2, 512], F32)
    sones = nc.alloc_sbuf_tensor("sones", [128, 1], F32)
    srow1 = nc.alloc_sbuf_tensor("srow1", [1, 128], F32)
    srow_stats = nc.alloc_sbuf_tensor("srow_stats", [1, 2, 512], F32)
    srow_ab = nc.alloc_sbuf_tensor("srow_ab", [1, 2, 512], F32)
    sA = nc.alloc_sbuf_tensor("sA", [128, 512], F32)
    sB = nc.alloc_sbuf_tensor("sB", [128, 512], F32)
    s_mr_col = nc.alloc_sbuf_tensor("s_mr_col", [128, 2], F32)
    sSrow = nc.alloc_sbuf_tensor("sSrow", [1, 8], F32)
    sSb = nc.alloc_sbuf_tensor("sSb", [128, 8], F32)
    slog = nc.alloc_sbuf_tensor("slog", [128, 8, 8], F32)
    sckv = nc.alloc_sbuf_tensor("sckv", [128, 4, T], BF16)
    scq = nc.alloc_sbuf_tensor("scq", [128, 8, T], BF16)
    skr_raw = nc.alloc_sbuf_tensor("skr_raw", [64, T], F32)
    skr = nc.alloc_sbuf_tensor("skr", [64, T], BF16)
    ssp = nc.alloc_sbuf_tensor("ssp", [64, T], F32)
    ssp2 = nc.alloc_sbuf_tensor("ssp2", [64, T], F32)
    sdelta = nc.alloc_sbuf_tensor("sdelta", [1, 32], F32)
    sdrow = nc.alloc_sbuf_tensor("sdrow", [1, 4, 32], F32)
    sdcol = nc.alloc_sbuf_tensor("sdcol", [32, 2], F32)
    scos0 = nc.alloc_sbuf_tensor("scos0", [32, T], F32)
    ssin0 = nc.alloc_sbuf_tensor("ssin0", [32, T], F32)
    seps = nc.alloc_sbuf_tensor("seps", [1, 1], F32)
    smu2 = nc.alloc_sbuf_tensor("smu2", [32, T], F32)
    smu2b = nc.alloc_sbuf_tensor("smu2b", [32, T], F32)

    ones_col = nc.inline_tensor(np.ones((128, 1), np.float32), name="c_ones_col")
    ones_row = nc.inline_tensor(np.ones((1, 128), np.float32), name="c_ones_row")
    eps_c = nc.inline_tensor(np.full((1, 1), LN_EPS, np.float32), name="c_eps")

    p_big = nc.alloc_psum_tensor("p_big", [128, 512], F32)
    p_big2 = nc.alloc_psum_tensor("p_big2", [128, 512], F32)
    p_s1 = nc.alloc_psum_tensor("p_s1", [1, 512], F32)
    p_s2 = nc.alloc_psum_tensor("p_s2", [1, 512], F32)
    p_bc = nc.alloc_psum_tensor("p_bc", [128, 512], F32)
    p_bc2 = nc.alloc_psum_tensor("p_bc2", [128, 512], F32)
    p_col = nc.alloc_psum_tensor("p_col", [128, 2], F32)
    p_row8 = nc.alloc_psum_tensor("p_row8", [1, 8], F32)

    sc.dma("sync", sx[:], x.rearrange("(c p) t -> p c t", p=128), ld)
    sc.dma("sync", sones[:], ones_col.ap(), ld)
    sc.dma("sync", srow1[:], ones_row.ap(), ld)
    sc.dma("sync", sdelta[:], delta, ld)
    sc.dma("sync", scos0[:], cos0s, ld)
    sc.dma("sync", ssin0[:], sin0s, ld)
    sc.dma("sync", sgw[:], gw_t.rearrange("(c p) o -> p c o", p=128), ld)
    sc.dma("sync", seps[:], eps_c.ap(), ld)
    ld_all = ld.n
    sc.dma("gpsimd", swdkv[:], w_dkv.rearrange("(c p) o -> p c o", p=128), ldw)
    sc.dma("gpsimd", swdq[:], w_dq.rearrange("(c p) o -> p c o", p=128)[:, :, 0:512], ldw)
    sc.dma("gpsimd", swkr[:], w_kr.rearrange("(c p) o -> p c o", p=128), ldw)
    ldw_all = ldw.n

    accmm(sc, p_row8[:], [(sones[:], sgw[:, c, :]) for c in range(16)],
          waits=[(ld, ld_all)])
    sc.st("scalar", lambda s: s.copy(sSrow[:], p_row8[:]))
    sc.st("tensor", lambda t: t.matmul(p_bc[:, 0:8], srow1[:], sSrow[:], start=True, stop=True))
    sc.st("scalar", lambda s: s.copy(sSb[:], p_bc[:, 0:8]))

    for tt in range(2):
        t0 = tt * 512
        accmm(sc, p_s1[:], [(sones[:], sx[:, c, t0:t0 + 512]) for c in range(16)],
              waits=[(ld, ld_all)])
        for c in range(16):
            sl = c % 2
            sc.st("vector",
                  lambda v, c=c, sl=sl, t0=t0:
                  v.tensor_mul(ssq[:, sl, :], sx[:, c, t0:t0 + 512], sx[:, c, t0:t0 + 512]),
                  waits=[(ld, ld_all)])
            sc.st("tensor",
                  lambda t, c=c, sl=sl:
                  t.matmul(p_s2[:], sones[:], ssq[:, sl, :], start=(c == 0), stop=(c == 15)))

        sc.ops("vector", [
            lambda v: v.tensor_scalar_mul(srow_stats[:, 0, :], p_s1[:], 1.0 / 2048.0),
            lambda v: v.tensor_scalar_mul(srow_stats[:, 1, :], p_s2[:], 1.0 / 2048.0),
            lambda v: v.tensor_mul(srow_ab[:, 0, :], srow_stats[:, 0, :], srow_stats[:, 0, :]),
            lambda v: v.tensor_sub(srow_ab[:, 1, :], srow_stats[:, 1, :], srow_ab[:, 0, :]),
        ])

        sc.ops("scalar", [
            lambda s: s.activation(srow_ab[:, 0, :], srow_ab[:, 1, :], AF.Ln, bias=seps[:], scale=1.0),
            lambda s: s.activation(srow_ab[:, 0, :], srow_ab[:, 0, :], AF.Exp, bias=0.0, scale=-0.5),
        ], waits=[(ld, ld_all)])

        sc.ops("vector", [
            lambda v: v.tensor_mul(srow_ab[:, 1, :], srow_stats[:, 0, :], srow_ab[:, 0, :]),
            lambda v: v.tensor_scalar_mul(srow_ab[:, 1, :], srow_ab[:, 1, :], -1.0),
        ])

        def bc_ab(t, tt=tt):
            t.matmul(p_bc[:], srow1[:], srow_ab[:, 0, :], start=True, stop=True)
            return t.matmul(p_bc2[:], srow1[:], srow_ab[:, 1, :], start=True, stop=True)
        sc.st("tensor", bc_ab)

        def evac_ab(s):
            s.copy(sA[:], p_bc[:])
            return s.copy(sB[:], p_bc2[:])
        sc.st("scalar", evac_ab)

        xn_ops = []
        for c in range(16):
            xn_ops.append(lambda v, c=c, t0=t0: v.tensor_mul(ssq[:, c % 2, :], sx[:, c, t0:t0 + 512], sA[:]))
            xn_ops.append(lambda v, c=c, t0=t0: v.tensor_add(sxn[:, c, t0:t0 + 512], ssq[:, c % 2, :], sB[:]))
        sc.ops("vector", xn_ops)

        for q in range(4):
            ti = tt * 4 + q
            tq = ti * 128
            accmm(sc, p_big[:, 0:8],
                  [(sx[:, c, tq:tq + 128], sgw[:, c, :]) for c in range(16)],
                  waits=[(ld, ld_all)])

            def tr_mr(t, off=q * 128):
                t.matmul(p_col[:, 0:1], srow_stats[0:1, 0, off:off + 128], sones[0:1, 0:1],
                         start=True, stop=True)
                return t.matmul(p_col[:, 1:2], srow_ab[0:1, 0, off:off + 128], sones[0:1, 0:1],
                                start=True, stop=True)
            sc.st("tensor", tr_mr)
            sc.st("scalar", lambda s: s.copy(s_mr_col[:], p_col[:, 0:2]))

            sc.ops("vector", [
                lambda v, ti=ti: v.tensor_scalar(slog[:, ti, :], sSb[:], s_mr_col[:, 0:1], None, ALU.mult),
                lambda v, ti=ti: v.tensor_sub(slog[:, ti, :], p_big[:, 0:8], slog[:, ti, :]),
                lambda v, ti=ti: v.tensor_scalar(slog[:, ti, :], slog[:, ti, :], s_mr_col[:, 1:2], None, ALU.mult),
            ])

    sc.dma("sync", o_xn.rearrange("(c p) t -> p c t", p=128), sxn[:], stv)
    sc.dma("sync", o_log.rearrange("(q p) o -> p q o", p=128), slog[:], stv)

    ldw2 = sc.sem("ldw2")
    for (wt, outt, nout, wsem, wval) in ((swdkv, sckv, 4, ldw, ldw_all),
                                         (swdq, scq, 4, ldw, ldw_all)):
        for oc in range(nout):
            for tt in range(2):
                t0 = tt * 512
                pb = p_big if (oc + tt) % 2 == 0 else p_big2
                accmm(sc, pb[:],
                      [(wt[:, c, oc * 128:(oc + 1) * 128], sxn[:, c, t0:t0 + 512]) for c in range(16)],
                      waits=[(wsem, wval)])
                sc.st("scalar",
                      lambda s, outt=outt, oc=oc, t0=t0, pb=pb:
                      s.copy(outt[:, oc, t0:t0 + 512], pb[:]))
    sc.dma("gpsimd", swdq[:], w_dq.rearrange("(c p) o -> p c o", p=128)[:, :, 512:1024], ldw2)
    for oc in range(4):
        for tt in range(2):
            t0 = tt * 512
            pb = p_big if (oc + tt) % 2 == 0 else p_big2
            accmm(sc, pb[:],
                  [(swdq[:, c, oc * 128:(oc + 1) * 128], sxn[:, c, t0:t0 + 512]) for c in range(16)],
                  waits=[(ldw2, 16)])
            sc.st("scalar",
                  lambda s, oc=oc, t0=t0, pb=pb:
                  s.copy(scq[:, 4 + oc, t0:t0 + 512], pb[:]))
    for tt in range(2):
        t0 = tt * 512
        pb = p_big if tt % 2 == 0 else p_big2
        accmm(sc, pb[0:64, :],
              [(swkr[:, c, :], sxn[:, c, t0:t0 + 512]) for c in range(16)],
              waits=[(ldw, ldw_all)])
        sc.st("scalar", lambda s, t0=t0, pb=pb: s.copy(skr_raw[:, t0:t0 + 512], pb[0:64, :]))

    sc.dma("sync", o_ckv.rearrange("(c p) t -> p c t", p=128), sckv[:], stv)
    sc.dma("sync", o_cq.rearrange("(c p) t -> p c t", p=128), scq[:], stv)

    sc.ops("vector", [
        lambda v: v.tensor_scalar(sdrow[:, 0, :], sdelta[:], 1.0 / (2.0 * np.pi), MAGIC, ALU.mult, ALU.add),
        lambda v: v.tensor_scalar(sdrow[:, 0, :], sdrow[:, 0, :], MAGIC, None, ALU.subtract),
        lambda v: v.tensor_scalar(sdrow[:, 0, :], sdrow[:, 0, :], 2.0 * np.pi, None, ALU.mult),
        lambda v: v.tensor_sub(sdrow[:, 1, :], sdelta[:], sdrow[:, 0, :]),
        lambda v: v.tensor_scalar_mul(sdrow[:, 2, :], sdrow[:, 1, :], 0.5),
    ], waits=[(ld, ld_all)])

    sc.ops("scalar", [
        lambda s: s.activation(sdrow[:, 1, :], sdrow[:, 1, :], AF.Sin, bias=0.0, scale=1.0),
        lambda s: s.activation(sdrow[:, 2, :], sdrow[:, 2, :], AF.Sin, bias=0.0, scale=1.0),
    ])

    sc.ops("vector", [
        lambda v: v.tensor_mul(sdrow[:, 3, :], sdrow[:, 2, :], sdrow[:, 2, :]),
        lambda v: v.tensor_scalar(sdrow[:, 3, :], sdrow[:, 3, :], -2.0, 1.0, ALU.mult, ALU.add),
    ])

    def tr_ds(t):
        t.matmul(p_col[0:32, 0:1], sdrow[0:1, 3, :], sones[0:1, 0:1], start=True, stop=True)
        return t.matmul(p_col[0:32, 1:2], sdrow[0:1, 1, :], sones[0:1, 0:1], start=True, stop=True)
    sc.st("tensor", tr_ds)
    sc.st("scalar", lambda s: s.copy(sdcol[:], p_col[0:32, 0:2]))

    sc.ops("vector", [
        lambda v: v.tensor_scalar(ssp[0:32, :], ssin0[:], sdcol[:, 1:2], None, ALU.mult),
        lambda v: v.tensor_scalar(ssp2[0:32, :], scos0[:], sdcol[:, 1:2], None, ALU.mult),
        lambda v: v.tensor_scalar(scos0[:], scos0[:], sdcol[:, 0:1], None, ALU.mult),
        lambda v: v.tensor_scalar(ssin0[:], ssin0[:], sdcol[:, 0:1], None, ALU.mult),
        lambda v: v.tensor_sub(scos0[:], scos0[:], ssp[0:32, :]),
        lambda v: v.tensor_add(ssin0[:], ssin0[:], ssp2[0:32, :]),
    ])

    sc.ops("scalar", [
        lambda s: s.activation(ssp[:], skr_raw[:], AF.Abs, bias=0.0, scale=1.0),
        lambda s: s.activation(ssp[:], ssp[:], AF.Exp, bias=0.0, scale=-1.0),
        lambda s: s.activation(ssp[:], ssp[:], AF.Ln, bias=sones[0:64, :], scale=1.0),
    ])

    sc.ops("vector", [
        lambda v: v.tensor_scalar_max(ssp2[:], skr_raw[:], 0.0),
        lambda v: v.tensor_add(ssp2[:], ssp2[:], ssp[:]),
    ])

    sc.ops("vector", [
        lambda v: v.tensor_copy(smu2[:], ssp2[32:64, :]),
        lambda v: v.tensor_mul(ssp[0:32, :], ssp2[0:32, :], scos0[:]),
        lambda v: v.tensor_mul(smu2b[:], smu2[:], ssin0[:]),
        lambda v: v.tensor_sub(skr[0:32, :], ssp[0:32, :], smu2b[:]),
        lambda v: v.tensor_mul(ssp[0:32, :], ssp2[0:32, :], ssin0[:]),
        lambda v: v.tensor_mul(smu2b[:], smu2[:], scos0[:]),
        lambda v: v.tensor_add(smu2b[:], ssp[0:32, :], smu2b[:]),
        lambda v: v.tensor_copy(skr[32:64, :], smu2b[:]),
    ])

    sc.dma("sync", o_kr, skr[:], stv)
    sc.st("sync", lambda e: None, waits=[(stv, stv.n)])
    sc.run()
    return nc


_L1_CACHE = {}


def run_launch1(inputs, xs_fm):
    if "nc" not in _L1_CACHE:
        _L1_CACHE["nc"] = build_launch1()
    nc = _L1_CACHE["nc"]
    cos0, sin0 = pope_tables()
    in_maps = []
    for c in range(NCORES):
        pos0 = (c % 4) * T
        in_maps.append({
            "x_fm": xs_fm[c],
            "w_dkv": np.ascontiguousarray(inputs["L0_attn_w_dkv"], np.float32),
            "w_dq": np.ascontiguousarray(inputs["L0_attn_w_dq"], np.float32),
            "w_kr": np.ascontiguousarray(inputs["L0_attn_w_kr"], np.float32),
            "gw_t": np.ascontiguousarray(np.asarray(inputs["L0_ffn_gate_w"], np.float32).T),
            "delta": np.asarray(inputs["L0_attn_delta"], np.float32).reshape(1, 32),
            "cos0s": np.ascontiguousarray(cos0[:, pos0:pos0 + T]),
            "sin0s": np.ascontiguousarray(sin0[:, pos0:pos0 + T]),
        })
    res = run_bass_kernel_spmd(nc, in_maps, core_ids=list(range(NCORES)))
    return res.results





SC192 = 1.0 / np.sqrt(192.0)


def build_launch2():
    nc = bass.Bass("TRN2", target_bir_lowering=False)
    ckv = nc.dram_tensor("ckv", [512, S], BF16, kind="ExternalInput").ap()
    krf = nc.dram_tensor("krf", [64, S], BF16, kind="ExternalInput").ap()
    cq = nc.dram_tensor("cq", [1024, S], BF16, kind="ExternalInput").ap()
    wuq = nc.dram_tensor("wuq", [1024, 512], F32, kind="ExternalInput").ap()
    wqr = nc.dram_tensor("wqr", [1024, 256], F32, kind="ExternalInput").ap()
    wuk = nc.dram_tensor("wuk", [512, 512], F32, kind="ExternalInput").ap()
    wuv = nc.dram_tensor("wuv", [512, 512], F32, kind="ExternalInput").ap()
    delta = nc.dram_tensor("delta", [1, 32], F32, kind="ExternalInput").ap()
    o_attn = nc.dram_tensor("attn", [4, 128, S], BF16, kind="ExternalOutput").ap()

    cos0, sin0 = pope_tables()
    c_cos0 = nc.inline_tensor(cos0.astype(BF16NP), name="c_cos0")
    c_sin0 = nc.inline_tensor(sin0.astype(BF16NP), name="c_sin0")
    masks = np.where((np.arange(4)[:, None, None] * 128 + np.arange(128)[None, :, None])
                     > np.arange(512)[None, None, :], NEG, 0.0).astype(np.float32)
    c_masks = nc.inline_tensor(masks.reshape(4 * 128, 512), name="c_masks")
    ones_col = nc.inline_tensor(np.ones((128, 1), np.float32), name="c_ones_col")
    ones_colb = nc.inline_tensor(np.ones((128, 1), np.float32).astype(BF16NP), name="c_ones_colb")
    ones_row = nc.inline_tensor(np.ones((1, 128), np.float32), name="c_ones_row")

    sc = Sched(nc)
    ld = sc.sem("ld")
    ldw = sc.sem("ldw")
    ldq = sc.sem("ldq")
    stv = sc.sem("stv")

    sckv = nc.alloc_sbuf_tensor("sckv", [128, 4, S], BF16)
    skr = nc.alloc_sbuf_tensor("skr", [64, S], BF16)
    scqb = nc.alloc_sbuf_tensor("scqb", [128, 2, 8, 512], BF16)
    suq = nc.alloc_sbuf_tensor("suq", [128, 8, 512], BF16)
    sqrw = nc.alloc_sbuf_tensor("sqrw", [128, 8, 256], BF16)
    suk = nc.alloc_sbuf_tensor("suk", [128, 4, 512], BF16)
    suv = nc.alloc_sbuf_tensor("suv", [128, 4, 512], BF16)
    sK = nc.alloc_sbuf_tensor("sK", [128, 8, 512], BF16)
    sV = nc.alloc_sbuf_tensor("sV", [128, 32, 128], BF16)
    sQ = nc.alloc_sbuf_tensor("sQ", [128, 8, 512], BF16)
    sQr = nc.alloc_sbuf_tensor("sQr", [64, S], BF16)
    sqr_raw = nc.alloc_sbuf_tensor("sqr_raw", [64, 512], F32)
    ssp = nc.alloc_sbuf_tensor("ssp", [64, 512], F32)
    ssp2 = nc.alloc_sbuf_tensor("ssp2", [64, 512], F32)
    smu2 = nc.alloc_sbuf_tensor("smu2", [32, 512], F32)
    smu2b = nc.alloc_sbuf_tensor("smu2b", [32, 512], F32)
    sPt = nc.alloc_sbuf_tensor("sPt", [128, 32, 512], BF16)
    sOut = nc.alloc_sbuf_tensor("sOut", [128, S], BF16)
    smask = nc.alloc_sbuf_tensor("smask", [128, 4, 512], F32)
    scos = nc.alloc_sbuf_tensor("scos", [32, S], BF16)
    ssin = nc.alloc_sbuf_tensor("ssin", [32, S], BF16)
    ssq = nc.alloc_sbuf_tensor("ssq", [128, 512], F32)
    stiny = nc.alloc_sbuf_tensor("stiny", [1, 2, 8], F32)
    sscal = nc.alloc_sbuf_tensor("sscal", [1, 8], F32)
    sbias = nc.alloc_sbuf_tensor("sbias", [128, 1], F32)
    sRec = nc.alloc_sbuf_tensor("sRec", [128, 512], F32)
    srd = nc.alloc_sbuf_tensor("srd", [1, 512], F32)
    sones = nc.alloc_sbuf_tensor("sones", [128, 1], F32)
    sonesb = nc.alloc_sbuf_tensor("sonesb", [128, 1], BF16)
    srow1 = nc.alloc_sbuf_tensor("srow1", [1, 128], F32)
    sdelta = nc.alloc_sbuf_tensor("sdelta", [1, 32], F32)
    sdrow = nc.alloc_sbuf_tensor("sdrow", [1, 4, 32], F32)
    sdcol = nc.alloc_sbuf_tensor("sdcol", [32, 2], F32)

    p_s = nc.alloc_psum_tensor("p_s", [128, 512], F32)
    p_s2 = nc.alloc_psum_tensor("p_s2", [128, 512], F32)
    p_o = nc.alloc_psum_tensor("p_o", [128, 512], F32)
    p_d = nc.alloc_psum_tensor("p_d", [1, 512], F32)
    p_bc = nc.alloc_psum_tensor("p_bc", [128, 512], F32)
    p_row = nc.alloc_psum_tensor("p_row", [1, 512], F32)
    p_col = nc.alloc_psum_tensor("p_col", [128, 2], F32)

    sc.dma("sync", sckv[:], ckv.rearrange("(c p) t -> p c t", p=128), ld)
    sc.dma("sync", skr[:], krf, ld)
    sc.dma("sync", smask[:], c_masks.ap().rearrange("(m p) t -> p m t", p=128), ld)
    sc.dma("sync", scos[:], c_cos0.ap(), ld)
    sc.dma("sync", ssin[:], c_sin0.ap(), ld)
    sc.dma("sync", sones[:], ones_col.ap(), ld)
    sc.dma("sync", sonesb[:], ones_colb.ap(), ld)
    sc.dma("sync", srow1[:], ones_row.ap(), ld)
    sc.dma("sync", sdelta[:], delta, ld)
    ld_all = ld.n
    sc.dma("gpsimd", suq[:], wuq.rearrange("(c p) o -> p c o", p=128), ldw)
    sc.dma("gpsimd", sqrw[:], wqr.rearrange("(c p) o -> p c o", p=128), ldw)
    sc.dma("gpsimd", suk[:], wuk.rearrange("(c p) o -> p c o", p=128), ldw)
    sc.dma("gpsimd", suv[:], wuv.rearrange("(c p) o -> p c o", p=128), ldw)
    ldw_all = ldw.n

    sc.ops("vector", [
        lambda v: v.tensor_scalar(sdrow[:, 0, :], sdelta[:], 1.0 / (2.0 * np.pi), MAGIC, ALU.mult, ALU.add),
        lambda v: v.tensor_scalar(sdrow[:, 0, :], sdrow[:, 0, :], MAGIC, None, ALU.subtract),
        lambda v: v.tensor_scalar(sdrow[:, 0, :], sdrow[:, 0, :], 2.0 * np.pi, None, ALU.mult),
        lambda v: v.tensor_sub(sdrow[:, 1, :], sdelta[:], sdrow[:, 0, :]),
        lambda v: v.tensor_scalar_mul(sdrow[:, 2, :], sdrow[:, 1, :], 0.5),
    ], waits=[(ld, ld_all)])
    sc.ops("scalar", [
        lambda s: s.activation(sdrow[:, 1, :], sdrow[:, 1, :], AF.Sin, bias=0.0, scale=1.0),
        lambda s: s.activation(sdrow[:, 2, :], sdrow[:, 2, :], AF.Sin, bias=0.0, scale=1.0),
    ])
    sc.ops("vector", [
        lambda v: v.tensor_mul(sdrow[:, 3, :], sdrow[:, 2, :], sdrow[:, 2, :]),
        lambda v: v.tensor_scalar(sdrow[:, 3, :], sdrow[:, 3, :], -2.0, 1.0, ALU.mult, ALU.add),
    ])
    def tr_ds(t):
        t.matmul(p_col[0:32, 0:1], sdrow[0:1, 3, :], sones[0:1, 0:1], start=True, stop=True)
        return t.matmul(p_col[0:32, 1:2], sdrow[0:1, 1, :], sones[0:1, 0:1], start=True, stop=True)
    sc.st("tensor", tr_ds)
    sc.st("scalar", lambda s: s.copy(sdcol[:], p_col[0:32, 0:2]))
    sstrip = nc.alloc_sbuf_tensor("sstrip", [32, 2, 1024], BF16)
    for st0 in range(0, S, 1024):
        sl = (slice(0, 32), slice(st0, st0 + 1024))
        sc.ops("vector", [
            lambda v, sl=sl: v.tensor_scalar(sstrip[:, 0, :], ssin[sl[0], sl[1]], sdcol[:, 1:2], None, ALU.mult),
            lambda v, sl=sl: v.tensor_scalar(sstrip[:, 1, :], scos[sl[0], sl[1]], sdcol[:, 1:2], None, ALU.mult),
            lambda v, sl=sl: v.tensor_scalar(scos[sl[0], sl[1]], scos[sl[0], sl[1]], sdcol[:, 0:1], None, ALU.mult),
            lambda v, sl=sl: v.tensor_scalar(ssin[sl[0], sl[1]], ssin[sl[0], sl[1]], sdcol[:, 0:1], None, ALU.mult),
            lambda v, sl=sl: v.tensor_sub(scos[sl[0], sl[1]], scos[sl[0], sl[1]], sstrip[:, 0, :]),
            lambda v, sl=sl: v.tensor_add(ssin[sl[0], sl[1]], ssin[sl[0], sl[1]], sstrip[:, 1, :]),
        ])

    for u in range(4):
        for kb8 in range(8):
            accmm(sc, p_s[:], [(suk[:, c, u * 128:(u + 1) * 128], sckv[:, c, kb8 * 512:(kb8 + 1) * 512]) for c in range(4)],
                  waits=[(ldw, ldw_all), (ld, ld_all), (stv, stv.n)])
            sc.st("scalar", lambda s, kb8=kb8: s.copy(sK[:, kb8, :], p_s[:]))
        for kb2 in range(32):
            accmm(sc, p_s[:, 0:128], [(sckv[:, c, kb2 * 128:(kb2 + 1) * 128], suv[:, c, u * 128:(u + 1) * 128]) for c in range(4)])
            sc.st("scalar", lambda s, kb2=kb2: s.copy(sV[:, kb2, :], p_s[:, 0:128]))
        for kb8 in range(8):
            sc.st("vector", lambda v, kb8=kb8: v.tensor_mul(ssq[:], sK[:, kb8, :], sK[:, kb8, :]))
            sc.st("vector", lambda v, kb8=kb8: v.tensor_mul(ssp[:], skr[:, kb8 * 512:(kb8 + 1) * 512], skr[:, kb8 * 512:(kb8 + 1) * 512]))
            accmm(sc, p_row[:], [(sones[:], ssq[:]), (sones[0:64, :], ssp[:])])
            sc.st("vector", lambda v, kb8=kb8: v.tensor_reduce(stiny[0:1, 0, kb8:kb8 + 1], p_row[:], AX.X, ALU.max))

        for qb in range(8):
            par = qb % 2
            sc.dma("gpsimd", scqb[:, par, :, :],
                   cq.rearrange("(c p) t -> p c t", p=128)[:, :, qb * 512:(qb + 1) * 512], ldq)
            ldq_now = ldq.n
            accmm(sc, p_s[:], [(suq[:, c, u * 128:(u + 1) * 128], scqb[:, par, c, :]) for c in range(8)],
                  waits=[(ldq, ldq_now)])
            sc.st("scalar", lambda s, qb=qb: s.copy(sQ[:, qb, :], p_s[:]))
            accmm(sc, p_s2[0:64, :],
                  [(sqrw[:, c, u * 64:(u + 1) * 64], scqb[:, par, c, :]) for c in range(8)],
                  waits=[(ldq, ldq_now)])
            sc.st("scalar", lambda s: s.copy(sqr_raw[:], p_s2[0:64, :]))
            sc.ops("scalar", [
                lambda s: s.activation(ssp[:], sqr_raw[:], AF.Abs, bias=0.0, scale=1.0),
                lambda s: s.activation(ssp[:], ssp[:], AF.Exp, bias=0.0, scale=-1.0),
                lambda s: s.activation(ssp[:], ssp[:], AF.Ln, bias=sones[0:64, :], scale=1.0),
            ])
            sc.ops("vector", [
                lambda v: v.tensor_scalar_max(ssp2[:], sqr_raw[:], 0.0),
                lambda v: v.tensor_add(ssp2[:], ssp2[:], ssp[:]),
            ])
            q0 = qb * 512
            sc.ops("vector", [
                lambda v: v.tensor_copy(smu2[:], ssp2[32:64, :]),
                lambda v, q0=q0: v.tensor_mul(ssp[0:32, :], ssp2[0:32, :], scos[:, q0:q0 + 512]),
                lambda v, q0=q0: v.tensor_mul(smu2b[:], smu2[:], ssin[:, q0:q0 + 512]),
                lambda v, q0=q0: v.tensor_sub(sQr[0:32, q0:q0 + 512], ssp[0:32, :], smu2b[:]),
                lambda v, q0=q0: v.tensor_mul(ssp[0:32, :], ssp2[0:32, :], ssin[:, q0:q0 + 512]),
                lambda v, q0=q0: v.tensor_mul(smu2b[:], smu2[:], scos[:, q0:q0 + 512]),
                lambda v: v.tensor_add(smu2b[:], ssp[0:32, :], smu2b[:]),
                lambda v, q0=q0: v.tensor_copy(sQr[32:64, q0:q0 + 512], smu2b[:]),
            ])
            sc.st("vector", lambda v, qb=qb: v.tensor_mul(ssq[:], sQ[:, qb, :], sQ[:, qb, :]))
            sc.st("vector", lambda v, q0=q0: v.tensor_mul(ssp[:], sQr[:, q0:q0 + 512], sQr[:, q0:q0 + 512]))
            accmm(sc, p_row[:], [(sones[:], ssq[:]), (sones[0:64, :], ssp[:])])
            sc.st("vector", lambda v, qb=qb: v.tensor_reduce(stiny[0:1, 1, qb:qb + 1], p_row[:], AX.X, ALU.max))

        sc.ops("vector", [
            lambda v: v.tensor_reduce(sscal[0:1, 0:1], stiny[0:1, 0, :], AX.X, ALU.max),
            lambda v: v.tensor_reduce(sscal[0:1, 1:2], stiny[0:1, 1, :], AX.X, ALU.max),
            lambda v: v.tensor_mul(sscal[0:1, 2:3], sscal[0:1, 0:1], sscal[0:1, 1:2]),
        ])
        sc.ops("scalar", [
            lambda s: s.activation(sscal[0:1, 3:4], sscal[0:1, 2:3], AF.Ln, bias=0.0, scale=1.0),
            lambda s: s.activation(sscal[0:1, 3:4], sscal[0:1, 3:4], AF.Exp, bias=0.0, scale=0.5),
        ])
        sc.st("vector", lambda v: v.tensor_scalar_mul(sscal[0:1, 4:5], sscal[0:1, 3:4], -SC192))
        sc.st("tensor", lambda t: t.matmul(p_col[:, 0:1], srow1[:], sscal[0:1, 4:5], start=True, stop=True))
        sc.st("scalar", lambda s: s.copy(sbias[:], p_col[:, 0:1]))

        emit_flash_pipelined(
            sc, range(8), lambda qc: 4 * (qc + 1),
            lambda qc, kb, ps: [(sK[:, kb // 4, (kb % 4) * 128:(kb % 4 + 1) * 128], sQ[:, qc, :]),
                                (skr[:, kb * 128:(kb + 1) * 128], sQr[:, qc * 512:(qc + 1) * 512])],
            lambda qc, kb: smask[:, kb % 4, :] if kb // 4 == qc else None,
            lambda qc, kb: sPt[:, kb, :], sbias[:], SC192,
            lambda qc, nkb: [(sV[:, kb, :], sPt[:, kb, :]) for kb in range(nkb)],
            lambda qc, nkb: [(sonesb[:], sPt[:, kb, :]) for kb in range(nkb)],
            lambda v: v.reciprocal(srd[:], p_d[:]),
            lambda t: t.matmul(p_bc[:], srow1[:], srd[:], start=True, stop=True),
            lambda s: s.copy(sRec[:], p_bc[:]),
            lambda v, qc: v.tensor_mul(sOut[:, qc * 512:(qc + 1) * 512], p_o[:], sRec[:]),
            (p_s, p_s2, p_o, p_d))
        sc.dma("sync", o_attn[u], sOut[:], stv)

    sc.st("sync", lambda e: None, waits=[(stv, stv.n)])
    sc.run()
    return nc


_L2_CACHE = {}


def run_launch2(inputs, ckv_full, kr_full, cq_full):
    if "nc" not in _L2_CACHE:
        _L2_CACHE["nc"] = build_launch2()
    nc = _L2_CACHE["nc"]
    wuq = np.asarray(inputs["L0_attn_w_uq"], np.float32)
    wqr = np.asarray(inputs["L0_attn_w_qr"], np.float32)
    wuk = np.asarray(inputs["L0_attn_w_uk"], np.float32)
    wuv = np.asarray(inputs["L0_attn_w_uv"], np.float32)
    in_maps = []
    for c in range(NCORES):
        s = c // 4
        hb = (c % 4) * 4
        in_maps.append({
            "ckv": ckv_full[s],
            "krf": kr_full[s],
            "cq": cq_full[s],
            "wuq": np.ascontiguousarray(wuq[:, hb * 128:(hb + 4) * 128]),
            "wqr": np.ascontiguousarray(wqr[:, hb * 64:(hb + 4) * 64]),
            "wuk": np.ascontiguousarray(wuk[:, hb * 128:(hb + 4) * 128]),
            "wuv": np.ascontiguousarray(wuv[:, hb * 128:(hb + 4) * 128]),
            "delta": np.asarray(inputs["L0_attn_delta"], np.float32).reshape(1, 32),
        })
    res = run_bass_kernel_spmd(nc, in_maps, core_ids=list(range(NCORES)))
    return res.results





def build_launch3():
    nc = bass.Bass("TRN2", target_bir_lowering=False)
    x = nc.dram_tensor("x_fm", [D, T], F32, kind="ExternalInput").ap()
    xn = nc.dram_tensor("xn_fm", [D, T], BF16, kind="ExternalInput").ap()
    att = nc.dram_tensor("att_fm", [D, T], BF16, kind="ExternalInput").ap()
    logits = nc.dram_tensor("logits", [T, 8], F32, kind="ExternalInput").ap()
    gbias = nc.dram_tensor("gbias", [1, 8], F32, kind="ExternalInput").ap()
    wout = nc.dram_tensor("wout", [D, D], F32, kind="ExternalInput").ap()
    sgw = nc.dram_tensor("sgw", [D, 5632], F32, kind="ExternalInput").ap()
    sdw = nc.dram_tensor("sdw", [2816, D], F32, kind="ExternalInput").ap()
    upw = nc.dram_tensor("upw", [8, D, 2816], F32, kind="ExternalInput").ap()
    dnw = nc.dram_tensor("dnw", [8, 1408, D], F32, kind="ExternalInput").ap()
    o_x1 = nc.dram_tensor("x1_fm", [D, T], F32, kind="ExternalOutput").ap()

    ones_col = nc.inline_tensor(np.ones((128, 1), np.float32), name="c_ones_col")
    ones_row = nc.inline_tensor(np.ones((1, 128), np.float32), name="c_ones_row")
    id8 = nc.inline_tensor(np.eye(8, dtype=np.float32), name="c_id8")
    id128 = nc.inline_tensor(np.eye(128, dtype=np.float32), name="c_id128")

    sc = Sched(nc)
    ld = sc.sem("ld")
    ldw = sc.sem("ldw")
    stv = sc.sem("stv")

    sxn = nc.alloc_sbuf_tensor("sxn", [128, 16, T], BF16)
    satt = nc.alloc_sbuf_tensor("satt", [128, 16, T], BF16)
    sacc = nc.alloc_sbuf_tensor("sacc", [128, 16, T], F32)
    sh = nc.alloc_sbuf_tensor("sh", [128, 11, T], BF16)
    swu = nc.alloc_sbuf_tensor("swu", [128, 2, 16, 128], BF16)
    swd = nc.alloc_sbuf_tensor("swd", [128, 2, 11, 128], BF16)
    swo = nc.alloc_sbuf_tensor("swo", [128, 2, 16, 128], BF16)
    sxs = nc.alloc_sbuf_tensor("sxs", [128, 2, T], F32)
    ssil = nc.alloc_sbuf_tensor("ssil", [128, 2, 512], F32)
    st2 = nc.alloc_sbuf_tensor("st2", [128, 512], F32)
    slog = nc.alloc_sbuf_tensor("slog", [128, 8, 8], F32)
    ssig = nc.alloc_sbuf_tensor("ssig", [128, 8, 8], F32)
    swe = nc.alloc_sbuf_tensor("swe", [128, 8, 8], F32)
    sm1 = nc.alloc_sbuf_tensor("sm1", [128, 4], F32)
    sBb = nc.alloc_sbuf_tensor("sBb", [128, 8], F32)
    sgb = nc.alloc_sbuf_tensor("sgb", [1, 8], F32)
    swet = nc.alloc_sbuf_tensor("swet", [8, 8, 128], F32)
    sweRow = nc.alloc_sbuf_tensor("sweRow", [1, T], F32)
    sweb = nc.alloc_sbuf_tensor("sweb", [128, T], BF16)
    sones = nc.alloc_sbuf_tensor("sones", [128, 1], F32)
    srow1 = nc.alloc_sbuf_tensor("srow1", [1, 128], F32)
    sid8 = nc.alloc_sbuf_tensor("sid8", [8, 8], F32)
    sid128 = nc.alloc_sbuf_tensor("sid128", [128, 128], F32)

    p_a = nc.alloc_psum_tensor("p_a", [128, 512], F32)
    p_b = nc.alloc_psum_tensor("p_b", [128, 512], F32)
    p_c = nc.alloc_psum_tensor("p_c", [128, 512], F32)
    p_d = nc.alloc_psum_tensor("p_d", [128, 512], F32)
    p_t = nc.alloc_psum_tensor("p_t", [128, 128], F32)
    p_r = nc.alloc_psum_tensor("p_r", [1, 512], F32)

    sc.dma("sync", sxn[:], xn.rearrange("(c p) t -> p c t", p=128), ld)
    sc.dma("sync", satt[:], att.rearrange("(c p) t -> p c t", p=128), ld)
    sc.dma("sync", slog[:], logits.rearrange("(q p) o -> p q o", p=128), ld)
    sc.dma("sync", sgb[:], gbias, ld)
    sc.dma("sync", sones[:], ones_col.ap(), ld)
    sc.dma("sync", srow1[:], ones_row.ap(), ld)
    sc.dma("sync", sid8[:], id8.ap(), ld)
    sc.dma("sync", sid128[:], id128.ap(), ld)
    ld_all = ld.n

    for ti in range(8):
        sc.st("scalar", lambda s, ti=ti: s.activation(ssig[:, ti, :], slog[:, ti, :], AF.Sigmoid, bias=0.0, scale=1.0),
              waits=[(ld, ld_all)])
    sc.st("tensor", lambda t: t.matmul(p_t[:, 0:8], srow1[:], sgb[:], start=True, stop=True))
    sc.st("scalar", lambda s: s.copy(sBb[:], p_t[:, 0:8]))
    for ti in range(8):
        sc.ops("vector", [
            lambda v, ti=ti: v.tensor_add(swe[:, ti, :], ssig[:, ti, :], sBb[:]),
            lambda v, ti=ti: v.tensor_reduce(sm1[:, 0:1], swe[:, ti, :], AX.X, ALU.max),
            lambda v, ti=ti: v.tensor_scalar(sm1[:, 1:2], sm1[:, 0:1], 1.0, None, ALU.mult),
            lambda v, ti=ti: v.tensor_scalar(st2[:, 0:8], swe[:, ti, :], sm1[:, 0:1], None, ALU.is_ge),
            lambda v, ti=ti: v.tensor_scalar(ssil[:, 0, 0:8], st2[:, 0:8], NEG, None, ALU.mult),
            lambda v, ti=ti: v.tensor_add(swe[:, ti, :], swe[:, ti, :], ssil[:, 0, 0:8]),
            lambda v, ti=ti: v.tensor_reduce(sm1[:, 2:3], swe[:, ti, :], AX.X, ALU.max),
            lambda v, ti=ti: v.tensor_scalar(ssil[:, 0, 0:8], swe[:, ti, :], sm1[:, 2:3], None, ALU.is_ge),
            lambda v, ti=ti: v.tensor_add(st2[:, 0:8], st2[:, 0:8], ssil[:, 0, 0:8]),
            lambda v, ti=ti: v.tensor_mul(swe[:, ti, :], st2[:, 0:8], ssig[:, ti, :]),
            lambda v, ti=ti: v.tensor_reduce(sm1[:, 3:4], swe[:, ti, :], AX.X, ALU.add),
            lambda v, ti=ti: v.reciprocal(sm1[:, 3:4], sm1[:, 3:4]),
            lambda v, ti=ti: v.tensor_scalar(swe[:, ti, :], swe[:, ti, :], sm1[:, 3:4], None, ALU.mult),
        ])
    for ti in range(8):
        sc.st("tensor", lambda t, ti=ti: t.transpose(p_t[0:8, 0:128], swe[:, ti, :], sid128[:]))
        sc.st("scalar", lambda s, ti=ti: s.copy(swet[:, ti, :], p_t[0:8, 0:128]))

    for oc in range(16):
        par = oc % 2
        sc.dma("sync", sxs[:, par, :], x.rearrange("(c p) t -> p c t", p=128)[:, oc, :], ld)
        sc.dma("gpsimd", swo[:, par, :, :],
               wout.rearrange("(c p) o -> p c o", p=128)[:, :, oc * 128:(oc + 1) * 128], ldw)
        ldn, lwn = ld.n, ldw.n
        for tt in range(2):
            pp = p_a if tt == 0 else p_b
            accmm(sc, pp[:], [(swo[:, par, hc, :], satt[:, hc, tt * 512:(tt + 1) * 512]) for hc in range(16)],
                  waits=[(ld, ldn), (ldw, lwn)])
        sc.st("vector", lambda v, oc=oc, par=par: v.tensor_add(sacc[:, oc, 0:512], p_a[:], sxs[:, par, 0:512]))
        sc.st("vector", lambda v, oc=oc, par=par: v.tensor_add(sacc[:, oc, 512:1024], p_b[:], sxs[:, par, 512:1024]))


    cmpS, peS = sc.cmp, sc.pe
    ldq = sc.sem("ldq")
    ldq2 = sc.sem("ldq2")

    def pipe_up(nj, gsrc, usrc, hsink):
        entry_c, entry_p = cmpS.n, peS.n
        silu_done = {0: entry_c, 1: entry_c}
        mul_done = {0: entry_c, 1: entry_c}
        dma_free = {0: entry_p, 1: entry_p}
        for j in range(nj):
            par = j % 2
            wv = (ldq, ldq2)[par]
            sc.dma("gpsimd", swu[:, par, :, :], gsrc(j), wv, waits=[(peS, dma_free[par])])
            sc.dma("gpsimd", swo[:, par, :, :], usrc(j), wv)
            lwn = wv.n
            for tt in range(2):
                pg = (p_a, p_b)[tt]
                pu = (p_c, p_d)[tt]
                sc.st("tensor", (lambda t, par=par, tt=tt, pg=pg: [
                    t.matmul(pg[:], swu[:, par, c, :], sxn[:, c, tt * 512:(tt + 1) * 512],
                             start=(c == 0), stop=(c == 15)) for c in range(16)][-1]),
                    waits=[(wv, lwn), (cmpS, silu_done[tt])], auto=False)
                gmk = peS.n
                sc.st("tensor", (lambda t, par=par, tt=tt, pu=pu: [
                    t.matmul(pu[:], swo[:, par, c, :], sxn[:, c, tt * 512:(tt + 1) * 512],
                             start=(c == 0), stop=(c == 15)) for c in range(16)][-1]),
                    waits=[(cmpS, mul_done[tt])], auto=False)
                umk = peS.n
                sc.st("scalar", lambda s, pg=pg, tt=tt: s.activation(ssil[:, tt, :], pg[:], AF.Silu, bias=0.0, scale=1.0),
                      waits=[(peS, gmk), (cmpS, mul_done[tt])], auto=False)
                silu_done[tt] = cmpS.n
                hsink(j, tt, pu, tt, umk)
                mul_done[tt] = cmpS.n
            dma_free[par] = peS.n

    def pipe_down(nj, wsrc, hsrc, sink):
        entry_c, entry_p = cmpS.n, peS.n
        add_done = {0: entry_c, 1: entry_c}
        dma_free = {0: entry_p, 1: entry_p}
        for oc in range(16):
            par = oc % 2
            wv = (ldq, ldq2)[par]
            sc.dma("gpsimd", swd[:, par, 0:nj, :], wsrc(oc), wv, waits=[(peS, dma_free[par])])
            lwn = wv.n
            for tt in range(2):
                pp = (p_a, p_b)[tt]
                sc.st("tensor", (lambda t, par=par, tt=tt, pp=pp, nj=nj: [
                    t.matmul(pp[:], swd[:, par, j, :], hsrc(j, tt),
                             start=(j == 0), stop=(j == nj - 1)) for j in range(nj)][-1]),
                    waits=[(wv, lwn), (cmpS, add_done[tt])], auto=False)
                mmk = peS.n
                sink(oc, tt, pp, mmk)
                add_done[tt] = cmpS.n
            dma_free[par] = peS.n

    sgw_r = sgw.rearrange("(c p) o -> p c o", p=128)
    sdw_r = sdw.rearrange("(c p) o -> p c o", p=128)
    for half in range(2):
        def hsink_sh(j, tt, pu, slot, umk):
            sc.st("vector", lambda v, j=j, tt=tt, pu=pu, slot=slot: v.tensor_mul(
                sh[:, j, tt * 512:(tt + 1) * 512], ssil[:, slot, :], pu[:]),
                waits=[(cmpS, cmpS.n), (peS, umk)], auto=False)
        pipe_up(11,
                lambda j, half=half: sgw_r[:, :, 2816 + (half * 11 + j) * 128:2816 + (half * 11 + j) * 128 + 128],
                lambda j, half=half: sgw_r[:, :, (half * 11 + j) * 128:(half * 11 + j) * 128 + 128],
                hsink_sh)
        def sink_sh(oc, tt, pp, mmk):
            sc.st("vector", lambda v, oc=oc, tt=tt, pp=pp: v.tensor_add(
                sacc[:, oc, tt * 512:(tt + 1) * 512], sacc[:, oc, tt * 512:(tt + 1) * 512], pp[:]),
                waits=[(peS, mmk)], auto=False)
        pipe_down(11,
                  lambda oc, half=half: sdw_r[:, half * 11:(half + 1) * 11, oc * 128:(oc + 1) * 128],
                  lambda j, tt: sh[:, j, tt * 512:(tt + 1) * 512],
                  sink_sh)

    upw_r = [upw[e].rearrange("(c p) o -> p c o", p=128) for e in range(8)]
    dnw_r = [dnw[e].rearrange("(c p) o -> p c o", p=128) for e in range(8)]
    for e in range(8):
        for ti in range(8):
            def row_e(t, e=e, ti=ti):
                return t.matmul(p_r[0:1, 0:128], sid8[:, e:e + 1], swet[:, ti, :], start=True, stop=True)
            sc.st("tensor", row_e)
            sc.st("scalar", lambda s, ti=ti: s.copy(sweRow[:, ti * 128:(ti + 1) * 128], p_r[0:1, 0:128]))
        for tt in range(2):
            sc.st("tensor", lambda t, tt=tt: t.matmul(p_a[:] if tt == 0 else p_b[:], srow1[:], sweRow[:, tt * 512:(tt + 1) * 512], start=True, stop=True))
        sc.st("scalar", lambda s: s.copy(sweb[:, 0:512], p_a[:]))
        sc.st("scalar", lambda s: s.copy(sweb[:, 512:1024], p_b[:]))
        def hsink_e(j, tt, pu, slot, umk):
            sc.st("vector", lambda v, pu=pu, slot=slot: v.tensor_mul(st2[:], ssil[:, slot, :], pu[:]),
                  waits=[(cmpS, cmpS.n), (peS, umk)], auto=False)
            sc.st("vector", lambda v, j=j, tt=tt: v.tensor_mul(
                sh[:, j, tt * 512:(tt + 1) * 512], st2[:], sweb[:, tt * 512:(tt + 1) * 512]),
                waits=[(cmpS, cmpS.n)], auto=False)
        pipe_up(11,
                lambda j, e=e: upw_r[e][:, :, j * 128:j * 128 + 128],
                lambda j, e=e: upw_r[e][:, :, 1408 + j * 128:1408 + j * 128 + 128],
                hsink_e)
        def sink_e(oc, tt, pp, mmk):
            sc.st("vector", lambda v, oc=oc, tt=tt, pp=pp: v.tensor_add(
                sacc[:, oc, tt * 512:(tt + 1) * 512], sacc[:, oc, tt * 512:(tt + 1) * 512], pp[:]),
                waits=[(peS, mmk)], auto=False)
        pipe_down(11,
                  lambda oc, e=e: dnw_r[e][:, :, oc * 128:(oc + 1) * 128],
                  lambda j, tt: sh[:, j, tt * 512:(tt + 1) * 512],
                  sink_e)

    sc.dma("sync", o_x1.rearrange("(c p) t -> p c t", p=128), sacc[:], stv)
    sc.st("sync", lambda e: None, waits=[(stv, stv.n)])
    sc.run()
    return nc


_L3_CACHE = {}


def run_launch3(inputs, xs_fm, xn_l, att_l, log_l):
    if "nc" not in _L3_CACHE:
        _L3_CACHE["nc"] = build_launch3()
    nc = _L3_CACHE["nc"]
    sgw_t = np.ascontiguousarray(np.asarray(inputs["L0_ffn_shared_gw"], np.float32).T)
    sdw_t = np.ascontiguousarray(np.asarray(inputs["L0_ffn_shared_dw"], np.float32).T)
    upw_t = np.ascontiguousarray(np.asarray(inputs["L0_ffn_up_w"], np.float32).transpose(0, 2, 1))
    dnw_t = np.ascontiguousarray(np.asarray(inputs["L0_ffn_down_w"], np.float32).transpose(0, 2, 1))
    wout = np.ascontiguousarray(np.asarray(inputs["L0_attn_w_out"], np.float32))
    gb = np.asarray(inputs["L0_ffn_gate_bias"], np.float32).reshape(1, 8)
    in_maps = []
    for c in range(NCORES):
        in_maps.append({
            "x_fm": xs_fm[c], "xn_fm": xn_l[c], "att_fm": att_l[c],
            "logits": log_l[c], "gbias": gb, "wout": wout,
            "sgw": sgw_t, "sdw": sdw_t, "upw": upw_t, "dnw": dnw_t,
        })
    res = run_bass_kernel_spmd(nc, in_maps, core_ids=list(range(NCORES)))
    return res.results





def build_launch4():
    nc = bass.Bass("TRN2", target_bir_lowering=False)
    x1 = nc.dram_tensor("x1_fm", [D, T], F32, kind="ExternalInput").ap()
    wq = nc.dram_tensor("wq", [D, D], F32, kind="ExternalInput").ap()
    wk = nc.dram_tensor("wk", [D, 512], F32, kind="ExternalInput").ap()
    wv = nc.dram_tensor("wv", [D, 512], F32, kind="ExternalInput").ap()
    cos1s = nc.dram_tensor("cos1s", [128, T], F32, kind="ExternalInput").ap()
    sin1s = nc.dram_tensor("sin1s", [128, T], F32, kind="ExternalInput").ap()
    o_xn = nc.dram_tensor("xn1_fm", [D, T], BF16, kind="ExternalOutput").ap()
    o_q = nc.dram_tensor("q_fm", [D, T], BF16, kind="ExternalOutput").ap()
    o_k = nc.dram_tensor("k_fm", [512, T], BF16, kind="ExternalOutput").ap()
    o_v = nc.dram_tensor("v_tm", [T, 512], BF16, kind="ExternalOutput").ap()

    ones_col = nc.inline_tensor(np.ones((128, 1), np.float32), name="c_ones_col")
    ones_row = nc.inline_tensor(np.ones((1, 128), np.float32), name="c_ones_row")
    eps_c = nc.inline_tensor(np.full((1, 1), LN_EPS, np.float32), name="c_eps")

    sc = Sched(nc)
    ld = sc.sem("ld")
    ldw = sc.sem("ldw")
    ldq = sc.sem("ldq")
    stv = sc.sem("stv")

    sx1 = nc.alloc_sbuf_tensor("sx1", [128, 16, 512], F32)
    sxn = nc.alloc_sbuf_tensor("sxn", [128, 16, 512], BF16)
    swk = nc.alloc_sbuf_tensor("swk", [128, 16, 512], BF16)
    swv = nc.alloc_sbuf_tensor("swv", [128, 16, 512], BF16)
    swq = nc.alloc_sbuf_tensor("swq", [128, 2, 16, 128], BF16)
    ssq = nc.alloc_sbuf_tensor("ssq", [128, 2, 512], F32)
    srow_stats = nc.alloc_sbuf_tensor("srow_stats", [1, 2, 512], F32)
    srow_ab = nc.alloc_sbuf_tensor("srow_ab", [1, 2, 512], F32)
    sA = nc.alloc_sbuf_tensor("sA", [128, 512], F32)
    sB = nc.alloc_sbuf_tensor("sB", [128, 512], F32)
    sones = nc.alloc_sbuf_tensor("sones", [128, 1], F32)
    srow1 = nc.alloc_sbuf_tensor("srow1", [1, 128], F32)
    seps = nc.alloc_sbuf_tensor("seps", [1, 1], F32)
    scos = nc.alloc_sbuf_tensor("scos", [128, T], F32)
    ssin = nc.alloc_sbuf_tensor("ssin", [128, T], F32)
    srp = nc.alloc_sbuf_tensor("srp", [128, 512], BF16)
    sdn = nc.alloc_sbuf_tensor("sdn", [64, 512], BF16)
    sup = nc.alloc_sbuf_tensor("sup", [128, 512], BF16)
    st1 = nc.alloc_sbuf_tensor("st1", [128, 512], F32)
    st2b = nc.alloc_sbuf_tensor("st2b", [128, 512], F32)
    sprj = nc.alloc_sbuf_tensor("sprj", [128, 512], BF16)

    p_a = nc.alloc_psum_tensor("p_a", [128, 512], F32)
    p_b = nc.alloc_psum_tensor("p_b", [128, 512], F32)
    p_s1 = nc.alloc_psum_tensor("p_s1", [1, 512], F32)
    p_s2 = nc.alloc_psum_tensor("p_s2", [1, 512], F32)
    p_bc = nc.alloc_psum_tensor("p_bc", [128, 512], F32)
    p_bc2 = nc.alloc_psum_tensor("p_bc2", [128, 512], F32)

    sc.dma("sync", sones[:], ones_col.ap(), ld)
    sc.dma("sync", srow1[:], ones_row.ap(), ld)
    sc.dma("sync", seps[:], eps_c.ap(), ld)
    sc.dma("sync", scos[:], cos1s, ld)
    sc.dma("sync", ssin[:], sin1s, ld)
    ld_all = ld.n
    sc.dma("gpsimd", swk[:], wk.rearrange("(c p) o -> p c o", p=128), ldw)
    sc.dma("gpsimd", swv[:], wv.rearrange("(c p) o -> p c o", p=128), ldw)
    ldw_all = ldw.n

    def rope_block(src128, t0, dst_store):
        c0, c1 = scos[0:64, t0:t0 + 512], scos[64:128, t0:t0 + 512]
        s0, s1 = ssin[0:64, t0:t0 + 512], ssin[64:128, t0:t0 + 512]
        sc.ops("vector", [
            lambda v: v.tensor_copy(sdn[:], src128[64:128, :]),
            lambda v: v.tensor_copy(sup[64:128, :], src128[0:64, :]),
            lambda v: v.tensor_mul(st1[0:64, :], src128[0:64, :], c0),
            lambda v: v.tensor_mul(st2b[0:64, :], sdn[:], s0),
            lambda v: v.tensor_sub(srp[0:64, :], st1[0:64, :], st2b[0:64, :]),
            lambda v: v.tensor_mul(st1[64:128, :], src128[64:128, :], c1),
            lambda v: v.tensor_mul(st2b[64:128, :], sup[64:128, :], s1),
            lambda v: v.tensor_add(srp[64:128, :], st1[64:128, :], st2b[64:128, :]),
        ])
        dst_store()

    for th in range(2):
        t0 = th * 512
        sc.dma("sync", sx1[:], x1.rearrange("(c p) t -> p c t", p=128)[:, :, t0:t0 + 512], ld,
               waits=[(stv, stv.n)])
        ldx = ld.n
        accmm(sc, p_s1[:], [(sones[:], sx1[:, c, :]) for c in range(16)], waits=[(ld, ldx)])
        for c in range(16):
            sc.st("vector", lambda v, c=c: v.tensor_mul(ssq[:, c % 2, :], sx1[:, c, :], sx1[:, c, :]),
                  waits=[(ld, ldx)])
            sc.st("tensor", lambda t, c=c: t.matmul(p_s2[:], sones[:], ssq[:, c % 2, :],
                                                    start=(c == 0), stop=(c == 15)))
        sc.ops("vector", [
            lambda v: v.tensor_scalar_mul(srow_stats[:, 0, :], p_s1[:], 1.0 / 2048.0),
            lambda v: v.tensor_scalar_mul(srow_stats[:, 1, :], p_s2[:], 1.0 / 2048.0),
            lambda v: v.tensor_mul(srow_ab[:, 0, :], srow_stats[:, 0, :], srow_stats[:, 0, :]),
            lambda v: v.tensor_sub(srow_ab[:, 1, :], srow_stats[:, 1, :], srow_ab[:, 0, :]),
        ])
        sc.ops("scalar", [
            lambda s: s.activation(srow_ab[:, 0, :], srow_ab[:, 1, :], AF.Ln, bias=seps[:], scale=1.0),
            lambda s: s.activation(srow_ab[:, 0, :], srow_ab[:, 0, :], AF.Exp, bias=0.0, scale=-0.5),
        ], waits=[ld])
        sc.ops("vector", [
            lambda v: v.tensor_mul(srow_ab[:, 1, :], srow_stats[:, 0, :], srow_ab[:, 0, :]),
            lambda v: v.tensor_scalar_mul(srow_ab[:, 1, :], srow_ab[:, 1, :], -1.0),
        ])
        def bc_ab(t):
            t.matmul(p_bc[:], srow1[:], srow_ab[:, 0, :], start=True, stop=True)
            return t.matmul(p_bc2[:], srow1[:], srow_ab[:, 1, :], start=True, stop=True)
        sc.st("tensor", bc_ab)
        def evac_ab(s):
            s.copy(sA[:], p_bc[:])
            return s.copy(sB[:], p_bc2[:])
        sc.st("scalar", evac_ab)
        xn_ops = []
        for c in range(16):
            xn_ops.append(lambda v, c=c: v.tensor_mul(ssq[:, c % 2, :], sx1[:, c, :], sA[:]))
            xn_ops.append(lambda v, c=c: v.tensor_add(sxn[:, c, :], ssq[:, c % 2, :], sB[:]))
        sc.ops("vector", xn_ops, waits=[(stv, stv.n)])
        sc.dma("sync", o_xn.rearrange("(c p) t -> p c t", p=128)[:, :, t0:t0 + 512], sxn[:], stv)

        for ti in range(4):
            accmm(sc, p_a[:], [(sxn[:, c, ti * 128:(ti + 1) * 128], swv[:, c, :]) for c in range(16)],
                  waits=[(ldw, ldw_all)])
            sc.st("scalar", lambda s: s.copy(sprj[:], p_a[:]), waits=[(stv, stv.n)])
            sc.dma("sync", o_v[t0 + ti * 128:t0 + (ti + 1) * 128, :], sprj[:], stv)

        for g in range(4):
            accmm(sc, p_a[:], [(swk[:, c, g * 128:(g + 1) * 128], sxn[:, c, :]) for c in range(16)],
                  waits=[(ldw, ldw_all)])
            sc.st("scalar", lambda s: s.copy(sprj[:], p_a[:]), waits=[(stv, stv.n)])
            def store_k(g=g, t0=t0):
                sc.dma("sync", o_k.rearrange("(g p) t -> p g t", p=128)[:, g, t0:t0 + 512], srp[:], stv)
            rope_block(sprj[:], t0, store_k)

        for h in range(16):
            par = h % 2
            sc.dma("gpsimd", swq[:, par, :, :],
                   wq.rearrange("(c p) o -> p c o", p=128)[:, :, h * 128:(h + 1) * 128], ldq)
            lqn = ldq.n
            accmm(sc, p_a[:], [(swq[:, par, c, :], sxn[:, c, :]) for c in range(16)],
                  waits=[(ldq, lqn)])
            sc.st("scalar", lambda s: s.copy(sprj[:], p_a[:]), waits=[(stv, stv.n)])
            def store_q(h=h, t0=t0):
                sc.dma("sync", o_q.rearrange("(h p) t -> p h t", p=128)[:, h, t0:t0 + 512], srp[:], stv)
            rope_block(sprj[:], t0, store_q)

    sc.st("sync", lambda e: None, waits=[(stv, stv.n)])
    sc.run()
    return nc


_L4_CACHE = {}


def run_launch4(inputs, x1_l):
    if "nc" not in _L4_CACHE:
        _L4_CACHE["nc"] = build_launch4()
    nc = _L4_CACHE["nc"]
    cos1, sin1 = rope1_tables()
    wq = np.ascontiguousarray(np.asarray(inputs["L1_attn_Wq"], np.float32))
    wk = np.ascontiguousarray(np.asarray(inputs["L1_attn_Wk"], np.float32))
    wv = np.ascontiguousarray(np.asarray(inputs["L1_attn_Wv"], np.float32))
    in_maps = []
    for c in range(NCORES):
        pos0 = (c % 4) * T
        in_maps.append({
            "x1_fm": x1_l[c], "wq": wq, "wk": wk, "wv": wv,
            "cos1s": np.ascontiguousarray(cos1[:, pos0:pos0 + T]),
            "sin1s": np.ascontiguousarray(sin1[:, pos0:pos0 + T]),
        })
    res = run_bass_kernel_spmd(nc, in_maps, core_ids=list(range(NCORES)))
    return res.results



SC128 = 1.0 / np.sqrt(128.0)


def build_launch5():
    nc = bass.Bass("TRN2", target_bir_lowering=False)
    qp = nc.dram_tensor("qp", [512, S], BF16, kind="ExternalInput").ap()
    kg = nc.dram_tensor("kg", [128, S], BF16, kind="ExternalInput").ap()
    vg = nc.dram_tensor("vg", [S, 128], BF16, kind="ExternalInput").ap()
    hsp = nc.dram_tensor("hsp", [1, 4], F32, kind="ExternalInput").ap()
    o_attn = nc.dram_tensor("attn", [4, 128, S], BF16, kind="ExternalOutput").ap()

    masks = np.where((np.arange(4)[:, None, None] * 128 + np.arange(128)[None, :, None])
                     > np.arange(512)[None, None, :], NEG, 0.0).astype(np.float32)
    c_masks = nc.inline_tensor(masks.reshape(4 * 128, 512), name="c_masks")
    ones_col = nc.inline_tensor(np.ones((128, 1), np.float32), name="c_ones_col")
    ones_colb = nc.inline_tensor(np.ones((128, 1), np.float32).astype(BF16NP), name="c_ones_colb")
    ones_row = nc.inline_tensor(np.ones((1, 128), np.float32), name="c_ones_row")

    sc = Sched(nc)
    ld = sc.sem("ld")
    ldq = sc.sem("ldq")
    stv = sc.sem("stv")

    sKg = nc.alloc_sbuf_tensor("sKg", [128, S], BF16)
    sV = nc.alloc_sbuf_tensor("sV", [128, 32, 128], BF16)
    sQ = nc.alloc_sbuf_tensor("sQ", [128, S], BF16)
    sPt = nc.alloc_sbuf_tensor("sPt", [128, 32, 512], BF16)
    sOut = nc.alloc_sbuf_tensor("sOut", [128, S], BF16)
    smask = nc.alloc_sbuf_tensor("smask", [128, 4, 512], F32)
    ssq = nc.alloc_sbuf_tensor("ssq", [128, 512], F32)
    stiny = nc.alloc_sbuf_tensor("stiny", [1, 2, 8], F32)
    sscal = nc.alloc_sbuf_tensor("sscal", [1, 8], F32)
    sbias = nc.alloc_sbuf_tensor("sbias", [128, 1], F32)
    sRec = nc.alloc_sbuf_tensor("sRec", [128, 512], F32)
    srd = nc.alloc_sbuf_tensor("srd", [1, 512], F32)
    sones = nc.alloc_sbuf_tensor("sones", [128, 1], F32)
    sonesb = nc.alloc_sbuf_tensor("sonesb", [128, 1], BF16)
    srow1 = nc.alloc_sbuf_tensor("srow1", [1, 128], F32)
    shs = nc.alloc_sbuf_tensor("shs", [1, 4], F32)

    p_s = nc.alloc_psum_tensor("p_s", [128, 512], F32)
    p_s2 = nc.alloc_psum_tensor("p_s2", [128, 512], F32)
    p_o = nc.alloc_psum_tensor("p_o", [128, 512], F32)
    p_d = nc.alloc_psum_tensor("p_d", [1, 512], F32)
    p_bc = nc.alloc_psum_tensor("p_bc", [128, 512], F32)
    p_row = nc.alloc_psum_tensor("p_row", [1, 512], F32)
    p_col = nc.alloc_psum_tensor("p_col", [128, 2], F32)

    sc.dma("sync", sKg[:], kg, ld)
    sc.dma("sync", sV[:], vg.rearrange("(b p) d -> p b d", p=128), ld)
    sc.dma("sync", smask[:], c_masks.ap().rearrange("(m p) t -> p m t", p=128), ld)
    sc.dma("sync", sones[:], ones_col.ap(), ld)
    sc.dma("sync", sonesb[:], ones_colb.ap(), ld)
    sc.dma("sync", srow1[:], ones_row.ap(), ld)
    sc.dma("sync", shs[:], hsp, ld)
    ld_all = ld.n

    for kb8 in range(8):
        sc.st("vector", lambda v, kb8=kb8: v.tensor_mul(ssq[:], sKg[:, kb8 * 512:(kb8 + 1) * 512], sKg[:, kb8 * 512:(kb8 + 1) * 512]),
              waits=[(ld, ld_all)])
        accmm(sc, p_row[:], [(sones[:], ssq[:])])
        sc.st("vector", lambda v, kb8=kb8: v.tensor_reduce(stiny[0:1, 0, kb8:kb8 + 1], p_row[:], AX.X, ALU.max))

    for u in range(4):
        sc.dma("sync", sQ[:], qp[u * 128:(u + 1) * 128, :], ldq, waits=[(stv, stv.n)])
        lqn = ldq.n
        for qb in range(8):
            sc.st("vector", lambda v, qb=qb: v.tensor_mul(ssq[:], sQ[:, qb * 512:(qb + 1) * 512], sQ[:, qb * 512:(qb + 1) * 512]),
                  waits=[(ldq, lqn)])
            accmm(sc, p_row[:], [(sones[:], ssq[:])])
            sc.st("vector", lambda v, qb=qb: v.tensor_reduce(stiny[0:1, 1, qb:qb + 1], p_row[:], AX.X, ALU.max))
        sc.ops("vector", [
            lambda v: v.tensor_reduce(sscal[0:1, 0:1], stiny[0:1, 0, :], AX.X, ALU.max),
            lambda v: v.tensor_reduce(sscal[0:1, 1:2], stiny[0:1, 1, :], AX.X, ALU.max),
            lambda v: v.tensor_mul(sscal[0:1, 2:3], sscal[0:1, 0:1], sscal[0:1, 1:2]),
        ])
        sc.ops("scalar", [
            lambda s: s.activation(sscal[0:1, 3:4], sscal[0:1, 2:3], AF.Ln, bias=0.0, scale=1.0),
            lambda s: s.activation(sscal[0:1, 3:4], sscal[0:1, 3:4], AF.Exp, bias=0.0, scale=0.5),
        ])
        sc.st("vector", lambda v: v.tensor_scalar_mul(sscal[0:1, 4:5], sscal[0:1, 3:4], -SC128))
        sc.st("tensor", lambda t: t.matmul(p_col[:, 0:1], srow1[:], sscal[0:1, 4:5], start=True, stop=True))
        sc.st("scalar", lambda s: s.copy(sbias[:], p_col[:, 0:1]))

        def recip_hs(v, u=u):
            v.reciprocal(srd[:], p_d[:])
            return v.tensor_scalar(srd[:], srd[:], shs[0:1, u:u + 1], None, ALU.mult)
        emit_flash_pipelined(
            sc, range(8), lambda qc: 4 * (qc + 1),
            lambda qc, kb, ps: [(sKg[:, kb * 128:(kb + 1) * 128], sQ[:, qc * 512:(qc + 1) * 512])],
            lambda qc, kb: smask[:, kb % 4, :] if kb // 4 == qc else None,
            lambda qc, kb: sPt[:, kb, :], sbias[:], SC128,
            lambda qc, nkb: [(sV[:, kb, :], sPt[:, kb, :]) for kb in range(nkb)],
            lambda qc, nkb: [(sonesb[:], sPt[:, kb, :]) for kb in range(nkb)],
            recip_hs,
            lambda t: t.matmul(p_bc[:], srow1[:], srd[:], start=True, stop=True),
            lambda s: s.copy(sRec[:], p_bc[:]),
            lambda v, qc: v.tensor_mul(sOut[:, qc * 512:(qc + 1) * 512], p_o[:], sRec[:]),
            (p_s, p_s2, p_o, p_d))
        sc.dma("sync", o_attn[u], sOut[:], stv)

    sc.st("sync", lambda e: None, waits=[(stv, stv.n)])
    sc.run()
    return nc


_L5_CACHE = {}


def run_launch5(inputs, q_full, k_full, v_full):
    if "nc" not in _L5_CACHE:
        _L5_CACHE["nc"] = build_launch5()
    nc = _L5_CACHE["nc"]
    hs = np.asarray(inputs["L1_attn_head_scale"], np.float32).reshape(16)
    in_maps = []
    for c in range(NCORES):
        s = c // 4
        g = c % 4
        in_maps.append({
            "qp": np.ascontiguousarray(q_full[s][g * 512:(g + 1) * 512, :]),
            "kg": np.ascontiguousarray(k_full[s][g * 128:(g + 1) * 128, :]),
            "vg": np.ascontiguousarray(v_full[s][:, g * 128:(g + 1) * 128]),
            "hsp": np.ascontiguousarray(hs[g * 4:(g + 1) * 4].reshape(1, 4)),
        })
    res = run_bass_kernel_spmd(nc, in_maps, core_ids=list(range(NCORES)))
    return res.results





RT2048 = float(np.sqrt(2048.0))
RT8192 = float(np.sqrt(8192.0))


def _hadamard128():
    h = np.array([[1.0]], np.float32)
    for _ in range(7):
        h = np.block([[h, h], [h, -h]])
    return h


def build_launch6():
    nc = bass.Bass("TRN2", target_bir_lowering=False)
    x1 = nc.dram_tensor("x1_fm", [D, T], F32, kind="ExternalInput").ap()
    xn1 = nc.dram_tensor("xn1_fm", [D, T], BF16, kind="ExternalInput").ap()
    att = nc.dram_tensor("att_fm", [D, T], BF16, kind="ExternalInput").ap()
    wo = nc.dram_tensor("wo", [D, D], F32, kind="ExternalInput").ap()
    wg = nc.dram_tensor("wg", [D, 8192], F32, kind="ExternalInput").ap()
    wu = nc.dram_tensor("wu", [D, 8192], F32, kind="ExternalInput").ap()
    wd = nc.dram_tensor("wd", [8192, D], F32, kind="ExternalInput").ap()
    o_x2 = nc.dram_tensor("x2_fm", [D, T], F32, kind="ExternalOutput").ap()
    h2scr = nc.dram_tensor("h2scr", [64, 128, T], BF16, kind="Internal").ap()
    tdscr = nc.dram_tensor("tdscr", [64, 128, D], BF16, kind="Internal").ap()

    ones_col = nc.inline_tensor(np.ones((128, 1), np.float32), name="c_ones_col")
    ones_row = nc.inline_tensor(np.ones((1, 128), np.float32), name="c_ones_row")
    magic_col = nc.inline_tensor(np.full((128, 1), MAGIC, np.float32), name="c_magic")
    h128 = nc.inline_tensor(_hadamard128().astype(BF16NP), name="c_h128")
    id128 = nc.inline_tensor(np.eye(128, dtype=np.float32), name="c_id128")

    sc = Sched(nc)
    cmpS, peS = sc.cmp, sc.pe
    ld = sc.sem("ld")
    ldw = sc.sem("ldw")
    ldw2 = sc.sem("ldw2")
    sth = sc.sem("sth")
    sthA = sc.sem("sthA")
    sthB = sc.sem("sthB")
    scnA = sc.sem("scnA")
    scnB = sc.sem("scnB")
    ldxA = sc.sem("ldxA")
    ldxB = sc.sem("ldxB")
    ldh = sc.sem("ldh")
    stv = sc.sem("stv")

    sU = nc.alloc_sbuf_tensor("sU", [128, 64, 512], BF16)
    sF1 = nc.alloc_sbuf_tensor("sF1", [128, 64, 128], BF16)
    sF2 = nc.alloc_sbuf_tensor("sF2", [128, 64, 128], BF16)
    swdb = nc.alloc_sbuf_tensor("swdb", [128, 2, 32, 128], BF16)
    sTg = nc.alloc_sbuf_tensor("sTg", [128, 2, 16, 128], BF16)
    sTu = nc.alloc_sbuf_tensor("sTu", [128, 2, 16, 128], BF16)
    sqz = nc.alloc_sbuf_tensor("sqz", [128, 2, 2048], F32)
    sgt = nc.alloc_sbuf_tensor("sgt", [128, 2, 512], F32)
    sut = nc.alloc_sbuf_tensor("sut", [128, 2, 512], F32)
    ssl = nc.alloc_sbuf_tensor("ssl", [128, 2, 512], F32)
    sh2 = nc.alloc_sbuf_tensor("sh2", [128, 2, 512], BF16)
    sMxF = nc.alloc_sbuf_tensor("sMxF", [128, 512], F32)
    sAbs = nc.alloc_sbuf_tensor("sAbs", [128, 512], F32)
    sMxC = nc.alloc_sbuf_tensor("sMxC", [128, 4], F32)
    srowm = nc.alloc_sbuf_tensor("srowm", [1, 8], F32)
    srowq = nc.alloc_sbuf_tensor("srowq", [1, 2, 512], F32)
    sQs = nc.alloc_sbuf_tensor("sQs", [128, 1, 512], BF16)
    sRg = nc.alloc_sbuf_tensor("sRg", [128, 2, 512], F32)
    sRu = nc.alloc_sbuf_tensor("sRu", [128, 2, 512], F32)
    sRd = nc.alloc_sbuf_tensor("sRd", [128, 512], F32)
    sAcc = nc.alloc_sbuf_tensor("sAcc", [128, 3], F32)
    sInv = nc.alloc_sbuf_tensor("sInv", [128, 3], F32)
    sScl = nc.alloc_sbuf_tensor("sScl", [128, 3], F32)
    sredc = nc.alloc_sbuf_tensor("sredc", [128, 1], F32)
    sones = nc.alloc_sbuf_tensor("sones", [128, 1], F32)
    srow1 = nc.alloc_sbuf_tensor("srow1", [1, 128], F32)
    smgc = nc.alloc_sbuf_tensor("smgc", [128, 1], F32)
    sH = nc.alloc_sbuf_tensor("sH", [128, 128], BF16)
    sid = nc.alloc_sbuf_tensor("sid", [128, 128], F32)
    sx1s = sF1.bitcast(F32)[:].rearrange("p a b -> p (a b)")[:, 0:2048].rearrange("p (a b) -> p a b", b=T)
    sot = sF2.bitcast(F32)[:].rearrange("p a b -> p (a b)")[:, 0:2048].rearrange("p (a b) -> p a b", b=T)
    sxq = sU[:, 32:64, :].rearrange("p a b -> p (a b)").rearrange("p (c t) -> p c t", t=1024)

    p_a = nc.alloc_psum_tensor("p_a", [128, 512], F32)
    p_b = nc.alloc_psum_tensor("p_b", [128, 512], F32)
    p_c = nc.alloc_psum_tensor("p_c", [128, 512], F32)
    p_d = nc.alloc_psum_tensor("p_d", [128, 512], F32)
    p_e = nc.alloc_psum_tensor("p_e", [128, 512], F32)
    p_t = nc.alloc_psum_tensor("p_t", [128, 128], F32)
    p_bc = nc.alloc_psum_tensor("p_bc", [128, 512], F32)

    sc.dma("sync", sones[:], ones_col.ap(), ld)
    sc.dma("sync", srow1[:], ones_row.ap(), ld)
    sc.dma("sync", smgc[:], magic_col.ap(), ld)
    sc.dma("sync", sH[:], h128.ap(), ld)
    sc.dma("sync", sid[:], id128.ap(), ld)
    ld_all = ld.n

    wg_r = wg.rearrange("(c p) o -> p c o", p=128)
    wu_r = wu.rearrange("(c p) o -> p c o", p=128)
    wd_r = wd.rearrange("(c p) o -> p c o", p=128)

    sc.cur_sect = "scan"
    scan_view = sU[:].rearrange("p a b -> p (a b)")
    tiles = ([("g", wg_r[:, c, :], 8192) for c in range(16)]
             + [("u", wu_r[:, c, :], 8192) for c in range(16)]
             + [("d", wd_r[:, c, :], 2048) for c in range(64)])
    mi = {"g": 0, "u": 1, "d": 2}
    first = {0: True, 1: True, 2: True}
    red_done = {0: cmpS.n, 1: cmpS.n}
    for i, (mm, ap, width) in enumerate(tiles):
        par = i % 2
        wv = (scnA, scnB)[par]
        dst = scan_view[:, par * 16384:par * 16384 + width]
        sc.dma("gpsimd", dst, ap, wv, waits=[(cmpS, red_done[par])])
        lwn = wv.n
        m = mi[mm]
        if first[m]:
            sc.st("vector", lambda v, m=m, dst=dst: v.tensor_reduce(
                sAcc[:, m:m + 1], dst, AX.X, ALU.add, apply_absolute_value=True),
                waits=[(wv, lwn)], auto=False)
            first[m] = False
        else:
            sc.st("vector", lambda v, dst=dst: v.tensor_reduce(
                sredc[:], dst, AX.X, ALU.add, apply_absolute_value=True),
                waits=[(wv, lwn)], auto=False)
            sc.st("vector", lambda v, m=m: v.tensor_add(sAcc[:, m:m + 1], sAcc[:, m:m + 1], sredc[:]),
                  auto=False)
        red_done[par] = cmpS.n
    accmm(sc, p_t[0:1, 0:3], [(sones[:], sAcc[:])])
    sc.st("scalar", lambda s: s.copy(srowm[0:1, 0:3], p_t[0:1, 0:3]))
    sc.ops("vector", [
        lambda v: v.tensor_scalar(srowm[0:1, 0:3], srowm[0:1, 0:3], 1.0 / 16777216.0, 1e-5, ALU.mult, ALU.max),
        lambda v: v.reciprocal(srowm[0:1, 3:6], srowm[0:1, 0:3]),
    ])
    for m in range(3):
        sc.st("tensor", lambda t, m=m: t.matmul(p_t[:, 0:1], srow1[:], srowm[0:1, 3 + m:4 + m], start=True, stop=True))
        sc.st("scalar", lambda s, m=m: s.copy(sInv[:, m:m + 1], p_t[:, 0:1]))
        sc.st("tensor", lambda t, m=m: t.matmul(p_t[:, 0:1], srow1[:], srowm[0:1, m:m + 1], start=True, stop=True))
        sc.st("scalar", lambda s, m=m: s.copy(sScl[:, m:m + 1], p_t[:, 0:1]))

    sc.cur_sect = "phaseA"
    satt = sU[:].rearrange("p a b -> p (a b)")[:, 0:16384].rearrange("p (c t) -> p c t", t=T)
    sc.dma("sync", satt, att.rearrange("(c p) t -> p c t", p=128), ld)
    ld_att = ld.n
    evac_done = {0: cmpS.n, 1: cmpS.n}
    dmaA_free = {0: peS.n, 1: peS.n}
    for oc in range(16):
        par = oc % 2
        xv = (ldxA, ldxB)[par]
        wv = (ldw, ldw2)[par]
        sc.dma("sync", sx1s[:, par, :], x1.rearrange("(c p) t -> p c t", p=128)[:, oc, :], xv,
               waits=[(stv, stv.n), (cmpS, evac_done[par])])
        sc.dma("gpsimd", sTg[:, par, :, :],
               wo.rearrange("(c p) o -> p c o", p=128)[:, :, oc * 128:(oc + 1) * 128], wv,
               waits=[(peS, dmaA_free[par])])
        ldn, lwn = xv.n, wv.n
        for tt in range(2):
            pp = (p_a, p_b, p_c, p_d)[par * 2 + tt]
            sc.st("tensor", (lambda t, par=par, tt=tt, pp=pp: [
                t.matmul(pp[:], sTg[:, par, hc, :], satt[:, hc, tt * 512:(tt + 1) * 512],
                         start=(hc == 0), stop=(hc == 15)) for hc in range(16)][-1]),
                waits=[(xv, ldn), (wv, lwn), (ld, ld_att), (cmpS, evac_done[par])], auto=False)
            mmk = peS.n
            sc.st("vector", lambda v, par=par, tt=tt, pp=pp: v.tensor_add(
                sot[:, par, tt * 512:(tt + 1) * 512], pp[:], sx1s[:, par, tt * 512:(tt + 1) * 512]),
                waits=[(peS, mmk), (stv, stv.n)], auto=False)
        evac_done[par] = cmpS.n
        sc.dma("sync", o_x2.rearrange("(c p) t -> p c t", p=128)[:, oc, :], sot[:, par, :], stv)
        dmaA_free[par] = peS.n
    stv_phaseA = stv.n

    sc.cur_sect = "td"
    tdq_done = {0: cmpS.n, 1: cmpS.n}
    tds_free = {0: sthA.n, 1: sthB.n}
    for c2 in range(64):
        par = c2 % 2
        wv = (ldw, ldw2)[par]
        sc.dma("sync", sqz[:, par, :].rearrange("p (c o) -> p c o", o=128),
               wd_r[:, c2, :].rearrange("p (c o) -> p c o", o=128), wv,
               waits=[(cmpS, tdq_done[par])])
        lwn = wv.n
        sc.st("scalar", lambda s, par=par: s.activation(
            sqz[:, par, :], sqz[:, par, :], AF.Identity, bias=smgc[:], scale=sInv[:, 2:3]),
            waits=[(wv, lwn)], auto=False)
        sc.st("vector", lambda v, par=par: v.tensor_scalar(
            sqz[:, par, :], sqz[:, par, :], MAGIC, 1.0, ALU.subtract, ALU.min),
            waits=[(cmpS, cmpS.n)], auto=False)
        hv = (sthA, sthB)[par]
        sc.st("vector", lambda v, par=par: v.tensor_scalar(
            swdb[:, par, 0:16, :].rearrange("p c o -> p (c o)"),
            sqz[:, par, :], -1.0, None, ALU.max),
            waits=[(hv, tds_free[par])], auto=False)
        tdq_done[par] = cmpS.n
        sc.dma("sync", tdscr[c2], swdb[:, par, 0:16, :].rearrange("p c o -> p (c o)"), hv)
        tds_free[par] = hv.n

    sc.cur_sect = "fw2"
    def butterfly_stages(src3, bufA3, bufB3, nch, fd):
        nst = int(np.log2(nch))
        cur = src3
        bufs = [bufA3, bufB3]
        for st_i in range(nst):
            h = 1 << st_i
            out = bufs[st_i % 2]
            ina = cur.rearrange("p (g two h) f -> p g two h f", two=2, h=h)
            outa = out.rearrange("p (g two h) f -> p g two h f", two=2, h=h)
            sc.st("vector", lambda v, ina=ina, outa=outa: v.tensor_add(
                outa[:, :, 0, :, :], ina[:, :, 0, :, :], ina[:, :, 1, :, :]))
            sc.st("vector", lambda v, ina=ina, outa=outa: v.tensor_sub(
                outa[:, :, 1, :, :], ina[:, :, 0, :, :], ina[:, :, 1, :, :]))
            cur = out
        return cur

    for th in range(2):
        t0 = th * 512
        fA = sF1[:].rearrange("p a b -> p (a b)").rearrange("p (c t) -> p c t", t=512)
        fB = sF2[:].rearrange("p a b -> p (a b)").rearrange("p (c t) -> p c t", t=512)
        sc.dma("sync", fB, xn1.rearrange("(c p) t -> p c t", p=128)[:, :, t0:t0 + 512], ldh)
        lxn6 = ldh.n
        sc.st("vector", lambda v: v.engine_nop(), waits=[(ldh, lxn6)])
        fin = butterfly_stages(fB, fA, fB, 16, 512)
        sfv = fA
        abs_done = cmpS.n
        val_done = cmpS.n
        for c in range(16):
            sc.st("tensor", lambda t, c=c, fin=fin: t.matmul(p_a[:] if c % 2 == 0 else p_b[:], sH[:], fin[:, c, :], start=True, stop=True),
                  waits=[(cmpS, max(abs_done, val_done) if c >= 2 else cmpS.n)], auto=(c < 2))
            mmk = peS.n
            pp = p_a if c % 2 == 0 else p_b
            sc.st("scalar", lambda s, pp=pp: s.activation(sAbs[:], pp[:], AF.Abs, bias=0.0, scale=1.0),
                  waits=[(peS, mmk), (cmpS, abs_done)], auto=False)
            sc.st("scalar", lambda s, c=c, sfv=sfv, pp=pp: s.copy(sfv[:, c, :], pp[:]), auto=False)
            val_done = cmpS.n
            if c == 0:
                sc.st("vector", lambda v: v.tensor_copy(sMxF[:], sAbs[:]), waits=[(cmpS, cmpS.n - 1)], auto=False)
            else:
                sc.st("vector", lambda v: v.tensor_max(sMxF[:], sMxF[:], sAbs[:]), waits=[(cmpS, cmpS.n - 1)], auto=False)
            abs_done = cmpS.n
        for ti in range(4):
            sc.st("tensor", lambda t, ti=ti: t.transpose(p_t[:], sMxF[:, ti * 128:(ti + 1) * 128], sid[:]))
            sc.st("vector", lambda v, ti=ti: v.tensor_reduce(sMxC[:, ti:ti + 1], p_t[:], AX.X, ALU.max))
        for ti in range(4):
            sc.st("tensor", lambda t, ti=ti: t.matmul(p_t[0:1, 0:128], sMxC[:, ti:ti + 1], sid[:], start=True, stop=True))
            sc.st("scalar", lambda s, ti=ti: s.copy(srowq[0:1, 0, ti * 128:(ti + 1) * 128], p_t[0:1, 0:128]))
        sc.ops("vector", [
            lambda v: v.tensor_scalar_max(srowq[:, 0, :], srowq[:, 0, :], 1e-5 * RT2048),
            lambda v: v.tensor_scalar_mul(srowq[:, 1, :], srowq[:, 0, :], 1.0 / (127.0 * RT2048)),
            lambda v: v.reciprocal(srowq[:, 0, :], srowq[:, 0, :]),
            lambda v: v.tensor_scalar_mul(srowq[:, 0, :], srowq[:, 0, :], 127.0),
        ])
        sc.st("tensor", lambda t: t.matmul(p_bc[:], srow1[:], srowq[0:1, 0, :], start=True, stop=True))
        sc.st("scalar", lambda s: s.copy(sQs[:, 0, :], p_bc[:]))
        sc.st("tensor", lambda t: t.matmul(p_bc[:], srow1[:], srowq[0:1, 1, :], start=True, stop=True))
        sc.st("vector", lambda v, th=th: v.tensor_scalar(sRg[:, th, :], p_bc[:], sScl[:, 0:1], None, ALU.mult))
        sc.st("vector", lambda v, th=th: v.tensor_scalar(sRu[:, th, :], p_bc[:], sScl[:, 1:2], None, ALU.mult))
        for g4 in range(4):
            cs = slice(g4 * 4, g4 * 4 + 4)
            qv = sqz[:, 0, :].rearrange("p (c t) -> p c t", t=512)[:, 0:4, :]
            def qmul(v, cs=cs, qv=qv, sfv=sfv):
                a, b = bass.broadcast_tensor_aps(sfv[:, cs, :], sQs[:])
                return v.tensor_tensor(qv, a, b, ALU.mult)
            sc.st("vector", qmul)
            sc.st("scalar", lambda s, qv=qv: s.activation(qv, qv, AF.Identity, bias=smgc[:], scale=1.0))
            sc.st("vector", lambda v, qv=qv: v.tensor_scalar(qv, qv, MAGIC, 127.0, ALU.subtract, ALU.min))
            sc.st("vector", lambda v, cs=cs, qv=qv, t0=t0: v.tensor_scalar(
                sxq[:, cs, t0:t0 + 512], qv, -128.0, None, ALU.max))

    sc.cur_sect = "gu"
    qz_done = {0: cmpS.n, 1: cmpS.n}
    mm_done = {0: cmpS.n, 1: cmpS.n}
    wdma_free = {0: peS.n, 1: peS.n}
    h2_free = {0: sthA.n, 1: sthB.n}
    for hc in range(64):
        par = hc % 2
        o0 = hc * 128
        wv = (ldw, ldw2)[par]
        sc.dma("sync", sqz[:, par, :].rearrange("p (c o) -> p c o", o=128), wg_r[:, :, o0:o0 + 128], wv,
               waits=[(cmpS, qz_done[par])])
        lwn_g = wv.n
        sc.st("scalar", lambda s, par=par: s.activation(
            sqz[:, par, :], sqz[:, par, :], AF.Identity, bias=smgc[:], scale=sInv[:, 0:1]),
            waits=[(wv, lwn_g)], auto=False)
        sc.st("vector", lambda v, par=par: v.tensor_scalar(sqz[:, par, :], sqz[:, par, :], MAGIC, 1.0, ALU.subtract, ALU.min),
              waits=[(cmpS, cmpS.n)], auto=False)
        sc.st("vector", lambda v, par=par: v.tensor_scalar(
            sTg[:, par, :, :].rearrange("p c o -> p (c o)"), sqz[:, par, :], -1.0, None, ALU.max),
            waits=[(peS, wdma_free[par])], auto=False)
        sc.dma("sync", sqz[:, par, :].rearrange("p (c o) -> p c o", o=128), wu_r[:, :, o0:o0 + 128], wv,
               waits=[(cmpS, cmpS.n)])
        lwn_u = wv.n
        sc.st("scalar", lambda s, par=par: s.activation(
            sqz[:, par, :], sqz[:, par, :], AF.Identity, bias=smgc[:], scale=sInv[:, 1:2]),
            waits=[(wv, lwn_u)], auto=False)
        sc.st("vector", lambda v, par=par: v.tensor_scalar(sqz[:, par, :], sqz[:, par, :], MAGIC, 1.0, ALU.subtract, ALU.min),
              waits=[(cmpS, cmpS.n)], auto=False)
        sc.st("vector", lambda v, par=par: v.tensor_scalar(
            sTu[:, par, :, :].rearrange("p c o -> p (c o)"), sqz[:, par, :], -1.0, None, ALU.max),
            waits=[(peS, wdma_free[par])], auto=False)
        qz_done[par] = cmpS.n
        tq_mark = cmpS.n
        for th in range(2):
            pg = (p_a, p_c)[th]
            pu = (p_b, p_d)[th]
            t0 = th * 512
            sc.st("tensor", (lambda t, par=par, th=th, pg=pg, t0=t0: [
                t.matmul(pg[:], sTg[:, par, c, :], sxq[:, c, t0:t0 + 512],
                         start=(c == 0), stop=(c == 15)) for c in range(16)][-1]),
                waits=[(cmpS, tq_mark), (cmpS, mm_done[th])], auto=False)
            gmk = peS.n
            sc.st("tensor", (lambda t, par=par, th=th, pu=pu, t0=t0: [
                t.matmul(pu[:], sTu[:, par, c, :], sxq[:, c, t0:t0 + 512],
                         start=(c == 0), stop=(c == 15)) for c in range(16)][-1]),
                waits=[], auto=False)
            umk = peS.n
            sc.st("vector", lambda v, th=th, pg=pg: v.tensor_mul(sgt[:, th, :], pg[:], sRg[:, th, :]),
                  waits=[(peS, gmk), (cmpS, mm_done[th])], auto=False)
            sc.st("scalar", lambda s, th=th: s.activation(ssl[:, th, :], sgt[:, th, :], AF.Silu, bias=0.0, scale=1.0),
                  waits=[(cmpS, cmpS.n)], auto=False)
            sc.st("vector", lambda v, th=th, pu=pu: v.tensor_mul(sut[:, th, :], pu[:], sRu[:, th, :]),
                  waits=[(peS, umk)], auto=False)
            hv = (sthA, sthB)[th]
            sc.st("vector", lambda v, th=th: v.tensor_mul(sh2[:, th, :], ssl[:, th, :], sut[:, th, :]),
                  waits=[(cmpS, cmpS.n), (hv, h2_free[th])], auto=False)
            mm_done[th] = cmpS.n
            sc.dma("sync", h2scr.rearrange("h p t -> p h t")[:, hc, t0:t0 + 512], sh2[:, th, :], hv)
            h2_free[th] = hv.n
        wdma_free[par] = peS.n

    sc.cur_sect = "fw8"
    for th in range(2):
        t0 = th * 512
        for tq in range(4):
            tq0 = t0 + tq * 128
            sc.dma("sync", sF1[:], h2scr.rearrange("h p t -> p h t")[:, :, tq0:tq0 + 128], ldh,
                   waits=[(sthA, sthA.n), (sthB, sthB.n), (stv, stv.n)])
            ldh_n = ldh.n
            sc.st("vector", lambda v: v.engine_nop(), waits=[(ldh, ldh_n)])
            fin8 = butterfly_stages(sF1[:], sF2[:], sF1[:], 64, 128)
            other = sF2[:]
            abs_done = cmpS.n
            val_done = cmpS.n
            for g4 in range(16):
                pp = p_a if g4 % 2 == 0 else p_b
                sc.st("tensor", lambda t, g4=g4, fin8=fin8, pp=pp: t.matmul(
                    pp[:], sH[:], fin8[:, g4 * 4:(g4 + 1) * 4, :].rearrange("p c t -> p (c t)"),
                    start=True, stop=True),
                    waits=[(cmpS, max(abs_done, val_done)) if g4 >= 2 else (cmpS, cmpS.n)], auto=(g4 < 2))
                mmk = peS.n
                sc.st("scalar", lambda s, pp=pp: s.activation(sAbs[:], pp[:], AF.Abs, bias=0.0, scale=1.0),
                      waits=[(peS, mmk), (cmpS, abs_done)], auto=False)
                sc.st("scalar", lambda s, g4=g4, other=other, pp=pp: s.copy(
                    other[:, g4 * 4:(g4 + 1) * 4, :].rearrange("p c t -> p (c t)"), pp[:]), auto=False)
                val_done = cmpS.n
                if g4 == 0:
                    sc.st("vector", lambda v: v.tensor_copy(sMxF[:], sAbs[:]), waits=[(cmpS, cmpS.n - 1)], auto=False)
                else:
                    sc.st("vector", lambda v: v.tensor_max(sMxF[:], sMxF[:], sAbs[:]), waits=[(cmpS, cmpS.n - 1)], auto=False)
                abs_done = cmpS.n
            sc.ops("vector", [
                lambda v: v.tensor_max(sMxF[:, 0:128], sMxF[:, 0:128], sMxF[:, 128:256]),
                lambda v: v.tensor_max(sMxF[:, 256:384], sMxF[:, 256:384], sMxF[:, 384:512]),
                lambda v: v.tensor_max(sMxF[:, 0:128], sMxF[:, 0:128], sMxF[:, 256:384]),
            ])
            sc.st("tensor", lambda t: t.transpose(p_t[:], sMxF[:, 0:128], sid[:]))
            sc.st("vector", lambda v: v.tensor_reduce(sMxC[:, 0:1], p_t[:], AX.X, ALU.max))
            sc.st("tensor", lambda t: t.matmul(p_t[0:1, 0:128], sMxC[:, 0:1], sid[:], start=True, stop=True))
            sc.st("scalar", lambda s: s.copy(srowq[0:1, 0, 0:128], p_t[0:1, 0:128]))
            sc.ops("vector", [
                lambda v: v.tensor_scalar_max(srowq[:, 0, 0:128], srowq[:, 0, 0:128], 1e-5 * RT8192),
                lambda v: v.tensor_scalar_mul(srowq[:, 1, 0:128], srowq[:, 0, 0:128], 1.0 / (127.0 * RT8192)),
                lambda v: v.reciprocal(srowq[:, 0, 0:128], srowq[:, 0, 0:128]),
                lambda v: v.tensor_scalar_mul(srowq[:, 0, 0:128], srowq[:, 0, 0:128], 127.0),
            ])
            sc.st("tensor", lambda t: t.matmul(p_bc[:, 0:128], srow1[:], srowq[0:1, 0, 0:128], start=True, stop=True))
            sc.st("scalar", lambda s: s.copy(sQs[:, 0, 0:128], p_bc[:, 0:128]))
            sc.st("tensor", lambda t: t.matmul(p_bc[:, 0:128], srow1[:], srowq[0:1, 1, 0:128], start=True, stop=True))
            sc.st("vector", lambda v, tq=tq: v.tensor_scalar(sRd[:, tq * 128:(tq + 1) * 128], p_bc[:, 0:128], sScl[:, 2:3], None, ALU.mult))
            for g16 in range(4):
                cs = slice(g16 * 16, g16 * 16 + 16)
                qv = sqz[:, 0, :].rearrange("p (c t) -> p c t", t=128)[:, 0:16, :]
                def qmul8(v, cs=cs, qv=qv, other=other):
                    a, b = bass.broadcast_tensor_aps(other[:, cs, :], sQs[:, :, 0:128])
                    return v.tensor_tensor(qv, a, b, ALU.mult)
                sc.st("vector", qmul8)
                sc.st("scalar", lambda s, qv=qv: s.activation(qv, qv, AF.Identity, bias=smgc[:], scale=1.0))
                sc.st("vector", lambda v, qv=qv: v.tensor_scalar(qv, qv, MAGIC, 127.0, ALU.subtract, ALU.min))
                sc.st("vector", lambda v, cs=cs, qv=qv, tq=tq: v.tensor_scalar(
                    sU[:, cs, tq * 128:(tq + 1) * 128], qv, -128.0, None, ALU.max))

        sc.cur_sect = "down"
        sc.st("sync", lambda e: None, waits=[(sthA, sthA.n), (sthB, sthB.n)])
        acc_done = {0: cmpS.n, 1: cmpS.n}
        mm_free = {0: peS.n, 1: peS.n}
        for oc in range(16):
            par = oc % 2
            wv = (ldw, ldw2)[par]
            sc.dma("sync", swdb[:, par, :, :],
                   tdscr.rearrange("c p o -> p c o")[:, 0:32, oc * 128:(oc + 1) * 128], wv,
                   waits=[(peS, mm_free[par])])
            lwn = wv.n
            sc.st("tensor", (lambda t, par=par: [
                t.matmul(p_e[:], swdb[:, par, c2, :], sU[:, c2, :],
                         start=(c2 == 0), stop=False) for c2 in range(32)][-1]),
                waits=[(wv, lwn), (cmpS, acc_done[par])], auto=False)
            wv2 = (ldw2, ldw)[par]
            sc.dma("sync", swdb[:, 1 - par, :, :],
                   tdscr.rearrange("c p o -> p c o")[:, 32:64, oc * 128:(oc + 1) * 128], wv2,
                   waits=[(peS, mm_free[1 - par])])
            lwn2 = wv2.n
            sc.st("tensor", (lambda t, par=par: [
                t.matmul(p_e[:], swdb[:, 1 - par, c2, :], sU[:, 32 + c2, :],
                         start=False, stop=(c2 == 31)) for c2 in range(32)][-1]),
                waits=[(wv2, lwn2)], auto=False)
            mm_free[par] = peS.n
            mm_free[1 - par] = peS.n
            mmk = peS.n
            sc.st("vector", lambda v, par=par: v.tensor_mul(sot[:, par, 0:512], p_e[:], sRd[:]),
                  waits=[(peS, mmk), (stv, stv.n)], auto=False)
            acc_done[par] = cmpS.n
            acc_done[1 - par] = cmpS.n
            sc.dma("gpsimd", o_x2.rearrange("(c p) t -> p c t", p=128)[:, oc, t0:t0 + 512],
                   sot[:, par, 0:512], stv, waits=[(stv, stv_phaseA)], accum=True)

    sc.st("sync", lambda e: None, waits=[(stv, stv.n), (sthA, sthA.n), (sthB, sthB.n)])
    sc.run()
    return nc


_L6_CACHE = {}


def run_launch6(inputs, x1_l, xn1_l, att1_l):
    if "nc" not in _L6_CACHE:
        _L6_CACHE["nc"] = build_launch6()
    nc = _L6_CACHE["nc"]
    wo = np.ascontiguousarray(np.asarray(inputs["L1_attn_Wo"], np.float32))
    wg_t = np.ascontiguousarray(np.asarray(inputs["L1_ffn_w_gate"], np.float32).T)
    wu_t = np.ascontiguousarray(np.asarray(inputs["L1_ffn_w_up"], np.float32).T)
    wd_t = np.ascontiguousarray(np.asarray(inputs["L1_ffn_w_down"], np.float32).T)
    in_maps = []
    for c in range(NCORES):
        in_maps.append({
            "x1_fm": x1_l[c], "xn1_fm": xn1_l[c], "att_fm": att1_l[c],
            "wo": wo, "wg": wg_t, "wu": wu_t, "wd": wd_t,
        })
    res = run_bass_kernel_spmd(nc, in_maps, core_ids=list(range(NCORES)))
    return res.results




def kernel(**inputs):
    x = np.ascontiguousarray(np.asarray(inputs["L0_attn_x"], np.float32).reshape(B * S, D))
    xs_fm = [np.ascontiguousarray(x[c * T:(c + 1) * T].T) for c in range(NCORES)]

    r1 = run_launch1(inputs, xs_fm)
    ckv_full = [np.ascontiguousarray(np.concatenate([r1[4 * s + j]["ckv_fm"] for j in range(4)], axis=1)) for s in range(2)]
    kr_full = [np.ascontiguousarray(np.concatenate([r1[4 * s + j]["kr_fm"] for j in range(4)], axis=1)) for s in range(2)]
    cq_full = [np.ascontiguousarray(np.concatenate([r1[4 * s + j]["cq_fm"] for j in range(4)], axis=1)) for s in range(2)]

    r2 = run_launch2(inputs, ckv_full, kr_full, cq_full)
    att_seq = [np.concatenate([np.asarray(r2[4 * s + j]["attn"]).reshape(512, S) for j in range(4)], axis=0)
               for s in range(2)]
    att_l = [np.ascontiguousarray(att_seq[c // 4][:, (c % 4) * T:(c % 4 + 1) * T]) for c in range(NCORES)]
    xn_l = [r1[c]["xn_fm"] for c in range(NCORES)]
    log_l = [r1[c]["logits"] for c in range(NCORES)]

    r3 = run_launch3(inputs, xs_fm, xn_l, att_l, log_l)
    x1_l = [np.ascontiguousarray(r3[c]["x1_fm"]) for c in range(NCORES)]

    r4 = run_launch4(inputs, x1_l)
    q_full = [np.ascontiguousarray(np.concatenate([np.asarray(r4[4 * s + j]["q_fm"]) for j in range(4)], axis=1)) for s in range(2)]
    k_full = [np.ascontiguousarray(np.concatenate([np.asarray(r4[4 * s + j]["k_fm"]) for j in range(4)], axis=1)) for s in range(2)]
    v_full = [np.ascontiguousarray(np.concatenate([np.asarray(r4[4 * s + j]["v_tm"]) for j in range(4)], axis=0)) for s in range(2)]
    xn1_l = [r4[c]["xn1_fm"] for c in range(NCORES)]

    r5 = run_launch5(inputs, q_full, k_full, v_full)
    att1_seq = [np.concatenate([np.asarray(r5[4 * s + j]["attn"]).reshape(512, S) for j in range(4)], axis=0)
                for s in range(2)]
    att1_l = [np.ascontiguousarray(att1_seq[c // 4][:, (c % 4) * T:(c % 4 + 1) * T]) for c in range(NCORES)]

    r6 = run_launch6(inputs, x1_l, xn1_l, att1_l)
    x2 = np.concatenate([np.asarray(r6[c]["x2_fm"], np.float32).T for c in range(NCORES)], axis=0)
    return np.ascontiguousarray(x2.reshape(B, S, D))
